# version 01
# baseline (speedup 1.0000x reference)
import math
from contextlib import ExitStack
import numpy as np

import concourse.bass as bass
import concourse.mybir as mybir
from concourse.bass_utils import run_bass_kernel_spmd
from concourse.ordered_engine_set import OrderedEngineSet

F32 = mybir.dt.float32
BF16 = mybir.dt.bfloat16
I32 = mybir.dt.int32
AF = mybir.ActivationFunctionType
ALU = mybir.AluOpType
ET = mybir.EngineType
AX = mybir.AxisListType.X

D = 4096
S = 4096
B = 2
NH = 32
KVH = 8
HD = 128
CD = 512
RD = 512
NQ = 1024
NBLK = 8
SCALE = 1.0 / math.sqrt(HD)
SHIFT = -30.0
NEG = -1.0e9
EPS = 1e-5
CAPACITY = float(int(3.0 * (B * S) / 8))

DUMP = None
PH = "ABCD"
RGN = 8


_TLAST = None


def _set_tlast(inst):
    global _TLAST
    _TLAST = inst


def _tlast():
    return _TLAST


class Sem:
    def __init__(self, h):
        self.h = h
        self.v = 0

    def inc(self, inst, n=1):
        self.v += n
        inst.then_inc(self.h, n)


def _build():
    nc = bass.Bass("TRN2", target_bir_lowering=False)
    dt = nc.dram_tensor
    meta = dt("meta", [1, 1], I32, kind="ExternalInput").ap()
    xb = dt("xb", [S, D], F32, kind="ExternalInput").ap()
    xq = dt("xq", [NQ, D], F32, kind="ExternalInput").ap()
    wqT = dt("wqT", [D, D], F32, kind="ExternalInput").ap()
    wkrT = dt("wkrT", [D, RD], F32, kind="ExternalInput").ap()
    wdkT = dt("wdkT", [D, CD], F32, kind="ExternalInput").ap()
    wupkT = dt("wupkT", [CD, 512], F32, kind="ExternalInput").ap()
    wupvT = dt("wupvT", [CD, 1024], F32, kind="ExternalInput").ap()
    woT = dt("woT", [D, D], F32, kind="ExternalInput").ap()
    wffT = dt("wffT", [D, D], F32, kind="ExternalInput").ap()
    lAT = dt("lAT", [D, 128], F32, kind="ExternalInput").ap()
    lB = dt("lB", [128, D], F32, kind="ExternalInput").ap()
    gw1T = dt("gw1T", [D, 32], F32, kind="ExternalInput").ap()
    gw2aug = dt("gw2aug", [33, 8], F32, kind="ExternalInput").ap()
    gcons = dt("gcons", [128, 96], F32, kind="ExternalInput").ap()
    triD = dt("tri", [128, 128], F32, kind="ExternalInput").ap()
    identD = dt("ident", [128, 128], BF16, kind="ExternalInput").ap()
    identfD = dt("identf", [128, 128], F32, kind="ExternalInput").ap()
    e16D = dt("e16", [8, 128], F32, kind="ExternalInput").ap()
    onescolD = dt("onescol", [128, 1], F32, kind="ExternalInput").ap()
    onesrowD = dt("onesrow", [1, 128], F32, kind="ExternalInput").ap()
    cosqD = dt("cosqD", [128, NQ], F32, kind="ExternalInput").ap()
    sinqD = dt("sinqD", [128, NQ], F32, kind="ExternalInput").ap()
    coskD = dt("coskD", [128, S], F32, kind="ExternalInput").ap()
    sinkD = dt("sinkD", [128, S], F32, kind="ExternalInput").ap()
    out = dt("out", [NQ, D], F32, kind="ExternalOutput").ap()

    scrK = dt("scrK", [KVH, 128, S], BF16).ap()
    scrV = dt("scrV", [S, KVH * HD], BF16).ap()
    scrXfm = dt("scrXfm", [32, 128, S], BF16).ap()
    scrAv = dt("scrAv", [NQ, D], BF16).ap()
    scrY2 = dt("scrY2", [NQ, D], F32).ap()
    cntI = dt("cntI", [1, 16], F32).ap()
    cntO = dt("cntO", [1, 16], F32).ap()

    dbg = {}
    if DUMP == "A":
        dbg["dK"] = dt("dK", [KVH, 128, S], BF16, kind="ExternalOutput").ap()
        dbg["dV"] = dt("dV", [S, KVH * HD], BF16, kind="ExternalOutput").ap()
        dbg["dX"] = dt("dX", [32, 128, S], BF16, kind="ExternalOutput").ap()
    if DUMP == "B":
        dbg["dAv"] = dt("dAv", [NQ, D], BF16, kind="ExternalOutput").ap()
    if DUMP == "C":
        dbg["dY2"] = dt("dY2", [NQ, D], F32, kind="ExternalOutput").ap()
        dbg["dSt"] = dt("dSt", [128, 8, 4], F32, kind="ExternalOutput").ap()

    sems = {}

    def sem(name):
        if name not in sems:
            sems[name] = Sem(nc.semaphore(name).__enter__())
        return sems[name]

    al = nc.alloc_sbuf_tensor
    cst = al("cst", [128, 8], F32).ap()
    tri = al("tri_sb", [128, 128], F32).ap()
    idb = al("id_sb", [128, 128], BF16).ap()
    idf = al("idf_sb", [128, 128], F32).ap()
    e16 = al("e16_sb", [8, 128], BF16).ap()
    ones_c = al("ones_c", [128, 1], BF16).ap()
    ones_r = al("ones_r", [1, 128], BF16).ap()
    gcons_sb = al("gcons_sb", [128, 96], F32).ap()
    gw2_sb = al("gw2_sb", [33, 8], BF16).ap()
    cosq = al("cosq", [128, NQ], F32).ap()
    sinq = al("sinq", [128, NQ], F32).ap()
    stC = al("stC", [128, 8, 4], F32).ap()

    V = nc.vector
    Sc = nc.scalar
    G = nc.gpsimd
    Sy = nc.sync

    class _TProxy:
        def __init__(self, eng):
            self._e = eng

        def matmul(self, *a, **k):
            global _TLAST
            inst = self._e.matmul(*a, **k)
            _set_tlast(inst)
            return inst

        def transpose(self, *a, **k):
            inst = self._e.transpose(*a, **k)
            _set_tlast(inst)
            return inst

        def __getattr__(self, name):
            return getattr(self._e, name)

    T = _TProxy(nc.tensor)

    pro = sem("pro")
    prog = sem("prog")
    for dst_, src_ in ((tri, triD), (gcons_sb, gcons), (cosq, cosqD), (sinq, sinqD),
                       (idf, identfD)):
        Sy.dma_start(out=dst_, in_=src_).then_inc(pro.h, 16)
        pro.v += 16
    for dst_, src_ in ((idb, identD), (e16, e16D), (ones_c, onescolD),
                       (ones_r, onesrowD), (gw2_sb, gw2aug)):
        G.dma_start(out=dst_, in_=src_).then_inc(prog.h, 16)
        prog.v += 16
    cstg = sem("cstg")
    V.memset(cst[:, 0:1], SHIFT)
    V.memset(cst[:, 1:2], EPS)
    V.memset(cst[:, 2:3], -1.0)
    V.memset(cst[:, 3:4], CAPACITY)
    V.memset(cst[:, 4:5], 1.0)
    V.engine_nop().then_inc(cstg.h, 1)
    cstg.v += 1

    regs = nc.alloc_registers(
        "jreg", OrderedEngineSet([ET.SP, ET.Activation, ET.PE, ET.DVE, ET.Pool]))
    for r in regs:
        nc.reg_load(r, meta.to_broadcast((1, 1)))
    sv = nc.snap(regs, min_val=0, max_val=3)

    nc.all_engine_barrier()

    if "A" in PH:
        es = ExitStack()
        with es:
            def sb(name, shape, dtyp):
                return es.enter_context(nc.sbuf_tensor(name, shape, dtyp)).ap()

            def ps(name, shape, dtyp):
                return es.enter_context(nc.psum_tensor(name, shape, dtyp)).ap()

            wkr = sb("wkr_sb", [128, 32, RD], BF16)
            wdk = sb("wdk_sb", [128, 32, CD], BF16)
            wupk = sb("wupk_sb", [128, 4, 512], BF16)
            wupv = sb("wupv_sb", [128, 4, 1024], BF16)
            xt = sb("xt_sb", [128, 2, D], BF16)
            xln = sb("xln_sb", [128, 2, D], BF16)
            xfm = sb("xfm_sb", [128, 32, 512], BF16)
            ckv = sb("ckv_sb", [128, 4, 512], BF16)
            krp = sb("krp_sb", [128, 2, 512], BF16)
            kc = sb("kc_sb", [128, 2, 512], BF16)
            vst = sb("vst_sb", [128, 2, 1024], BF16)
            rk1 = sb("rk1_sb", [128, 512], F32)
            rk2 = sb("rk2_sb", [128, 512], F32)
            rk3 = sb("rk3_sb", [128, 512], F32)
            coskb = sb("cosk_sb", [128, 2, 512], F32)
            sinkb = sb("sink_sb", [128, 2, 512], F32)
            lns = sb("lns_sb", [128, 16], F32)
            trash = sb("trash_sb", [128, D], BF16)
            ptr = ps("ptr_ps", [128, 16, 128], BF16)
            prj = ps("prj_ps", [128, 2, 512], F32)
            vps = ps("v_ps", [128, 4, 512], F32)

            aw = sem("aw"); act = sem("act")
            axp = [sem("ax0"), sem("ax1")]
            aln = sem("aln"); alg = sem("alg"); aap = sem("aap")
            atp = sem("atp"); aev = sem("aev"); amm = sem("amm")
            akr = sem("akr"); ack = sem("ack"); akc = sem("akc")
            avv = sem("avv"); axw = sem("axw")
            akw = [sem("akw0"), sem("akw1")]
            kcwp = [sem("kcw0"), sem("kcw1")]
            avw = [sem("avw0"), sem("avw1")]
            actp = [sem("act0"), sem("act1")]

            prj_rota = []

            def issue_x(i):
                if i >= 2:
                    G.wait_ge(aap.h, i - 1)
                G.dma_start(out=xt[:, i % 2, :],
                            in_=xb[i * 128:(i + 1) * 128, :]).then_inc(axp[i % 2].h, 16)
                axp[i % 2].v += 16

            def issue_ct(b):
                if b >= 2:
                    Sy.wait_ge(akr.h, 4 * (b - 1))
                Sy.dma_start(out=coskb[:, b % 2, :],
                             in_=coskD[:, b * 512:(b + 1) * 512]).then_inc(actp[b % 2].h, 16)
                Sy.dma_start(out=sinkb[:, b % 2, :],
                             in_=sinkD[:, b * 512:(b + 1) * 512]).then_inc(actp[b % 2].h, 16)
                actp[b % 2].v += 32

            issue_x(0); issue_x(1)
            G.dma_start(out=wkr, in_=wkrT.rearrange("(c p) f -> p c f", p=128)).then_inc(aw.h, 16)
            G.dma_start(out=wdk, in_=wdkT.rearrange("(c p) f -> p c f", p=128)).then_inc(aw.h, 16)
            G.dma_start(out=wupk, in_=wupkT.rearrange("(c p) f -> p c f", p=128)).then_inc(aw.h, 16)
            G.dma_start(out=wupv, in_=wupvT.rearrange("(c p) f -> p c f", p=128)).then_inc(aw.h, 16)
            aw.v = 64
            issue_ct(0); issue_ct(1)

            for blk in range(NBLK):
                for t in range(4):
                    i = blk * 4 + t
                    bi = i % 2
                    if i + 2 < 4 * NBLK:
                        issue_x(i + 2)
                    V.wait_ge(axp[i % 2].h, (i // 2 + 1) * 16)
                    V.scalar_tensor_tensor(trash, xt[:, bi, :], 1.0, xt[:, bi, :],
                                           ALU.mult, ALU.mult,
                                           accum_out=lns[:, 1:2]).then_inc(alg.h, 1)
                    alg.v += 1
                    V.reduce_sum(lns[:, 0:1], xt[:, bi, :], axis=AX).then_inc(alg.h, 1)
                    alg.v += 1
                    V.wait_ge(alg.h, alg.v)
                    V.tensor_scalar_mul(lns[:, 2:3], lns[:, 0:1], 1.0 / D).then_inc(alg.h, 1)
                    alg.v += 1
                    V.wait_ge(alg.h, alg.v)
                    V.scalar_tensor_tensor(lns[:, 3:4], lns[:, 2:3], 1.0,
                                           lns[:, 2:3], ALU.mult, ALU.mult).then_inc(alg.h, 1)
                    alg.v += 1
                    V.wait_ge(alg.h, alg.v)
                    V.tensor_scalar(lns[:, 4:5], lns[:, 1:2], 1.0 / D,
                                    lns[:, 3:4], ALU.mult, ALU.subtract).then_inc(aln.h, 1)
                    aln.v += 1
                    Sc.wait_ge(aln.h, aln.v)
                    Sc.wait_ge(cstg.h, 1)
                    Sc.activation(lns[:, 5:6], lns[:, 4:5], AF.Sqrt,
                                  bias=cst[:, 1:2], scale=1.0).then_inc(alg.h, 1)
                    alg.v += 1
                    V.wait_ge(alg.h, alg.v)
                    V.reciprocal(lns[:, 6:7], lns[:, 5:6]).then_inc(alg.h, 1)
                    alg.v += 1
                    V.wait_ge(alg.h, alg.v)
                    V.scalar_tensor_tensor(lns[:, 7:8], lns[:, 2:3], -1.0,
                                           lns[:, 6:7], ALU.mult, ALU.mult).then_inc(aln.h, 1)
                    aln.v += 1
                    Sc.wait_ge(aln.h, aln.v)
                    if i >= 2:
                        Sc.wait_ge(atp.h, 4 * (i - 1))
                    Sc.activation(xln[:, bi, :], xt[:, bi, :], AF.Identity,
                                  bias=lns[:, 7:8], scale=lns[:, 6:7]).then_inc(aap.h, 1)
                    aap.v += 1
                    T.wait_ge(aap.h, aap.v)
                    if i == 0:
                        T.wait_ge(pro.h, pro.v)
                        T.wait_ge(prog.h, prog.v)
                    for c8 in range(4):
                        bk = c8 % 2
                        if atp.v >= 2:
                            T.wait_ge(aev.h, atp.v - 1)
                        for cc in range(8):
                            c = c8 * 8 + cc
                            T.transpose(ptr[:, bk * 8 + cc, :],
                                        xln[:, bi, c * 128:(c + 1) * 128], idb)
                        _tlast().then_inc(atp.h, 1); atp.v += 1
                        V.wait_ge(atp.h, atp.v)
                        if blk > 0 and t == 0 and c8 == 0:
                            V.wait_ge(axw.h, blk * 16)
                        V.tensor_copy(
                            xfm[:, c8 * 8:(c8 + 1) * 8, t * 128:(t + 1) * 128],
                            ptr[:, bk * 8:bk * 8 + 8, :]).then_inc(aev.h, 1)
                        aev.v += 1

                Sy.wait_ge(aev.h, aev.v)
                Sy.dma_start(
                    out=scrXfm[:, :, blk * 512:(blk + 1) * 512].rearrange("c p t -> p c t"),
                    in_=xfm).then_inc(axw.h, 16)
                axw.v += 16

                T.wait_ge(aev.h, aev.v)
                if blk == 0:
                    T.wait_ge(aw.h, 64)
                for m in range(4):
                    bk = len(prj_rota) % 2
                    if len(prj_rota) >= 2:
                        T.wait_ge(*prj_rota[-2])
                    for c in range(32):
                        T.matmul(prj[:, bk, :], wkr[:, c, m * 128:(m + 1) * 128],
                                 xfm[:, c, :], start=(c == 0), stop=(c == 31))
                    _tlast().then_inc(amm.h, 1); amm.v += 1
                    V.wait_ge(amm.h, amm.v)
                    V.wait_ge(actp[blk % 2].h, (blk // 2 + 1) * 32)
                    V.wait_ge(akr.h, akr.v)
                    if akr.v >= 2:
                        n = akr.v
                        V.wait_ge(akw[n % 2].h, 32 * (n // 2))
                    for (d0, d1), (s0, s1) in (((0, 32), (32, 64)), ((32, 64), (0, 32)),
                                               ((64, 96), (96, 128)), ((96, 128), (64, 96))):
                        V.tensor_copy(rk1[d0:d1, :], prj[s0:s1, bk, :]).then_inc(alg.h, 1)
                        alg.v += 1
                    V.tensor_mul(rk2, prj[:, bk, :], coskb[:, blk % 2, :]).then_inc(alg.h, 1)
                    alg.v += 1
                    V.wait_ge(alg.h, alg.v)
                    V.tensor_mul(rk3, rk1, sinkb[:, blk % 2, :]).then_inc(alg.h, 1)
                    alg.v += 1
                    V.wait_ge(alg.h, alg.v)
                    V.tensor_add(krp[:, m % 2, :], rk2, rk3).then_inc(akr.h, 1)
                    akr.v += 1
                    prj_rota.append((akr.h, akr.v))
                    Sy.wait_ge(akr.h, akr.v)
                    n = akr.v - 1
                    Sy.dma_start(out=scrK[2 * m, 0:64, blk * 512:(blk + 1) * 512],
                                 in_=krp[0:64, m % 2, :]).then_inc(akw[n % 2].h, 16)
                    Sy.dma_start(out=scrK[2 * m + 1, 0:64, blk * 512:(blk + 1) * 512],
                                 in_=krp[64:128, m % 2, :]).then_inc(akw[n % 2].h, 16)
                    akw[n % 2].v += 32

                if blk + 2 < NBLK:
                    issue_ct(blk + 2)

                for m in range(4):
                    bk = len(prj_rota) % 2
                    T.wait_ge(*prj_rota[-2])
                    for c in range(32):
                        T.matmul(prj[:, bk, :], wdk[:, c, m * 128:(m + 1) * 128],
                                 xfm[:, c, :], start=(c == 0), stop=(c == 31))
                    _tlast().then_inc(amm.h, 1); amm.v += 1
                    V.wait_ge(amm.h, amm.v)
                    V.tensor_copy(ckv[:, m, :], prj[:, bk, :]).then_inc(ack.h, 1)
                    ack.v += 1
                    prj_rota.append((ack.h, ack.v))

                T.wait_ge(ack.h, ack.v)
                for m in range(4):
                    bk = len(prj_rota) % 2
                    T.wait_ge(*prj_rota[-2])
                    for c2 in range(4):
                        T.matmul(prj[:, bk, :], wupk[:, c2, m * 128:(m + 1) * 128],
                                 ckv[:, c2, :], start=(c2 == 0), stop=(c2 == 3))
                    _tlast().then_inc(amm.h, 1); amm.v += 1
                    V.wait_ge(amm.h, amm.v)
                    if akc.v >= 2:
                        n = akc.v
                        V.wait_ge(kcwp[n % 2].h, 32 * (n // 2))
                    V.tensor_copy(kc[:, m % 2, :], prj[:, bk, :]).then_inc(akc.h, 1)
                    akc.v += 1
                    prj_rota.append((akc.h, akc.v))
                    Sy.wait_ge(akc.h, akc.v)
                    n = akc.v - 1
                    Sy.dma_start(out=scrK[2 * m, 64:128, blk * 512:(blk + 1) * 512],
                                 in_=kc[0:64, m % 2, :]).then_inc(kcwp[n % 2].h, 16)
                    Sy.dma_start(out=scrK[2 * m + 1, 64:128, blk * 512:(blk + 1) * 512],
                                 in_=kc[64:128, m % 2, :]).then_inc(kcwp[n % 2].h, 16)
                    kcwp[n % 2].v += 32

                for t in range(4):
                    pr = t % 2
                    if t >= 2:
                        T.wait_ge(avv.h, avv.v - 2)
                    for half in range(2):
                        for c2 in range(4):
                            T.matmul(vps[:, 2 * pr + half, :],
                                     ckv[:, c2, t * 128:(t + 1) * 128],
                                     wupv[:, c2, half * 512:(half + 1) * 512],
                                     start=(c2 == 0), stop=(c2 == 3))
                    _tlast().then_inc(amm.h, 1); amm.v += 1
                    V.wait_ge(amm.h, amm.v)
                    nuse = avv.v // 2
                    if nuse >= 2:
                        V.wait_ge(avw[nuse % 2].h, 16 * (nuse // 2))
                    V.tensor_copy(vst[:, pr, 0:512], vps[:, 2 * pr, :]).then_inc(avv.h, 1)
                    avv.v += 1
                    V.tensor_copy(vst[:, pr, 512:1024], vps[:, 2 * pr + 1, :]).then_inc(avv.h, 1)
                    avv.v += 1
                    Sy.wait_ge(avv.h, avv.v)
                    n = avv.v // 2 - 1
                    Sy.dma_start(
                        out=scrV[blk * 512 + t * 128: blk * 512 + (t + 1) * 128, :],
                        in_=vst[:, pr, :]).then_inc(avw[n % 2].h, 16)
                    avw[n % 2].v += 16

            for s_ in (akw[0], akw[1], kcwp[0], kcwp[1], avw[0], avw[1]):
                Sy.wait_ge(s_.h, s_.v)
            Sy.wait_ge(axw.h, axw.v)
            G.wait_ge(axp[0].h, axp[0].v)
            G.wait_ge(axp[1].h, axp[1].v)

            if DUMP == "A":
                dd = sem("dd")
                Sy.dma_start(out=dbg["dK"], in_=scrK).then_inc(dd.h, 16)
                Sy.dma_start(out=dbg["dV"], in_=scrV).then_inc(dd.h, 16)
                Sy.dma_start(out=dbg["dX"], in_=scrXfm).then_inc(dd.h, 16)
                dd.v += 48
                Sy.wait_ge(dd.h, dd.v)

        nc.all_engine_barrier()

    if "B" in PH:
        es = ExitStack()
        with es:
            def sb(name, shape, dtyp):
                return es.enter_context(nc.sbuf_tensor(name, shape, dtyp)).ap()

            def ps(name, shape, dtyp):
                return es.enter_context(nc.psum_tensor(name, shape, dtyp)).ap()

            xfq = sb("xfq_sb", [128, 32, NQ], BF16)
            kgb = sb("kg_sb", [128, 2, S], BF16)
            vgb = sb("vg_sb", [128, 2, 32, 128], BF16)
            wqb = sb("wq_sb", [128, 2, 32, 128], BF16)
            qq = sb("qq_sb", [128, 2, NQ], BF16)
            qr1 = sb("qr1_sb", [128, 512], F32)
            qr2 = sb("qr2_sb", [128, 512], F32)
            qr3 = sb("qr3_sb", [128, 512], F32)
            pp = sb("p_sb", [128, 3, 512], BF16)
            pss = sb("pss_sb", [128, NQ], F32)
            den = sb("den_sb", [128, 16], F32)
            avf = sb("avf_sb", [128, NQ], BF16)
            avt = sb("avt_sb", [128, 2, 8, 128], BF16)
            pq = ps("pq_ps", [128, 2, 512], F32)
            sc = ps("sc_ps", [128, 2, 512], F32)
            av = ps("av_ps", [128, 2, 512], F32)
            at = ps("at_ps", [128, 8, 128], BF16)

            names = ["bxq", "bkv0", "bkv1", "bwq0", "bwq1", "bqm", "brp", "brg",
                     "bsc", "bmk", "bex", "bps", "bavm", "bpt", "bdn", "brc",
                     "bevc", "bavt", "bscl", "bao0", "bao1"]
            bs = {n: sem(n) for n in names}
            snap0 = {n: bs[n].v for n in names}
            deltas = []

            for jb in nc.Switch(engines=[Sy, Sc, nc.tensor, V, G], index=[sv] * 5, n=4):
                for n in names:
                    bs[n].v = snap0[n]
                (bxq, bkv0, bkv1, bwq0, bwq1, bqm, brp, brg, bsc, bmk, bex,
                 bps, bavm, bpt, bdn, brc, bevc, bavt, bscl, bao0, bao1) = (
                    bs[n] for n in names)
                bkvp = [bkv0, bkv1]
                bwqp = [bwq0, bwq1]
                baop = [bao0, bao1]
                base = snap0

                blocks = (jb, 7 - jb)
                spans = []
                for b in blocks:
                    sl = [(kt, 512, 0, False) for kt in range(4 * b)]
                    sl += [(4 * b + d, 512 - 128 * d, 128 * d, True) for d in range(4)]
                    spans.append(sl)

                for h2, b in enumerate(blocks):
                    Sy.dma_start(
                        out=xfq[:, :, h2 * 512:(h2 + 1) * 512],
                        in_=scrXfm[:, :, b * 512:(b + 1) * 512].rearrange("c p t -> p c t"),
                    ).then_inc(bxq.h, 16)
                    bxq.v += 16

                def load_kv(g):
                    if g >= 2:
                        Sy.wait_ge(bavm.h, base["bavm"] + 36 * 4 * (g - 1))
                    s_ = bkvp[g % 2]
                    Sy.dma_start(out=kgb[:, g % 2, :], in_=scrK[g]).then_inc(s_.h, 16)
                    Sy.dma_start(
                        out=vgb[:, g % 2, :, :],
                        in_=scrV[:, g * 128:(g + 1) * 128].rearrange("(c p) h -> p c h", p=128),
                    ).then_inc(s_.h, 16)
                    s_.v += 32

                def load_wq(h):
                    if h >= 2:
                        G.wait_ge(bqm.h, base["bqm"] + 2 * (h - 1))
                    s_ = bwqp[h % 2]
                    G.dma_start(
                        out=wqb[:, h % 2, :, :],
                        in_=wqT[:, h * 128:(h + 1) * 128].rearrange("(c p) f -> p c f", p=128),
                    ).then_inc(s_.h, 16)
                    s_.v += 16

                load_kv(0)
                load_wq(0); load_wq(1)

                for h in range(NH):
                    g = h // 4
                    if h % 4 == 0 and g + 1 < KVH:
                        load_kv(g + 1)
                    if h + 2 < NH:
                        load_wq(h + 2)

                    T.wait_ge(bwqp[h % 2].h, base["bwq%d" % (h % 2)] + (h // 2 + 1) * 16)
                    if h == 0:
                        T.wait_ge(bxq.h, base["bxq"] + 32)
                    T.wait_ge(brp.h, base["brp"] + h)
                    for span in range(2):
                        for c in range(32):
                            T.matmul(pq[:, span, :], wqb[:, h % 2, c, :],
                                     xfq[:, c, span * 512:(span + 1) * 512],
                                     start=(c == 0), stop=(c == 31))
                        _tlast().then_inc(bqm.h, 1); bqm.v += 1

                    for span in range(2):
                        V.wait_ge(bqm.h, bqm.v - 1 + span)
                        V.wait_ge(brg.h, brg.v)
                        for (d0, d1), (s0, s1) in (((0, 32), (32, 64)), ((32, 64), (0, 32)),
                                                   ((64, 96), (96, 128)), ((96, 128), (64, 96))):
                            V.tensor_copy(qr1[d0:d1, :], pq[s0:s1, span, :]).then_inc(brg.h, 1)
                            brg.v += 1
                        V.tensor_mul(qr2, pq[:, span, :],
                                     cosq[:, span * 512:(span + 1) * 512]).then_inc(brg.h, 1)
                        brg.v += 1
                        V.wait_ge(brg.h, brg.v)
                        V.tensor_mul(qr3, qr1,
                                     sinq[:, span * 512:(span + 1) * 512]).then_inc(brg.h, 1)
                        brg.v += 1
                        V.wait_ge(brg.h, brg.v)
                        if span == 0:
                            V.tensor_add(qq[:, h % 2, 0:512], qr2, qr3).then_inc(brg.h, 1)
                            brg.v += 1
                        else:
                            V.tensor_add(qq[:, h % 2, 512:1024], qr2, qr3).then_inc(brp.h, 1)
                            brp.v += 1

                    T.wait_ge(brp.h, base["brp"] + h + 1)
                    T.wait_ge(bkvp[g % 2].h, base["bkv%d" % (g % 2)] + (g // 2 + 1) * 32)
                    if h >= 1:
                        T.wait_ge(bdn.h, base["bdn"] + 8 * h)
                    base_sc = bsc.v; base_ex = bex.v
                    base_av = bavm.v; base_ps = bps.v
                    slot = 0
                    nslots = sum(len(sl) for sl in spans)
                    for span, sl in enumerate(spans):
                        for si, (kt, w, qoff, diag) in enumerate(sl):
                            bk = slot % 2
                            pb = slot % 3
                            if slot >= 2:
                                T.wait_ge(bex.h, base_ex + slot - 1)
                            if si == 0 and h >= 1:
                                T.wait_ge(bevc.h, base["bevc"] + 2 * (h - 1) + span + 1)
                            T.matmul(
                                sc[:, bk, 0:w],
                                kgb[:, g % 2, kt * 128:(kt + 1) * 128],
                                qq[:, h % 2, span * 512 + qoff: span * 512 + qoff + w],
                                start=True, stop=True).then_inc(bsc.h, 1)
                            bsc.v += 1
                            if diag:
                                V.wait_ge(bsc.h, base_sc + slot + 1)
                                V.tensor_add(sc[:, bk, 0:128], sc[:, bk, 0:128],
                                             tri).then_inc(bmk.h, 1)
                                bmk.v += 1
                            Sc.wait_ge(bmk.h if diag else bsc.h,
                                       bmk.v if diag else base_sc + slot + 1)
                            if slot >= 3:
                                Sc.wait_ge(bavm.h, base_av + slot - 2)
                                Sc.wait_ge(bps.h, base_ps + slot - 2)
                            Sc.activation(pp[:, pb, 0:w], sc[:, bk, 0:w], AF.Exp,
                                          bias=cst[:, 0:1], scale=SCALE).then_inc(bex.h, 1)
                            bex.v += 1
                            V.wait_ge(bex.h, base_ex + slot + 1)
                            V.wait_ge(bps.h, bps.v)
                            pcols = pss[:, span * 512 + qoff: span * 512 + qoff + w]
                            if si == 0:
                                if span == 0 and h >= 1:
                                    V.wait_ge(bpt.h, base["bpt"] + h)
                                V.tensor_copy(pcols, pp[:, pb, 0:w]).then_inc(bps.h, 1)
                            else:
                                V.tensor_add(pcols, pcols, pp[:, pb, 0:w]).then_inc(bps.h, 1)
                            bps.v += 1
                            T.wait_ge(bex.h, base_ex + slot + 1)
                            T.matmul(av[:, span, qoff:qoff + w],
                                     vgb[:, g % 2, kt, :], pp[:, pb, 0:w],
                                     start=(si == 0), stop=(si == len(sl) - 1)
                                     ).then_inc(bavm.h, 1)
                            bavm.v += 1
                            slot += 1
                        V.wait_ge(bavm.h, base_av + slot)
                        if h >= 1:
                            V.wait_ge(bavt.h, base["bavt"] + h)
                        V.tensor_copy(avf[:, span * 512:(span + 1) * 512],
                                      av[:, span, :]).then_inc(bevc.h, 1)
                        bevc.v += 1

                    T.wait_ge(bex.h, base_ex + nslots)
                    T.wait_ge(bps.h, base_ps + nslots)
                    for s8 in range(8):
                        T.transpose(sc[:, s8 // 4, (s8 % 4) * 128:(s8 % 4 + 1) * 128],
                                    pss[:, s8 * 128:(s8 + 1) * 128], idf)
                    _tlast().then_inc(bpt.h, 1); bpt.v += 1
                    V.wait_ge(bpt.h, bpt.v)
                    for s8 in range(8):
                        V.reduce_sum(den[:, s8:s8 + 1],
                                     sc[:, s8 // 4, (s8 % 4) * 128:(s8 % 4 + 1) * 128],
                                     axis=AX).then_inc(bdn.h, 1)
                        bdn.v += 1
                    V.wait_ge(bdn.h, bdn.v)
                    V.reciprocal(den[:, 8:16], den[:, 0:8]).then_inc(brc.h, 1)
                    brc.v += 1

                    T.wait_ge(bevc.h, base["bevc"] + 2 * (h + 1))
                    if h >= 1:
                        T.wait_ge(bscl.h, base["bscl"] + 8 * h)
                    for s8 in range(8):
                        T.transpose(at[:, s8, :], avf[:, s8 * 128:(s8 + 1) * 128], idb)
                    _tlast().then_inc(bavt.h, 1); bavt.v += 1
                    V.wait_ge(bavt.h, bavt.v)
                    V.wait_ge(brc.h, brc.v)
                    if h >= 2:
                        V.wait_ge(baop[h % 2].h,
                                  base["bao%d" % (h % 2)] + 16 * ((h - 2) // 2 + 1))
                    for s8 in range(8):
                        V.tensor_scalar(avt[:, h % 2, s8, :], at[:, s8, :],
                                        den[:, 8 + s8:9 + s8], None,
                                        ALU.mult).then_inc(bscl.h, 1)
                        bscl.v += 1
                    Sy.wait_ge(bscl.h, bscl.v)
                    Sy.dma_start(
                        out=scrAv[:, h * 128:(h + 1) * 128].rearrange("(s p) f -> p s f", p=128),
                        in_=avt[:, h % 2, :, :]).then_inc(baop[h % 2].h, 16)
                    baop[h % 2].v += 16

                Sy.wait_ge(bao0.h, bao0.v)
                Sy.wait_ge(bao1.h, bao1.v)
                G.wait_ge(bwq0.h, bwq0.v)
                G.wait_ge(bwq1.h, bwq1.v)
                deltas.append({n: bs[n].v - snap0[n] for n in names})

            assert all(d == deltas[0] for d in deltas[1:]), "switch bodies differ"
            for n in names:
                bs[n].v = snap0[n] + deltas[0][n]

            nc.all_engine_barrier()

            if DUMP == "B":
                dd = sem("dd2")
                Sy.dma_start(out=dbg["dAv"], in_=scrAv).then_inc(dd.h, 16)
                dd.v += 16
                Sy.wait_ge(dd.h, dd.v)
                nc.all_engine_barrier()

    if "C" in PH:
        es = ExitStack()
        with es:
            def sb(name, shape, dtyp):
                return es.enter_context(nc.sbuf_tensor(name, shape, dtyp)).ap()

            def ps(name, shape, dtyp):
                return es.enter_context(nc.psum_tensor(name, shape, dtyp)).ap()

            avT = sb("avT_sb", [128, 32, NQ], BF16)
            wo = sb("wo_sb", [128, 2, 32, 128], BF16)
            ybf = sb("ybf_sb", [128, 2, NQ], BF16)
            xqr = sb("xqr_sb", [128, 2, 8, 128], F32)
            y2t = sb("y2t_sb", [128, 2, 8, 128], F32)
            sumc = sb("sumc_sb", [128, 8, 32], F32)
            ssqc = sb("ssqc_sb", [128, 8, 32], F32)
            ctr = sb("ctrash_sb", [128, 128], BF16)
            py = ps("py_ps", [128, 4, 512], F32)
            pt = ps("pt_ps", [128, 2, 8, 128], BF16)

            cav = sem("cav")
            cwop = [sem("cwo0"), sem("cwo1")]
            cxqp = [sem("cxq0"), sem("cxq1")]
            cswp = [sem("csw0"), sem("csw1")]
            cym = sem("cym"); cev = sem("cev"); ctp = sem("ctp")
            cy2 = sem("cy2"); cg = sem("cg")

            for c in range(32):
                Sy.dma_start_transpose(avT[:, c, :],
                                       scrAv[:, c * 128:(c + 1) * 128]).then_inc(cav.h, 16)
                cav.v += 16

            def load_wo(m):
                s_ = cwop[m % 2]
                G.dma_start(
                    out=wo[:, m % 2, :, :],
                    in_=woT[:, m * 128:(m + 1) * 128].rearrange("(c p) f -> p c f", p=128),
                ).then_inc(s_.h, 16)
                s_.v += 16

            def load_xq(m):
                s_ = cxqp[m % 2]
                Sy.dma_start(
                    out=xqr[:, m % 2, :, :],
                    in_=xq[:, m * 128:(m + 1) * 128].rearrange("(s p) f -> p s f", p=128),
                ).then_inc(s_.h, 16)
                s_.v += 16

            load_wo(0); load_wo(1)
            load_xq(0); load_xq(1)

            for m in range(32):
                T.wait_ge(cwop[m % 2].h, (m // 2 + 1) * 16)
                if m == 0:
                    T.wait_ge(cav.h, cav.v)
                if m >= 2:
                    T.wait_ge(cev.h, 2 * (m - 1))
                for span in range(2):
                    for c in range(32):
                        T.matmul(py[:, 2 * (m % 2) + span, :], wo[:, m % 2, c, :],
                                 avT[:, c, span * 512:(span + 1) * 512],
                                 start=(c == 0), stop=(c == 31))
                _tlast().then_inc(cym.h, 1); cym.v += 1
                if m + 2 < 32:
                    G.wait_ge(cym.h, m + 1)
                    load_wo(m + 2)
                V.wait_ge(cym.h, cym.v)
                V.tensor_copy(ybf[:, m % 2, 0:512], py[:, 2 * (m % 2), :]).then_inc(cev.h, 1)
                cev.v += 1
                V.tensor_copy(ybf[:, m % 2, 512:1024], py[:, 2 * (m % 2) + 1, :]).then_inc(cev.h, 1)
                cev.v += 1
                T.wait_ge(cev.h, cev.v)
                if m >= 2:
                    T.wait_ge(cy2.h, 24 * (m - 1))
                for s in range(8):
                    T.transpose(pt[:, m % 2, s, :],
                                ybf[:, m % 2, s * 128:(s + 1) * 128], idb)
                _tlast().then_inc(ctp.h, 1); ctp.v += 1
                V.wait_ge(ctp.h, ctp.v)
                V.wait_ge(cxqp[m % 2].h, (m // 2 + 1) * 16)
                if m >= 2:
                    V.wait_ge(cswp[m % 2].h, 16 * ((m - 2) // 2 + 1))
                for s in range(8):
                    V.tensor_add(y2t[:, m % 2, s, :], pt[:, m % 2, s, :],
                                 xqr[:, m % 2, s, :]).then_inc(cy2.h, 1)
                    cy2.v += 1
                    V.wait_ge(cy2.h, cy2.v)
                    V.reduce_sum(sumc[:, s, m:m + 1], y2t[:, m % 2, s, :],
                                 axis=AX).then_inc(cy2.h, 1)
                    cy2.v += 1
                    V.scalar_tensor_tensor(ctr, y2t[:, m % 2, s, :], 1.0,
                                           y2t[:, m % 2, s, :], ALU.mult, ALU.mult,
                                           accum_out=ssqc[:, s, m:m + 1]).then_inc(cy2.h, 1)
                    cy2.v += 1
                if m + 2 < 32:
                    Sy.wait_ge(cy2.h, 24 * (m + 1))
                    load_xq(m + 2)
                Sy.wait_ge(cy2.h, cy2.v)
                Sy.dma_start(
                    out=scrY2[:, m * 128:(m + 1) * 128].rearrange("(s p) f -> p s f", p=128),
                    in_=y2t[:, m % 2, :, :]).then_inc(cswp[m % 2].h, 16)
                cswp[m % 2].v += 16

            V.wait_ge(cy2.h, cy2.v)
            for s in range(8):
                V.reduce_sum(stC[:, s, 0:1], sumc[:, s, :], axis=AX).then_inc(cg.h, 1)
                V.reduce_sum(stC[:, s, 1:2], ssqc[:, s, :], axis=AX).then_inc(cg.h, 1)
                cg.v += 2
            V.wait_ge(cg.h, cg.v)
            for s in range(8):
                V.tensor_scalar_mul(stC[:, s, 0:1], stC[:, s, 0:1], 1.0 / D).then_inc(cg.h, 1)
                V.tensor_scalar_mul(stC[:, s, 1:2], stC[:, s, 1:2], 1.0 / D).then_inc(cg.h, 1)
                cg.v += 2
            V.wait_ge(cg.h, cg.v)
            for s in range(8):
                V.scalar_tensor_tensor(stC[:, s, 2:3], stC[:, s, 0:1], 1.0,
                                       stC[:, s, 0:1], ALU.mult, ALU.mult).then_inc(cg.h, 1)
                cg.v += 1
            V.wait_ge(cg.h, cg.v)
            for s in range(8):
                V.tensor_tensor(stC[:, s, 1:2], stC[:, s, 1:2], stC[:, s, 2:3],
                                ALU.subtract).then_inc(cg.h, 1)
                cg.v += 1
            Sc.wait_ge(cg.h, cg.v)
            for s in range(8):
                Sc.activation(stC[:, s, 2:3], stC[:, s, 1:2], AF.Sqrt,
                              bias=cst[:, 1:2], scale=1.0).then_inc(cg.h, 1)
                cg.v += 1
            V.wait_ge(cg.h, cg.v)
            for s in range(8):
                V.reciprocal(stC[:, s, 3:4], stC[:, s, 2:3]).then_inc(cg.h, 1)
                cg.v += 1
            Sy.wait_ge(cswp[0].h, cswp[0].v)
            Sy.wait_ge(cswp[1].h, cswp[1].v)

            if DUMP == "C":
                dd = sem("dd3")
                Sy.dma_start(out=dbg["dY2"], in_=scrY2).then_inc(dd.h, 16)
                dd.v += 16
                Sy.wait_ge(cg.h, cg.v)
                Sy.dma_start(out=dbg["dSt"], in_=stC).then_inc(dd.h, 16)
                dd.v += 16
                Sy.wait_ge(dd.h, dd.v)

        nc.all_engine_barrier()

    if "D" in PH:
        es = ExitStack()
        with es:
            def sb(name, shape, dtyp):
                return es.enter_context(nc.sbuf_tensor(name, shape, dtyp)).ap()

            zfm = sb("zfm_sb", [128, 32, NQ], BF16)
            y2r = sb("y2r_sb", [128, 2, D], F32)
            ztm = sb("ztm_sb", [128, 2, D], BF16)
            gw1s = sb("gw1s_sb", [128, 32, 32], BF16)
            lAs = sb("lA_sb", [128, 32, 128], BF16)
            lBs = sb("lB_sb", [128, 32, 128], BF16)
            hraw = sb("hraw_sb", [32, NQ], F32)
            htm = sb("htm_sb", [128, 8, 32], F32)
            gst = sb("gst_sb", [128, 32], F32)
            gst2 = sb("gst2_sb", [128, 8], F32)
            g1a = sb("g1a_sb", [128, 8], F32)
            g2a = sb("g2a_sb", [128, 8], F32)
            hrel = sb("hrel_sb", [128, 32], F32)
            hrel2 = sb("hrel2_sb", [128, 32], F32)
            hrT = sb("hrT_sb", [33, 128], BF16)
            gts = sb("gts_sb", [128, 8, 8], F32)
            m1 = sb("m1_sb", [128, 8, 8], F32)
            m2 = sb("m2_sb", [128, 8, 8], F32)
            mc = sb("mc_sb", [128, 16], BF16)
            d21 = sb("d21_sb", [128, 8, 3], F32)
            wts = sb("wts_sb", [128, 8], F32)
            wts2 = sb("wts2_sb", [128, 8], F32)
            wT = sb("wT_sb", [8, NQ], BF16)
            cnt = sb("cnt_sb", [1, 16], F32)
            cnt2 = sb("cnt2_sb", [1, 16], F32)
            cntb = sb("cntb_sb", [1, 16], BF16)
            arep = sb("arep_sb", [128, 16], F32)
            tall = sb("tall_sb", [128, NQ], BF16)
            tp = sb("tp_sb", [128, NQ], BF16)
            wff = sb("wff_sb", [128, 2, 32, 128], BF16)
            y3bf = sb("y3bf_sb", [128, 2, NQ], BF16)
            y2rt = sb("y2rt_sb", [128, 2, 8, 128], F32)
            outt = sb("outt_sb", [128, 2, 8, 128], F32)
            dtr = sb("dtrash_sb", [128, 32], BF16)

            dld = sem("dld"); dzn = sem("dzn")
            dy2p = [sem("dy20"), sem("dy21")]
            dzt = sem("dzt"); dze = sem("dze"); dgm = sem("dgm")
            dgv = sem("dgv"); dgg = sem("dgg"); dcm = sem("dcm")
            dt1 = sem("dt1"); dwx = sem("dwx")
            dwfp = [sem("dwf0"), sem("dwf1")]
            dyep = [sem("dye0"), sem("dye1")]
            dstp = [sem("dst0"), sem("dst1")]
            dym = sem("dym"); dyt = sem("dyt"); dou = sem("dou")

            G.dma_start(out=gw1s, in_=gw1T.rearrange("(c p) f -> p c f", p=128)).then_inc(dld.h, 16)
            G.dma_start(out=lAs, in_=lAT.rearrange("(c p) f -> p c f", p=128)).then_inc(dld.h, 16)
            G.dma_start(out=lBs, in_=lB.rearrange("p (c f) -> p c f", f=128)).then_inc(dld.h, 16)
            dld.v += 48
            V.memset(hrT[32:33, :], 1.0)
            V.engine_nop().then_inc(dgg.h, 1); dgg.v += 1

            def load_y2(s):
                s_ = dy2p[s % 2]
                Sy.dma_start(out=y2r[:, s % 2, :],
                             in_=scrY2[s * 128:(s + 1) * 128, :]).then_inc(s_.h, 16)
                s_.v += 16

            load_y2(0); load_y2(1)

            es1 = ExitStack()
            with es1:
                def ps1(name, shape, dtyp):
                    return es1.enter_context(nc.psum_tensor(name, shape, dtyp)).ap()

                ztp = ps1("zt_ps", [128, 16, 128], BF16)
                pg = ps1("pg_ps", [32, 2, 512], F32)
                pt2 = ps1("pt2_ps", [128, 8, 32], F32)
                ptg1 = ps1("ptg1_ps", [32, 128], F32)
                pgt2 = ps1("pgt2_ps", [128, 32], F32)
                pcnt = ps1("pcnt_ps", [1, 16], F32)

                for s in range(8):
                    V.wait_ge(dy2p[s % 2].h, (s // 2 + 1) * 16)
                    if s >= 2:
                        V.wait_ge(dzt.h, 4 * (s - 1))
                    V.tensor_scalar(ztm[:, s % 2, :], y2r[:, s % 2, :],
                                    stC[:, s, 0:1], stC[:, s, 3:4],
                                    ALU.subtract, ALU.mult).then_inc(dzn.h, 1)
                    dzn.v += 1
                    if s + 2 < 8:
                        Sy.wait_ge(dzn.h, s + 1)
                        load_y2(s + 2)

                    T.wait_ge(dzn.h, dzn.v)
                    for c8 in range(4):
                        bk = c8 % 2
                        if dzt.v >= 2:
                            T.wait_ge(dze.h, dzt.v - 1)
                        for cc in range(8):
                            c = c8 * 8 + cc
                            T.transpose(ztp[:, bk * 8 + cc, :],
                                        ztm[:, s % 2, c * 128:(c + 1) * 128], idb)
                        _tlast().then_inc(dzt.h, 1); dzt.v += 1
                        V.wait_ge(dzt.h, dzt.v)
                        V.tensor_copy(zfm[:, c8 * 8:(c8 + 1) * 8, s * 128:(s + 1) * 128],
                                      ztp[:, bk * 8:bk * 8 + 8, :]).then_inc(dze.h, 1)
                        dze.v += 1

                T.wait_ge(dze.h, dze.v)
                T.wait_ge(dld.h, dld.v)
                for span in range(2):
                    for c in range(32):
                        T.matmul(pg[:, span, :], gw1s[:, c, :],
                                 zfm[:, c, span * 512:(span + 1) * 512],
                                 start=(c == 0), stop=(c == 31))
                _tlast().then_inc(dgm.h, 1); dgm.v += 1
                V.wait_ge(dgm.h, dgm.v)
                V.tensor_copy(hraw[:, 0:512], pg[:, 0, :]).then_inc(dgg.h, 1)
                dgg.v += 1
                V.tensor_copy(hraw[:, 512:1024], pg[:, 1, :]).then_inc(dgg.h, 1)
                dgg.v += 1

                T.wait_ge(dgg.h, dgg.v)
                for s in range(8):
                    T.transpose(pt2[:, s, :], hraw[:, s * 128:(s + 1) * 128], idf[0:32, 0:32])
                _tlast().then_inc(dgm.h, 1); dgm.v += 1
                V.wait_ge(dgm.h, dgm.v)
                for s in range(8):
                    V.tensor_add(htm[:, s, :], pt2[:, s, :],
                                 gcons_sb[:, 0:32]).then_inc(dgg.h, 1)
                    dgg.v += 1
                V.wait_ge(dgg.h, dgg.v)
                for s in range(8):
                    V.reduce_sum(gst[:, s:s + 1], htm[:, s, :],
                                 axis=AX).then_inc(dgg.h, 1)
                    dgg.v += 1
                    V.wait_ge(dgg.h, dgg.v)
                    V.scalar_tensor_tensor(dtr, htm[:, s, :], 1.0, htm[:, s, :],
                                           ALU.mult, ALU.mult,
                                           accum_out=gst[:, 8 + s:9 + s]).then_inc(dgg.h, 1)
                    dgg.v += 1
                    V.wait_ge(dgg.h, dgg.v)
                V.tensor_scalar_mul(gst[:, 0:8], gst[:, 0:8], 1.0 / 32).then_inc(dgg.h, 1)
                dgg.v += 1
                V.tensor_scalar_mul(gst[:, 8:16], gst[:, 8:16], 1.0 / 32).then_inc(dgg.h, 1)
                dgg.v += 1
                V.wait_ge(dgg.h, dgg.v)
                V.scalar_tensor_tensor(gst2[:, 0:8], gst[:, 0:8], 1.0,
                                       gst[:, 0:8], ALU.mult, ALU.mult).then_inc(dgg.h, 1)
                dgg.v += 1
                V.wait_ge(dgg.h, dgg.v)
                V.tensor_tensor(gst[:, 8:16], gst[:, 8:16], gst2[:, 0:8],
                                ALU.subtract).then_inc(dgg.h, 1)
                dgg.v += 1
                Sc.wait_ge(dgg.h, dgg.v)
                Sc.activation(gst[:, 16:24], gst[:, 8:16], AF.Sqrt,
                              bias=cst[:, 1:2], scale=1.0).then_inc(dgg.h, 1)
                dgg.v += 1
                V.wait_ge(dgg.h, dgg.v)
                V.reciprocal(gst[:, 24:32], gst[:, 16:24]).then_inc(dgg.h, 1)
                dgg.v += 1

                for s in range(8):
                    V.wait_ge(dgg.h, dgg.v)
                    V.wait_ge(dgv.h, dgv.v)
                    V.tensor_scalar(hrel, htm[:, s, :], gst[:, s:s + 1],
                                    gst[:, 24 + s:25 + s], ALU.subtract,
                                    ALU.mult).then_inc(dgv.h, 1)
                    dgv.v += 1
                    V.wait_ge(dgv.h, dgv.v)
                    V.scalar_tensor_tensor(hrel2, hrel, 1.0, gcons_sb[:, 32:64],
                                           ALU.mult, ALU.mult).then_inc(dgv.h, 1)
                    dgv.v += 1
                    V.wait_ge(dgv.h, dgv.v)
                    V.scalar_tensor_tensor(hrel, hrel2, 1.0, gcons_sb[:, 64:96],
                                           ALU.mult, ALU.add).then_inc(dgv.h, 1)
                    dgv.v += 1
                    V.wait_ge(dgv.h, dgv.v)
                    V.tensor_scalar_max(hrel2, hrel, 0.0).then_inc(dgv.h, 1)
                    dgv.v += 1
                    T.wait_ge(dgv.h, dgv.v)
                    T.transpose(ptg1, hrel2, idf).then_inc(dgm.h, 1); dgm.v += 1
                    V.wait_ge(dgm.h, dgm.v)
                    V.tensor_copy(hrT[0:32, :], ptg1[0:32, :]).then_inc(dgv.h, 1)
                    dgv.v += 1
                    T.wait_ge(dgv.h, dgv.v)
                    T.matmul(pgt2[:, 0:8], hrT, gw2_sb, start=True, stop=True
                             ).then_inc(dgm.h, 1)
                    dgm.v += 1
                    V.wait_ge(dgm.h, dgm.v)
                    V.tensor_copy(gts[:, s, :], pgt2[:, 0:8]).then_inc(dgv.h, 1)
                    dgv.v += 1
                    V.wait_ge(dgv.h, dgv.v)
                    V.reduce_max(d21[:, s, 0:1], gts[:, s, :], axis=AX).then_inc(dgv.h, 1)
                    dgv.v += 1
                    V.wait_ge(dgv.h, dgv.v)
                    V.tensor_scalar(m1[:, s, :], gts[:, s, :], d21[:, s, 0:1],
                                    None, ALU.is_equal).then_inc(dgv.h, 1)
                    dgv.v += 1
                    V.wait_ge(dgv.h, dgv.v)
                    V.scalar_tensor_tensor(m2[:, s, :], m1[:, s, :], -1e30,
                                           gts[:, s, :], ALU.mult, ALU.add).then_inc(dgv.h, 1)
                    dgv.v += 1
                    V.wait_ge(dgv.h, dgv.v)
                    V.reduce_max(d21[:, s, 1:2], m2[:, s, :], axis=AX).then_inc(dgv.h, 1)
                    dgv.v += 1
                    V.wait_ge(dgv.h, dgv.v)
                    V.tensor_scalar(m2[:, s, :], m2[:, s, :], d21[:, s, 1:2],
                                    None, ALU.is_equal).then_inc(dgv.h, 1)
                    dgv.v += 1
                    V.tensor_tensor(d21[:, s, 2:3], d21[:, s, 1:2], d21[:, s, 0:1],
                                    ALU.subtract).then_inc(dgv.h, 1)
                    dgv.v += 1
                    V.wait_ge(dgv.h, dgv.v)
                    V.wait_ge(dcm.h, dcm.v)
                    V.tensor_copy(mc[:, 0:8], m1[:, s, :]).then_inc(dgv.h, 1)
                    dgv.v += 1
                    V.tensor_copy(mc[:, 8:16], m2[:, s, :]).then_inc(dgv.h, 1)
                    dgv.v += 1
                    T.wait_ge(dgv.h, dgv.v)
                    T.matmul(pcnt, ones_c, mc, start=(s == 0), stop=(s == 7)
                             ).then_inc(dcm.h, 1)
                    dcm.v += 1

                Sc.wait_ge(dgv.h, dgv.v)
                Sc.activation(g1a, d21[:, :, 2:3].rearrange("p s c -> p (s c)"),
                              AF.Sigmoid, bias=0.0, scale=cst[:, 2:3]).then_inc(dgg.h, 1)
                dgg.v += 1
                Sc.activation(g2a, d21[:, :, 2:3].rearrange("p s c -> p (s c)"),
                              AF.Sigmoid, bias=0.0, scale=1.0).then_inc(dgg.h, 1)
                dgg.v += 1

                dccs = sem("dccs"); dccc = sem("dccc"); dccg = sem("dccg")
                V.wait_ge(dcm.h, dcm.v)
                V.tensor_copy(cnt, pcnt).then_inc(dgv.h, 1)
                dgv.v += 1
                Sy.wait_ge(dgv.h, dgv.v)
                Sy.dma_start(out=cntI, in_=cnt).then_inc(dccs.h, 16)
                dccs.v += 16
                G.wait_ge(dccs.h, dccs.v)
                if RGN > 1:
                    G.collective_compute(
                        "AllReduce", ALU.add, replica_groups=[list(range(RGN))],
                        ins=[cntI], outs=[cntO]).then_inc(dccc.h, 1)
                    dccc.v += 1
                    G.wait_ge(dccc.h, dccc.v)
                    G.dma_start(out=cnt2, in_=cntO).then_inc(dccg.h, 16)
                else:
                    G.dma_start(out=cnt2, in_=cntI).then_inc(dccg.h, 16)
                dccg.v += 16
                V.wait_ge(dccg.h, dccg.v)
                V.tensor_scalar(cnt, cnt2, cst[0:1, 3:4], None,
                                ALU.is_le).then_inc(dgv.h, 1)
                dgv.v += 1
                V.wait_ge(dgv.h, dgv.v)
                V.tensor_copy(cntb, cnt).then_inc(dgv.h, 1)
                dgv.v += 1
                T.wait_ge(dgv.h, dgv.v)
                T.matmul(pgt2[:, 0:16], ones_r, cntb, start=True, stop=True
                         ).then_inc(dgm.h, 1)
                dgm.v += 1
                V.wait_ge(dgm.h, dgm.v)
                V.tensor_copy(arep, pgt2[:, 0:16]).then_inc(dgv.h, 1)
                dgv.v += 1

                for s in range(8):
                    V.wait_ge(dgg.h, dgg.v)
                    V.wait_ge(dgv.h, dgv.v)
                    V.tensor_scalar(wts, m1[:, s, :], g1a[:, s:s + 1],
                                    None, ALU.mult).then_inc(dgv.h, 1)
                    dgv.v += 1
                    V.tensor_scalar(wts2, m2[:, s, :], g2a[:, s:s + 1],
                                    None, ALU.mult).then_inc(dgv.h, 1)
                    dgv.v += 1
                    V.wait_ge(dgv.h, dgv.v)
                    V.tensor_mul(wts, wts, arep[:, 0:8]).then_inc(dgv.h, 1)
                    dgv.v += 1
                    V.tensor_mul(wts2, wts2, arep[:, 8:16]).then_inc(dgv.h, 1)
                    dgv.v += 1
                    V.wait_ge(dgv.h, dgv.v)
                    V.tensor_add(wts, wts, wts2).then_inc(dgv.h, 1)
                    dgv.v += 1
                    T.wait_ge(dgv.h, dgv.v)
                    T.transpose(ptg1[0:8, :], wts, idf).then_inc(dgm.h, 1)
                    dgm.v += 1
                    V.wait_ge(dgm.h, dgm.v)
                    V.tensor_copy(wT[:, s * 128:(s + 1) * 128],
                                  ptg1[0:8, :]).then_inc(dgv.h, 1)
                    dgv.v += 1

            es2 = ExitStack()
            with es2:
                def ps2(name, shape, dtyp):
                    return es2.enter_context(nc.psum_tensor(name, shape, dtyp)).ap()

                py3 = ps2("py3_ps", [128, 4, 512], F32)
                pt3 = ps2("pt3_ps", [128, 2, 8, 128], BF16)

                T.wait_ge(dld.h, dld.v)
                for span in range(2):
                    for c in range(32):
                        T.matmul(py3[:, span, :], lAs[:, c, :],
                                 zfm[:, c, span * 512:(span + 1) * 512],
                                 start=(c == 0), stop=(c == 31))
                _tlast().then_inc(dt1.h, 1); dt1.v += 1
                V.wait_ge(dt1.h, dt1.v)
                V.tensor_copy(tall[:, 0:512], py3[:, 0, :]).then_inc(dt1.h, 1)
                dt1.v += 1
                V.tensor_copy(tall[:, 512:1024], py3[:, 1, :]).then_inc(dt1.h, 1)
                dt1.v += 1

                T.wait_ge(dgv.h, dgv.v)
                T.wait_ge(dt1.h, dt1.v)
                for span in range(2):
                    T.matmul(py3[:, span, :], e16,
                             wT[:, span * 512:(span + 1) * 512],
                             start=True, stop=True)
                _tlast().then_inc(dwx.h, 1); dwx.v += 1
                V.wait_ge(dwx.h, dwx.v)
                V.tensor_mul(tp[:, 0:512], tall[:, 0:512], py3[:, 0, :]).then_inc(dwx.h, 1)
                dwx.v += 1
                V.tensor_mul(tp[:, 512:1024], tall[:, 512:1024],
                             py3[:, 1, :]).then_inc(dwx.h, 1)
                dwx.v += 1

                def load_wff(m):
                    s_ = dwfp[m % 2]
                    G.dma_start(
                        out=wff[:, m % 2, :, :],
                        in_=wffT[:, m * 128:(m + 1) * 128].rearrange("(c p) f -> p c f", p=128),
                    ).then_inc(s_.h, 16)
                    s_.v += 16

                def load_y2t(m):
                    s_ = dyep[m % 2]
                    Sy.dma_start(
                        out=y2rt[:, m % 2, :, :],
                        in_=scrY2[:, m * 128:(m + 1) * 128].rearrange("(s p) f -> p s f", p=128),
                    ).then_inc(s_.h, 16)
                    s_.v += 16

                load_wff(0); load_wff(1)
                load_y2t(0); load_y2t(1)

                T.wait_ge(dwx.h, dwx.v)
                for m in range(32):
                    T.wait_ge(dwfp[m % 2].h, (m // 2 + 1) * 16)
                    if m >= 2:
                        T.wait_ge(dym.h, 3 * (m - 1))
                    for span in range(2):
                        for c in range(32):
                            T.matmul(py3[:, 2 * (m % 2) + span, :], wff[:, m % 2, c, :],
                                     zfm[:, c, span * 512:(span + 1) * 512],
                                     start=(c == 0), stop=False)
                        T.matmul(py3[:, 2 * (m % 2) + span, :], lBs[:, m, :],
                                 tp[:, span * 512:(span + 1) * 512],
                                 start=False, stop=True)
                    _tlast().then_inc(dym.h, 1); dym.v += 1
                    if m + 2 < 32:
                        G.wait_ge(dym.h, 3 * m + 1)
                        load_wff(m + 2)
                    V.wait_ge(dym.h, dym.v)
                    V.tensor_copy(y3bf[:, m % 2, 0:512],
                                  py3[:, 2 * (m % 2), :]).then_inc(dym.h, 1)
                    dym.v += 1
                    V.tensor_copy(y3bf[:, m % 2, 512:1024],
                                  py3[:, 2 * (m % 2) + 1, :]).then_inc(dym.h, 1)
                    dym.v += 1
                    T.wait_ge(dym.h, dym.v)
                    if m >= 2:
                        T.wait_ge(dou.h, 8 * (m - 1))
                    for s in range(8):
                        T.transpose(pt3[:, m % 2, s, :],
                                    y3bf[:, m % 2, s * 128:(s + 1) * 128], idb)
                    _tlast().then_inc(dyt.h, 1); dyt.v += 1
                    V.wait_ge(dyt.h, dyt.v)
                    V.wait_ge(dyep[m % 2].h, (m // 2 + 1) * 16)
                    if m >= 2:
                        V.wait_ge(dstp[m % 2].h, 16 * ((m - 2) // 2 + 1))
                    for s in range(8):
                        V.tensor_add(outt[:, m % 2, s, :], pt3[:, m % 2, s, :],
                                     y2rt[:, m % 2, s, :]).then_inc(dou.h, 1)
                        dou.v += 1
                    if m + 2 < 32:
                        Sy.wait_ge(dou.h, 8 * (m + 1))
                        load_y2t(m + 2)
                    Sy.wait_ge(dou.h, dou.v)
                    Sy.dma_start(
                        out=out[:, m * 128:(m + 1) * 128].rearrange("(s p) f -> p s f", p=128),
                        in_=outt[:, m % 2, :, :]).then_inc(dstp[m % 2].h, 16)
                    dstp[m % 2].v += 16

                Sy.wait_ge(dstp[0].h, dstp[0].v)
                Sy.wait_ge(dstp[1].h, dstp[1].v)
                G.wait_ge(dwfp[0].h, dwfp[0].v)
                G.wait_ge(dwfp[1].h, dwfp[1].v)

    return nc, dbg




def _host_prep(inputs):
    x = np.ascontiguousarray(np.asarray(inputs["L0_attn_hidden_states"], dtype=np.float32))
    Wq = np.asarray(inputs["L0_attn_Wq"], dtype=np.float32)
    Wkr = np.asarray(inputs["L0_attn_Wkr"], dtype=np.float32)
    Wdk = np.asarray(inputs["L0_attn_Wdk"], dtype=np.float32)
    Wupk = np.asarray(inputs["L0_attn_Wupk"], dtype=np.float32)
    Wupv = np.asarray(inputs["L0_attn_Wupv"], dtype=np.float32)
    Wo = np.asarray(inputs["L0_attn_Wo"], dtype=np.float32)
    Wff = np.asarray(inputs["L0_ffn_weight"], dtype=np.float32)
    lora_A = np.asarray(inputs["L0_ffn_lora_A"], dtype=np.float32)
    lora_B = np.asarray(inputs["L0_ffn_lora_B"], dtype=np.float32)
    gw1 = np.asarray(inputs["L0_ffn_gw1"], dtype=np.float32)
    gb1 = np.asarray(inputs["L0_ffn_gb1"], dtype=np.float32)
    lng = np.asarray(inputs["L0_ffn_ln_gamma"], dtype=np.float32)
    lnb = np.asarray(inputs["L0_ffn_ln_beta"], dtype=np.float32)
    gw2 = np.asarray(inputs["L0_ffn_gw2"], dtype=np.float32)
    gb2 = np.asarray(inputs["L0_ffn_gb2"], dtype=np.float32)

    blockperm = np.concatenate([np.arange(0, 32), np.arange(64, 96),
                                np.arange(32, 64), np.arange(96, 128)])
    qperm = np.concatenate([h * 128 + blockperm for h in range(32)])

    C = np.ascontiguousarray
    common = {
        "wqT": C(Wq.T[:, qperm]),
        "wkrT": C(Wkr.T),
        "wdkT": C(Wdk.T),
        "wupkT": C(Wupk.T),
        "wupvT": C(Wupv.T),
        "woT": C(Wo.T),
        "wffT": C(Wff.T),
        "lAT": C(lora_A.transpose(1, 0, 2).reshape(D, 128)),
        "lB": C(lora_B.reshape(128, D)),
        "gw1T": C(gw1.T),
        "gw2aug": C(np.concatenate([gw2.T, gb2[None, :]], axis=0)),
        "gcons": C(np.broadcast_to(
            np.concatenate([gb1, lng, lnb])[None, :], (128, 96)).copy()),
    }

    p = np.arange(128)
    common["tri"] = C(np.where(p[:, None] <= p[None, :], 0.0, NEG).astype(np.float32))
    import ml_dtypes
    common["ident"] = np.eye(128, dtype=ml_dtypes.bfloat16)
    common["identf"] = np.eye(128, dtype=np.float32)
    e16m = np.zeros((8, 128), dtype=np.float32)
    for e in range(8):
        e16m[e, e * 16:(e + 1) * 16] = 1.0 / 16.0
    common["e16"] = e16m
    common["onescol"] = np.ones((128, 1), dtype=np.float32)
    common["onesrow"] = np.ones((1, 128), dtype=np.float32)

    invf = (10000.0 ** (-np.arange(0, HD, 2, dtype=np.float32) / HD))
    pos = np.arange(S, dtype=np.float32)
    i32 = np.arange(32)
    qfreq = np.concatenate([invf[i32], invf[i32], invf[32 + i32], invf[32 + i32]])
    qsign = np.concatenate([-np.ones(32), np.ones(32), -np.ones(32), np.ones(32)])
    kfreq = np.concatenate([invf[i32]] * 4)
    ksign = np.concatenate([-np.ones(32), np.ones(32)] * 2)

    ang_k = kfreq[:, None] * pos[None, :]
    common["coskD"] = C(np.cos(ang_k).astype(np.float32))
    common["sinkD"] = C((np.sin(ang_k) * ksign[:, None]).astype(np.float32))

    per_core = []
    qpos_all = []
    for c in range(8):
        beta, j = c // 4, c % 4
        qpos = np.concatenate([np.arange(j * 512, (j + 1) * 512),
                               np.arange((7 - j) * 512, (8 - j) * 512)])
        qpos_all.append((beta, qpos))
        ang_q = qfreq[:, None] * qpos[None, :].astype(np.float32)
        d = dict(common)
        d["meta"] = np.array([[j]], dtype=np.int32)
        d["xb"] = C(x[beta])
        d["xq"] = C(x[beta][qpos])
        d["cosqD"] = C(np.cos(ang_q).astype(np.float32))
        d["sinqD"] = C((np.sin(ang_q) * qsign[:, None]).astype(np.float32))
        per_core.append(d)
    return per_core, qpos_all


def kernel(**inputs):
    per_core, qpos_all = _host_prep(inputs)
    nc, _ = _build()
    res = run_bass_kernel_spmd(nc, per_core, core_ids=list(range(8)))
    out = np.empty((B, S, D), dtype=np.float32)
    for c in range(8):
        beta, qpos = qpos_all[c]
        out[beta, qpos] = res.results[c]["out"]
    return out


# version 03
# speedup vs baseline: 1.0108x; 1.0108x over previous
import math
from contextlib import ExitStack
import numpy as np

import concourse.bass as bass
import concourse.mybir as mybir
from concourse.bass_utils import run_bass_kernel_spmd
from concourse.ordered_engine_set import OrderedEngineSet

F32 = mybir.dt.float32
BF16 = mybir.dt.bfloat16
I32 = mybir.dt.int32
AF = mybir.ActivationFunctionType
ALU = mybir.AluOpType
ET = mybir.EngineType
AX = mybir.AxisListType.X

D = 4096
S = 4096
B = 2
NH = 32
KVH = 8
HD = 128
CD = 512
RD = 512
NQ = 1024
NBLK = 8
SCALE = 1.0 / math.sqrt(HD)
SHIFT = -30.0
NEG = -1.0e9
EPS = 1e-5
CAPACITY = float(int(3.0 * (B * S) / 8))

DUMP = None
PH = "ABCD"
RGN = 8


_TLAST = None


def _set_tlast(inst):
    global _TLAST
    _TLAST = inst


def _tlast():
    return _TLAST


class Sem:
    def __init__(self, h):
        self.h = h
        self.v = 0

    def inc(self, inst, n=1):
        self.v += n
        inst.then_inc(self.h, n)


def _build():
    nc = bass.Bass("TRN2", target_bir_lowering=False)
    dt = nc.dram_tensor
    meta = dt("meta", [1, 1], I32, kind="ExternalInput").ap()
    xb = dt("xb", [S, D], F32, kind="ExternalInput").ap()
    xq = dt("xq", [NQ, D], F32, kind="ExternalInput").ap()
    wqT = dt("wqT", [D, D], F32, kind="ExternalInput").ap()
    wkrT = dt("wkrT", [D, RD], F32, kind="ExternalInput").ap()
    wdkT = dt("wdkT", [D, CD], F32, kind="ExternalInput").ap()
    wupkT = dt("wupkT", [CD, 512], F32, kind="ExternalInput").ap()
    wupvT = dt("wupvT", [CD, 1024], F32, kind="ExternalInput").ap()
    woT = dt("woT", [D, D], F32, kind="ExternalInput").ap()
    wffT = dt("wffT", [D, D], F32, kind="ExternalInput").ap()
    lAT = dt("lAT", [D, 128], F32, kind="ExternalInput").ap()
    lB = dt("lB", [128, D], F32, kind="ExternalInput").ap()
    gw1T = dt("gw1T", [D, 32], F32, kind="ExternalInput").ap()
    gw2aug = dt("gw2aug", [33, 8], F32, kind="ExternalInput").ap()
    gcons = dt("gcons", [128, 96], F32, kind="ExternalInput").ap()
    triD = dt("tri", [128, 128], F32, kind="ExternalInput").ap()
    identD = dt("ident", [128, 128], BF16, kind="ExternalInput").ap()
    identfD = dt("identf", [128, 128], F32, kind="ExternalInput").ap()
    e16D = dt("e16", [8, 128], F32, kind="ExternalInput").ap()
    onescolD = dt("onescol", [128, 1], F32, kind="ExternalInput").ap()
    onesrowD = dt("onesrow", [1, 128], F32, kind="ExternalInput").ap()
    cosqD = dt("cosqD", [128, NQ], F32, kind="ExternalInput").ap()
    sinqD = dt("sinqD", [128, NQ], F32, kind="ExternalInput").ap()
    coskD = dt("coskD", [128, S], F32, kind="ExternalInput").ap()
    sinkD = dt("sinkD", [128, S], F32, kind="ExternalInput").ap()
    out = dt("out", [NQ, D], F32, kind="ExternalOutput").ap()

    scrK = dt("scrK", [KVH, 128, S], BF16).ap()
    scrV = dt("scrV", [S, KVH * HD], BF16).ap()
    scrXfm = dt("scrXfm", [32, 128, S], BF16).ap()
    scrAv = dt("scrAv", [NQ, D], BF16).ap()
    scrY2 = dt("scrY2", [NQ, D], F32).ap()
    cntI = dt("cntI", [1, 16], F32).ap()
    cntO = dt("cntO", [1, 16], F32).ap()

    dbg = {}
    if DUMP == "A":
        dbg["dK"] = dt("dK", [KVH, 128, S], BF16, kind="ExternalOutput").ap()
        dbg["dV"] = dt("dV", [S, KVH * HD], BF16, kind="ExternalOutput").ap()
        dbg["dX"] = dt("dX", [32, 128, S], BF16, kind="ExternalOutput").ap()
    if DUMP == "B":
        dbg["dAv"] = dt("dAv", [NQ, D], BF16, kind="ExternalOutput").ap()
    if DUMP == "C":
        dbg["dY2"] = dt("dY2", [NQ, D], F32, kind="ExternalOutput").ap()
        dbg["dSt"] = dt("dSt", [128, 8, 4], F32, kind="ExternalOutput").ap()

    sems = {}

    def sem(name):
        if name not in sems:
            sems[name] = Sem(nc.semaphore(name).__enter__())
        return sems[name]

    al = nc.alloc_sbuf_tensor
    cst = al("cst", [128, 8], F32).ap()
    tri = al("tri_sb", [128, 128], F32).ap()
    idb = al("id_sb", [128, 128], BF16).ap()
    idf = al("idf_sb", [128, 128], F32).ap()
    e16 = al("e16_sb", [8, 128], BF16).ap()
    ones_c = al("ones_c", [128, 1], BF16).ap()
    ones_r = al("ones_r", [1, 128], BF16).ap()
    gcons_sb = al("gcons_sb", [128, 96], F32).ap()
    gw2_sb = al("gw2_sb", [33, 8], BF16).ap()
    cosq = al("cosq", [128, NQ], F32).ap()
    sinq = al("sinq", [128, NQ], F32).ap()
    stC = al("stC", [128, 8, 4], F32).ap()

    V = nc.vector
    Sc = nc.scalar
    G = nc.gpsimd
    Sy = nc.sync

    class _TProxy:
        def __init__(self, eng):
            self._e = eng

        def matmul(self, *a, **k):
            global _TLAST
            inst = self._e.matmul(*a, **k)
            _set_tlast(inst)
            return inst

        def transpose(self, *a, **k):
            inst = self._e.transpose(*a, **k)
            _set_tlast(inst)
            return inst

        def __getattr__(self, name):
            return getattr(self._e, name)

    T = _TProxy(nc.tensor)

    pro = sem("pro")
    prog = sem("prog")
    for dst_, src_ in ((tri, triD), (gcons_sb, gcons), (cosq, cosqD), (sinq, sinqD),
                       (idf, identfD)):
        Sy.dma_start(out=dst_, in_=src_).then_inc(pro.h, 16)
        pro.v += 16
    for dst_, src_ in ((idb, identD), (e16, e16D), (ones_c, onescolD),
                       (ones_r, onesrowD), (gw2_sb, gw2aug)):
        G.dma_start(out=dst_, in_=src_).then_inc(prog.h, 16)
        prog.v += 16
    cstg = sem("cstg")
    V.memset(cst[:, 0:1], SHIFT)
    V.memset(cst[:, 1:2], EPS)
    V.memset(cst[:, 2:3], -1.0)
    V.memset(cst[:, 3:4], CAPACITY)
    V.memset(cst[:, 4:5], 1.0)
    V.engine_nop().then_inc(cstg.h, 1)
    cstg.v += 1

    regs = nc.alloc_registers(
        "jreg", OrderedEngineSet([ET.SP, ET.Activation, ET.PE, ET.DVE, ET.Pool]))
    for r in regs:
        nc.reg_load(r, meta.to_broadcast((1, 1)))
    sv = nc.snap(regs, min_val=0, max_val=3)

    nc.all_engine_barrier()

    if "A" in PH:
        es = ExitStack()
        with es:
            def sb(name, shape, dtyp):
                return es.enter_context(nc.sbuf_tensor(name, shape, dtyp)).ap()

            def ps(name, shape, dtyp):
                return es.enter_context(nc.psum_tensor(name, shape, dtyp)).ap()

            wkr = sb("wkr_sb", [128, 32, RD], BF16)
            wdk = sb("wdk_sb", [128, 32, CD], BF16)
            wupk = sb("wupk_sb", [128, 4, 512], BF16)
            wupv = sb("wupv_sb", [128, 4, 1024], BF16)
            xt = sb("xt_sb", [128, 3, D], BF16)
            xln = sb("xln_sb", [128, 3, D], BF16)
            xfm = sb("xfm_sb", [128, 32, 512], BF16)
            ckv = sb("ckv_sb", [128, 4, 512], BF16)
            krp = sb("krp_sb", [128, 2, 512], BF16)
            kc = sb("kc_sb", [128, 2, 512], BF16)
            vst = sb("vst_sb", [128, 2, 1024], BF16)
            rk1 = sb("rk1_sb", [128, 512], F32)
            rk2 = sb("rk2_sb", [128, 512], F32)
            rk3 = sb("rk3_sb", [128, 512], F32)
            coskb = sb("cosk_sb", [128, 2, 512], F32)
            sinkb = sb("sink_sb", [128, 2, 512], F32)
            lns = sb("lns_sb", [128, 16], F32)
            trash = sb("trash_sb", [128, D], BF16)
            ptr = ps("ptr_ps", [128, 16, 128], BF16)
            prj = ps("prj_ps", [128, 2, 512], F32)
            vps = ps("v_ps", [128, 4, 512], F32)

            aw = sem("aw"); act = sem("act")
            axp = [sem("ax0"), sem("ax1"), sem("ax2")]
            aln = sem("aln"); alg = sem("alg"); aap = sem("aap")
            atp = sem("atp"); aev = sem("aev"); amm = sem("amm")
            akr = sem("akr"); ack = sem("ack"); akc = sem("akc")
            avv = sem("avv"); axw = sem("axw")
            akw = [sem("akw0"), sem("akw1")]
            kcwp = [sem("kcw0"), sem("kcw1")]
            avw = [sem("avw0"), sem("avw1")]
            actp = [sem("act0"), sem("act1")]

            prj_rota = []

            def issue_x(i):
                if i >= 3:
                    G.wait_ge(aap.h, i - 2)
                G.dma_start(out=xt[:, i % 3, :],
                            in_=xb[i * 128:(i + 1) * 128, :]).then_inc(axp[i % 3].h, 16)
                axp[i % 3].v += 16

            def issue_ct(b):
                if b >= 2:
                    Sy.wait_ge(akr.h, 4 * (b - 1))
                Sy.dma_start(out=coskb[:, b % 2, :],
                             in_=coskD[:, b * 512:(b + 1) * 512]).then_inc(actp[b % 2].h, 16)
                Sy.dma_start(out=sinkb[:, b % 2, :],
                             in_=sinkD[:, b * 512:(b + 1) * 512]).then_inc(actp[b % 2].h, 16)
                actp[b % 2].v += 32

            issue_x(0); issue_x(1); issue_x(2)
            G.dma_start(out=wkr, in_=wkrT.rearrange("(c p) f -> p c f", p=128)).then_inc(aw.h, 16)
            G.dma_start(out=wdk, in_=wdkT.rearrange("(c p) f -> p c f", p=128)).then_inc(aw.h, 16)
            G.dma_start(out=wupk, in_=wupkT.rearrange("(c p) f -> p c f", p=128)).then_inc(aw.h, 16)
            G.dma_start(out=wupv, in_=wupvT.rearrange("(c p) f -> p c f", p=128)).then_inc(aw.h, 16)
            aw.v = 64
            issue_ct(0); issue_ct(1)

            for blk in range(NBLK):
                for t in range(4):
                    i = blk * 4 + t
                    bi = i % 3
                    if i + 3 < 4 * NBLK:
                        issue_x(i + 3)
                    V.wait_ge(axp[i % 3].h, (i // 3 + 1) * 16)
                    V.scalar_tensor_tensor(trash, xt[:, bi, :], 1.0, xt[:, bi, :],
                                           ALU.mult, ALU.mult,
                                           accum_out=lns[:, 1:2]).then_inc(alg.h, 1)
                    alg.v += 1
                    V.reduce_sum(lns[:, 0:1], xt[:, bi, :], axis=AX).then_inc(alg.h, 1)
                    alg.v += 1
                    V.wait_ge(alg.h, alg.v)
                    V.tensor_scalar_mul(lns[:, 2:3], lns[:, 0:1], 1.0 / D).then_inc(alg.h, 1)
                    alg.v += 1
                    V.wait_ge(alg.h, alg.v)
                    V.scalar_tensor_tensor(lns[:, 3:4], lns[:, 2:3], 1.0,
                                           lns[:, 2:3], ALU.mult, ALU.mult).then_inc(alg.h, 1)
                    alg.v += 1
                    V.wait_ge(alg.h, alg.v)
                    V.tensor_scalar(lns[:, 4:5], lns[:, 1:2], 1.0 / D,
                                    lns[:, 3:4], ALU.mult, ALU.subtract).then_inc(aln.h, 1)
                    aln.v += 1
                    Sc.wait_ge(aln.h, aln.v)
                    Sc.wait_ge(cstg.h, 1)
                    Sc.activation(lns[:, 5:6], lns[:, 4:5], AF.Sqrt,
                                  bias=cst[:, 1:2], scale=1.0).then_inc(alg.h, 1)
                    alg.v += 1
                    V.wait_ge(alg.h, alg.v)
                    V.reciprocal(lns[:, 6:7], lns[:, 5:6]).then_inc(alg.h, 1)
                    alg.v += 1
                    V.wait_ge(alg.h, alg.v)
                    V.scalar_tensor_tensor(lns[:, 7:8], lns[:, 2:3], -1.0,
                                           lns[:, 6:7], ALU.mult, ALU.mult).then_inc(aln.h, 1)
                    aln.v += 1
                    Sc.wait_ge(aln.h, aln.v)
                    if i >= 3:
                        Sc.wait_ge(atp.h, 4 * (i - 2))
                    Sc.activation(xln[:, bi, :], xt[:, bi, :], AF.Identity,
                                  bias=lns[:, 7:8], scale=lns[:, 6:7]).then_inc(aap.h, 1)
                    aap.v += 1
                    T.wait_ge(aap.h, aap.v)
                    if i == 0:
                        T.wait_ge(pro.h, pro.v)
                        T.wait_ge(prog.h, prog.v)
                    for c8 in range(4):
                        bk = c8 % 2
                        if atp.v >= 2:
                            T.wait_ge(aev.h, atp.v - 1)
                        for cc in range(8):
                            c = c8 * 8 + cc
                            T.transpose(ptr[:, bk * 8 + cc, :],
                                        xln[:, bi, c * 128:(c + 1) * 128], idb)
                        _tlast().then_inc(atp.h, 1); atp.v += 1
                        V.wait_ge(atp.h, atp.v)
                        if blk > 0 and t == 0 and c8 == 0:
                            V.wait_ge(axw.h, blk * 16)
                        V.tensor_copy(
                            xfm[:, c8 * 8:(c8 + 1) * 8, t * 128:(t + 1) * 128],
                            ptr[:, bk * 8:bk * 8 + 8, :]).then_inc(aev.h, 1)
                        aev.v += 1

                Sy.wait_ge(aev.h, aev.v)
                Sy.dma_start(
                    out=scrXfm[:, :, blk * 512:(blk + 1) * 512].rearrange("c p t -> p c t"),
                    in_=xfm).then_inc(axw.h, 16)
                axw.v += 16

                T.wait_ge(aev.h, aev.v)
                if blk == 0:
                    T.wait_ge(aw.h, 64)
                for m in range(4):
                    bk = len(prj_rota) % 2
                    if len(prj_rota) >= 2:
                        T.wait_ge(*prj_rota[-2])
                    for c in range(32):
                        T.matmul(prj[:, bk, :], wkr[:, c, m * 128:(m + 1) * 128],
                                 xfm[:, c, :], start=(c == 0), stop=(c == 31))
                    _tlast().then_inc(amm.h, 1); amm.v += 1
                    V.wait_ge(amm.h, amm.v)
                    V.wait_ge(actp[blk % 2].h, (blk // 2 + 1) * 32)
                    V.wait_ge(akr.h, akr.v)
                    if akr.v >= 2:
                        n = akr.v
                        V.wait_ge(akw[n % 2].h, 32 * (n // 2))
                    for (d0, d1), (s0, s1) in (((0, 32), (32, 64)), ((32, 64), (0, 32)),
                                               ((64, 96), (96, 128)), ((96, 128), (64, 96))):
                        V.tensor_copy(rk1[d0:d1, :], prj[s0:s1, bk, :]).then_inc(alg.h, 1)
                        alg.v += 1
                    V.tensor_mul(rk2, prj[:, bk, :], coskb[:, blk % 2, :]).then_inc(alg.h, 1)
                    alg.v += 1
                    V.wait_ge(alg.h, alg.v)
                    V.tensor_mul(rk3, rk1, sinkb[:, blk % 2, :]).then_inc(alg.h, 1)
                    alg.v += 1
                    V.wait_ge(alg.h, alg.v)
                    V.tensor_add(krp[:, m % 2, :], rk2, rk3).then_inc(akr.h, 1)
                    akr.v += 1
                    prj_rota.append((akr.h, akr.v))
                    Sy.wait_ge(akr.h, akr.v)
                    n = akr.v - 1
                    Sy.dma_start(out=scrK[2 * m, 0:64, blk * 512:(blk + 1) * 512],
                                 in_=krp[0:64, m % 2, :]).then_inc(akw[n % 2].h, 16)
                    Sy.dma_start(out=scrK[2 * m + 1, 0:64, blk * 512:(blk + 1) * 512],
                                 in_=krp[64:128, m % 2, :]).then_inc(akw[n % 2].h, 16)
                    akw[n % 2].v += 32

                if blk + 2 < NBLK:
                    issue_ct(blk + 2)

                for m in range(4):
                    bk = len(prj_rota) % 2
                    T.wait_ge(*prj_rota[-2])
                    for c in range(32):
                        T.matmul(prj[:, bk, :], wdk[:, c, m * 128:(m + 1) * 128],
                                 xfm[:, c, :], start=(c == 0), stop=(c == 31))
                    _tlast().then_inc(amm.h, 1); amm.v += 1
                    V.wait_ge(amm.h, amm.v)
                    V.tensor_copy(ckv[:, m, :], prj[:, bk, :]).then_inc(ack.h, 1)
                    ack.v += 1
                    prj_rota.append((ack.h, ack.v))

                T.wait_ge(ack.h, ack.v)
                for m in range(4):
                    bk = len(prj_rota) % 2
                    T.wait_ge(*prj_rota[-2])
                    for c2 in range(4):
                        T.matmul(prj[:, bk, :], wupk[:, c2, m * 128:(m + 1) * 128],
                                 ckv[:, c2, :], start=(c2 == 0), stop=(c2 == 3))
                    _tlast().then_inc(amm.h, 1); amm.v += 1
                    V.wait_ge(amm.h, amm.v)
                    if akc.v >= 2:
                        n = akc.v
                        V.wait_ge(kcwp[n % 2].h, 32 * (n // 2))
                    V.tensor_copy(kc[:, m % 2, :], prj[:, bk, :]).then_inc(akc.h, 1)
                    akc.v += 1
                    prj_rota.append((akc.h, akc.v))
                    Sy.wait_ge(akc.h, akc.v)
                    n = akc.v - 1
                    Sy.dma_start(out=scrK[2 * m, 64:128, blk * 512:(blk + 1) * 512],
                                 in_=kc[0:64, m % 2, :]).then_inc(kcwp[n % 2].h, 16)
                    Sy.dma_start(out=scrK[2 * m + 1, 64:128, blk * 512:(blk + 1) * 512],
                                 in_=kc[64:128, m % 2, :]).then_inc(kcwp[n % 2].h, 16)
                    kcwp[n % 2].v += 32

                for t in range(4):
                    pr = t % 2
                    if t >= 2:
                        T.wait_ge(avv.h, avv.v - 2)
                    for half in range(2):
                        for c2 in range(4):
                            T.matmul(vps[:, 2 * pr + half, :],
                                     ckv[:, c2, t * 128:(t + 1) * 128],
                                     wupv[:, c2, half * 512:(half + 1) * 512],
                                     start=(c2 == 0), stop=(c2 == 3))
                    _tlast().then_inc(amm.h, 1); amm.v += 1
                    V.wait_ge(amm.h, amm.v)
                    nuse = avv.v // 2
                    if nuse >= 2:
                        V.wait_ge(avw[nuse % 2].h, 16 * (nuse // 2))
                    V.tensor_copy(vst[:, pr, 0:512], vps[:, 2 * pr, :]).then_inc(avv.h, 1)
                    avv.v += 1
                    V.tensor_copy(vst[:, pr, 512:1024], vps[:, 2 * pr + 1, :]).then_inc(avv.h, 1)
                    avv.v += 1
                    Sy.wait_ge(avv.h, avv.v)
                    n = avv.v // 2 - 1
                    Sy.dma_start(
                        out=scrV[blk * 512 + t * 128: blk * 512 + (t + 1) * 128, :],
                        in_=vst[:, pr, :]).then_inc(avw[n % 2].h, 16)
                    avw[n % 2].v += 16

            for s_ in (akw[0], akw[1], kcwp[0], kcwp[1], avw[0], avw[1]):
                Sy.wait_ge(s_.h, s_.v)
            Sy.wait_ge(axw.h, axw.v)
            G.wait_ge(axp[0].h, axp[0].v)
            G.wait_ge(axp[1].h, axp[1].v)

            if DUMP == "A":
                dd = sem("dd")
                Sy.dma_start(out=dbg["dK"], in_=scrK).then_inc(dd.h, 16)
                Sy.dma_start(out=dbg["dV"], in_=scrV).then_inc(dd.h, 16)
                Sy.dma_start(out=dbg["dX"], in_=scrXfm).then_inc(dd.h, 16)
                dd.v += 48
                Sy.wait_ge(dd.h, dd.v)

        nc.all_engine_barrier()

    if "B" in PH:
        es = ExitStack()
        with es:
            def sb(name, shape, dtyp):
                return es.enter_context(nc.sbuf_tensor(name, shape, dtyp)).ap()

            def ps(name, shape, dtyp):
                return es.enter_context(nc.psum_tensor(name, shape, dtyp)).ap()

            xfq = sb("xfq_sb", [128, 32, NQ], BF16)
            kgb = sb("kg_sb", [128, 2, S], BF16)
            vgb = sb("vg_sb", [128, 2, 32, 128], BF16)
            wqb = sb("wq_sb", [128, 2, 32, 128], BF16)
            qq = sb("qq_sb", [128, 2, NQ], BF16)
            qr1 = sb("qr1_sb", [128, 512], F32)
            qr2 = sb("qr2_sb", [128, 512], F32)
            qr3 = sb("qr3_sb", [128, 512], F32)
            pp = sb("p_sb", [128, 6, 512], BF16)
            pss = sb("pss_sb", [128, NQ], F32)
            den = sb("den_sb", [128, 16], F32)
            avf = sb("avf_sb", [128, NQ], BF16)
            avt = sb("avt_sb", [128, 2, 8, 128], BF16)
            sc = ps("sc_ps", [128, 4, 512], F32)
            av = ps("av_ps", [128, 2, 512], F32)
            at = ps("at_ps", [128, 8, 128], BF16)
            pq = sc

            names = ["bxq", "bkv0", "bkv1", "bwq0", "bwq1", "bqm", "brp", "brg",
                     "bsc", "bmk", "bex", "bps", "bavm", "bpt", "bdn", "brc",
                     "bevc", "bavt", "bscl", "bao0", "bao1"]
            bs = {n: sem(n) for n in names}
            snap0 = {n: bs[n].v for n in names}
            deltas = []

            for jb in nc.Switch(engines=[Sy, Sc, nc.tensor, V, G], index=[sv] * 5, n=4):
                for n in names:
                    bs[n].v = snap0[n]
                (bxq, bkv0, bkv1, bwq0, bwq1, bqm, brp, brg, bsc, bmk, bex,
                 bps, bavm, bpt, bdn, brc, bevc, bavt, bscl, bao0, bao1) = (
                    bs[n] for n in names)
                bkvp = [bkv0, bkv1]
                bwqp = [bwq0, bwq1]
                baop = [bao0, bao1]
                base = snap0

                blocks = (jb, 7 - jb)
                spans = []
                for b in blocks:
                    sl = [(kt, 512, 0, False) for kt in range(4 * b)]
                    sl += [(4 * b + d, 512 - 128 * d, 128 * d, True) for d in range(4)]
                    spans.append(sl)

                for h2, b in enumerate(blocks):
                    Sy.dma_start(
                        out=xfq[:, :, h2 * 512:(h2 + 1) * 512],
                        in_=scrXfm[:, :, b * 512:(b + 1) * 512].rearrange("c p t -> p c t"),
                    ).then_inc(bxq.h, 16)
                    bxq.v += 16

                def load_kv(g):
                    if g >= 2:
                        Sy.wait_ge(bavm.h, base["bavm"] + 36 * 4 * (g - 1))
                    s_ = bkvp[g % 2]
                    Sy.dma_start(out=kgb[:, g % 2, :], in_=scrK[g]).then_inc(s_.h, 16)
                    Sy.dma_start(
                        out=vgb[:, g % 2, :, :],
                        in_=scrV[:, g * 128:(g + 1) * 128].rearrange("(c p) h -> p c h", p=128),
                    ).then_inc(s_.h, 16)
                    s_.v += 32

                def load_wq(h):
                    if h >= 2:
                        G.wait_ge(bqm.h, base["bqm"] + 2 * (h - 1))
                    s_ = bwqp[h % 2]
                    G.dma_start(
                        out=wqb[:, h % 2, :, :],
                        in_=wqT[:, h * 128:(h + 1) * 128].rearrange("(c p) f -> p c f", p=128),
                    ).then_inc(s_.h, 16)
                    s_.v += 16

                load_kv(0)
                load_wq(0); load_wq(1)

                for h in range(NH):
                    g = h // 4
                    if h % 4 == 0 and g + 1 < KVH:
                        load_kv(g + 1)
                    if h + 2 < NH:
                        load_wq(h + 2)

                    T.wait_ge(bwqp[h % 2].h, base["bwq%d" % (h % 2)] + (h // 2 + 1) * 16)
                    if h == 0:
                        T.wait_ge(bxq.h, base["bxq"] + 32)
                    if h >= 1:
                        T.wait_ge(bdn.h, base["bdn"] + 8 * h)
                    for span in range(2):
                        for c in range(32):
                            T.matmul(pq[:, span, :], wqb[:, h % 2, c, :],
                                     xfq[:, c, span * 512:(span + 1) * 512],
                                     start=(c == 0), stop=(c == 31))
                        _tlast().then_inc(bqm.h, 1); bqm.v += 1

                    for span in range(2):
                        V.wait_ge(bqm.h, bqm.v - 1 + span)
                        V.wait_ge(brg.h, brg.v)
                        for (d0, d1), (s0, s1) in (((0, 32), (32, 64)), ((32, 64), (0, 32)),
                                                   ((64, 96), (96, 128)), ((96, 128), (64, 96))):
                            V.tensor_copy(qr1[d0:d1, :], pq[s0:s1, span, :]).then_inc(brg.h, 1)
                            brg.v += 1
                        V.tensor_mul(qr2, pq[:, span, :],
                                     cosq[:, span * 512:(span + 1) * 512]).then_inc(brg.h, 1)
                        brg.v += 1
                        V.wait_ge(brg.h, brg.v)
                        V.tensor_mul(qr3, qr1,
                                     sinq[:, span * 512:(span + 1) * 512]).then_inc(brg.h, 1)
                        brg.v += 1
                        V.wait_ge(brg.h, brg.v)
                        if span == 0:
                            V.tensor_add(qq[:, h % 2, 0:512], qr2, qr3).then_inc(brg.h, 1)
                            brg.v += 1
                        else:
                            V.tensor_add(qq[:, h % 2, 512:1024], qr2, qr3).then_inc(brp.h, 1)
                            brp.v += 1

                    T.wait_ge(brp.h, base["brp"] + h + 1)
                    T.wait_ge(bkvp[g % 2].h, base["bkv%d" % (g % 2)] + (g // 2 + 1) * 32)
                    base_sc = bsc.v; base_ex = bex.v
                    base_av = bavm.v; base_ps = bps.v
                    slot = 0
                    nslots = sum(len(sl) for sl in spans)
                    for span, sl in enumerate(spans):
                        for si, (kt, w, qoff, diag) in enumerate(sl):
                            bk = slot % 4
                            pb = slot % 6
                            if slot >= 4:
                                T.wait_ge(bex.h, base_ex + slot - 3)
                            if si == 0 and h >= 1:
                                T.wait_ge(bevc.h, base["bevc"] + 2 * (h - 1) + span + 1)
                            T.matmul(
                                sc[:, bk, 0:w],
                                kgb[:, g % 2, kt * 128:(kt + 1) * 128],
                                qq[:, h % 2, span * 512 + qoff: span * 512 + qoff + w],
                                start=True, stop=True).then_inc(bsc.h, 1)
                            bsc.v += 1
                            if diag:
                                V.wait_ge(bsc.h, base_sc + slot + 1)
                                V.tensor_add(sc[:, bk, 0:128], sc[:, bk, 0:128],
                                             tri).then_inc(bmk.h, 1)
                                bmk.v += 1
                            Sc.wait_ge(bmk.h if diag else bsc.h,
                                       bmk.v if diag else base_sc + slot + 1)
                            if slot >= 6:
                                Sc.wait_ge(bavm.h, base_av + slot - 5)
                                Sc.wait_ge(bps.h, base_ps + slot - 5)
                            Sc.activation(pp[:, pb, 0:w], sc[:, bk, 0:w], AF.Exp,
                                          bias=cst[:, 0:1], scale=SCALE).then_inc(bex.h, 1)
                            bex.v += 1
                            V.wait_ge(bex.h, base_ex + slot + 1)
                            V.wait_ge(bps.h, bps.v)
                            pcols = pss[:, span * 512 + qoff: span * 512 + qoff + w]
                            if si == 0:
                                if span == 0 and h >= 1:
                                    V.wait_ge(bpt.h, base["bpt"] + h)
                                V.tensor_copy(pcols, pp[:, pb, 0:w]).then_inc(bps.h, 1)
                            else:
                                V.tensor_add(pcols, pcols, pp[:, pb, 0:w]).then_inc(bps.h, 1)
                            bps.v += 1
                            T.wait_ge(bex.h, base_ex + slot + 1)
                            T.matmul(av[:, span, qoff:qoff + w],
                                     vgb[:, g % 2, kt, :], pp[:, pb, 0:w],
                                     start=(si == 0), stop=(si == len(sl) - 1)
                                     ).then_inc(bavm.h, 1)
                            bavm.v += 1
                            slot += 1
                        V.wait_ge(bavm.h, base_av + slot)
                        if h >= 1:
                            V.wait_ge(bavt.h, base["bavt"] + h)
                        V.tensor_copy(avf[:, span * 512:(span + 1) * 512],
                                      av[:, span, :]).then_inc(bevc.h, 1)
                        bevc.v += 1

                    T.wait_ge(bex.h, base_ex + nslots)
                    T.wait_ge(bps.h, base_ps + nslots)
                    for s8 in range(8):
                        T.transpose(sc[:, s8 // 2, (s8 % 2) * 128:(s8 % 2 + 1) * 128],
                                    pss[:, s8 * 128:(s8 + 1) * 128], idf)
                    _tlast().then_inc(bpt.h, 1); bpt.v += 1
                    V.wait_ge(bpt.h, bpt.v)
                    for s8 in range(8):
                        V.reduce_sum(den[:, s8:s8 + 1],
                                     sc[:, s8 // 2, (s8 % 2) * 128:(s8 % 2 + 1) * 128],
                                     axis=AX).then_inc(bdn.h, 1)
                        bdn.v += 1
                    V.wait_ge(bdn.h, bdn.v)
                    V.reciprocal(den[:, 8:16], den[:, 0:8]).then_inc(brc.h, 1)
                    brc.v += 1

                    T.wait_ge(bevc.h, base["bevc"] + 2 * (h + 1))
                    if h >= 1:
                        T.wait_ge(bscl.h, base["bscl"] + 8 * h)
                    for s8 in range(8):
                        T.transpose(at[:, s8, :], avf[:, s8 * 128:(s8 + 1) * 128], idb)
                    _tlast().then_inc(bavt.h, 1); bavt.v += 1
                    V.wait_ge(bavt.h, bavt.v)
                    V.wait_ge(brc.h, brc.v)
                    if h >= 2:
                        V.wait_ge(baop[h % 2].h,
                                  base["bao%d" % (h % 2)] + 16 * ((h - 2) // 2 + 1))
                    for s8 in range(8):
                        V.tensor_scalar(avt[:, h % 2, s8, :], at[:, s8, :],
                                        den[:, 8 + s8:9 + s8], None,
                                        ALU.mult).then_inc(bscl.h, 1)
                        bscl.v += 1
                    Sy.wait_ge(bscl.h, bscl.v)
                    Sy.dma_start(
                        out=scrAv[:, h * 128:(h + 1) * 128].rearrange("(s p) f -> p s f", p=128),
                        in_=avt[:, h % 2, :, :]).then_inc(baop[h % 2].h, 16)
                    baop[h % 2].v += 16

                Sy.wait_ge(bao0.h, bao0.v)
                Sy.wait_ge(bao1.h, bao1.v)
                G.wait_ge(bwq0.h, bwq0.v)
                G.wait_ge(bwq1.h, bwq1.v)
                deltas.append({n: bs[n].v - snap0[n] for n in names})

            assert all(d == deltas[0] for d in deltas[1:]), "switch bodies differ"
            for n in names:
                bs[n].v = snap0[n] + deltas[0][n]

            nc.all_engine_barrier()

            if DUMP == "B":
                dd = sem("dd2")
                Sy.dma_start(out=dbg["dAv"], in_=scrAv).then_inc(dd.h, 16)
                dd.v += 16
                Sy.wait_ge(dd.h, dd.v)
                nc.all_engine_barrier()

    if "C" in PH:
        es = ExitStack()
        with es:
            def sb(name, shape, dtyp):
                return es.enter_context(nc.sbuf_tensor(name, shape, dtyp)).ap()

            def ps(name, shape, dtyp):
                return es.enter_context(nc.psum_tensor(name, shape, dtyp)).ap()

            avT = sb("avT_sb", [128, 32, NQ], BF16)
            wo = sb("wo_sb", [128, 2, 32, 128], BF16)
            ybf = sb("ybf_sb", [128, 2, NQ], BF16)
            xqr = sb("xqr_sb", [128, 2, 8, 128], F32)
            y2t = sb("y2t_sb", [128, 2, 8, 128], F32)
            sumc = sb("sumc_sb", [128, 8, 32], F32)
            ssqc = sb("ssqc_sb", [128, 8, 32], F32)
            ctr = sb("ctrash_sb", [128, 128], BF16)
            py = ps("py_ps", [128, 4, 512], F32)
            pt = ps("pt_ps", [128, 2, 8, 128], BF16)

            cav = sem("cav")
            cwop = [sem("cwo0"), sem("cwo1")]
            cxqp = [sem("cxq0"), sem("cxq1")]
            cswp = [sem("csw0"), sem("csw1")]
            cym = sem("cym"); cev = sem("cev"); ctp = sem("ctp")
            cy2 = sem("cy2"); cg = sem("cg")

            for c in range(32):
                Sy.dma_start_transpose(avT[:, c, :],
                                       scrAv[:, c * 128:(c + 1) * 128]).then_inc(cav.h, 16)
                cav.v += 16

            def load_wo(m):
                s_ = cwop[m % 2]
                G.dma_start(
                    out=wo[:, m % 2, :, :],
                    in_=woT[:, m * 128:(m + 1) * 128].rearrange("(c p) f -> p c f", p=128),
                ).then_inc(s_.h, 16)
                s_.v += 16

            def load_xq(m):
                s_ = cxqp[m % 2]
                Sy.dma_start(
                    out=xqr[:, m % 2, :, :],
                    in_=xq[:, m * 128:(m + 1) * 128].rearrange("(s p) f -> p s f", p=128),
                ).then_inc(s_.h, 16)
                s_.v += 16

            load_wo(0); load_wo(1)
            load_xq(0); load_xq(1)

            for m in range(32):
                T.wait_ge(cwop[m % 2].h, (m // 2 + 1) * 16)
                if m == 0:
                    T.wait_ge(cav.h, cav.v)
                if m >= 2:
                    T.wait_ge(cev.h, 2 * (m - 1))
                for span in range(2):
                    for c in range(32):
                        T.matmul(py[:, 2 * (m % 2) + span, :], wo[:, m % 2, c, :],
                                 avT[:, c, span * 512:(span + 1) * 512],
                                 start=(c == 0), stop=(c == 31))
                _tlast().then_inc(cym.h, 1); cym.v += 1
                if m + 2 < 32:
                    G.wait_ge(cym.h, m + 1)
                    load_wo(m + 2)
                V.wait_ge(cym.h, cym.v)
                V.tensor_copy(ybf[:, m % 2, 0:512], py[:, 2 * (m % 2), :]).then_inc(cev.h, 1)
                cev.v += 1
                V.tensor_copy(ybf[:, m % 2, 512:1024], py[:, 2 * (m % 2) + 1, :]).then_inc(cev.h, 1)
                cev.v += 1
                T.wait_ge(cev.h, cev.v)
                if m >= 2:
                    T.wait_ge(cy2.h, 24 * (m - 1))
                for s in range(8):
                    T.transpose(pt[:, m % 2, s, :],
                                ybf[:, m % 2, s * 128:(s + 1) * 128], idb)
                _tlast().then_inc(ctp.h, 1); ctp.v += 1
                V.wait_ge(ctp.h, ctp.v)
                V.wait_ge(cxqp[m % 2].h, (m // 2 + 1) * 16)
                if m >= 2:
                    V.wait_ge(cswp[m % 2].h, 16 * ((m - 2) // 2 + 1))
                for s in range(8):
                    V.tensor_add(y2t[:, m % 2, s, :], pt[:, m % 2, s, :],
                                 xqr[:, m % 2, s, :]).then_inc(cy2.h, 1)
                    cy2.v += 1
                    V.wait_ge(cy2.h, cy2.v)
                    V.reduce_sum(sumc[:, s, m:m + 1], y2t[:, m % 2, s, :],
                                 axis=AX).then_inc(cy2.h, 1)
                    cy2.v += 1
                    V.scalar_tensor_tensor(ctr, y2t[:, m % 2, s, :], 1.0,
                                           y2t[:, m % 2, s, :], ALU.mult, ALU.mult,
                                           accum_out=ssqc[:, s, m:m + 1]).then_inc(cy2.h, 1)
                    cy2.v += 1
                if m + 2 < 32:
                    Sy.wait_ge(cy2.h, 24 * (m + 1))
                    load_xq(m + 2)
                Sy.wait_ge(cy2.h, cy2.v)
                Sy.dma_start(
                    out=scrY2[:, m * 128:(m + 1) * 128].rearrange("(s p) f -> p s f", p=128),
                    in_=y2t[:, m % 2, :, :]).then_inc(cswp[m % 2].h, 16)
                cswp[m % 2].v += 16

            V.wait_ge(cy2.h, cy2.v)
            for s in range(8):
                V.reduce_sum(stC[:, s, 0:1], sumc[:, s, :], axis=AX).then_inc(cg.h, 1)
                V.reduce_sum(stC[:, s, 1:2], ssqc[:, s, :], axis=AX).then_inc(cg.h, 1)
                cg.v += 2
            V.wait_ge(cg.h, cg.v)
            for s in range(8):
                V.tensor_scalar_mul(stC[:, s, 0:1], stC[:, s, 0:1], 1.0 / D).then_inc(cg.h, 1)
                V.tensor_scalar_mul(stC[:, s, 1:2], stC[:, s, 1:2], 1.0 / D).then_inc(cg.h, 1)
                cg.v += 2
            V.wait_ge(cg.h, cg.v)
            for s in range(8):
                V.scalar_tensor_tensor(stC[:, s, 2:3], stC[:, s, 0:1], 1.0,
                                       stC[:, s, 0:1], ALU.mult, ALU.mult).then_inc(cg.h, 1)
                cg.v += 1
            V.wait_ge(cg.h, cg.v)
            for s in range(8):
                V.tensor_tensor(stC[:, s, 1:2], stC[:, s, 1:2], stC[:, s, 2:3],
                                ALU.subtract).then_inc(cg.h, 1)
                cg.v += 1
            Sc.wait_ge(cg.h, cg.v)
            for s in range(8):
                Sc.activation(stC[:, s, 2:3], stC[:, s, 1:2], AF.Sqrt,
                              bias=cst[:, 1:2], scale=1.0).then_inc(cg.h, 1)
                cg.v += 1
            V.wait_ge(cg.h, cg.v)
            for s in range(8):
                V.reciprocal(stC[:, s, 3:4], stC[:, s, 2:3]).then_inc(cg.h, 1)
                cg.v += 1
            Sy.wait_ge(cswp[0].h, cswp[0].v)
            Sy.wait_ge(cswp[1].h, cswp[1].v)

            if DUMP == "C":
                dd = sem("dd3")
                Sy.dma_start(out=dbg["dY2"], in_=scrY2).then_inc(dd.h, 16)
                dd.v += 16
                Sy.wait_ge(cg.h, cg.v)
                Sy.dma_start(out=dbg["dSt"], in_=stC).then_inc(dd.h, 16)
                dd.v += 16
                Sy.wait_ge(dd.h, dd.v)

        nc.all_engine_barrier()

    if "D" in PH:
        es = ExitStack()
        with es:
            def sb(name, shape, dtyp):
                return es.enter_context(nc.sbuf_tensor(name, shape, dtyp)).ap()

            zfm = sb("zfm_sb", [128, 32, NQ], BF16)
            y2r = sb("y2r_sb", [128, 2, D], F32)
            ztm = sb("ztm_sb", [128, 2, D], BF16)
            gw1s = sb("gw1s_sb", [128, 32, 32], BF16)
            lAs = sb("lA_sb", [128, 32, 128], BF16)
            lBs = sb("lB_sb", [128, 32, 128], BF16)
            hraw = sb("hraw_sb", [32, NQ], F32)
            htm = sb("htm_sb", [128, 8, 32], F32)
            gst = sb("gst_sb", [128, 32], F32)
            gst2 = sb("gst2_sb", [128, 8], F32)
            g1a = sb("g1a_sb", [128, 8], F32)
            g2a = sb("g2a_sb", [128, 8], F32)
            hrel = sb("hrel_sb", [128, 32], F32)
            hrel2 = sb("hrel2_sb", [128, 32], F32)
            hrT = sb("hrT_sb", [33, 128], BF16)
            gts = sb("gts_sb", [128, 8, 8], F32)
            m1 = sb("m1_sb", [128, 8, 8], F32)
            m2 = sb("m2_sb", [128, 8, 8], F32)
            mc = sb("mc_sb", [128, 16], BF16)
            d21 = sb("d21_sb", [128, 8, 3], F32)
            wts = sb("wts_sb", [128, 8], F32)
            wts2 = sb("wts2_sb", [128, 8], F32)
            wT = sb("wT_sb", [8, NQ], BF16)
            cnt = sb("cnt_sb", [1, 16], F32)
            cnt2 = sb("cnt2_sb", [1, 16], F32)
            cntb = sb("cntb_sb", [1, 16], BF16)
            arep = sb("arep_sb", [128, 16], F32)
            tall = sb("tall_sb", [128, NQ], BF16)
            tp = sb("tp_sb", [128, NQ], BF16)
            wff = sb("wff_sb", [128, 2, 32, 128], BF16)
            y3bf = sb("y3bf_sb", [128, 2, NQ], BF16)
            y2rt = sb("y2rt_sb", [128, 2, 8, 128], F32)
            outt = sb("outt_sb", [128, 2, 8, 128], F32)
            dtr = sb("dtrash_sb", [128, 32], BF16)

            dld = sem("dld"); dzn = sem("dzn")
            dy2p = [sem("dy20"), sem("dy21")]
            dzt = sem("dzt"); dze = sem("dze"); dgm = sem("dgm")
            dgv = sem("dgv"); dgg = sem("dgg"); dcm = sem("dcm")
            dt1 = sem("dt1"); dwx = sem("dwx")
            dwfp = [sem("dwf0"), sem("dwf1")]
            dyep = [sem("dye0"), sem("dye1")]
            dstp = [sem("dst0"), sem("dst1")]
            dym = sem("dym"); dyt = sem("dyt"); dou = sem("dou")

            G.dma_start(out=gw1s, in_=gw1T.rearrange("(c p) f -> p c f", p=128)).then_inc(dld.h, 16)
            G.dma_start(out=lAs, in_=lAT.rearrange("(c p) f -> p c f", p=128)).then_inc(dld.h, 16)
            G.dma_start(out=lBs, in_=lB.rearrange("p (c f) -> p c f", f=128)).then_inc(dld.h, 16)
            dld.v += 48
            V.memset(hrT[32:33, :], 1.0)
            V.engine_nop().then_inc(dgg.h, 1); dgg.v += 1

            def load_y2(s):
                s_ = dy2p[s % 2]
                Sy.dma_start(out=y2r[:, s % 2, :],
                             in_=scrY2[s * 128:(s + 1) * 128, :]).then_inc(s_.h, 16)
                s_.v += 16

            load_y2(0); load_y2(1)

            es1 = ExitStack()
            with es1:
                def ps1(name, shape, dtyp):
                    return es1.enter_context(nc.psum_tensor(name, shape, dtyp)).ap()

                ztp = ps1("zt_ps", [128, 16, 128], BF16)
                pg = ps1("pg_ps", [32, 2, 512], F32)
                pt2 = ps1("pt2_ps", [128, 8, 32], F32)
                ptg1 = ps1("ptg1_ps", [32, 128], F32)
                pgt2 = ps1("pgt2_ps", [128, 32], F32)
                pcnt = ps1("pcnt_ps", [1, 16], F32)

                for s in range(8):
                    V.wait_ge(dy2p[s % 2].h, (s // 2 + 1) * 16)
                    if s >= 2:
                        V.wait_ge(dzt.h, 4 * (s - 1))
                    V.tensor_scalar(ztm[:, s % 2, :], y2r[:, s % 2, :],
                                    stC[:, s, 0:1], stC[:, s, 3:4],
                                    ALU.subtract, ALU.mult).then_inc(dzn.h, 1)
                    dzn.v += 1
                    if s + 2 < 8:
                        Sy.wait_ge(dzn.h, s + 1)
                        load_y2(s + 2)

                    T.wait_ge(dzn.h, dzn.v)
                    for c8 in range(4):
                        bk = c8 % 2
                        if dzt.v >= 2:
                            T.wait_ge(dze.h, dzt.v - 1)
                        for cc in range(8):
                            c = c8 * 8 + cc
                            T.transpose(ztp[:, bk * 8 + cc, :],
                                        ztm[:, s % 2, c * 128:(c + 1) * 128], idb)
                        _tlast().then_inc(dzt.h, 1); dzt.v += 1
                        V.wait_ge(dzt.h, dzt.v)
                        V.tensor_copy(zfm[:, c8 * 8:(c8 + 1) * 8, s * 128:(s + 1) * 128],
                                      ztp[:, bk * 8:bk * 8 + 8, :]).then_inc(dze.h, 1)
                        dze.v += 1

                T.wait_ge(dze.h, dze.v)
                T.wait_ge(dld.h, dld.v)
                for span in range(2):
                    for c in range(32):
                        T.matmul(pg[:, span, :], gw1s[:, c, :],
                                 zfm[:, c, span * 512:(span + 1) * 512],
                                 start=(c == 0), stop=(c == 31))
                _tlast().then_inc(dgm.h, 1); dgm.v += 1
                V.wait_ge(dgm.h, dgm.v)
                V.tensor_copy(hraw[:, 0:512], pg[:, 0, :]).then_inc(dgg.h, 1)
                dgg.v += 1
                V.tensor_copy(hraw[:, 512:1024], pg[:, 1, :]).then_inc(dgg.h, 1)
                dgg.v += 1

                T.wait_ge(dgg.h, dgg.v)
                for s in range(8):
                    T.transpose(pt2[:, s, :], hraw[:, s * 128:(s + 1) * 128], idf[0:32, 0:32])
                _tlast().then_inc(dgm.h, 1); dgm.v += 1
                V.wait_ge(dgm.h, dgm.v)
                for s in range(8):
                    V.tensor_add(htm[:, s, :], pt2[:, s, :],
                                 gcons_sb[:, 0:32]).then_inc(dgg.h, 1)
                    dgg.v += 1
                V.wait_ge(dgg.h, dgg.v)
                for s in range(8):
                    V.reduce_sum(gst[:, s:s + 1], htm[:, s, :],
                                 axis=AX).then_inc(dgg.h, 1)
                    dgg.v += 1
                    V.wait_ge(dgg.h, dgg.v)
                    V.scalar_tensor_tensor(dtr, htm[:, s, :], 1.0, htm[:, s, :],
                                           ALU.mult, ALU.mult,
                                           accum_out=gst[:, 8 + s:9 + s]).then_inc(dgg.h, 1)
                    dgg.v += 1
                    V.wait_ge(dgg.h, dgg.v)
                V.tensor_scalar_mul(gst[:, 0:8], gst[:, 0:8], 1.0 / 32).then_inc(dgg.h, 1)
                dgg.v += 1
                V.tensor_scalar_mul(gst[:, 8:16], gst[:, 8:16], 1.0 / 32).then_inc(dgg.h, 1)
                dgg.v += 1
                V.wait_ge(dgg.h, dgg.v)
                V.scalar_tensor_tensor(gst2[:, 0:8], gst[:, 0:8], 1.0,
                                       gst[:, 0:8], ALU.mult, ALU.mult).then_inc(dgg.h, 1)
                dgg.v += 1
                V.wait_ge(dgg.h, dgg.v)
                V.tensor_tensor(gst[:, 8:16], gst[:, 8:16], gst2[:, 0:8],
                                ALU.subtract).then_inc(dgg.h, 1)
                dgg.v += 1
                Sc.wait_ge(dgg.h, dgg.v)
                Sc.activation(gst[:, 16:24], gst[:, 8:16], AF.Sqrt,
                              bias=cst[:, 1:2], scale=1.0).then_inc(dgg.h, 1)
                dgg.v += 1
                V.wait_ge(dgg.h, dgg.v)
                V.reciprocal(gst[:, 24:32], gst[:, 16:24]).then_inc(dgg.h, 1)
                dgg.v += 1

                for s in range(8):
                    V.wait_ge(dgg.h, dgg.v)
                    V.wait_ge(dgv.h, dgv.v)
                    V.tensor_scalar(hrel, htm[:, s, :], gst[:, s:s + 1],
                                    gst[:, 24 + s:25 + s], ALU.subtract,
                                    ALU.mult).then_inc(dgv.h, 1)
                    dgv.v += 1
                    V.wait_ge(dgv.h, dgv.v)
                    V.scalar_tensor_tensor(hrel2, hrel, 1.0, gcons_sb[:, 32:64],
                                           ALU.mult, ALU.mult).then_inc(dgv.h, 1)
                    dgv.v += 1
                    V.wait_ge(dgv.h, dgv.v)
                    V.scalar_tensor_tensor(hrel, hrel2, 1.0, gcons_sb[:, 64:96],
                                           ALU.mult, ALU.add).then_inc(dgv.h, 1)
                    dgv.v += 1
                    V.wait_ge(dgv.h, dgv.v)
                    V.tensor_scalar_max(hrel2, hrel, 0.0).then_inc(dgv.h, 1)
                    dgv.v += 1
                    T.wait_ge(dgv.h, dgv.v)
                    T.transpose(ptg1, hrel2, idf).then_inc(dgm.h, 1); dgm.v += 1
                    V.wait_ge(dgm.h, dgm.v)
                    V.tensor_copy(hrT[0:32, :], ptg1[0:32, :]).then_inc(dgv.h, 1)
                    dgv.v += 1
                    T.wait_ge(dgv.h, dgv.v)
                    T.matmul(pgt2[:, 0:8], hrT, gw2_sb, start=True, stop=True
                             ).then_inc(dgm.h, 1)
                    dgm.v += 1
                    V.wait_ge(dgm.h, dgm.v)
                    V.tensor_copy(gts[:, s, :], pgt2[:, 0:8]).then_inc(dgv.h, 1)
                    dgv.v += 1
                    V.wait_ge(dgv.h, dgv.v)
                    V.reduce_max(d21[:, s, 0:1], gts[:, s, :], axis=AX).then_inc(dgv.h, 1)
                    dgv.v += 1
                    V.wait_ge(dgv.h, dgv.v)
                    V.tensor_scalar(m1[:, s, :], gts[:, s, :], d21[:, s, 0:1],
                                    None, ALU.is_equal).then_inc(dgv.h, 1)
                    dgv.v += 1
                    V.wait_ge(dgv.h, dgv.v)
                    V.scalar_tensor_tensor(m2[:, s, :], m1[:, s, :], -1e30,
                                           gts[:, s, :], ALU.mult, ALU.add).then_inc(dgv.h, 1)
                    dgv.v += 1
                    V.wait_ge(dgv.h, dgv.v)
                    V.reduce_max(d21[:, s, 1:2], m2[:, s, :], axis=AX).then_inc(dgv.h, 1)
                    dgv.v += 1
                    V.wait_ge(dgv.h, dgv.v)
                    V.tensor_scalar(m2[:, s, :], m2[:, s, :], d21[:, s, 1:2],
                                    None, ALU.is_equal).then_inc(dgv.h, 1)
                    dgv.v += 1
                    V.tensor_tensor(d21[:, s, 2:3], d21[:, s, 1:2], d21[:, s, 0:1],
                                    ALU.subtract).then_inc(dgv.h, 1)
                    dgv.v += 1
                    V.wait_ge(dgv.h, dgv.v)
                    V.wait_ge(dcm.h, dcm.v)
                    V.tensor_copy(mc[:, 0:8], m1[:, s, :]).then_inc(dgv.h, 1)
                    dgv.v += 1
                    V.tensor_copy(mc[:, 8:16], m2[:, s, :]).then_inc(dgv.h, 1)
                    dgv.v += 1
                    T.wait_ge(dgv.h, dgv.v)
                    T.matmul(pcnt, ones_c, mc, start=(s == 0), stop=(s == 7)
                             ).then_inc(dcm.h, 1)
                    dcm.v += 1

                Sc.wait_ge(dgv.h, dgv.v)
                Sc.activation(g1a, d21[:, :, 2:3].rearrange("p s c -> p (s c)"),
                              AF.Sigmoid, bias=0.0, scale=cst[:, 2:3]).then_inc(dgg.h, 1)
                dgg.v += 1
                Sc.activation(g2a, d21[:, :, 2:3].rearrange("p s c -> p (s c)"),
                              AF.Sigmoid, bias=0.0, scale=1.0).then_inc(dgg.h, 1)
                dgg.v += 1

                dccs = sem("dccs"); dccc = sem("dccc"); dccg = sem("dccg")
                V.wait_ge(dcm.h, dcm.v)
                V.tensor_copy(cnt, pcnt).then_inc(dgv.h, 1)
                dgv.v += 1
                Sy.wait_ge(dgv.h, dgv.v)
                Sy.dma_start(out=cntI, in_=cnt).then_inc(dccs.h, 16)
                dccs.v += 16
                G.wait_ge(dccs.h, dccs.v)
                if RGN > 1:
                    G.collective_compute(
                        "AllReduce", ALU.add, replica_groups=[list(range(RGN))],
                        ins=[cntI], outs=[cntO]).then_inc(dccc.h, 1)
                    dccc.v += 1
                    G.wait_ge(dccc.h, dccc.v)
                    G.dma_start(out=cnt2, in_=cntO).then_inc(dccg.h, 16)
                else:
                    G.dma_start(out=cnt2, in_=cntI).then_inc(dccg.h, 16)
                dccg.v += 16
                V.wait_ge(dccg.h, dccg.v)
                V.tensor_scalar(cnt, cnt2, cst[0:1, 3:4], None,
                                ALU.is_le).then_inc(dgv.h, 1)
                dgv.v += 1
                V.wait_ge(dgv.h, dgv.v)
                V.tensor_copy(cntb, cnt).then_inc(dgv.h, 1)
                dgv.v += 1
                T.wait_ge(dgv.h, dgv.v)
                T.matmul(pgt2[:, 0:16], ones_r, cntb, start=True, stop=True
                         ).then_inc(dgm.h, 1)
                dgm.v += 1
                V.wait_ge(dgm.h, dgm.v)
                V.tensor_copy(arep, pgt2[:, 0:16]).then_inc(dgv.h, 1)
                dgv.v += 1

                for s in range(8):
                    V.wait_ge(dgg.h, dgg.v)
                    V.wait_ge(dgv.h, dgv.v)
                    V.tensor_scalar(wts, m1[:, s, :], g1a[:, s:s + 1],
                                    None, ALU.mult).then_inc(dgv.h, 1)
                    dgv.v += 1
                    V.tensor_scalar(wts2, m2[:, s, :], g2a[:, s:s + 1],
                                    None, ALU.mult).then_inc(dgv.h, 1)
                    dgv.v += 1
                    V.wait_ge(dgv.h, dgv.v)
                    V.tensor_mul(wts, wts, arep[:, 0:8]).then_inc(dgv.h, 1)
                    dgv.v += 1
                    V.tensor_mul(wts2, wts2, arep[:, 8:16]).then_inc(dgv.h, 1)
                    dgv.v += 1
                    V.wait_ge(dgv.h, dgv.v)
                    V.tensor_add(wts, wts, wts2).then_inc(dgv.h, 1)
                    dgv.v += 1
                    T.wait_ge(dgv.h, dgv.v)
                    T.transpose(ptg1[0:8, :], wts, idf).then_inc(dgm.h, 1)
                    dgm.v += 1
                    V.wait_ge(dgm.h, dgm.v)
                    V.tensor_copy(wT[:, s * 128:(s + 1) * 128],
                                  ptg1[0:8, :]).then_inc(dgv.h, 1)
                    dgv.v += 1

            es2 = ExitStack()
            with es2:
                def ps2(name, shape, dtyp):
                    return es2.enter_context(nc.psum_tensor(name, shape, dtyp)).ap()

                py3 = ps2("py3_ps", [128, 4, 512], F32)
                pt3 = ps2("pt3_ps", [128, 2, 8, 128], BF16)
                pwx = ps2("pwx_ps", [128, 2, 512], F32)

                T.wait_ge(dld.h, dld.v)
                for span in range(2):
                    for c in range(32):
                        T.matmul(py3[:, span, :], lAs[:, c, :],
                                 zfm[:, c, span * 512:(span + 1) * 512],
                                 start=(c == 0), stop=(c == 31))
                _tlast().then_inc(dt1.h, 1); dt1.v += 1
                V.wait_ge(dt1.h, dt1.v)
                V.tensor_copy(tall[:, 0:512], py3[:, 0, :]).then_inc(dt1.h, 1)
                dt1.v += 1
                V.tensor_copy(tall[:, 512:1024], py3[:, 1, :]).then_inc(dt1.h, 1)
                dt1.v += 1

                T.wait_ge(dgv.h, dgv.v)
                T.wait_ge(dt1.h, dt1.v)
                for span in range(2):
                    T.matmul(pwx[:, span, :], e16,
                             wT[:, span * 512:(span + 1) * 512],
                             start=True, stop=True)
                _tlast().then_inc(dwx.h, 1); dwx.v += 1
                V.wait_ge(dwx.h, dwx.v)
                V.tensor_mul(tp[:, 0:512], tall[:, 0:512], pwx[:, 0, :]).then_inc(dwx.h, 1)
                dwx.v += 1
                V.tensor_mul(tp[:, 512:1024], tall[:, 512:1024],
                             pwx[:, 1, :]).then_inc(dwx.h, 1)
                dwx.v += 1

                def load_wff(m):
                    s_ = dwfp[m % 2]
                    G.dma_start(
                        out=wff[:, m % 2, :, :],
                        in_=wffT[:, m * 128:(m + 1) * 128].rearrange("(c p) f -> p c f", p=128),
                    ).then_inc(s_.h, 16)
                    s_.v += 16

                def load_y2t(m):
                    s_ = dyep[m % 2]
                    Sy.dma_start(
                        out=y2rt[:, m % 2, :, :],
                        in_=scrY2[:, m * 128:(m + 1) * 128].rearrange("(s p) f -> p s f", p=128),
                    ).then_inc(s_.h, 16)
                    s_.v += 16

                load_wff(0); load_wff(1)
                load_y2t(0); load_y2t(1)

                for m in range(32):
                    T.wait_ge(dwfp[m % 2].h, (m // 2 + 1) * 16)
                    if m == 0:
                        T.wait_ge(dt1.h, dt1.v)
                    if m >= 2:
                        T.wait_ge(dym.h, 3 * (m - 1))
                    for span in range(2):
                        for c in range(32):
                            T.matmul(py3[:, 2 * (m % 2) + span, :], wff[:, m % 2, c, :],
                                     zfm[:, c, span * 512:(span + 1) * 512],
                                     start=(c == 0), stop=False)
                        T.wait_ge(dwx.h, dwx.v)
                        T.matmul(py3[:, 2 * (m % 2) + span, :], lBs[:, m, :],
                                 tp[:, span * 512:(span + 1) * 512],
                                 start=False, stop=True)
                    _tlast().then_inc(dym.h, 1); dym.v += 1
                    if m + 2 < 32:
                        G.wait_ge(dym.h, 3 * m + 1)
                        load_wff(m + 2)
                    V.wait_ge(dym.h, dym.v)
                    V.tensor_copy(y3bf[:, m % 2, 0:512],
                                  py3[:, 2 * (m % 2), :]).then_inc(dym.h, 1)
                    dym.v += 1
                    V.tensor_copy(y3bf[:, m % 2, 512:1024],
                                  py3[:, 2 * (m % 2) + 1, :]).then_inc(dym.h, 1)
                    dym.v += 1
                    T.wait_ge(dym.h, dym.v)
                    if m == 0:
                        T.wait_ge(dgv.h, dgv.v)
                    if m >= 2:
                        T.wait_ge(dou.h, 8 * (m - 1))
                    for s in range(8):
                        T.transpose(pt3[:, m % 2, s, :],
                                    y3bf[:, m % 2, s * 128:(s + 1) * 128], idb)
                    _tlast().then_inc(dyt.h, 1); dyt.v += 1
                    V.wait_ge(dyt.h, dyt.v)
                    V.wait_ge(dyep[m % 2].h, (m // 2 + 1) * 16)
                    if m >= 2:
                        V.wait_ge(dstp[m % 2].h, 16 * ((m - 2) // 2 + 1))
                    for s in range(8):
                        V.tensor_add(outt[:, m % 2, s, :], pt3[:, m % 2, s, :],
                                     y2rt[:, m % 2, s, :]).then_inc(dou.h, 1)
                        dou.v += 1
                    if m + 2 < 32:
                        Sy.wait_ge(dou.h, 8 * (m + 1))
                        load_y2t(m + 2)
                    Sy.wait_ge(dou.h, dou.v)
                    Sy.dma_start(
                        out=out[:, m * 128:(m + 1) * 128].rearrange("(s p) f -> p s f", p=128),
                        in_=outt[:, m % 2, :, :]).then_inc(dstp[m % 2].h, 16)
                    dstp[m % 2].v += 16

                Sy.wait_ge(dstp[0].h, dstp[0].v)
                Sy.wait_ge(dstp[1].h, dstp[1].v)
                G.wait_ge(dwfp[0].h, dwfp[0].v)
                G.wait_ge(dwfp[1].h, dwfp[1].v)

    return nc, dbg




def _host_prep(inputs):
    x = np.ascontiguousarray(np.asarray(inputs["L0_attn_hidden_states"], dtype=np.float32))
    Wq = np.asarray(inputs["L0_attn_Wq"], dtype=np.float32)
    Wkr = np.asarray(inputs["L0_attn_Wkr"], dtype=np.float32)
    Wdk = np.asarray(inputs["L0_attn_Wdk"], dtype=np.float32)
    Wupk = np.asarray(inputs["L0_attn_Wupk"], dtype=np.float32)
    Wupv = np.asarray(inputs["L0_attn_Wupv"], dtype=np.float32)
    Wo = np.asarray(inputs["L0_attn_Wo"], dtype=np.float32)
    Wff = np.asarray(inputs["L0_ffn_weight"], dtype=np.float32)
    lora_A = np.asarray(inputs["L0_ffn_lora_A"], dtype=np.float32)
    lora_B = np.asarray(inputs["L0_ffn_lora_B"], dtype=np.float32)
    gw1 = np.asarray(inputs["L0_ffn_gw1"], dtype=np.float32)
    gb1 = np.asarray(inputs["L0_ffn_gb1"], dtype=np.float32)
    lng = np.asarray(inputs["L0_ffn_ln_gamma"], dtype=np.float32)
    lnb = np.asarray(inputs["L0_ffn_ln_beta"], dtype=np.float32)
    gw2 = np.asarray(inputs["L0_ffn_gw2"], dtype=np.float32)
    gb2 = np.asarray(inputs["L0_ffn_gb2"], dtype=np.float32)

    blockperm = np.concatenate([np.arange(0, 32), np.arange(64, 96),
                                np.arange(32, 64), np.arange(96, 128)])
    qperm = np.concatenate([h * 128 + blockperm for h in range(32)])

    C = np.ascontiguousarray
    common = {
        "wqT": C(Wq.T[:, qperm]),
        "wkrT": C(Wkr.T),
        "wdkT": C(Wdk.T),
        "wupkT": C(Wupk.T),
        "wupvT": C(Wupv.T),
        "woT": C(Wo.T),
        "wffT": C(Wff.T),
        "lAT": C(lora_A.transpose(1, 0, 2).reshape(D, 128)),
        "lB": C(lora_B.reshape(128, D)),
        "gw1T": C(gw1.T),
        "gw2aug": C(np.concatenate([gw2.T, gb2[None, :]], axis=0)),
        "gcons": C(np.broadcast_to(
            np.concatenate([gb1, lng, lnb])[None, :], (128, 96)).copy()),
    }

    p = np.arange(128)
    common["tri"] = C(np.where(p[:, None] <= p[None, :], 0.0, NEG).astype(np.float32))
    import ml_dtypes
    common["ident"] = np.eye(128, dtype=ml_dtypes.bfloat16)
    common["identf"] = np.eye(128, dtype=np.float32)
    e16m = np.zeros((8, 128), dtype=np.float32)
    for e in range(8):
        e16m[e, e * 16:(e + 1) * 16] = 1.0 / 16.0
    common["e16"] = e16m
    common["onescol"] = np.ones((128, 1), dtype=np.float32)
    common["onesrow"] = np.ones((1, 128), dtype=np.float32)

    invf = (10000.0 ** (-np.arange(0, HD, 2, dtype=np.float32) / HD))
    pos = np.arange(S, dtype=np.float32)
    i32 = np.arange(32)
    qfreq = np.concatenate([invf[i32], invf[i32], invf[32 + i32], invf[32 + i32]])
    qsign = np.concatenate([-np.ones(32), np.ones(32), -np.ones(32), np.ones(32)])
    kfreq = np.concatenate([invf[i32]] * 4)
    ksign = np.concatenate([-np.ones(32), np.ones(32)] * 2)

    ang_k = kfreq[:, None] * pos[None, :]
    common["coskD"] = C(np.cos(ang_k).astype(np.float32))
    common["sinkD"] = C((np.sin(ang_k) * ksign[:, None]).astype(np.float32))

    per_core = []
    qpos_all = []
    for c in range(8):
        beta, j = c // 4, c % 4
        qpos = np.concatenate([np.arange(j * 512, (j + 1) * 512),
                               np.arange((7 - j) * 512, (8 - j) * 512)])
        qpos_all.append((beta, qpos))
        ang_q = qfreq[:, None] * qpos[None, :].astype(np.float32)
        d = dict(common)
        d["meta"] = np.array([[j]], dtype=np.int32)
        d["xb"] = C(x[beta])
        d["xq"] = C(x[beta][qpos])
        d["cosqD"] = C(np.cos(ang_q).astype(np.float32))
        d["sinqD"] = C((np.sin(ang_q) * qsign[:, None]).astype(np.float32))
        per_core.append(d)
    return per_core, qpos_all


def kernel(**inputs):
    per_core, qpos_all = _host_prep(inputs)
    nc, _ = _build()
    res = run_bass_kernel_spmd(nc, per_core, core_ids=list(range(8)))
    out = np.empty((B, S, D), dtype=np.float32)
    for c in range(8):
        beta, qpos = qpos_all[c]
        out[beta, qpos] = res.results[c]["out"]
    return out


# version 04
# speedup vs baseline: 1.2306x; 1.2174x over previous
import math
from contextlib import ExitStack
import numpy as np

import concourse.bass as bass
import concourse.mybir as mybir
from concourse.bass_utils import run_bass_kernel_spmd
from concourse.ordered_engine_set import OrderedEngineSet

F32 = mybir.dt.float32
BF16 = mybir.dt.bfloat16
I32 = mybir.dt.int32
AF = mybir.ActivationFunctionType
ALU = mybir.AluOpType
ET = mybir.EngineType
AX = mybir.AxisListType.X

D = 4096
S = 4096
B = 2
NH = 32
KVH = 8
HD = 128
CD = 512
RD = 512
NQ = 1024
NBLK = 8
SCALE = 1.0 / math.sqrt(HD)
SHIFT = -30.0
NEG = -1.0e9
EPS = 1e-5
CAPACITY = float(int(3.0 * (B * S) / 8))

DUMP = None
PH = "ABCD"
RGN = 8


_TLAST = None


def _set_tlast(inst):
    global _TLAST
    _TLAST = inst


def _tlast():
    return _TLAST


class Sem:
    def __init__(self, h):
        self.h = h
        self.v = 0

    def inc(self, inst, n=1):
        self.v += n
        inst.then_inc(self.h, n)


def _build():
    nc = bass.Bass("TRN2", target_bir_lowering=False)
    dt = nc.dram_tensor
    meta = dt("meta", [1, 1], I32, kind="ExternalInput").ap()
    xb = dt("xb", [S, D], F32, kind="ExternalInput").ap()
    xq = dt("xq", [NQ, D], F32, kind="ExternalInput").ap()
    wqT = dt("wqT", [D, D], F32, kind="ExternalInput").ap()
    wkrT = dt("wkrT", [D, RD], F32, kind="ExternalInput").ap()
    wdkT = dt("wdkT", [D, CD], F32, kind="ExternalInput").ap()
    wupkT = dt("wupkT", [CD, 512], F32, kind="ExternalInput").ap()
    wupvT = dt("wupvT", [CD, 1024], F32, kind="ExternalInput").ap()
    woT = dt("woT", [D, D], F32, kind="ExternalInput").ap()
    wffT = dt("wffT", [D, D], F32, kind="ExternalInput").ap()
    lAT = dt("lAT", [D, 128], F32, kind="ExternalInput").ap()
    lB = dt("lB", [128, D], F32, kind="ExternalInput").ap()
    gw1T = dt("gw1T", [D, 32], F32, kind="ExternalInput").ap()
    gw2aug = dt("gw2aug", [33, 8], F32, kind="ExternalInput").ap()
    gcons = dt("gcons", [128, 96], F32, kind="ExternalInput").ap()
    triD = dt("tri", [128, 128], F32, kind="ExternalInput").ap()
    identD = dt("ident", [128, 128], BF16, kind="ExternalInput").ap()
    identfD = dt("identf", [128, 128], F32, kind="ExternalInput").ap()
    e16D = dt("e16", [8, 128], F32, kind="ExternalInput").ap()
    onescolD = dt("onescol", [128, 1], F32, kind="ExternalInput").ap()
    onesrowD = dt("onesrow", [1, 128], F32, kind="ExternalInput").ap()
    cosqD = dt("cosqD", [128, NQ], F32, kind="ExternalInput").ap()
    sinqD = dt("sinqD", [128, NQ], F32, kind="ExternalInput").ap()
    coskD = dt("coskD", [128, S], F32, kind="ExternalInput").ap()
    sinkD = dt("sinkD", [128, S], F32, kind="ExternalInput").ap()
    out = dt("out", [NQ, D], F32, kind="ExternalOutput").ap()

    scrK = dt("scrK", [KVH, 128, S], BF16).ap()
    scrV = dt("scrV", [S, KVH * HD], BF16).ap()
    scrXfm = dt("scrXfm", [32, 128, S], BF16).ap()
    scrAv = dt("scrAv", [NQ, D], BF16).ap()
    scrY2 = dt("scrY2", [NQ, D], F32).ap()
    cntI = dt("cntI", [1, 16], F32).ap()
    cntO = dt("cntO", [1, 16], F32).ap()

    dbg = {}
    if DUMP == "A":
        dbg["dK"] = dt("dK", [KVH, 128, S], BF16, kind="ExternalOutput").ap()
        dbg["dV"] = dt("dV", [S, KVH * HD], BF16, kind="ExternalOutput").ap()
        dbg["dX"] = dt("dX", [32, 128, S], BF16, kind="ExternalOutput").ap()
    if DUMP == "B":
        dbg["dAv"] = dt("dAv", [NQ, D], BF16, kind="ExternalOutput").ap()
    if DUMP == "C":
        dbg["dY2"] = dt("dY2", [NQ, D], F32, kind="ExternalOutput").ap()
        dbg["dSt"] = dt("dSt", [128, 8, 4], F32, kind="ExternalOutput").ap()

    sems = {}

    def sem(name):
        if name not in sems:
            sems[name] = Sem(nc.semaphore(name).__enter__())
        return sems[name]

    al = nc.alloc_sbuf_tensor
    cst = al("cst", [128, 8], F32).ap()
    tri = al("tri_sb", [128, 128], F32).ap()
    idb = al("id_sb", [128, 128], BF16).ap()
    idf = al("idf_sb", [128, 128], F32).ap()
    e16 = al("e16_sb", [8, 128], BF16).ap()
    ones_c = al("ones_c", [128, 1], BF16).ap()
    ones_r = al("ones_r", [1, 128], BF16).ap()
    gcons_sb = al("gcons_sb", [128, 96], F32).ap()
    gw2_sb = al("gw2_sb", [33, 8], BF16).ap()
    cosq = al("cosq", [128, NQ], F32).ap()
    sinq = al("sinq", [128, NQ], F32).ap()
    stC = al("stC", [128, 8, 4], F32).ap()

    V = nc.vector
    Sc = nc.scalar
    G = nc.gpsimd
    Sy = nc.sync

    class _TProxy:
        def __init__(self, eng):
            self._e = eng

        def matmul(self, *a, **k):
            global _TLAST
            inst = self._e.matmul(*a, **k)
            _set_tlast(inst)
            return inst

        def transpose(self, *a, **k):
            inst = self._e.transpose(*a, **k)
            _set_tlast(inst)
            return inst

        def __getattr__(self, name):
            return getattr(self._e, name)

    T = _TProxy(nc.tensor)

    pro = sem("pro")
    prog = sem("prog")
    for dst_, src_ in ((tri, triD), (gcons_sb, gcons), (cosq, cosqD), (sinq, sinqD),
                       (idf, identfD)):
        Sy.dma_start(out=dst_, in_=src_).then_inc(pro.h, 16)
        pro.v += 16
    for dst_, src_ in ((idb, identD), (e16, e16D), (ones_c, onescolD),
                       (ones_r, onesrowD), (gw2_sb, gw2aug)):
        G.dma_start(out=dst_, in_=src_).then_inc(prog.h, 16)
        prog.v += 16
    cstg = sem("cstg")
    V.memset(cst[:, 0:1], SHIFT)
    V.memset(cst[:, 1:2], EPS)
    V.memset(cst[:, 2:3], -1.0)
    V.memset(cst[:, 3:4], CAPACITY)
    V.memset(cst[:, 4:5], 1.0)
    V.engine_nop().then_inc(cstg.h, 1)
    cstg.v += 1

    regs = nc.alloc_registers(
        "jreg", OrderedEngineSet([ET.SP, ET.Activation, ET.PE, ET.DVE, ET.Pool]))
    for r in regs:
        nc.reg_load(r, meta.to_broadcast((1, 1)))
    sv = nc.snap(regs, min_val=0, max_val=3)

    nc.all_engine_barrier()

    if "A" in PH:
        es = ExitStack()
        with es:
            def sb(name, shape, dtyp):
                return es.enter_context(nc.sbuf_tensor(name, shape, dtyp)).ap()

            def ps(name, shape, dtyp):
                return es.enter_context(nc.psum_tensor(name, shape, dtyp)).ap()

            wkr = sb("wkr_sb", [128, 32, RD], BF16)
            wdk = sb("wdk_sb", [128, 32, CD], BF16)
            wupk = sb("wupk_sb", [128, 4, 512], BF16)
            wupv = sb("wupv_sb", [128, 4, 1024], BF16)
            xt = sb("xt_sb", [128, 3, D], BF16)
            xln = sb("xln_sb", [128, 3, D], BF16)
            xfm = sb("xfm_sb", [128, 32, 512], BF16)
            ckv = sb("ckv_sb", [128, 4, 512], BF16)
            krp = sb("krp_sb", [128, 2, 512], BF16)
            kc = sb("kc_sb", [128, 2, 512], BF16)
            vst = sb("vst_sb", [128, 2, 1024], BF16)
            rk1 = sb("rk1_sb", [128, 512], F32)
            rk2 = sb("rk2_sb", [128, 512], F32)
            rk3 = sb("rk3_sb", [128, 512], F32)
            coskb = sb("cosk_sb", [128, 2, 512], F32)
            sinkb = sb("sink_sb", [128, 2, 512], F32)
            lns = sb("lns_sb", [128, 16], F32)
            trash = sb("trash_sb", [128, D], BF16)
            ptr = ps("ptr_ps", [128, 16, 128], BF16)
            prj = ps("prj_ps", [128, 2, 512], F32)
            vps = ps("v_ps", [128, 4, 512], F32)

            aw = sem("aw"); act = sem("act")
            axp = [sem("ax0"), sem("ax1"), sem("ax2")]
            aln = sem("aln"); alg = sem("alg"); aap = sem("aap")
            atp = sem("atp"); aev = sem("aev"); amm = sem("amm")
            akr = sem("akr"); ack = sem("ack"); akc = sem("akc")
            avv = sem("avv"); axw = sem("axw")
            akw = [sem("akw0"), sem("akw1")]
            kcwp = [sem("kcw0"), sem("kcw1")]
            avw = [sem("avw0"), sem("avw1")]
            actp = [sem("act0"), sem("act1")]

            prj_rota = []

            def issue_x(i):
                if i >= 3:
                    G.wait_ge(aap.h, i - 2)
                G.dma_start(out=xt[:, i % 3, :],
                            in_=xb[i * 128:(i + 1) * 128, :]).then_inc(axp[i % 3].h, 16)
                axp[i % 3].v += 16

            def issue_ct(b):
                if b >= 2:
                    Sy.wait_ge(akr.h, 4 * (b - 1))
                Sy.dma_start(out=coskb[:, b % 2, :],
                             in_=coskD[:, b * 512:(b + 1) * 512]).then_inc(actp[b % 2].h, 16)
                Sy.dma_start(out=sinkb[:, b % 2, :],
                             in_=sinkD[:, b * 512:(b + 1) * 512]).then_inc(actp[b % 2].h, 16)
                actp[b % 2].v += 32

            issue_x(0); issue_x(1); issue_x(2)
            G.dma_start(out=wkr, in_=wkrT.rearrange("(c p) f -> p c f", p=128)).then_inc(aw.h, 16)
            G.dma_start(out=wdk, in_=wdkT.rearrange("(c p) f -> p c f", p=128)).then_inc(aw.h, 16)
            G.dma_start(out=wupk, in_=wupkT.rearrange("(c p) f -> p c f", p=128)).then_inc(aw.h, 16)
            G.dma_start(out=wupv, in_=wupvT.rearrange("(c p) f -> p c f", p=128)).then_inc(aw.h, 16)
            aw.v = 64
            issue_ct(0); issue_ct(1)

            for blk in range(NBLK):
                for t in range(4):
                    i = blk * 4 + t
                    bi = i % 3
                    if i + 3 < 4 * NBLK:
                        issue_x(i + 3)
                    V.wait_ge(axp[i % 3].h, (i // 3 + 1) * 16)
                    V.scalar_tensor_tensor(trash, xt[:, bi, :], 1.0, xt[:, bi, :],
                                           ALU.mult, ALU.mult,
                                           accum_out=lns[:, 1:2]).then_inc(alg.h, 1)
                    alg.v += 1
                    V.reduce_sum(lns[:, 0:1], xt[:, bi, :], axis=AX).then_inc(alg.h, 1)
                    alg.v += 1
                    V.wait_ge(alg.h, alg.v)
                    V.tensor_scalar_mul(lns[:, 2:3], lns[:, 0:1], 1.0 / D).then_inc(alg.h, 1)
                    alg.v += 1
                    V.wait_ge(alg.h, alg.v)
                    V.scalar_tensor_tensor(lns[:, 3:4], lns[:, 2:3], 1.0,
                                           lns[:, 2:3], ALU.mult, ALU.mult).then_inc(alg.h, 1)
                    alg.v += 1
                    V.wait_ge(alg.h, alg.v)
                    V.tensor_scalar(lns[:, 4:5], lns[:, 1:2], 1.0 / D,
                                    lns[:, 3:4], ALU.mult, ALU.subtract).then_inc(aln.h, 1)
                    aln.v += 1
                    Sc.wait_ge(aln.h, aln.v)
                    Sc.wait_ge(cstg.h, 1)
                    Sc.activation(lns[:, 5:6], lns[:, 4:5], AF.Sqrt,
                                  bias=cst[:, 1:2], scale=1.0).then_inc(alg.h, 1)
                    alg.v += 1
                    V.wait_ge(alg.h, alg.v)
                    V.reciprocal(lns[:, 6:7], lns[:, 5:6]).then_inc(alg.h, 1)
                    alg.v += 1
                    V.wait_ge(alg.h, alg.v)
                    V.scalar_tensor_tensor(lns[:, 7:8], lns[:, 2:3], -1.0,
                                           lns[:, 6:7], ALU.mult, ALU.mult).then_inc(aln.h, 1)
                    aln.v += 1
                    Sc.wait_ge(aln.h, aln.v)
                    if i >= 3:
                        Sc.wait_ge(atp.h, 4 * (i - 2))
                    Sc.activation(xln[:, bi, :], xt[:, bi, :], AF.Identity,
                                  bias=lns[:, 7:8], scale=lns[:, 6:7]).then_inc(aap.h, 1)
                    aap.v += 1
                    T.wait_ge(aap.h, aap.v)
                    if i == 0:
                        T.wait_ge(pro.h, pro.v)
                        T.wait_ge(prog.h, prog.v)
                    for c8 in range(4):
                        bk = c8 % 2
                        if atp.v >= 2:
                            T.wait_ge(aev.h, atp.v - 1)
                        for cc in range(8):
                            c = c8 * 8 + cc
                            T.transpose(ptr[:, bk * 8 + cc, :],
                                        xln[:, bi, c * 128:(c + 1) * 128], idb)
                        _tlast().then_inc(atp.h, 1); atp.v += 1
                        V.wait_ge(atp.h, atp.v)
                        if blk > 0 and t == 0 and c8 == 0:
                            V.wait_ge(axw.h, blk * 16)
                        V.tensor_copy(
                            xfm[:, c8 * 8:(c8 + 1) * 8, t * 128:(t + 1) * 128],
                            ptr[:, bk * 8:bk * 8 + 8, :]).then_inc(aev.h, 1)
                        aev.v += 1

                Sy.wait_ge(aev.h, aev.v)
                Sy.dma_start(
                    out=scrXfm[:, :, blk * 512:(blk + 1) * 512].rearrange("c p t -> p c t"),
                    in_=xfm).then_inc(axw.h, 16)
                axw.v += 16

                T.wait_ge(aev.h, aev.v)
                if blk == 0:
                    T.wait_ge(aw.h, 64)
                for m in range(4):
                    bk = len(prj_rota) % 2
                    if len(prj_rota) >= 2:
                        T.wait_ge(*prj_rota[-2])
                    for c in range(32):
                        T.matmul(prj[:, bk, :], wkr[:, c, m * 128:(m + 1) * 128],
                                 xfm[:, c, :], start=(c == 0), stop=(c == 31))
                    _tlast().then_inc(amm.h, 1); amm.v += 1
                    V.wait_ge(amm.h, amm.v)
                    V.wait_ge(actp[blk % 2].h, (blk // 2 + 1) * 32)
                    V.wait_ge(akr.h, akr.v)
                    if akr.v >= 2:
                        n = akr.v
                        V.wait_ge(akw[n % 2].h, 32 * (n // 2))
                    for (d0, d1), (s0, s1) in (((0, 32), (32, 64)), ((32, 64), (0, 32)),
                                               ((64, 96), (96, 128)), ((96, 128), (64, 96))):
                        V.tensor_copy(rk1[d0:d1, :], prj[s0:s1, bk, :]).then_inc(alg.h, 1)
                        alg.v += 1
                    V.tensor_mul(rk2, prj[:, bk, :], coskb[:, blk % 2, :]).then_inc(alg.h, 1)
                    alg.v += 1
                    V.wait_ge(alg.h, alg.v)
                    V.tensor_mul(rk3, rk1, sinkb[:, blk % 2, :]).then_inc(alg.h, 1)
                    alg.v += 1
                    V.wait_ge(alg.h, alg.v)
                    V.tensor_add(krp[:, m % 2, :], rk2, rk3).then_inc(akr.h, 1)
                    akr.v += 1
                    prj_rota.append((akr.h, akr.v))
                    Sy.wait_ge(akr.h, akr.v)
                    n = akr.v - 1
                    Sy.dma_start(out=scrK[2 * m, 0:64, blk * 512:(blk + 1) * 512],
                                 in_=krp[0:64, m % 2, :]).then_inc(akw[n % 2].h, 16)
                    Sy.dma_start(out=scrK[2 * m + 1, 0:64, blk * 512:(blk + 1) * 512],
                                 in_=krp[64:128, m % 2, :]).then_inc(akw[n % 2].h, 16)
                    akw[n % 2].v += 32

                if blk + 2 < NBLK:
                    issue_ct(blk + 2)

                for m in range(4):
                    bk = len(prj_rota) % 2
                    T.wait_ge(*prj_rota[-2])
                    for c in range(32):
                        T.matmul(prj[:, bk, :], wdk[:, c, m * 128:(m + 1) * 128],
                                 xfm[:, c, :], start=(c == 0), stop=(c == 31))
                    _tlast().then_inc(amm.h, 1); amm.v += 1
                    V.wait_ge(amm.h, amm.v)
                    V.tensor_copy(ckv[:, m, :], prj[:, bk, :]).then_inc(ack.h, 1)
                    ack.v += 1
                    prj_rota.append((ack.h, ack.v))

                T.wait_ge(ack.h, ack.v)
                for m in range(4):
                    bk = len(prj_rota) % 2
                    T.wait_ge(*prj_rota[-2])
                    for c2 in range(4):
                        T.matmul(prj[:, bk, :], wupk[:, c2, m * 128:(m + 1) * 128],
                                 ckv[:, c2, :], start=(c2 == 0), stop=(c2 == 3))
                    _tlast().then_inc(amm.h, 1); amm.v += 1
                    V.wait_ge(amm.h, amm.v)
                    if akc.v >= 2:
                        n = akc.v
                        V.wait_ge(kcwp[n % 2].h, 32 * (n // 2))
                    V.tensor_copy(kc[:, m % 2, :], prj[:, bk, :]).then_inc(akc.h, 1)
                    akc.v += 1
                    prj_rota.append((akc.h, akc.v))
                    Sy.wait_ge(akc.h, akc.v)
                    n = akc.v - 1
                    Sy.dma_start(out=scrK[2 * m, 64:128, blk * 512:(blk + 1) * 512],
                                 in_=kc[0:64, m % 2, :]).then_inc(kcwp[n % 2].h, 16)
                    Sy.dma_start(out=scrK[2 * m + 1, 64:128, blk * 512:(blk + 1) * 512],
                                 in_=kc[64:128, m % 2, :]).then_inc(kcwp[n % 2].h, 16)
                    kcwp[n % 2].v += 32

                for t in range(4):
                    pr = t % 2
                    if t >= 2:
                        T.wait_ge(avv.h, avv.v - 2)
                    for half in range(2):
                        for c2 in range(4):
                            T.matmul(vps[:, 2 * pr + half, :],
                                     ckv[:, c2, t * 128:(t + 1) * 128],
                                     wupv[:, c2, half * 512:(half + 1) * 512],
                                     start=(c2 == 0), stop=(c2 == 3))
                    _tlast().then_inc(amm.h, 1); amm.v += 1
                    V.wait_ge(amm.h, amm.v)
                    nuse = avv.v // 2
                    if nuse >= 2:
                        V.wait_ge(avw[nuse % 2].h, 16 * (nuse // 2))
                    V.tensor_copy(vst[:, pr, 0:512], vps[:, 2 * pr, :]).then_inc(avv.h, 1)
                    avv.v += 1
                    V.tensor_copy(vst[:, pr, 512:1024], vps[:, 2 * pr + 1, :]).then_inc(avv.h, 1)
                    avv.v += 1
                    Sy.wait_ge(avv.h, avv.v)
                    n = avv.v // 2 - 1
                    Sy.dma_start(
                        out=scrV[blk * 512 + t * 128: blk * 512 + (t + 1) * 128, :],
                        in_=vst[:, pr, :]).then_inc(avw[n % 2].h, 16)
                    avw[n % 2].v += 16

            for s_ in (akw[0], akw[1], kcwp[0], kcwp[1], avw[0], avw[1]):
                Sy.wait_ge(s_.h, s_.v)
            Sy.wait_ge(axw.h, axw.v)
            G.wait_ge(axp[0].h, axp[0].v)
            G.wait_ge(axp[1].h, axp[1].v)

            if DUMP == "A":
                dd = sem("dd")
                Sy.dma_start(out=dbg["dK"], in_=scrK).then_inc(dd.h, 16)
                Sy.dma_start(out=dbg["dV"], in_=scrV).then_inc(dd.h, 16)
                Sy.dma_start(out=dbg["dX"], in_=scrXfm).then_inc(dd.h, 16)
                dd.v += 48
                Sy.wait_ge(dd.h, dd.v)

        nc.all_engine_barrier()

    if "B" in PH:
        es = ExitStack()
        with es:
            def sb(name, shape, dtyp):
                return es.enter_context(nc.sbuf_tensor(name, shape, dtyp)).ap()

            def ps(name, shape, dtyp):
                return es.enter_context(nc.psum_tensor(name, shape, dtyp)).ap()

            xfq = sb("xfq_sb", [128, 32, NQ], BF16)
            kgb = sb("kg_sb", [128, 2, S], BF16)
            vgb = sb("vg_sb", [128, 2, 32, 128], BF16)
            wqb = sb("wq_sb", [128, 2, 32, 128], BF16)
            qq = sb("qq_sb", [128, 2, NQ], BF16)
            qr1 = sb("qr1_sb", [128, 512], F32)
            qr2 = sb("qr2_sb", [128, 512], F32)
            qr3 = sb("qr3_sb", [128, 512], F32)
            pp = sb("p_sb", [128, 6, 512], BF16)
            pss = sb("pss_sb", [128, NQ], F32)
            den = sb("den_sb", [128, 16], F32)
            avf = sb("avf_sb", [128, NQ], BF16)
            avt = sb("avt_sb", [128, 2, 8, 128], BF16)
            sc = ps("sc_ps", [128, 4, 512], F32)
            av = ps("av_ps", [128, 2, 512], F32)
            at = ps("at_ps", [128, 8, 128], BF16)
            pq = sc

            names = ["bxq", "bkv0", "bkv1", "bwq0", "bwq1", "bqm", "brp", "brg",
                     "bsc", "bmk", "bex", "bps", "bavm", "bpt", "bdn", "brc",
                     "bevc", "bavt", "bscl", "bao0", "bao1"]
            bs = {n: sem(n) for n in names}
            snap0 = {n: bs[n].v for n in names}
            deltas = []

            for jb in nc.Switch(engines=[Sy, Sc, nc.tensor, V, G], index=[sv] * 5, n=4):
                for n in names:
                    bs[n].v = snap0[n]
                (bxq, bkv0, bkv1, bwq0, bwq1, bqm, brp, brg, bsc, bmk, bex,
                 bps, bavm, bpt, bdn, brc, bevc, bavt, bscl, bao0, bao1) = (
                    bs[n] for n in names)
                bkvp = [bkv0, bkv1]
                bwqp = [bwq0, bwq1]
                baop = [bao0, bao1]
                base = snap0

                blocks = (jb, 7 - jb)
                spans = []
                for b in blocks:
                    sl = [(kt, 512, 0, False) for kt in range(4 * b)]
                    sl += [(4 * b + d, 512 - 128 * d, 128 * d, True) for d in range(4)]
                    spans.append(sl)

                for h2, b in enumerate(blocks):
                    Sy.dma_start(
                        out=xfq[:, :, h2 * 512:(h2 + 1) * 512],
                        in_=scrXfm[:, :, b * 512:(b + 1) * 512].rearrange("c p t -> p c t"),
                    ).then_inc(bxq.h, 16)
                    bxq.v += 16

                def load_kv(g):
                    if g >= 2:
                        Sy.wait_ge(bavm.h, base["bavm"] + 36 * 4 * (g - 1))
                    s_ = bkvp[g % 2]
                    Sy.dma_start(out=kgb[:, g % 2, :], in_=scrK[g]).then_inc(s_.h, 16)
                    Sy.dma_start(
                        out=vgb[:, g % 2, :, :],
                        in_=scrV[:, g * 128:(g + 1) * 128].rearrange("(c p) h -> p c h", p=128),
                    ).then_inc(s_.h, 16)
                    s_.v += 32

                def load_wq(h):
                    if h >= 2:
                        G.wait_ge(bqm.h, base["bqm"] + 2 * (h - 1))
                    s_ = bwqp[h % 2]
                    G.dma_start(
                        out=wqb[:, h % 2, :, :],
                        in_=wqT[:, h * 128:(h + 1) * 128].rearrange("(c p) f -> p c f", p=128),
                    ).then_inc(s_.h, 16)
                    s_.v += 16

                load_kv(0)
                load_wq(0); load_wq(1)

                for h in range(NH):
                    g = h // 4
                    if h % 4 == 0 and g + 1 < KVH:
                        load_kv(g + 1)
                    if h + 2 < NH:
                        load_wq(h + 2)

                    T.wait_ge(bwqp[h % 2].h, base["bwq%d" % (h % 2)] + (h // 2 + 1) * 16)
                    if h == 0:
                        T.wait_ge(bxq.h, base["bxq"] + 32)
                    if h >= 1:
                        T.wait_ge(bdn.h, base["bdn"] + 8 * h)
                    for span in range(2):
                        for c in range(32):
                            T.matmul(pq[:, span, :], wqb[:, h % 2, c, :],
                                     xfq[:, c, span * 512:(span + 1) * 512],
                                     start=(c == 0), stop=(c == 31))
                        _tlast().then_inc(bqm.h, 1); bqm.v += 1

                    for span in range(2):
                        V.wait_ge(bqm.h, bqm.v - 1 + span)
                        V.wait_ge(brg.h, brg.v)
                        for (d0, d1), (s0, s1) in (((0, 32), (32, 64)), ((32, 64), (0, 32)),
                                                   ((64, 96), (96, 128)), ((96, 128), (64, 96))):
                            V.tensor_copy(qr1[d0:d1, :], pq[s0:s1, span, :]).then_inc(brg.h, 1)
                            brg.v += 1
                        V.tensor_mul(qr2, pq[:, span, :],
                                     cosq[:, span * 512:(span + 1) * 512]).then_inc(brg.h, 1)
                        brg.v += 1
                        V.wait_ge(brg.h, brg.v)
                        V.tensor_mul(qr3, qr1,
                                     sinq[:, span * 512:(span + 1) * 512]).then_inc(brg.h, 1)
                        brg.v += 1
                        V.wait_ge(brg.h, brg.v)
                        if span == 0:
                            V.tensor_add(qq[:, h % 2, 0:512], qr2, qr3).then_inc(brg.h, 1)
                            brg.v += 1
                        else:
                            V.tensor_add(qq[:, h % 2, 512:1024], qr2, qr3).then_inc(brp.h, 1)
                            brp.v += 1

                    T.wait_ge(brp.h, base["brp"] + h + 1)
                    T.wait_ge(bkvp[g % 2].h, base["bkv%d" % (g % 2)] + (g // 2 + 1) * 32)
                    base_sc = bsc.v; base_ex = bex.v
                    base_av = bavm.v; base_ps = bps.v
                    slot = 0
                    nslots = sum(len(sl) for sl in spans)
                    flat = []
                    for span, sl in enumerate(spans):
                        for si, (kt, w, qoff, diag) in enumerate(sl):
                            flat.append((span, si, len(sl), kt, w, qoff, diag))

                    LAG = 3

                    def emit_av(j):
                        span, si, sln, kt, w, qoff, diag = flat[j]
                        if si == 0 and h >= 1:
                            T.wait_ge(bevc.h, base["bevc"] + 2 * (h - 1) + span + 1)
                        T.wait_ge(bex.h, base_ex + j + 1)
                        T.matmul(av[:, span, qoff:qoff + w],
                                 vgb[:, g % 2, kt, :], pp[:, j % 6, 0:w],
                                 start=(si == 0), stop=(si == sln - 1)
                                 ).then_inc(bavm.h, 1)
                        bavm.v += 1

                    for i, (span, si, sln, kt, w, qoff, diag) in enumerate(flat):
                        bk = i % 4
                        pb = i % 6
                        if i >= 4:
                            T.wait_ge(bex.h, base_ex + i - 3)
                        T.matmul(
                            sc[:, bk, 0:w],
                            kgb[:, g % 2, kt * 128:(kt + 1) * 128],
                            qq[:, h % 2, span * 512 + qoff: span * 512 + qoff + w],
                            start=True, stop=True).then_inc(bsc.h, 1)
                        bsc.v += 1
                        if diag:
                            V.wait_ge(bsc.h, base_sc + i + 1)
                            V.tensor_add(sc[:, bk, 0:128], sc[:, bk, 0:128],
                                         tri).then_inc(bmk.h, 1)
                            bmk.v += 1
                        Sc.wait_ge(bmk.h if diag else bsc.h,
                                   bmk.v if diag else base_sc + i + 1)
                        if i >= 6:
                            Sc.wait_ge(bavm.h, base_av + i - 5)
                            Sc.wait_ge(bps.h, base_ps + i - 5)
                        Sc.activation(pp[:, pb, 0:w], sc[:, bk, 0:w], AF.Exp,
                                      bias=cst[:, 0:1], scale=SCALE).then_inc(bex.h, 1)
                        bex.v += 1
                        V.wait_ge(bex.h, base_ex + i + 1)
                        V.wait_ge(bps.h, bps.v)
                        pcols = pss[:, span * 512 + qoff: span * 512 + qoff + w]
                        if si == 0:
                            if span == 0 and h >= 1:
                                V.wait_ge(bpt.h, base["bpt"] + h)
                            V.tensor_copy(pcols, pp[:, pb, 0:w]).then_inc(bps.h, 1)
                        else:
                            V.tensor_add(pcols, pcols, pp[:, pb, 0:w]).then_inc(bps.h, 1)
                        bps.v += 1
                        if i >= LAG:
                            emit_av(i - LAG)
                        if si == sln - 1:
                            pass
                        slot += 1
                    for j in range(nslots - LAG, nslots):
                        emit_av(j)
                    for span, sl in enumerate(spans):
                        V.wait_ge(bavm.h, base_av + sum(len(s) for s in spans[:span + 1]))
                        if h >= 1:
                            V.wait_ge(bavt.h, base["bavt"] + h)
                        V.tensor_copy(avf[:, span * 512:(span + 1) * 512],
                                      av[:, span, :]).then_inc(bevc.h, 1)
                        bevc.v += 1

                    T.wait_ge(bex.h, base_ex + nslots)
                    T.wait_ge(bps.h, base_ps + nslots)
                    for s8 in range(8):
                        T.transpose(sc[:, s8 // 2, (s8 % 2) * 128:(s8 % 2 + 1) * 128],
                                    pss[:, s8 * 128:(s8 + 1) * 128], idf)
                    _tlast().then_inc(bpt.h, 1); bpt.v += 1
                    V.wait_ge(bpt.h, bpt.v)
                    for s8 in range(8):
                        V.reduce_sum(den[:, s8:s8 + 1],
                                     sc[:, s8 // 2, (s8 % 2) * 128:(s8 % 2 + 1) * 128],
                                     axis=AX).then_inc(bdn.h, 1)
                        bdn.v += 1
                    V.wait_ge(bdn.h, bdn.v)
                    V.reciprocal(den[:, 8:16], den[:, 0:8]).then_inc(brc.h, 1)
                    brc.v += 1

                    T.wait_ge(bevc.h, base["bevc"] + 2 * (h + 1))
                    if h >= 1:
                        T.wait_ge(bscl.h, base["bscl"] + 8 * h)
                    for s8 in range(8):
                        T.transpose(at[:, s8, :], avf[:, s8 * 128:(s8 + 1) * 128], idb)
                    _tlast().then_inc(bavt.h, 1); bavt.v += 1
                    V.wait_ge(bavt.h, bavt.v)
                    V.wait_ge(brc.h, brc.v)
                    if h >= 2:
                        V.wait_ge(baop[h % 2].h,
                                  base["bao%d" % (h % 2)] + 16 * ((h - 2) // 2 + 1))
                    for s8 in range(8):
                        V.tensor_scalar(avt[:, h % 2, s8, :], at[:, s8, :],
                                        den[:, 8 + s8:9 + s8], None,
                                        ALU.mult).then_inc(bscl.h, 1)
                        bscl.v += 1
                    Sy.wait_ge(bscl.h, bscl.v)
                    Sy.dma_start(
                        out=scrAv[:, h * 128:(h + 1) * 128].rearrange("(s p) f -> p s f", p=128),
                        in_=avt[:, h % 2, :, :]).then_inc(baop[h % 2].h, 16)
                    baop[h % 2].v += 16

                Sy.wait_ge(bao0.h, bao0.v)
                Sy.wait_ge(bao1.h, bao1.v)
                G.wait_ge(bwq0.h, bwq0.v)
                G.wait_ge(bwq1.h, bwq1.v)
                deltas.append({n: bs[n].v - snap0[n] for n in names})

            assert all(d == deltas[0] for d in deltas[1:]), "switch bodies differ"
            for n in names:
                bs[n].v = snap0[n] + deltas[0][n]

            nc.all_engine_barrier()

            if DUMP == "B":
                dd = sem("dd2")
                Sy.dma_start(out=dbg["dAv"], in_=scrAv).then_inc(dd.h, 16)
                dd.v += 16
                Sy.wait_ge(dd.h, dd.v)
                nc.all_engine_barrier()

    if "C" in PH:
        es = ExitStack()
        with es:
            def sb(name, shape, dtyp):
                return es.enter_context(nc.sbuf_tensor(name, shape, dtyp)).ap()

            def ps(name, shape, dtyp):
                return es.enter_context(nc.psum_tensor(name, shape, dtyp)).ap()

            avT = sb("avT_sb", [128, 32, NQ], BF16)
            wo = sb("wo_sb", [128, 2, 32, 128], BF16)
            ybf = sb("ybf_sb", [128, 2, NQ], BF16)
            xqr = sb("xqr_sb", [128, 2, 8, 128], F32)
            y2t = sb("y2t_sb", [128, 2, 8, 128], F32)
            sumc = sb("sumc_sb", [128, 8, 32], F32)
            ssqc = sb("ssqc_sb", [128, 8, 32], F32)
            ctr = sb("ctrash_sb", [128, 128], BF16)
            py = ps("py_ps", [128, 4, 512], F32)
            pt = ps("pt_ps", [128, 2, 8, 128], BF16)

            cav = sem("cav")
            cwop = [sem("cwo0"), sem("cwo1")]
            cxqp = [sem("cxq0"), sem("cxq1")]
            cswp = [sem("csw0"), sem("csw1")]
            cym = sem("cym"); cev = sem("cev"); ctp = sem("ctp")
            cy2 = sem("cy2"); cg = sem("cg")

            for c in range(32):
                Sy.dma_start_transpose(avT[:, c, :],
                                       scrAv[:, c * 128:(c + 1) * 128]).then_inc(cav.h, 16)
                cav.v += 16

            def load_wo(m):
                s_ = cwop[m % 2]
                G.dma_start(
                    out=wo[:, m % 2, :, :],
                    in_=woT[:, m * 128:(m + 1) * 128].rearrange("(c p) f -> p c f", p=128),
                ).then_inc(s_.h, 16)
                s_.v += 16

            def load_xq(m):
                s_ = cxqp[m % 2]
                Sy.dma_start(
                    out=xqr[:, m % 2, :, :],
                    in_=xq[:, m * 128:(m + 1) * 128].rearrange("(s p) f -> p s f", p=128),
                ).then_inc(s_.h, 16)
                s_.v += 16

            load_wo(0); load_wo(1)
            load_xq(0); load_xq(1)

            for m in range(32):
                T.wait_ge(cwop[m % 2].h, (m // 2 + 1) * 16)
                if m == 0:
                    T.wait_ge(cav.h, cav.v)
                if m >= 2:
                    T.wait_ge(cev.h, 2 * (m - 1))
                for span in range(2):
                    for c in range(32):
                        T.matmul(py[:, 2 * (m % 2) + span, :], wo[:, m % 2, c, :],
                                 avT[:, c, span * 512:(span + 1) * 512],
                                 start=(c == 0), stop=(c == 31))
                _tlast().then_inc(cym.h, 1); cym.v += 1
                if m + 2 < 32:
                    G.wait_ge(cym.h, m + 1)
                    load_wo(m + 2)
                V.wait_ge(cym.h, cym.v)
                V.tensor_copy(ybf[:, m % 2, 0:512], py[:, 2 * (m % 2), :]).then_inc(cev.h, 1)
                cev.v += 1
                V.tensor_copy(ybf[:, m % 2, 512:1024], py[:, 2 * (m % 2) + 1, :]).then_inc(cev.h, 1)
                cev.v += 1
                T.wait_ge(cev.h, cev.v)
                if m >= 2:
                    T.wait_ge(cy2.h, 24 * (m - 1))
                for s in range(8):
                    T.transpose(pt[:, m % 2, s, :],
                                ybf[:, m % 2, s * 128:(s + 1) * 128], idb)
                _tlast().then_inc(ctp.h, 1); ctp.v += 1
                V.wait_ge(ctp.h, ctp.v)
                V.wait_ge(cxqp[m % 2].h, (m // 2 + 1) * 16)
                if m >= 2:
                    V.wait_ge(cswp[m % 2].h, 16 * ((m - 2) // 2 + 1))
                for s in range(8):
                    V.tensor_add(y2t[:, m % 2, s, :], pt[:, m % 2, s, :],
                                 xqr[:, m % 2, s, :]).then_inc(cy2.h, 1)
                    cy2.v += 1
                    V.wait_ge(cy2.h, cy2.v)
                    V.reduce_sum(sumc[:, s, m:m + 1], y2t[:, m % 2, s, :],
                                 axis=AX).then_inc(cy2.h, 1)
                    cy2.v += 1
                    V.scalar_tensor_tensor(ctr, y2t[:, m % 2, s, :], 1.0,
                                           y2t[:, m % 2, s, :], ALU.mult, ALU.mult,
                                           accum_out=ssqc[:, s, m:m + 1]).then_inc(cy2.h, 1)
                    cy2.v += 1
                if m + 2 < 32:
                    Sy.wait_ge(cy2.h, 24 * (m + 1))
                    load_xq(m + 2)
                Sy.wait_ge(cy2.h, cy2.v)
                Sy.dma_start(
                    out=scrY2[:, m * 128:(m + 1) * 128].rearrange("(s p) f -> p s f", p=128),
                    in_=y2t[:, m % 2, :, :]).then_inc(cswp[m % 2].h, 16)
                cswp[m % 2].v += 16

            V.wait_ge(cy2.h, cy2.v)
            for s in range(8):
                V.reduce_sum(stC[:, s, 0:1], sumc[:, s, :], axis=AX).then_inc(cg.h, 1)
                V.reduce_sum(stC[:, s, 1:2], ssqc[:, s, :], axis=AX).then_inc(cg.h, 1)
                cg.v += 2
            V.wait_ge(cg.h, cg.v)
            for s in range(8):
                V.tensor_scalar_mul(stC[:, s, 0:1], stC[:, s, 0:1], 1.0 / D).then_inc(cg.h, 1)
                V.tensor_scalar_mul(stC[:, s, 1:2], stC[:, s, 1:2], 1.0 / D).then_inc(cg.h, 1)
                cg.v += 2
            V.wait_ge(cg.h, cg.v)
            for s in range(8):
                V.scalar_tensor_tensor(stC[:, s, 2:3], stC[:, s, 0:1], 1.0,
                                       stC[:, s, 0:1], ALU.mult, ALU.mult).then_inc(cg.h, 1)
                cg.v += 1
            V.wait_ge(cg.h, cg.v)
            for s in range(8):
                V.tensor_tensor(stC[:, s, 1:2], stC[:, s, 1:2], stC[:, s, 2:3],
                                ALU.subtract).then_inc(cg.h, 1)
                cg.v += 1
            Sc.wait_ge(cg.h, cg.v)
            for s in range(8):
                Sc.activation(stC[:, s, 2:3], stC[:, s, 1:2], AF.Sqrt,
                              bias=cst[:, 1:2], scale=1.0).then_inc(cg.h, 1)
                cg.v += 1
            V.wait_ge(cg.h, cg.v)
            for s in range(8):
                V.reciprocal(stC[:, s, 3:4], stC[:, s, 2:3]).then_inc(cg.h, 1)
                cg.v += 1
            Sy.wait_ge(cswp[0].h, cswp[0].v)
            Sy.wait_ge(cswp[1].h, cswp[1].v)

            if DUMP == "C":
                dd = sem("dd3")
                Sy.dma_start(out=dbg["dY2"], in_=scrY2).then_inc(dd.h, 16)
                dd.v += 16
                Sy.wait_ge(cg.h, cg.v)
                Sy.dma_start(out=dbg["dSt"], in_=stC).then_inc(dd.h, 16)
                dd.v += 16
                Sy.wait_ge(dd.h, dd.v)

        nc.all_engine_barrier()

    if "D" in PH:
        es = ExitStack()
        with es:
            def sb(name, shape, dtyp):
                return es.enter_context(nc.sbuf_tensor(name, shape, dtyp)).ap()

            zfm = sb("zfm_sb", [128, 32, NQ], BF16)
            y2r = sb("y2r_sb", [128, 2, D], F32)
            ztm = sb("ztm_sb", [128, 2, D], BF16)
            gw1s = sb("gw1s_sb", [128, 32, 32], BF16)
            lAs = sb("lA_sb", [128, 32, 128], BF16)
            lBs = sb("lB_sb", [128, 32, 128], BF16)
            hraw = sb("hraw_sb", [32, NQ], F32)
            htm = sb("htm_sb", [128, 8, 32], F32)
            gst = sb("gst_sb", [128, 32], F32)
            gst2 = sb("gst2_sb", [128, 8], F32)
            g1a = sb("g1a_sb", [128, 8], F32)
            g2a = sb("g2a_sb", [128, 8], F32)
            hrel = sb("hrel_sb", [128, 32], F32)
            hrel2 = sb("hrel2_sb", [128, 32], F32)
            hrT = sb("hrT_sb", [33, 128], BF16)
            gts = sb("gts_sb", [128, 8, 8], F32)
            m1 = sb("m1_sb", [128, 8, 8], F32)
            m2 = sb("m2_sb", [128, 8, 8], F32)
            mc = sb("mc_sb", [128, 16], BF16)
            d21 = sb("d21_sb", [128, 8, 3], F32)
            wts = sb("wts_sb", [128, 8], F32)
            wts2 = sb("wts2_sb", [128, 8], F32)
            wT = sb("wT_sb", [8, NQ], BF16)
            cnt = sb("cnt_sb", [1, 16], F32)
            cnt2 = sb("cnt2_sb", [1, 16], F32)
            cntb = sb("cntb_sb", [1, 16], BF16)
            arep = sb("arep_sb", [128, 16], F32)
            tall = sb("tall_sb", [128, NQ], BF16)
            tp = sb("tp_sb", [128, NQ], BF16)
            wff = sb("wff_sb", [128, 2, 32, 128], BF16)
            y3bf = sb("y3bf_sb", [128, 2, NQ], BF16)
            y2rt = sb("y2rt_sb", [128, 2, 8, 128], F32)
            outt = sb("outt_sb", [128, 2, 8, 128], F32)
            dtr = sb("dtrash_sb", [128, 32], BF16)

            dld = sem("dld"); dzn = sem("dzn")
            dy2p = [sem("dy20"), sem("dy21")]
            dzt = sem("dzt"); dze = sem("dze"); dgm = sem("dgm")
            dgv = sem("dgv"); dgg = sem("dgg"); dcm = sem("dcm")
            dt1 = sem("dt1"); dwx = sem("dwx")
            dwfp = [sem("dwf0"), sem("dwf1")]
            dyep = [sem("dye0"), sem("dye1")]
            dstp = [sem("dst0"), sem("dst1")]
            dym = sem("dym"); dyt = sem("dyt"); dou = sem("dou")

            G.dma_start(out=gw1s, in_=gw1T.rearrange("(c p) f -> p c f", p=128)).then_inc(dld.h, 16)
            G.dma_start(out=lAs, in_=lAT.rearrange("(c p) f -> p c f", p=128)).then_inc(dld.h, 16)
            G.dma_start(out=lBs, in_=lB.rearrange("p (c f) -> p c f", f=128)).then_inc(dld.h, 16)
            dld.v += 48
            V.memset(hrT[32:33, :], 1.0)
            V.engine_nop().then_inc(dgg.h, 1); dgg.v += 1

            def load_y2(s):
                s_ = dy2p[s % 2]
                Sy.dma_start(out=y2r[:, s % 2, :],
                             in_=scrY2[s * 128:(s + 1) * 128, :]).then_inc(s_.h, 16)
                s_.v += 16

            load_y2(0); load_y2(1)

            es1 = ExitStack()
            with es1:
                def ps1(name, shape, dtyp):
                    return es1.enter_context(nc.psum_tensor(name, shape, dtyp)).ap()

                ztp = ps1("zt_ps", [128, 16, 128], BF16)
                pg = ps1("pg_ps", [32, 2, 512], F32)
                pt2 = ps1("pt2_ps", [128, 8, 32], F32)
                ptg1 = ps1("ptg1_ps", [32, 128], F32)
                pgt2 = ps1("pgt2_ps", [128, 32], F32)
                pcnt = ps1("pcnt_ps", [1, 16], F32)

                for s in range(8):
                    V.wait_ge(dy2p[s % 2].h, (s // 2 + 1) * 16)
                    if s >= 2:
                        V.wait_ge(dzt.h, 4 * (s - 1))
                    V.tensor_scalar(ztm[:, s % 2, :], y2r[:, s % 2, :],
                                    stC[:, s, 0:1], stC[:, s, 3:4],
                                    ALU.subtract, ALU.mult).then_inc(dzn.h, 1)
                    dzn.v += 1
                    if s + 2 < 8:
                        Sy.wait_ge(dzn.h, s + 1)
                        load_y2(s + 2)

                    T.wait_ge(dzn.h, dzn.v)
                    for c8 in range(4):
                        bk = c8 % 2
                        if dzt.v >= 2:
                            T.wait_ge(dze.h, dzt.v - 1)
                        for cc in range(8):
                            c = c8 * 8 + cc
                            T.transpose(ztp[:, bk * 8 + cc, :],
                                        ztm[:, s % 2, c * 128:(c + 1) * 128], idb)
                        _tlast().then_inc(dzt.h, 1); dzt.v += 1
                        V.wait_ge(dzt.h, dzt.v)
                        V.tensor_copy(zfm[:, c8 * 8:(c8 + 1) * 8, s * 128:(s + 1) * 128],
                                      ztp[:, bk * 8:bk * 8 + 8, :]).then_inc(dze.h, 1)
                        dze.v += 1

                T.wait_ge(dze.h, dze.v)
                T.wait_ge(dld.h, dld.v)
                for span in range(2):
                    for c in range(32):
                        T.matmul(pg[:, span, :], gw1s[:, c, :],
                                 zfm[:, c, span * 512:(span + 1) * 512],
                                 start=(c == 0), stop=(c == 31))
                _tlast().then_inc(dgm.h, 1); dgm.v += 1
                V.wait_ge(dgm.h, dgm.v)
                V.tensor_copy(hraw[:, 0:512], pg[:, 0, :]).then_inc(dgg.h, 1)
                dgg.v += 1
                V.tensor_copy(hraw[:, 512:1024], pg[:, 1, :]).then_inc(dgg.h, 1)
                dgg.v += 1

                T.wait_ge(dgg.h, dgg.v)
                for s in range(8):
                    T.transpose(pt2[:, s, :], hraw[:, s * 128:(s + 1) * 128], idf[0:32, 0:32])
                _tlast().then_inc(dgm.h, 1); dgm.v += 1
                V.wait_ge(dgm.h, dgm.v)
                for s in range(8):
                    V.tensor_add(htm[:, s, :], pt2[:, s, :],
                                 gcons_sb[:, 0:32]).then_inc(dgg.h, 1)
                    dgg.v += 1
                V.wait_ge(dgg.h, dgg.v)
                for s in range(8):
                    V.reduce_sum(gst[:, s:s + 1], htm[:, s, :],
                                 axis=AX).then_inc(dgg.h, 1)
                    dgg.v += 1
                    V.wait_ge(dgg.h, dgg.v)
                    V.scalar_tensor_tensor(dtr, htm[:, s, :], 1.0, htm[:, s, :],
                                           ALU.mult, ALU.mult,
                                           accum_out=gst[:, 8 + s:9 + s]).then_inc(dgg.h, 1)
                    dgg.v += 1
                    V.wait_ge(dgg.h, dgg.v)
                V.tensor_scalar_mul(gst[:, 0:8], gst[:, 0:8], 1.0 / 32).then_inc(dgg.h, 1)
                dgg.v += 1
                V.tensor_scalar_mul(gst[:, 8:16], gst[:, 8:16], 1.0 / 32).then_inc(dgg.h, 1)
                dgg.v += 1
                V.wait_ge(dgg.h, dgg.v)
                V.scalar_tensor_tensor(gst2[:, 0:8], gst[:, 0:8], 1.0,
                                       gst[:, 0:8], ALU.mult, ALU.mult).then_inc(dgg.h, 1)
                dgg.v += 1
                V.wait_ge(dgg.h, dgg.v)
                V.tensor_tensor(gst[:, 8:16], gst[:, 8:16], gst2[:, 0:8],
                                ALU.subtract).then_inc(dgg.h, 1)
                dgg.v += 1
                Sc.wait_ge(dgg.h, dgg.v)
                Sc.activation(gst[:, 16:24], gst[:, 8:16], AF.Sqrt,
                              bias=cst[:, 1:2], scale=1.0).then_inc(dgg.h, 1)
                dgg.v += 1
                V.wait_ge(dgg.h, dgg.v)
                V.reciprocal(gst[:, 24:32], gst[:, 16:24]).then_inc(dgg.h, 1)
                dgg.v += 1

                for s in range(8):
                    V.wait_ge(dgg.h, dgg.v)
                    V.wait_ge(dgv.h, dgv.v)
                    V.tensor_scalar(hrel, htm[:, s, :], gst[:, s:s + 1],
                                    gst[:, 24 + s:25 + s], ALU.subtract,
                                    ALU.mult).then_inc(dgv.h, 1)
                    dgv.v += 1
                    V.wait_ge(dgv.h, dgv.v)
                    V.scalar_tensor_tensor(hrel2, hrel, 1.0, gcons_sb[:, 32:64],
                                           ALU.mult, ALU.mult).then_inc(dgv.h, 1)
                    dgv.v += 1
                    V.wait_ge(dgv.h, dgv.v)
                    V.scalar_tensor_tensor(hrel, hrel2, 1.0, gcons_sb[:, 64:96],
                                           ALU.mult, ALU.add).then_inc(dgv.h, 1)
                    dgv.v += 1
                    V.wait_ge(dgv.h, dgv.v)
                    V.tensor_scalar_max(hrel2, hrel, 0.0).then_inc(dgv.h, 1)
                    dgv.v += 1
                    T.wait_ge(dgv.h, dgv.v)
                    T.transpose(ptg1, hrel2, idf).then_inc(dgm.h, 1); dgm.v += 1
                    V.wait_ge(dgm.h, dgm.v)
                    V.tensor_copy(hrT[0:32, :], ptg1[0:32, :]).then_inc(dgv.h, 1)
                    dgv.v += 1
                    T.wait_ge(dgv.h, dgv.v)
                    T.matmul(pgt2[:, 0:8], hrT, gw2_sb, start=True, stop=True
                             ).then_inc(dgm.h, 1)
                    dgm.v += 1
                    V.wait_ge(dgm.h, dgm.v)
                    V.tensor_copy(gts[:, s, :], pgt2[:, 0:8]).then_inc(dgv.h, 1)
                    dgv.v += 1
                    V.wait_ge(dgv.h, dgv.v)
                    V.reduce_max(d21[:, s, 0:1], gts[:, s, :], axis=AX).then_inc(dgv.h, 1)
                    dgv.v += 1
                    V.wait_ge(dgv.h, dgv.v)
                    V.tensor_scalar(m1[:, s, :], gts[:, s, :], d21[:, s, 0:1],
                                    None, ALU.is_equal).then_inc(dgv.h, 1)
                    dgv.v += 1
                    V.wait_ge(dgv.h, dgv.v)
                    V.scalar_tensor_tensor(m2[:, s, :], m1[:, s, :], -1e30,
                                           gts[:, s, :], ALU.mult, ALU.add).then_inc(dgv.h, 1)
                    dgv.v += 1
                    V.wait_ge(dgv.h, dgv.v)
                    V.reduce_max(d21[:, s, 1:2], m2[:, s, :], axis=AX).then_inc(dgv.h, 1)
                    dgv.v += 1
                    V.wait_ge(dgv.h, dgv.v)
                    V.tensor_scalar(m2[:, s, :], m2[:, s, :], d21[:, s, 1:2],
                                    None, ALU.is_equal).then_inc(dgv.h, 1)
                    dgv.v += 1
                    V.tensor_tensor(d21[:, s, 2:3], d21[:, s, 1:2], d21[:, s, 0:1],
                                    ALU.subtract).then_inc(dgv.h, 1)
                    dgv.v += 1
                    V.wait_ge(dgv.h, dgv.v)
                    V.wait_ge(dcm.h, dcm.v)
                    V.tensor_copy(mc[:, 0:8], m1[:, s, :]).then_inc(dgv.h, 1)
                    dgv.v += 1
                    V.tensor_copy(mc[:, 8:16], m2[:, s, :]).then_inc(dgv.h, 1)
                    dgv.v += 1
                    T.wait_ge(dgv.h, dgv.v)
                    T.matmul(pcnt, ones_c, mc, start=(s == 0), stop=(s == 7)
                             ).then_inc(dcm.h, 1)
                    dcm.v += 1

                Sc.wait_ge(dgv.h, dgv.v)
                Sc.activation(g1a, d21[:, :, 2:3].rearrange("p s c -> p (s c)"),
                              AF.Sigmoid, bias=0.0, scale=cst[:, 2:3]).then_inc(dgg.h, 1)
                dgg.v += 1
                Sc.activation(g2a, d21[:, :, 2:3].rearrange("p s c -> p (s c)"),
                              AF.Sigmoid, bias=0.0, scale=1.0).then_inc(dgg.h, 1)
                dgg.v += 1

                dccs = sem("dccs"); dccc = sem("dccc"); dccg = sem("dccg")
                V.wait_ge(dcm.h, dcm.v)
                V.tensor_copy(cnt, pcnt).then_inc(dgv.h, 1)
                dgv.v += 1
                Sy.wait_ge(dgv.h, dgv.v)
                Sy.dma_start(out=cntI, in_=cnt).then_inc(dccs.h, 16)
                dccs.v += 16
                G.wait_ge(dccs.h, dccs.v)
                if RGN > 1:
                    G.collective_compute(
                        "AllReduce", ALU.add, replica_groups=[list(range(RGN))],
                        ins=[cntI], outs=[cntO]).then_inc(dccc.h, 1)
                    dccc.v += 1
                    G.wait_ge(dccc.h, dccc.v)
                    G.dma_start(out=cnt2, in_=cntO).then_inc(dccg.h, 16)
                else:
                    G.dma_start(out=cnt2, in_=cntI).then_inc(dccg.h, 16)
                dccg.v += 16
                V.wait_ge(dccg.h, dccg.v)
                V.tensor_scalar(cnt, cnt2, cst[0:1, 3:4], None,
                                ALU.is_le).then_inc(dgv.h, 1)
                dgv.v += 1
                V.wait_ge(dgv.h, dgv.v)
                V.tensor_copy(cntb, cnt).then_inc(dgv.h, 1)
                dgv.v += 1
                T.wait_ge(dgv.h, dgv.v)
                T.matmul(pgt2[:, 0:16], ones_r, cntb, start=True, stop=True
                         ).then_inc(dgm.h, 1)
                dgm.v += 1
                V.wait_ge(dgm.h, dgm.v)
                V.tensor_copy(arep, pgt2[:, 0:16]).then_inc(dgv.h, 1)
                dgv.v += 1

                for s in range(8):
                    V.wait_ge(dgg.h, dgg.v)
                    V.wait_ge(dgv.h, dgv.v)
                    V.tensor_scalar(wts, m1[:, s, :], g1a[:, s:s + 1],
                                    None, ALU.mult).then_inc(dgv.h, 1)
                    dgv.v += 1
                    V.tensor_scalar(wts2, m2[:, s, :], g2a[:, s:s + 1],
                                    None, ALU.mult).then_inc(dgv.h, 1)
                    dgv.v += 1
                    V.wait_ge(dgv.h, dgv.v)
                    V.tensor_mul(wts, wts, arep[:, 0:8]).then_inc(dgv.h, 1)
                    dgv.v += 1
                    V.tensor_mul(wts2, wts2, arep[:, 8:16]).then_inc(dgv.h, 1)
                    dgv.v += 1
                    V.wait_ge(dgv.h, dgv.v)
                    V.tensor_add(wts, wts, wts2).then_inc(dgv.h, 1)
                    dgv.v += 1
                    T.wait_ge(dgv.h, dgv.v)
                    T.transpose(ptg1[0:8, :], wts, idf).then_inc(dgm.h, 1)
                    dgm.v += 1
                    V.wait_ge(dgm.h, dgm.v)
                    V.tensor_copy(wT[:, s * 128:(s + 1) * 128],
                                  ptg1[0:8, :]).then_inc(dgv.h, 1)
                    dgv.v += 1

            es2 = ExitStack()
            with es2:
                def ps2(name, shape, dtyp):
                    return es2.enter_context(nc.psum_tensor(name, shape, dtyp)).ap()

                py3 = ps2("py3_ps", [128, 4, 512], F32)
                pt3 = ps2("pt3_ps", [128, 2, 8, 128], BF16)
                pwx = ps2("pwx_ps", [128, 2, 512], F32)

                T.wait_ge(dld.h, dld.v)
                for span in range(2):
                    for c in range(32):
                        T.matmul(py3[:, span, :], lAs[:, c, :],
                                 zfm[:, c, span * 512:(span + 1) * 512],
                                 start=(c == 0), stop=(c == 31))
                _tlast().then_inc(dt1.h, 1); dt1.v += 1
                V.wait_ge(dt1.h, dt1.v)
                V.tensor_copy(tall[:, 0:512], py3[:, 0, :]).then_inc(dt1.h, 1)
                dt1.v += 1
                V.tensor_copy(tall[:, 512:1024], py3[:, 1, :]).then_inc(dt1.h, 1)
                dt1.v += 1

                T.wait_ge(dgv.h, dgv.v)
                T.wait_ge(dt1.h, dt1.v)
                for span in range(2):
                    T.matmul(pwx[:, span, :], e16,
                             wT[:, span * 512:(span + 1) * 512],
                             start=True, stop=True)
                _tlast().then_inc(dwx.h, 1); dwx.v += 1
                V.wait_ge(dwx.h, dwx.v)
                V.tensor_mul(tp[:, 0:512], tall[:, 0:512], pwx[:, 0, :]).then_inc(dwx.h, 1)
                dwx.v += 1
                V.tensor_mul(tp[:, 512:1024], tall[:, 512:1024],
                             pwx[:, 1, :]).then_inc(dwx.h, 1)
                dwx.v += 1

                def load_wff(m):
                    s_ = dwfp[m % 2]
                    G.dma_start(
                        out=wff[:, m % 2, :, :],
                        in_=wffT[:, m * 128:(m + 1) * 128].rearrange("(c p) f -> p c f", p=128),
                    ).then_inc(s_.h, 16)
                    s_.v += 16

                def load_y2t(m):
                    s_ = dyep[m % 2]
                    Sy.dma_start(
                        out=y2rt[:, m % 2, :, :],
                        in_=scrY2[:, m * 128:(m + 1) * 128].rearrange("(s p) f -> p s f", p=128),
                    ).then_inc(s_.h, 16)
                    s_.v += 16

                load_wff(0); load_wff(1)
                load_y2t(0); load_y2t(1)

                for m in range(32):
                    T.wait_ge(dwfp[m % 2].h, (m // 2 + 1) * 16)
                    if m == 0:
                        T.wait_ge(dt1.h, dt1.v)
                    if m >= 2:
                        T.wait_ge(dym.h, 3 * (m - 1))
                    for span in range(2):
                        for c in range(32):
                            T.matmul(py3[:, 2 * (m % 2) + span, :], wff[:, m % 2, c, :],
                                     zfm[:, c, span * 512:(span + 1) * 512],
                                     start=(c == 0), stop=False)
                        T.wait_ge(dwx.h, dwx.v)
                        T.matmul(py3[:, 2 * (m % 2) + span, :], lBs[:, m, :],
                                 tp[:, span * 512:(span + 1) * 512],
                                 start=False, stop=True)
                    _tlast().then_inc(dym.h, 1); dym.v += 1
                    if m + 2 < 32:
                        G.wait_ge(dym.h, 3 * m + 1)
                        load_wff(m + 2)
                    V.wait_ge(dym.h, dym.v)
                    V.tensor_copy(y3bf[:, m % 2, 0:512],
                                  py3[:, 2 * (m % 2), :]).then_inc(dym.h, 1)
                    dym.v += 1
                    V.tensor_copy(y3bf[:, m % 2, 512:1024],
                                  py3[:, 2 * (m % 2) + 1, :]).then_inc(dym.h, 1)
                    dym.v += 1
                    T.wait_ge(dym.h, dym.v)
                    if m == 0:
                        T.wait_ge(dgv.h, dgv.v)
                    if m >= 2:
                        T.wait_ge(dou.h, 8 * (m - 1))
                    for s in range(8):
                        T.transpose(pt3[:, m % 2, s, :],
                                    y3bf[:, m % 2, s * 128:(s + 1) * 128], idb)
                    _tlast().then_inc(dyt.h, 1); dyt.v += 1
                    V.wait_ge(dyt.h, dyt.v)
                    V.wait_ge(dyep[m % 2].h, (m // 2 + 1) * 16)
                    if m >= 2:
                        V.wait_ge(dstp[m % 2].h, 16 * ((m - 2) // 2 + 1))
                    for s in range(8):
                        V.tensor_add(outt[:, m % 2, s, :], pt3[:, m % 2, s, :],
                                     y2rt[:, m % 2, s, :]).then_inc(dou.h, 1)
                        dou.v += 1
                    if m + 2 < 32:
                        Sy.wait_ge(dou.h, 8 * (m + 1))
                        load_y2t(m + 2)
                    Sy.wait_ge(dou.h, dou.v)
                    Sy.dma_start(
                        out=out[:, m * 128:(m + 1) * 128].rearrange("(s p) f -> p s f", p=128),
                        in_=outt[:, m % 2, :, :]).then_inc(dstp[m % 2].h, 16)
                    dstp[m % 2].v += 16

                Sy.wait_ge(dstp[0].h, dstp[0].v)
                Sy.wait_ge(dstp[1].h, dstp[1].v)
                G.wait_ge(dwfp[0].h, dwfp[0].v)
                G.wait_ge(dwfp[1].h, dwfp[1].v)

    return nc, dbg




def _host_prep(inputs):
    x = np.ascontiguousarray(np.asarray(inputs["L0_attn_hidden_states"], dtype=np.float32))
    Wq = np.asarray(inputs["L0_attn_Wq"], dtype=np.float32)
    Wkr = np.asarray(inputs["L0_attn_Wkr"], dtype=np.float32)
    Wdk = np.asarray(inputs["L0_attn_Wdk"], dtype=np.float32)
    Wupk = np.asarray(inputs["L0_attn_Wupk"], dtype=np.float32)
    Wupv = np.asarray(inputs["L0_attn_Wupv"], dtype=np.float32)
    Wo = np.asarray(inputs["L0_attn_Wo"], dtype=np.float32)
    Wff = np.asarray(inputs["L0_ffn_weight"], dtype=np.float32)
    lora_A = np.asarray(inputs["L0_ffn_lora_A"], dtype=np.float32)
    lora_B = np.asarray(inputs["L0_ffn_lora_B"], dtype=np.float32)
    gw1 = np.asarray(inputs["L0_ffn_gw1"], dtype=np.float32)
    gb1 = np.asarray(inputs["L0_ffn_gb1"], dtype=np.float32)
    lng = np.asarray(inputs["L0_ffn_ln_gamma"], dtype=np.float32)
    lnb = np.asarray(inputs["L0_ffn_ln_beta"], dtype=np.float32)
    gw2 = np.asarray(inputs["L0_ffn_gw2"], dtype=np.float32)
    gb2 = np.asarray(inputs["L0_ffn_gb2"], dtype=np.float32)

    blockperm = np.concatenate([np.arange(0, 32), np.arange(64, 96),
                                np.arange(32, 64), np.arange(96, 128)])
    qperm = np.concatenate([h * 128 + blockperm for h in range(32)])

    C = np.ascontiguousarray
    common = {
        "wqT": C(Wq.T[:, qperm]),
        "wkrT": C(Wkr.T),
        "wdkT": C(Wdk.T),
        "wupkT": C(Wupk.T),
        "wupvT": C(Wupv.T),
        "woT": C(Wo.T),
        "wffT": C(Wff.T),
        "lAT": C(lora_A.transpose(1, 0, 2).reshape(D, 128)),
        "lB": C(lora_B.reshape(128, D)),
        "gw1T": C(gw1.T),
        "gw2aug": C(np.concatenate([gw2.T, gb2[None, :]], axis=0)),
        "gcons": C(np.broadcast_to(
            np.concatenate([gb1, lng, lnb])[None, :], (128, 96)).copy()),
    }

    p = np.arange(128)
    common["tri"] = C(np.where(p[:, None] <= p[None, :], 0.0, NEG).astype(np.float32))
    import ml_dtypes
    common["ident"] = np.eye(128, dtype=ml_dtypes.bfloat16)
    common["identf"] = np.eye(128, dtype=np.float32)
    e16m = np.zeros((8, 128), dtype=np.float32)
    for e in range(8):
        e16m[e, e * 16:(e + 1) * 16] = 1.0 / 16.0
    common["e16"] = e16m
    common["onescol"] = np.ones((128, 1), dtype=np.float32)
    common["onesrow"] = np.ones((1, 128), dtype=np.float32)

    invf = (10000.0 ** (-np.arange(0, HD, 2, dtype=np.float32) / HD))
    pos = np.arange(S, dtype=np.float32)
    i32 = np.arange(32)
    qfreq = np.concatenate([invf[i32], invf[i32], invf[32 + i32], invf[32 + i32]])
    qsign = np.concatenate([-np.ones(32), np.ones(32), -np.ones(32), np.ones(32)])
    kfreq = np.concatenate([invf[i32]] * 4)
    ksign = np.concatenate([-np.ones(32), np.ones(32)] * 2)

    ang_k = kfreq[:, None] * pos[None, :]
    common["coskD"] = C(np.cos(ang_k).astype(np.float32))
    common["sinkD"] = C((np.sin(ang_k) * ksign[:, None]).astype(np.float32))

    per_core = []
    qpos_all = []
    for c in range(8):
        beta, j = c // 4, c % 4
        qpos = np.concatenate([np.arange(j * 512, (j + 1) * 512),
                               np.arange((7 - j) * 512, (8 - j) * 512)])
        qpos_all.append((beta, qpos))
        ang_q = qfreq[:, None] * qpos[None, :].astype(np.float32)
        d = dict(common)
        d["meta"] = np.array([[j]], dtype=np.int32)
        d["xb"] = C(x[beta])
        d["xq"] = C(x[beta][qpos])
        d["cosqD"] = C(np.cos(ang_q).astype(np.float32))
        d["sinqD"] = C((np.sin(ang_q) * qsign[:, None]).astype(np.float32))
        per_core.append(d)
    return per_core, qpos_all


def kernel(**inputs):
    per_core, qpos_all = _host_prep(inputs)
    nc, _ = _build()
    res = run_bass_kernel_spmd(nc, per_core, core_ids=list(range(8)))
    out = np.empty((B, S, D), dtype=np.float32)
    for c in range(8):
        beta, qpos = qpos_all[c]
        out[beta, qpos] = res.results[c]["out"]
    return out


# version 05
# speedup vs baseline: 1.2479x; 1.0141x over previous
import math
from contextlib import ExitStack
import numpy as np

import concourse.bass as bass
import concourse.mybir as mybir
from concourse.bass_utils import run_bass_kernel_spmd
from concourse.ordered_engine_set import OrderedEngineSet

F32 = mybir.dt.float32
BF16 = mybir.dt.bfloat16
I32 = mybir.dt.int32
AF = mybir.ActivationFunctionType
ALU = mybir.AluOpType
ET = mybir.EngineType
AX = mybir.AxisListType.X

D = 4096
S = 4096
B = 2
NH = 32
KVH = 8
HD = 128
CD = 512
RD = 512
NQ = 1024
NBLK = 8
SCALE = 1.0 / math.sqrt(HD)
SHIFT = -30.0
NEG = -1.0e9
EPS = 1e-5
CAPACITY = float(int(3.0 * (B * S) / 8))

DUMP = None
PH = "ABCD"
RGN = 8


_TLAST = None


def _set_tlast(inst):
    global _TLAST
    _TLAST = inst


def _tlast():
    return _TLAST


class Sem:
    def __init__(self, h):
        self.h = h
        self.v = 0

    def inc(self, inst, n=1):
        self.v += n
        inst.then_inc(self.h, n)


def _build():
    nc = bass.Bass("TRN2", target_bir_lowering=False)
    dt = nc.dram_tensor
    meta = dt("meta", [1, 1], I32, kind="ExternalInput").ap()
    xb = dt("xb", [S, D], F32, kind="ExternalInput").ap()
    xq = dt("xq", [NQ, D], F32, kind="ExternalInput").ap()
    wqT = dt("wqT", [D, D], F32, kind="ExternalInput").ap()
    wkrT = dt("wkrT", [D, RD], F32, kind="ExternalInput").ap()
    wdkT = dt("wdkT", [D, CD], F32, kind="ExternalInput").ap()
    wupkT = dt("wupkT", [CD, 512], F32, kind="ExternalInput").ap()
    wupvT = dt("wupvT", [CD, 1024], F32, kind="ExternalInput").ap()
    woT = dt("woT", [D, D], F32, kind="ExternalInput").ap()
    wffT = dt("wffT", [D, D], F32, kind="ExternalInput").ap()
    lAT = dt("lAT", [D, 128], F32, kind="ExternalInput").ap()
    lB = dt("lB", [128, D], F32, kind="ExternalInput").ap()
    gw1T = dt("gw1T", [D, 32], F32, kind="ExternalInput").ap()
    gw2aug = dt("gw2aug", [33, 8], F32, kind="ExternalInput").ap()
    gcons = dt("gcons", [128, 96], F32, kind="ExternalInput").ap()
    triD = dt("tri", [128, 128], F32, kind="ExternalInput").ap()
    identD = dt("ident", [128, 128], BF16, kind="ExternalInput").ap()
    identfD = dt("identf", [128, 128], F32, kind="ExternalInput").ap()
    e16D = dt("e16", [8, 128], F32, kind="ExternalInput").ap()
    onescolD = dt("onescol", [128, 1], F32, kind="ExternalInput").ap()
    onesrowD = dt("onesrow", [1, 128], F32, kind="ExternalInput").ap()
    cosqD = dt("cosqD", [128, NQ], F32, kind="ExternalInput").ap()
    sinqD = dt("sinqD", [128, NQ], F32, kind="ExternalInput").ap()
    coskD = dt("coskD", [128, S], F32, kind="ExternalInput").ap()
    sinkD = dt("sinkD", [128, S], F32, kind="ExternalInput").ap()
    out = dt("out", [NQ, D], F32, kind="ExternalOutput").ap()

    scrK = dt("scrK", [KVH, 128, S], BF16).ap()
    scrV = dt("scrV", [S, KVH * HD], BF16).ap()
    scrXfm = dt("scrXfm", [32, 128, S], BF16).ap()
    scrAv = dt("scrAv", [NQ, D], BF16).ap()
    scrY2 = dt("scrY2", [NQ, D], F32).ap()
    cntI = dt("cntI", [1, 16], F32).ap()
    cntO = dt("cntO", [1, 16], F32).ap()

    dbg = {}
    if DUMP == "A":
        dbg["dK"] = dt("dK", [KVH, 128, S], BF16, kind="ExternalOutput").ap()
        dbg["dV"] = dt("dV", [S, KVH * HD], BF16, kind="ExternalOutput").ap()
        dbg["dX"] = dt("dX", [32, 128, S], BF16, kind="ExternalOutput").ap()
    if DUMP == "B":
        dbg["dAv"] = dt("dAv", [NQ, D], BF16, kind="ExternalOutput").ap()
    if DUMP == "C":
        dbg["dY2"] = dt("dY2", [NQ, D], F32, kind="ExternalOutput").ap()
        dbg["dSt"] = dt("dSt", [128, 8, 4], F32, kind="ExternalOutput").ap()

    sems = {}

    def sem(name):
        if name not in sems:
            sems[name] = Sem(nc.semaphore(name).__enter__())
        return sems[name]

    al = nc.alloc_sbuf_tensor
    cst = al("cst", [128, 8], F32).ap()
    tri = al("tri_sb", [128, 128], F32).ap()
    idb = al("id_sb", [128, 128], BF16).ap()
    idf = al("idf_sb", [128, 128], F32).ap()
    e16 = al("e16_sb", [8, 128], BF16).ap()
    ones_c = al("ones_c", [128, 1], BF16).ap()
    ones_r = al("ones_r", [1, 128], BF16).ap()
    gcons_sb = al("gcons_sb", [128, 96], F32).ap()
    gw2_sb = al("gw2_sb", [33, 8], BF16).ap()
    cosq = al("cosq", [128, NQ], F32).ap()
    sinq = al("sinq", [128, NQ], F32).ap()
    stC = al("stC", [128, 8, 4], F32).ap()

    V = nc.vector
    Sc = nc.scalar
    G = nc.gpsimd
    Sy = nc.sync

    class _TProxy:
        def __init__(self, eng):
            self._e = eng

        def matmul(self, *a, **k):
            global _TLAST
            inst = self._e.matmul(*a, **k)
            _set_tlast(inst)
            return inst

        def transpose(self, *a, **k):
            inst = self._e.transpose(*a, **k)
            _set_tlast(inst)
            return inst

        def __getattr__(self, name):
            return getattr(self._e, name)

    T = _TProxy(nc.tensor)

    pro = sem("pro")
    prog = sem("prog")
    for dst_, src_ in ((tri, triD), (gcons_sb, gcons), (cosq, cosqD), (sinq, sinqD),
                       (idf, identfD)):
        Sy.dma_start(out=dst_, in_=src_).then_inc(pro.h, 16)
        pro.v += 16
    for dst_, src_ in ((idb, identD), (e16, e16D), (ones_c, onescolD),
                       (ones_r, onesrowD), (gw2_sb, gw2aug)):
        G.dma_start(out=dst_, in_=src_).then_inc(prog.h, 16)
        prog.v += 16
    cstg = sem("cstg")
    V.memset(cst[:, 0:1], SHIFT)
    V.memset(cst[:, 1:2], EPS)
    V.memset(cst[:, 2:3], -1.0)
    V.memset(cst[:, 3:4], CAPACITY)
    V.memset(cst[:, 4:5], 1.0)
    V.engine_nop().then_inc(cstg.h, 1)
    cstg.v += 1

    regs = nc.alloc_registers(
        "jreg", OrderedEngineSet([ET.SP, ET.Activation, ET.PE, ET.DVE, ET.Pool]))
    for r in regs:
        nc.reg_load(r, meta.to_broadcast((1, 1)))
    sv = nc.snap(regs, min_val=0, max_val=3)

    nc.all_engine_barrier()

    if "A" in PH:
        es = ExitStack()
        with es:
            def sb(name, shape, dtyp):
                return es.enter_context(nc.sbuf_tensor(name, shape, dtyp)).ap()

            def ps(name, shape, dtyp):
                return es.enter_context(nc.psum_tensor(name, shape, dtyp)).ap()

            wkr = sb("wkr_sb", [128, 32, RD], BF16)
            wdk = sb("wdk_sb", [128, 32, CD], BF16)
            wupk = sb("wupk_sb", [128, 4, 512], BF16)
            wupv = sb("wupv_sb", [128, 4, 1024], BF16)
            xt = sb("xt_sb", [128, 3, D], BF16)
            xln = sb("xln_sb", [128, 3, D], BF16)
            xfm = sb("xfm_sb", [128, 32, 512], BF16)
            ckv = sb("ckv_sb", [128, 4, 512], BF16)
            krp = sb("krp_sb", [128, 2, 512], BF16)
            kc = sb("kc_sb", [128, 2, 512], BF16)
            vst = sb("vst_sb", [128, 2, 1024], BF16)
            rk1 = sb("rk1_sb", [128, 512], F32)
            rk2 = sb("rk2_sb", [128, 512], F32)
            rk3 = sb("rk3_sb", [128, 512], F32)
            coskb = sb("cosk_sb", [128, 2, 512], F32)
            sinkb = sb("sink_sb", [128, 2, 512], F32)
            lns = sb("lns_sb", [128, 16], F32)
            trash = sb("trash_sb", [128, D], BF16)
            ptr = ps("ptr_ps", [128, 16, 128], BF16)
            prj = ps("prj_ps", [128, 2, 512], F32)
            vps = ps("v_ps", [128, 4, 512], F32)

            aw = sem("aw"); act = sem("act")
            axp = [sem("ax0"), sem("ax1"), sem("ax2")]
            aln = sem("aln"); alg = sem("alg"); aap = sem("aap")
            atp = sem("atp"); aev = sem("aev"); amm = sem("amm")
            akr = sem("akr"); ack = sem("ack"); akc = sem("akc")
            avv = sem("avv"); axw = sem("axw")
            akw = [sem("akw0"), sem("akw1")]
            kcwp = [sem("kcw0"), sem("kcw1")]
            avw = [sem("avw0"), sem("avw1")]
            actp = [sem("act0"), sem("act1")]

            prj_rota = []

            def issue_x(i):
                if i >= 3:
                    G.wait_ge(aap.h, i - 2)
                G.dma_start(out=xt[:, i % 3, :],
                            in_=xb[i * 128:(i + 1) * 128, :]).then_inc(axp[i % 3].h, 16)
                axp[i % 3].v += 16

            def issue_ct(b):
                if b >= 2:
                    Sy.wait_ge(akr.h, 4 * (b - 1))
                Sy.dma_start(out=coskb[:, b % 2, :],
                             in_=coskD[:, b * 512:(b + 1) * 512]).then_inc(actp[b % 2].h, 16)
                Sy.dma_start(out=sinkb[:, b % 2, :],
                             in_=sinkD[:, b * 512:(b + 1) * 512]).then_inc(actp[b % 2].h, 16)
                actp[b % 2].v += 32

            issue_x(0); issue_x(1); issue_x(2)
            G.dma_start(out=wkr, in_=wkrT.rearrange("(c p) f -> p c f", p=128)).then_inc(aw.h, 16)
            G.dma_start(out=wdk, in_=wdkT.rearrange("(c p) f -> p c f", p=128)).then_inc(aw.h, 16)
            G.dma_start(out=wupk, in_=wupkT.rearrange("(c p) f -> p c f", p=128)).then_inc(aw.h, 16)
            G.dma_start(out=wupv, in_=wupvT.rearrange("(c p) f -> p c f", p=128)).then_inc(aw.h, 16)
            aw.v = 64
            issue_ct(0); issue_ct(1)

            for blk in range(NBLK):
                for t in range(4):
                    i = blk * 4 + t
                    bi = i % 3
                    if i + 3 < 4 * NBLK:
                        issue_x(i + 3)
                    V.wait_ge(axp[i % 3].h, (i // 3 + 1) * 16)
                    V.scalar_tensor_tensor(trash, xt[:, bi, :], 1.0, xt[:, bi, :],
                                           ALU.mult, ALU.mult,
                                           accum_out=lns[:, 1:2]).then_inc(alg.h, 1)
                    alg.v += 1
                    V.reduce_sum(lns[:, 0:1], xt[:, bi, :], axis=AX).then_inc(alg.h, 1)
                    alg.v += 1
                    V.wait_ge(alg.h, alg.v)
                    V.tensor_scalar_mul(lns[:, 2:3], lns[:, 0:1], 1.0 / D).then_inc(alg.h, 1)
                    alg.v += 1
                    V.wait_ge(alg.h, alg.v)
                    V.scalar_tensor_tensor(lns[:, 3:4], lns[:, 2:3], 1.0,
                                           lns[:, 2:3], ALU.mult, ALU.mult).then_inc(alg.h, 1)
                    alg.v += 1
                    V.wait_ge(alg.h, alg.v)
                    V.tensor_scalar(lns[:, 4:5], lns[:, 1:2], 1.0 / D,
                                    lns[:, 3:4], ALU.mult, ALU.subtract).then_inc(aln.h, 1)
                    aln.v += 1
                    Sc.wait_ge(aln.h, aln.v)
                    Sc.wait_ge(cstg.h, 1)
                    Sc.activation(lns[:, 5:6], lns[:, 4:5], AF.Sqrt,
                                  bias=cst[:, 1:2], scale=1.0).then_inc(alg.h, 1)
                    alg.v += 1
                    V.wait_ge(alg.h, alg.v)
                    V.reciprocal(lns[:, 6:7], lns[:, 5:6]).then_inc(alg.h, 1)
                    alg.v += 1
                    V.wait_ge(alg.h, alg.v)
                    V.scalar_tensor_tensor(lns[:, 7:8], lns[:, 2:3], -1.0,
                                           lns[:, 6:7], ALU.mult, ALU.mult).then_inc(aln.h, 1)
                    aln.v += 1
                    Sc.wait_ge(aln.h, aln.v)
                    if i >= 3:
                        Sc.wait_ge(atp.h, 4 * (i - 2))
                    Sc.activation(xln[:, bi, :], xt[:, bi, :], AF.Identity,
                                  bias=lns[:, 7:8], scale=lns[:, 6:7]).then_inc(aap.h, 1)
                    aap.v += 1
                    T.wait_ge(aap.h, aap.v)
                    if i == 0:
                        T.wait_ge(pro.h, pro.v)
                        T.wait_ge(prog.h, prog.v)
                    for c8 in range(4):
                        bk = c8 % 2
                        if atp.v >= 2:
                            T.wait_ge(aev.h, atp.v - 1)
                        for cc in range(8):
                            c = c8 * 8 + cc
                            T.transpose(ptr[:, bk * 8 + cc, :],
                                        xln[:, bi, c * 128:(c + 1) * 128], idb)
                        _tlast().then_inc(atp.h, 1); atp.v += 1
                        V.wait_ge(atp.h, atp.v)
                        if blk > 0 and t == 0 and c8 == 0:
                            V.wait_ge(axw.h, blk * 16)
                        V.tensor_copy(
                            xfm[:, c8 * 8:(c8 + 1) * 8, t * 128:(t + 1) * 128],
                            ptr[:, bk * 8:bk * 8 + 8, :]).then_inc(aev.h, 1)
                        aev.v += 1

                Sy.wait_ge(aev.h, aev.v)
                Sy.dma_start(
                    out=scrXfm[:, :, blk * 512:(blk + 1) * 512].rearrange("c p t -> p c t"),
                    in_=xfm).then_inc(axw.h, 16)
                axw.v += 16

                T.wait_ge(aev.h, aev.v)
                if blk == 0:
                    T.wait_ge(aw.h, 64)
                for m in range(4):
                    bk = len(prj_rota) % 2
                    if len(prj_rota) >= 2:
                        T.wait_ge(*prj_rota[-2])
                    for c in range(32):
                        T.matmul(prj[:, bk, :], wkr[:, c, m * 128:(m + 1) * 128],
                                 xfm[:, c, :], start=(c == 0), stop=(c == 31))
                    _tlast().then_inc(amm.h, 1); amm.v += 1
                    V.wait_ge(amm.h, amm.v)
                    V.wait_ge(actp[blk % 2].h, (blk // 2 + 1) * 32)
                    V.wait_ge(akr.h, akr.v)
                    if akr.v >= 2:
                        n = akr.v
                        V.wait_ge(akw[n % 2].h, 32 * (n // 2))
                    for (d0, d1), (s0, s1) in (((0, 32), (32, 64)), ((32, 64), (0, 32)),
                                               ((64, 96), (96, 128)), ((96, 128), (64, 96))):
                        V.tensor_copy(rk1[d0:d1, :], prj[s0:s1, bk, :]).then_inc(alg.h, 1)
                        alg.v += 1
                    V.tensor_mul(rk2, prj[:, bk, :], coskb[:, blk % 2, :]).then_inc(alg.h, 1)
                    alg.v += 1
                    V.wait_ge(alg.h, alg.v)
                    V.tensor_mul(rk3, rk1, sinkb[:, blk % 2, :]).then_inc(alg.h, 1)
                    alg.v += 1
                    V.wait_ge(alg.h, alg.v)
                    V.tensor_add(krp[:, m % 2, :], rk2, rk3).then_inc(akr.h, 1)
                    akr.v += 1
                    prj_rota.append((akr.h, akr.v))
                    Sy.wait_ge(akr.h, akr.v)
                    n = akr.v - 1
                    Sy.dma_start(out=scrK[2 * m, 0:64, blk * 512:(blk + 1) * 512],
                                 in_=krp[0:64, m % 2, :]).then_inc(akw[n % 2].h, 16)
                    Sy.dma_start(out=scrK[2 * m + 1, 0:64, blk * 512:(blk + 1) * 512],
                                 in_=krp[64:128, m % 2, :]).then_inc(akw[n % 2].h, 16)
                    akw[n % 2].v += 32

                if blk + 2 < NBLK:
                    issue_ct(blk + 2)

                for m in range(4):
                    bk = len(prj_rota) % 2
                    T.wait_ge(*prj_rota[-2])
                    for c in range(32):
                        T.matmul(prj[:, bk, :], wdk[:, c, m * 128:(m + 1) * 128],
                                 xfm[:, c, :], start=(c == 0), stop=(c == 31))
                    _tlast().then_inc(amm.h, 1); amm.v += 1
                    V.wait_ge(amm.h, amm.v)
                    V.tensor_copy(ckv[:, m, :], prj[:, bk, :]).then_inc(ack.h, 1)
                    ack.v += 1
                    prj_rota.append((ack.h, ack.v))

                T.wait_ge(ack.h, ack.v)
                for m in range(4):
                    bk = len(prj_rota) % 2
                    T.wait_ge(*prj_rota[-2])
                    for c2 in range(4):
                        T.matmul(prj[:, bk, :], wupk[:, c2, m * 128:(m + 1) * 128],
                                 ckv[:, c2, :], start=(c2 == 0), stop=(c2 == 3))
                    _tlast().then_inc(amm.h, 1); amm.v += 1
                    V.wait_ge(amm.h, amm.v)
                    if akc.v >= 2:
                        n = akc.v
                        V.wait_ge(kcwp[n % 2].h, 32 * (n // 2))
                    V.tensor_copy(kc[:, m % 2, :], prj[:, bk, :]).then_inc(akc.h, 1)
                    akc.v += 1
                    prj_rota.append((akc.h, akc.v))
                    Sy.wait_ge(akc.h, akc.v)
                    n = akc.v - 1
                    Sy.dma_start(out=scrK[2 * m, 64:128, blk * 512:(blk + 1) * 512],
                                 in_=kc[0:64, m % 2, :]).then_inc(kcwp[n % 2].h, 16)
                    Sy.dma_start(out=scrK[2 * m + 1, 64:128, blk * 512:(blk + 1) * 512],
                                 in_=kc[64:128, m % 2, :]).then_inc(kcwp[n % 2].h, 16)
                    kcwp[n % 2].v += 32

                for t in range(4):
                    pr = t % 2
                    if t >= 2:
                        T.wait_ge(avv.h, avv.v - 2)
                    for half in range(2):
                        for c2 in range(4):
                            T.matmul(vps[:, 2 * pr + half, :],
                                     ckv[:, c2, t * 128:(t + 1) * 128],
                                     wupv[:, c2, half * 512:(half + 1) * 512],
                                     start=(c2 == 0), stop=(c2 == 3))
                    _tlast().then_inc(amm.h, 1); amm.v += 1
                    V.wait_ge(amm.h, amm.v)
                    nuse = avv.v // 2
                    if nuse >= 2:
                        V.wait_ge(avw[nuse % 2].h, 16 * (nuse // 2))
                    V.tensor_copy(vst[:, pr, 0:512], vps[:, 2 * pr, :]).then_inc(avv.h, 1)
                    avv.v += 1
                    V.tensor_copy(vst[:, pr, 512:1024], vps[:, 2 * pr + 1, :]).then_inc(avv.h, 1)
                    avv.v += 1
                    Sy.wait_ge(avv.h, avv.v)
                    n = avv.v // 2 - 1
                    Sy.dma_start(
                        out=scrV[blk * 512 + t * 128: blk * 512 + (t + 1) * 128, :],
                        in_=vst[:, pr, :]).then_inc(avw[n % 2].h, 16)
                    avw[n % 2].v += 16

            for s_ in (akw[0], akw[1], kcwp[0], kcwp[1], avw[0], avw[1]):
                Sy.wait_ge(s_.h, s_.v)
            Sy.wait_ge(axw.h, axw.v)
            G.wait_ge(axp[0].h, axp[0].v)
            G.wait_ge(axp[1].h, axp[1].v)

            if DUMP == "A":
                dd = sem("dd")
                Sy.dma_start(out=dbg["dK"], in_=scrK).then_inc(dd.h, 16)
                Sy.dma_start(out=dbg["dV"], in_=scrV).then_inc(dd.h, 16)
                Sy.dma_start(out=dbg["dX"], in_=scrXfm).then_inc(dd.h, 16)
                dd.v += 48
                Sy.wait_ge(dd.h, dd.v)

        nc.all_engine_barrier()

    if "B" in PH:
        es = ExitStack()
        with es:
            def sb(name, shape, dtyp):
                return es.enter_context(nc.sbuf_tensor(name, shape, dtyp)).ap()

            def ps(name, shape, dtyp):
                return es.enter_context(nc.psum_tensor(name, shape, dtyp)).ap()

            xfq = sb("xfq_sb", [128, 32, NQ], BF16)
            kgb = sb("kg_sb", [128, 2, S], BF16)
            vgb = sb("vg_sb", [128, 2, 32, 128], BF16)
            wqb = sb("wq_sb", [128, 2, 32, 128], BF16)
            qq = sb("qq_sb", [128, 2, NQ], BF16)
            qr1 = sb("qr1_sb", [128, 512], F32)
            qr2 = sb("qr2_sb", [128, 512], F32)
            qr3 = sb("qr3_sb", [128, 512], F32)
            pp = sb("p_sb", [128, 6, 512], BF16)
            pss = sb("pss_sb", [128, NQ], F32)
            den = sb("den_sb", [128, 16], F32)
            avf = sb("avf_sb", [128, NQ], BF16)
            avt = sb("avt_sb", [128, 2, 8, 128], BF16)
            sc = ps("sc_ps", [128, 4, 512], F32)
            av = ps("av_ps", [128, 2, 512], F32)
            at = ps("at_ps", [128, 8, 128], BF16)
            pq = sc

            names = ["bxq", "bkv0", "bkv1", "bwq0", "bwq1", "bqm", "brp", "brg",
                     "bsc", "bmk", "bex", "bps", "bavm", "bpt", "bdn", "brc",
                     "bevc", "bavt", "bscl", "bao0", "bao1"]
            bs = {n: sem(n) for n in names}
            snap0 = {n: bs[n].v for n in names}
            deltas = []

            for jb in nc.Switch(engines=[Sy, Sc, nc.tensor, V, G], index=[sv] * 5, n=4):
                for n in names:
                    bs[n].v = snap0[n]
                (bxq, bkv0, bkv1, bwq0, bwq1, bqm, brp, brg, bsc, bmk, bex,
                 bps, bavm, bpt, bdn, brc, bevc, bavt, bscl, bao0, bao1) = (
                    bs[n] for n in names)
                bkvp = [bkv0, bkv1]
                bwqp = [bwq0, bwq1]
                baop = [bao0, bao1]
                base = snap0

                blocks = (jb, 7 - jb)
                spans = []
                for b in blocks:
                    sl = [(kt, 512, 0, False) for kt in range(4 * b)]
                    sl += [(4 * b + d, 512 - 128 * d, 128 * d, True) for d in range(4)]
                    spans.append(sl)

                for h2, b in enumerate(blocks):
                    Sy.dma_start(
                        out=xfq[:, :, h2 * 512:(h2 + 1) * 512],
                        in_=scrXfm[:, :, b * 512:(b + 1) * 512].rearrange("c p t -> p c t"),
                    ).then_inc(bxq.h, 16)
                    bxq.v += 16

                def load_kv(g):
                    if g >= 2:
                        Sy.wait_ge(bavm.h, base["bavm"] + 36 * 4 * (g - 1))
                    s_ = bkvp[g % 2]
                    Sy.dma_start(out=kgb[:, g % 2, :], in_=scrK[g]).then_inc(s_.h, 16)
                    Sy.dma_start(
                        out=vgb[:, g % 2, :, :],
                        in_=scrV[:, g * 128:(g + 1) * 128].rearrange("(c p) h -> p c h", p=128),
                    ).then_inc(s_.h, 16)
                    s_.v += 32

                def load_wq(h):
                    if h >= 2:
                        G.wait_ge(bqm.h, base["bqm"] + 2 * (h - 1))
                    s_ = bwqp[h % 2]
                    G.dma_start(
                        out=wqb[:, h % 2, :, :],
                        in_=wqT[:, h * 128:(h + 1) * 128].rearrange("(c p) f -> p c f", p=128),
                    ).then_inc(s_.h, 16)
                    s_.v += 16

                load_kv(0)
                load_wq(0); load_wq(1)

                for h in range(NH):
                    g = h // 4
                    if h % 4 == 0 and g + 1 < KVH:
                        load_kv(g + 1)
                    if h + 2 < NH:
                        load_wq(h + 2)

                    T.wait_ge(bwqp[h % 2].h, base["bwq%d" % (h % 2)] + (h // 2 + 1) * 16)
                    if h == 0:
                        T.wait_ge(bxq.h, base["bxq"] + 32)
                    if h >= 1:
                        T.wait_ge(bdn.h, base["bdn"] + 8 * h)
                    for span in range(2):
                        for c in range(32):
                            T.matmul(pq[:, span, :], wqb[:, h % 2, c, :],
                                     xfq[:, c, span * 512:(span + 1) * 512],
                                     start=(c == 0), stop=(c == 31))
                        _tlast().then_inc(bqm.h, 1); bqm.v += 1

                    for span in range(2):
                        V.wait_ge(bqm.h, bqm.v - 1 + span)
                        V.wait_ge(brg.h, brg.v)
                        for (d0, d1), (s0, s1) in (((0, 32), (32, 64)), ((32, 64), (0, 32)),
                                                   ((64, 96), (96, 128)), ((96, 128), (64, 96))):
                            V.tensor_copy(qr1[d0:d1, :], pq[s0:s1, span, :]).then_inc(brg.h, 1)
                            brg.v += 1
                        V.tensor_mul(qr2, pq[:, span, :],
                                     cosq[:, span * 512:(span + 1) * 512]).then_inc(brg.h, 1)
                        brg.v += 1
                        V.wait_ge(brg.h, brg.v)
                        V.tensor_mul(qr3, qr1,
                                     sinq[:, span * 512:(span + 1) * 512]).then_inc(brg.h, 1)
                        brg.v += 1
                        V.wait_ge(brg.h, brg.v)
                        if span == 0:
                            V.tensor_add(qq[:, h % 2, 0:512], qr2, qr3).then_inc(brg.h, 1)
                            brg.v += 1
                        else:
                            V.tensor_add(qq[:, h % 2, 512:1024], qr2, qr3).then_inc(brp.h, 1)
                            brp.v += 1

                    T.wait_ge(brp.h, base["brp"] + h + 1)
                    T.wait_ge(bkvp[g % 2].h, base["bkv%d" % (g % 2)] + (g // 2 + 1) * 32)
                    base_sc = bsc.v; base_ex = bex.v
                    base_av = bavm.v; base_ps = bps.v
                    slot = 0
                    nslots = sum(len(sl) for sl in spans)
                    flat = []
                    for span, sl in enumerate(spans):
                        for si, (kt, w, qoff, diag) in enumerate(sl):
                            flat.append((span, si, len(sl), kt, w, qoff, diag))

                    LAG = 3

                    def emit_av(j):
                        span, si, sln, kt, w, qoff, diag = flat[j]
                        if si == 0 and h >= 1:
                            T.wait_ge(bevc.h, base["bevc"] + 2 * (h - 1) + span + 1)
                        T.wait_ge(bex.h, base_ex + j + 1)
                        T.matmul(av[:, span, qoff:qoff + w],
                                 vgb[:, g % 2, kt, :], pp[:, j % 6, 0:w],
                                 start=(si == 0), stop=(si == sln - 1)
                                 ).then_inc(bavm.h, 1)
                        bavm.v += 1

                    for i, (span, si, sln, kt, w, qoff, diag) in enumerate(flat):
                        bk = i % 4
                        pb = i % 6
                        if i >= 4:
                            T.wait_ge(bex.h, base_ex + i - 3)
                        T.matmul(
                            sc[:, bk, 0:w],
                            kgb[:, g % 2, kt * 128:(kt + 1) * 128],
                            qq[:, h % 2, span * 512 + qoff: span * 512 + qoff + w],
                            start=True, stop=True).then_inc(bsc.h, 1)
                        bsc.v += 1
                        if diag:
                            V.wait_ge(bsc.h, base_sc + i + 1)
                            V.tensor_add(sc[:, bk, 0:128], sc[:, bk, 0:128],
                                         tri).then_inc(bmk.h, 1)
                            bmk.v += 1
                        Sc.wait_ge(bmk.h if diag else bsc.h,
                                   bmk.v if diag else base_sc + i + 1)
                        if i >= 6:
                            Sc.wait_ge(bavm.h, base_av + i - 5)
                            Sc.wait_ge(bps.h, base_ps + i - 5)
                        Sc.activation(pp[:, pb, 0:w], sc[:, bk, 0:w], AF.Exp,
                                      bias=cst[:, 0:1], scale=SCALE).then_inc(bex.h, 1)
                        bex.v += 1
                        V.wait_ge(bex.h, base_ex + i + 1)
                        V.wait_ge(bps.h, bps.v)
                        pcols = pss[:, span * 512 + qoff: span * 512 + qoff + w]
                        if si == 0:
                            if span == 0 and h >= 1:
                                V.wait_ge(bpt.h, base["bpt"] + h)
                            V.tensor_copy(pcols, pp[:, pb, 0:w]).then_inc(bps.h, 1)
                        else:
                            V.tensor_add(pcols, pcols, pp[:, pb, 0:w]).then_inc(bps.h, 1)
                        bps.v += 1
                        if i >= LAG:
                            emit_av(i - LAG)
                        if si == sln - 1:
                            pass
                        slot += 1
                    for j in range(nslots - LAG, nslots):
                        emit_av(j)
                    for span, sl in enumerate(spans):
                        V.wait_ge(bavm.h, base_av + sum(len(s) for s in spans[:span + 1]))
                        if h >= 1:
                            V.wait_ge(bavt.h, base["bavt"] + h)
                        V.tensor_copy(avf[:, span * 512:(span + 1) * 512],
                                      av[:, span, :]).then_inc(bevc.h, 1)
                        bevc.v += 1

                    T.wait_ge(bex.h, base_ex + nslots)
                    T.wait_ge(bps.h, base_ps + nslots)
                    for s8 in range(8):
                        T.transpose(sc[:, s8 // 2, (s8 % 2) * 128:(s8 % 2 + 1) * 128],
                                    pss[:, s8 * 128:(s8 + 1) * 128], idf)
                    _tlast().then_inc(bpt.h, 1); bpt.v += 1
                    V.wait_ge(bpt.h, bpt.v)
                    for s8 in range(8):
                        V.reduce_sum(den[:, s8:s8 + 1],
                                     sc[:, s8 // 2, (s8 % 2) * 128:(s8 % 2 + 1) * 128],
                                     axis=AX).then_inc(bdn.h, 1)
                        bdn.v += 1
                    V.wait_ge(bdn.h, bdn.v)
                    V.reciprocal(den[:, 8:16], den[:, 0:8]).then_inc(brc.h, 1)
                    brc.v += 1

                    T.wait_ge(bevc.h, base["bevc"] + 2 * (h + 1))
                    if h >= 1:
                        T.wait_ge(bscl.h, base["bscl"] + 8 * h)
                    for s8 in range(8):
                        T.transpose(at[:, s8, :], avf[:, s8 * 128:(s8 + 1) * 128], idb)
                    _tlast().then_inc(bavt.h, 1); bavt.v += 1
                    V.wait_ge(bavt.h, bavt.v)
                    V.wait_ge(brc.h, brc.v)
                    if h >= 2:
                        V.wait_ge(baop[h % 2].h,
                                  base["bao%d" % (h % 2)] + 16 * ((h - 2) // 2 + 1))
                    for s8 in range(8):
                        V.tensor_scalar(avt[:, h % 2, s8, :], at[:, s8, :],
                                        den[:, 8 + s8:9 + s8], None,
                                        ALU.mult).then_inc(bscl.h, 1)
                        bscl.v += 1
                    Sy.wait_ge(bscl.h, bscl.v)
                    Sy.dma_start(
                        out=scrAv[:, h * 128:(h + 1) * 128].rearrange("(s p) f -> p s f", p=128),
                        in_=avt[:, h % 2, :, :]).then_inc(baop[h % 2].h, 16)
                    baop[h % 2].v += 16

                Sy.wait_ge(bao0.h, bao0.v)
                Sy.wait_ge(bao1.h, bao1.v)
                G.wait_ge(bwq0.h, bwq0.v)
                G.wait_ge(bwq1.h, bwq1.v)
                deltas.append({n: bs[n].v - snap0[n] for n in names})

            assert all(d == deltas[0] for d in deltas[1:]), "switch bodies differ"
            for n in names:
                bs[n].v = snap0[n] + deltas[0][n]

            nc.all_engine_barrier()

            if DUMP == "B":
                dd = sem("dd2")
                Sy.dma_start(out=dbg["dAv"], in_=scrAv).then_inc(dd.h, 16)
                dd.v += 16
                Sy.wait_ge(dd.h, dd.v)
                nc.all_engine_barrier()

    if "C" in PH:
        es = ExitStack()
        with es:
            def sb(name, shape, dtyp):
                return es.enter_context(nc.sbuf_tensor(name, shape, dtyp)).ap()

            def ps(name, shape, dtyp):
                return es.enter_context(nc.psum_tensor(name, shape, dtyp)).ap()

            avT = sb("avT_sb", [128, 32, NQ], BF16)
            wo = sb("wo_sb", [128, 2, 32, 128], BF16)
            ybf = sb("ybf_sb", [128, 2, NQ], BF16)
            xqr = sb("xqr_sb", [128, 2, 8, 128], F32)
            y2t = sb("y2t_sb", [128, 2, 8, 128], F32)
            sumc = sb("sumc_sb", [128, 8, 32], F32)
            ssqc = sb("ssqc_sb", [128, 8, 32], F32)
            ctr = sb("ctrash_sb", [128, 8, 128], BF16)
            py = ps("py_ps", [128, 4, 512], F32)
            pt = ps("pt_ps", [128, 2, 8, 128], BF16)

            cav = sem("cav")
            cwop = [sem("cwo0"), sem("cwo1")]
            cxqp = [sem("cxq0"), sem("cxq1")]
            cswp = [sem("csw0"), sem("csw1")]
            cym = sem("cym"); cev = sem("cev"); ctp = sem("ctp")
            cy2 = sem("cy2"); cg = sem("cg")

            for c in range(32):
                Sy.dma_start_transpose(avT[:, c, :],
                                       scrAv[:, c * 128:(c + 1) * 128]).then_inc(cav.h, 16)
                cav.v += 16

            def load_wo(m):
                s_ = cwop[m % 2]
                G.dma_start(
                    out=wo[:, m % 2, :, :],
                    in_=woT[:, m * 128:(m + 1) * 128].rearrange("(c p) f -> p c f", p=128),
                ).then_inc(s_.h, 16)
                s_.v += 16

            def load_xq(m):
                s_ = cxqp[m % 2]
                Sy.dma_start(
                    out=xqr[:, m % 2, :, :],
                    in_=xq[:, m * 128:(m + 1) * 128].rearrange("(s p) f -> p s f", p=128),
                ).then_inc(s_.h, 16)
                s_.v += 16

            load_wo(0); load_wo(1)
            load_xq(0); load_xq(1)

            csq = sem("csq")

            def c_post(m):
                T.wait_ge(cev.h, 2 * (m + 1))
                if m >= 2:
                    T.wait_ge(cy2.h, 9 * (m - 1))
                for s in range(8):
                    T.transpose(pt[:, m % 2, s, :],
                                ybf[:, m % 2, s * 128:(s + 1) * 128], idb)
                _tlast().then_inc(ctp.h, 1); ctp.v += 1
                V.wait_ge(ctp.h, ctp.v)
                V.wait_ge(cxqp[m % 2].h, (m // 2 + 1) * 16)
                if m >= 2:
                    V.wait_ge(cswp[m % 2].h, 16 * ((m - 2) // 2 + 1))
                    V.wait_ge(csq.h, 8 * (m - 1))
                    V.wait_ge(cy2.h, 9 * (m - 1))
                V.tensor_add(y2t[:, m % 2, :, :], pt[:, m % 2, :, :],
                             xqr[:, m % 2, :, :]).then_inc(cy2.h, 1)
                cy2.v += 1
                V.wait_ge(cy2.h, cy2.v)
                Sc.wait_ge(cy2.h, cy2.v)
                Sc.wait_ge(csq.h, csq.v)
                for s in range(8):
                    V.reduce_sum(sumc[:, s, m:m + 1], y2t[:, m % 2, s, :],
                                 axis=AX).then_inc(cy2.h, 1)
                    cy2.v += 1
                    Sc.activation(ctr[:, s, :], y2t[:, m % 2, s, :], AF.Square,
                                  accum_out=ssqc[:, s, m:m + 1]).then_inc(csq.h, 1)
                    csq.v += 1
                Sy.wait_ge(cy2.h, cy2.v)
                Sy.wait_ge(csq.h, csq.v)
                Sy.dma_start(
                    out=scrY2[:, m * 128:(m + 1) * 128].rearrange("(s p) f -> p s f", p=128),
                    in_=y2t[:, m % 2, :, :]).then_inc(cswp[m % 2].h, 16)
                cswp[m % 2].v += 16
                if m + 2 < 32:
                    Sy.wait_ge(cy2.h, cy2.v)
                    load_xq(m + 2)

            for m in range(32):
                T.wait_ge(cwop[m % 2].h, (m // 2 + 1) * 16)
                if m == 0:
                    T.wait_ge(cav.h, cav.v)
                if m >= 2:
                    T.wait_ge(cev.h, 2 * (m - 1))
                for span in range(2):
                    for c in range(32):
                        T.matmul(py[:, 2 * (m % 2) + span, :], wo[:, m % 2, c, :],
                                 avT[:, c, span * 512:(span + 1) * 512],
                                 start=(c == 0), stop=(c == 31))
                _tlast().then_inc(cym.h, 1); cym.v += 1
                if m + 2 < 32:
                    G.wait_ge(cym.h, m + 1)
                    load_wo(m + 2)
                V.wait_ge(cym.h, cym.v)
                if m >= 2:
                    V.wait_ge(ctp.h, m - 1)
                V.tensor_copy(ybf[:, m % 2, 0:512], py[:, 2 * (m % 2), :]).then_inc(cev.h, 1)
                cev.v += 1
                V.tensor_copy(ybf[:, m % 2, 512:1024], py[:, 2 * (m % 2) + 1, :]).then_inc(cev.h, 1)
                cev.v += 1
                if m >= 1:
                    c_post(m - 1)
            c_post(31)

            V.wait_ge(cy2.h, cy2.v)
            V.wait_ge(csq.h, csq.v)
            for s in range(8):
                V.reduce_sum(stC[:, s, 0:1], sumc[:, s, :], axis=AX).then_inc(cg.h, 1)
                V.reduce_sum(stC[:, s, 1:2], ssqc[:, s, :], axis=AX).then_inc(cg.h, 1)
                cg.v += 2
            V.wait_ge(cg.h, cg.v)
            for s in range(8):
                V.tensor_scalar_mul(stC[:, s, 0:1], stC[:, s, 0:1], 1.0 / D).then_inc(cg.h, 1)
                V.tensor_scalar_mul(stC[:, s, 1:2], stC[:, s, 1:2], 1.0 / D).then_inc(cg.h, 1)
                cg.v += 2
            V.wait_ge(cg.h, cg.v)
            for s in range(8):
                V.scalar_tensor_tensor(stC[:, s, 2:3], stC[:, s, 0:1], 1.0,
                                       stC[:, s, 0:1], ALU.mult, ALU.mult).then_inc(cg.h, 1)
                cg.v += 1
            V.wait_ge(cg.h, cg.v)
            for s in range(8):
                V.tensor_tensor(stC[:, s, 1:2], stC[:, s, 1:2], stC[:, s, 2:3],
                                ALU.subtract).then_inc(cg.h, 1)
                cg.v += 1
            Sc.wait_ge(cg.h, cg.v)
            for s in range(8):
                Sc.activation(stC[:, s, 2:3], stC[:, s, 1:2], AF.Sqrt,
                              bias=cst[:, 1:2], scale=1.0).then_inc(cg.h, 1)
                cg.v += 1
            V.wait_ge(cg.h, cg.v)
            for s in range(8):
                V.reciprocal(stC[:, s, 3:4], stC[:, s, 2:3]).then_inc(cg.h, 1)
                cg.v += 1
            Sy.wait_ge(cswp[0].h, cswp[0].v)
            Sy.wait_ge(cswp[1].h, cswp[1].v)

            if DUMP == "C":
                dd = sem("dd3")
                Sy.dma_start(out=dbg["dY2"], in_=scrY2).then_inc(dd.h, 16)
                dd.v += 16
                Sy.wait_ge(cg.h, cg.v)
                Sy.dma_start(out=dbg["dSt"], in_=stC).then_inc(dd.h, 16)
                dd.v += 16
                Sy.wait_ge(dd.h, dd.v)

        nc.all_engine_barrier()

    if "D" in PH:
        es = ExitStack()
        with es:
            def sb(name, shape, dtyp):
                return es.enter_context(nc.sbuf_tensor(name, shape, dtyp)).ap()

            zfm = sb("zfm_sb", [128, 32, NQ], BF16)
            y2r = sb("y2r_sb", [128, 2, D], F32)
            ztm = sb("ztm_sb", [128, 2, D], BF16)
            gw1s = sb("gw1s_sb", [128, 32, 32], BF16)
            lAs = sb("lA_sb", [128, 32, 128], BF16)
            lBs = sb("lB_sb", [128, 32, 128], BF16)
            hraw = sb("hraw_sb", [32, NQ], F32)
            htm = sb("htm_sb", [128, 8, 32], F32)
            gst = sb("gst_sb", [128, 32], F32)
            gst2 = sb("gst2_sb", [128, 8], F32)
            g1a = sb("g1a_sb", [128, 8], F32)
            g2a = sb("g2a_sb", [128, 8], F32)
            hrel = sb("hrel_sb", [128, 32], F32)
            hrel2 = sb("hrel2_sb", [128, 32], F32)
            hrT = sb("hrT_sb", [33, 128], BF16)
            gts = sb("gts_sb", [128, 8, 8], F32)
            m1 = sb("m1_sb", [128, 8, 8], F32)
            m2 = sb("m2_sb", [128, 8, 8], F32)
            mc = sb("mc_sb", [128, 16], BF16)
            d21 = sb("d21_sb", [128, 8, 3], F32)
            wts = sb("wts_sb", [128, 8], F32)
            wts2 = sb("wts2_sb", [128, 8], F32)
            wT = sb("wT_sb", [8, NQ], BF16)
            cnt = sb("cnt_sb", [1, 16], F32)
            cnt2 = sb("cnt2_sb", [1, 16], F32)
            cntb = sb("cntb_sb", [1, 16], BF16)
            arep = sb("arep_sb", [128, 16], F32)
            tall = sb("tall_sb", [128, NQ], BF16)
            tp = sb("tp_sb", [128, NQ], BF16)
            wff = sb("wff_sb", [128, 2, 32, 128], BF16)
            y3bf = sb("y3bf_sb", [128, 2, NQ], BF16)
            y2rt = sb("y2rt_sb", [128, 2, 8, 128], F32)
            outt = sb("outt_sb", [128, 2, 8, 128], F32)
            dtr = sb("dtrash_sb", [128, 32], BF16)

            dld = sem("dld"); dzn = sem("dzn")
            dy2p = [sem("dy20"), sem("dy21")]
            dzt = sem("dzt"); dze = sem("dze"); dgm = sem("dgm")
            dgv = sem("dgv"); dgg = sem("dgg"); dcm = sem("dcm")
            dt1 = sem("dt1"); dwx = sem("dwx")
            dwfp = [sem("dwf0"), sem("dwf1")]
            dmm = sem("dmm"); dev2 = sem("dev2")
            dyep = [sem("dye0"), sem("dye1")]
            dstp = [sem("dst0"), sem("dst1")]
            dym = sem("dym"); dyt = sem("dyt"); dou = sem("dou")

            G.dma_start(out=gw1s, in_=gw1T.rearrange("(c p) f -> p c f", p=128)).then_inc(dld.h, 16)
            G.dma_start(out=lAs, in_=lAT.rearrange("(c p) f -> p c f", p=128)).then_inc(dld.h, 16)
            G.dma_start(out=lBs, in_=lB.rearrange("p (c f) -> p c f", f=128)).then_inc(dld.h, 16)
            dld.v += 48
            V.memset(hrT[32:33, :], 1.0)
            V.engine_nop().then_inc(dgg.h, 1); dgg.v += 1

            def load_y2(s):
                s_ = dy2p[s % 2]
                Sy.dma_start(out=y2r[:, s % 2, :],
                             in_=scrY2[s * 128:(s + 1) * 128, :]).then_inc(s_.h, 16)
                s_.v += 16

            load_y2(0); load_y2(1)

            es1 = ExitStack()
            with es1:
                def ps1(name, shape, dtyp):
                    return es1.enter_context(nc.psum_tensor(name, shape, dtyp)).ap()

                ztp = ps1("zt_ps", [128, 16, 128], BF16)
                pg = ps1("pg_ps", [32, 2, 512], F32)
                pt2 = ps1("pt2_ps", [128, 8, 32], F32)
                ptg1 = ps1("ptg1_ps", [32, 128], F32)
                pgt2 = ps1("pgt2_ps", [128, 32], F32)
                pcnt = ps1("pcnt_ps", [1, 16], F32)

                for s in range(8):
                    V.wait_ge(dy2p[s % 2].h, (s // 2 + 1) * 16)
                    if s >= 2:
                        V.wait_ge(dzt.h, 4 * (s - 1))
                    V.tensor_scalar(ztm[:, s % 2, :], y2r[:, s % 2, :],
                                    stC[:, s, 0:1], stC[:, s, 3:4],
                                    ALU.subtract, ALU.mult).then_inc(dzn.h, 1)
                    dzn.v += 1
                    if s + 2 < 8:
                        Sy.wait_ge(dzn.h, s + 1)
                        load_y2(s + 2)

                    T.wait_ge(dzn.h, dzn.v)
                    for c8 in range(4):
                        bk = c8 % 2
                        if dzt.v >= 2:
                            T.wait_ge(dze.h, dzt.v - 1)
                        for cc in range(8):
                            c = c8 * 8 + cc
                            T.transpose(ztp[:, bk * 8 + cc, :],
                                        ztm[:, s % 2, c * 128:(c + 1) * 128], idb)
                        _tlast().then_inc(dzt.h, 1); dzt.v += 1
                        V.wait_ge(dzt.h, dzt.v)
                        V.tensor_copy(zfm[:, c8 * 8:(c8 + 1) * 8, s * 128:(s + 1) * 128],
                                      ztp[:, bk * 8:bk * 8 + 8, :]).then_inc(dze.h, 1)
                        dze.v += 1

                T.wait_ge(dze.h, dze.v)
                T.wait_ge(dld.h, dld.v)
                for span in range(2):
                    for c in range(32):
                        T.matmul(pg[:, span, :], gw1s[:, c, :],
                                 zfm[:, c, span * 512:(span + 1) * 512],
                                 start=(c == 0), stop=(c == 31))
                _tlast().then_inc(dgm.h, 1); dgm.v += 1
                V.wait_ge(dgm.h, dgm.v)
                V.tensor_copy(hraw[:, 0:512], pg[:, 0, :]).then_inc(dgg.h, 1)
                dgg.v += 1
                V.tensor_copy(hraw[:, 512:1024], pg[:, 1, :]).then_inc(dgg.h, 1)
                dgg.v += 1

                T.wait_ge(dgg.h, dgg.v)
                for s in range(8):
                    T.transpose(pt2[:, s, :], hraw[:, s * 128:(s + 1) * 128], idf[0:32, 0:32])
                _tlast().then_inc(dgm.h, 1); dgm.v += 1
                V.wait_ge(dgm.h, dgm.v)
                for s in range(8):
                    V.tensor_add(htm[:, s, :], pt2[:, s, :],
                                 gcons_sb[:, 0:32]).then_inc(dgg.h, 1)
                    dgg.v += 1
                V.wait_ge(dgg.h, dgg.v)
                for s in range(8):
                    V.reduce_sum(gst[:, s:s + 1], htm[:, s, :],
                                 axis=AX).then_inc(dgg.h, 1)
                    dgg.v += 1
                    V.wait_ge(dgg.h, dgg.v)
                    V.scalar_tensor_tensor(dtr, htm[:, s, :], 1.0, htm[:, s, :],
                                           ALU.mult, ALU.mult,
                                           accum_out=gst[:, 8 + s:9 + s]).then_inc(dgg.h, 1)
                    dgg.v += 1
                    V.wait_ge(dgg.h, dgg.v)
                V.tensor_scalar_mul(gst[:, 0:8], gst[:, 0:8], 1.0 / 32).then_inc(dgg.h, 1)
                dgg.v += 1
                V.tensor_scalar_mul(gst[:, 8:16], gst[:, 8:16], 1.0 / 32).then_inc(dgg.h, 1)
                dgg.v += 1
                V.wait_ge(dgg.h, dgg.v)
                V.scalar_tensor_tensor(gst2[:, 0:8], gst[:, 0:8], 1.0,
                                       gst[:, 0:8], ALU.mult, ALU.mult).then_inc(dgg.h, 1)
                dgg.v += 1
                V.wait_ge(dgg.h, dgg.v)
                V.tensor_tensor(gst[:, 8:16], gst[:, 8:16], gst2[:, 0:8],
                                ALU.subtract).then_inc(dgg.h, 1)
                dgg.v += 1
                Sc.wait_ge(dgg.h, dgg.v)
                Sc.activation(gst[:, 16:24], gst[:, 8:16], AF.Sqrt,
                              bias=cst[:, 1:2], scale=1.0).then_inc(dgg.h, 1)
                dgg.v += 1
                V.wait_ge(dgg.h, dgg.v)
                V.reciprocal(gst[:, 24:32], gst[:, 16:24]).then_inc(dgg.h, 1)
                dgg.v += 1

                for s in range(8):
                    V.wait_ge(dgg.h, dgg.v)
                    V.wait_ge(dgv.h, dgv.v)
                    V.tensor_scalar(hrel, htm[:, s, :], gst[:, s:s + 1],
                                    gst[:, 24 + s:25 + s], ALU.subtract,
                                    ALU.mult).then_inc(dgv.h, 1)
                    dgv.v += 1
                    V.wait_ge(dgv.h, dgv.v)
                    V.scalar_tensor_tensor(hrel2, hrel, 1.0, gcons_sb[:, 32:64],
                                           ALU.mult, ALU.mult).then_inc(dgv.h, 1)
                    dgv.v += 1
                    V.wait_ge(dgv.h, dgv.v)
                    V.scalar_tensor_tensor(hrel, hrel2, 1.0, gcons_sb[:, 64:96],
                                           ALU.mult, ALU.add).then_inc(dgv.h, 1)
                    dgv.v += 1
                    V.wait_ge(dgv.h, dgv.v)
                    V.tensor_scalar_max(hrel2, hrel, 0.0).then_inc(dgv.h, 1)
                    dgv.v += 1
                    T.wait_ge(dgv.h, dgv.v)
                    T.transpose(ptg1, hrel2, idf).then_inc(dgm.h, 1); dgm.v += 1
                    V.wait_ge(dgm.h, dgm.v)
                    V.tensor_copy(hrT[0:32, :], ptg1[0:32, :]).then_inc(dgv.h, 1)
                    dgv.v += 1
                    T.wait_ge(dgv.h, dgv.v)
                    T.matmul(pgt2[:, 0:8], hrT, gw2_sb, start=True, stop=True
                             ).then_inc(dgm.h, 1)
                    dgm.v += 1
                    V.wait_ge(dgm.h, dgm.v)
                    V.tensor_copy(gts[:, s, :], pgt2[:, 0:8]).then_inc(dgv.h, 1)
                    dgv.v += 1
                    V.wait_ge(dgv.h, dgv.v)
                    V.reduce_max(d21[:, s, 0:1], gts[:, s, :], axis=AX).then_inc(dgv.h, 1)
                    dgv.v += 1
                    V.wait_ge(dgv.h, dgv.v)
                    V.tensor_scalar(m1[:, s, :], gts[:, s, :], d21[:, s, 0:1],
                                    None, ALU.is_equal).then_inc(dgv.h, 1)
                    dgv.v += 1
                    V.wait_ge(dgv.h, dgv.v)
                    V.scalar_tensor_tensor(m2[:, s, :], m1[:, s, :], -1e30,
                                           gts[:, s, :], ALU.mult, ALU.add).then_inc(dgv.h, 1)
                    dgv.v += 1
                    V.wait_ge(dgv.h, dgv.v)
                    V.reduce_max(d21[:, s, 1:2], m2[:, s, :], axis=AX).then_inc(dgv.h, 1)
                    dgv.v += 1
                    V.wait_ge(dgv.h, dgv.v)
                    V.tensor_scalar(m2[:, s, :], m2[:, s, :], d21[:, s, 1:2],
                                    None, ALU.is_equal).then_inc(dgv.h, 1)
                    dgv.v += 1
                    V.tensor_tensor(d21[:, s, 2:3], d21[:, s, 1:2], d21[:, s, 0:1],
                                    ALU.subtract).then_inc(dgv.h, 1)
                    dgv.v += 1
                    V.wait_ge(dgv.h, dgv.v)
                    V.wait_ge(dcm.h, dcm.v)
                    V.tensor_copy(mc[:, 0:8], m1[:, s, :]).then_inc(dgv.h, 1)
                    dgv.v += 1
                    V.tensor_copy(mc[:, 8:16], m2[:, s, :]).then_inc(dgv.h, 1)
                    dgv.v += 1
                    T.wait_ge(dgv.h, dgv.v)
                    T.matmul(pcnt, ones_c, mc, start=(s == 0), stop=(s == 7)
                             ).then_inc(dcm.h, 1)
                    dcm.v += 1

                Sc.wait_ge(dgv.h, dgv.v)
                Sc.activation(g1a, d21[:, :, 2:3].rearrange("p s c -> p (s c)"),
                              AF.Sigmoid, bias=0.0, scale=cst[:, 2:3]).then_inc(dgg.h, 1)
                dgg.v += 1
                Sc.activation(g2a, d21[:, :, 2:3].rearrange("p s c -> p (s c)"),
                              AF.Sigmoid, bias=0.0, scale=1.0).then_inc(dgg.h, 1)
                dgg.v += 1

                dccs = sem("dccs"); dccc = sem("dccc"); dccg = sem("dccg")
                V.wait_ge(dcm.h, dcm.v)
                V.tensor_copy(cnt, pcnt).then_inc(dgv.h, 1)
                dgv.v += 1
                Sy.wait_ge(dgv.h, dgv.v)
                Sy.dma_start(out=cntI, in_=cnt).then_inc(dccs.h, 16)
                dccs.v += 16
                G.wait_ge(dccs.h, dccs.v)
                if RGN > 1:
                    G.collective_compute(
                        "AllReduce", ALU.add, replica_groups=[list(range(RGN))],
                        ins=[cntI], outs=[cntO]).then_inc(dccc.h, 1)
                    dccc.v += 1
                    G.wait_ge(dccc.h, dccc.v)
                    G.dma_start(out=cnt2, in_=cntO).then_inc(dccg.h, 16)
                else:
                    G.dma_start(out=cnt2, in_=cntI).then_inc(dccg.h, 16)
                dccg.v += 16
                V.wait_ge(dccg.h, dccg.v)
                V.tensor_scalar(cnt, cnt2, cst[0:1, 3:4], None,
                                ALU.is_le).then_inc(dgv.h, 1)
                dgv.v += 1
                V.wait_ge(dgv.h, dgv.v)
                V.tensor_copy(cntb, cnt).then_inc(dgv.h, 1)
                dgv.v += 1
                T.wait_ge(dgv.h, dgv.v)
                T.matmul(pgt2[:, 0:16], ones_r, cntb, start=True, stop=True
                         ).then_inc(dgm.h, 1)
                dgm.v += 1
                V.wait_ge(dgm.h, dgm.v)
                V.tensor_copy(arep, pgt2[:, 0:16]).then_inc(dgv.h, 1)
                dgv.v += 1

                for s in range(8):
                    V.wait_ge(dgg.h, dgg.v)
                    V.wait_ge(dgv.h, dgv.v)
                    V.tensor_scalar(wts, m1[:, s, :], g1a[:, s:s + 1],
                                    None, ALU.mult).then_inc(dgv.h, 1)
                    dgv.v += 1
                    V.tensor_scalar(wts2, m2[:, s, :], g2a[:, s:s + 1],
                                    None, ALU.mult).then_inc(dgv.h, 1)
                    dgv.v += 1
                    V.wait_ge(dgv.h, dgv.v)
                    V.tensor_mul(wts, wts, arep[:, 0:8]).then_inc(dgv.h, 1)
                    dgv.v += 1
                    V.tensor_mul(wts2, wts2, arep[:, 8:16]).then_inc(dgv.h, 1)
                    dgv.v += 1
                    V.wait_ge(dgv.h, dgv.v)
                    V.tensor_add(wts, wts, wts2).then_inc(dgv.h, 1)
                    dgv.v += 1
                    T.wait_ge(dgv.h, dgv.v)
                    T.transpose(ptg1[0:8, :], wts, idf).then_inc(dgm.h, 1)
                    dgm.v += 1
                    V.wait_ge(dgm.h, dgm.v)
                    V.tensor_copy(wT[:, s * 128:(s + 1) * 128],
                                  ptg1[0:8, :]).then_inc(dgv.h, 1)
                    dgv.v += 1

            es2 = ExitStack()
            with es2:
                def ps2(name, shape, dtyp):
                    return es2.enter_context(nc.psum_tensor(name, shape, dtyp)).ap()

                py3 = ps2("py3_ps", [128, 4, 512], F32)
                pt3 = ps2("pt3_ps", [128, 2, 8, 128], BF16)
                pwx = ps2("pwx_ps", [128, 2, 512], F32)

                T.wait_ge(dld.h, dld.v)
                for span in range(2):
                    for c in range(32):
                        T.matmul(py3[:, span, :], lAs[:, c, :],
                                 zfm[:, c, span * 512:(span + 1) * 512],
                                 start=(c == 0), stop=(c == 31))
                _tlast().then_inc(dt1.h, 1); dt1.v += 1
                V.wait_ge(dt1.h, dt1.v)
                V.tensor_copy(tall[:, 0:512], py3[:, 0, :]).then_inc(dt1.h, 1)
                dt1.v += 1
                V.tensor_copy(tall[:, 512:1024], py3[:, 1, :]).then_inc(dt1.h, 1)
                dt1.v += 1

                T.wait_ge(dgv.h, dgv.v)
                T.wait_ge(dt1.h, dt1.v)
                for span in range(2):
                    T.matmul(pwx[:, span, :], e16,
                             wT[:, span * 512:(span + 1) * 512],
                             start=True, stop=True)
                _tlast().then_inc(dwx.h, 1); dwx.v += 1
                V.wait_ge(dwx.h, dwx.v)
                V.tensor_mul(tp[:, 0:512], tall[:, 0:512], pwx[:, 0, :]).then_inc(dwx.h, 1)
                dwx.v += 1
                V.tensor_mul(tp[:, 512:1024], tall[:, 512:1024],
                             pwx[:, 1, :]).then_inc(dwx.h, 1)
                dwx.v += 1

                def load_wff(m):
                    s_ = dwfp[m % 2]
                    G.dma_start(
                        out=wff[:, m % 2, :, :],
                        in_=wffT[:, m * 128:(m + 1) * 128].rearrange("(c p) f -> p c f", p=128),
                    ).then_inc(s_.h, 16)
                    s_.v += 16

                def load_y2t(m):
                    s_ = dyep[m % 2]
                    Sy.dma_start(
                        out=y2rt[:, m % 2, :, :],
                        in_=scrY2[:, m * 128:(m + 1) * 128].rearrange("(s p) f -> p s f", p=128),
                    ).then_inc(s_.h, 16)
                    s_.v += 16

                load_wff(0); load_wff(1)
                load_y2t(0); load_y2t(1)

                def d_post(m):
                    T.wait_ge(dev2.h, 2 * (m + 1))
                    if m == 0:
                        T.wait_ge(dgv.h, dgv.v)
                    if m >= 2:
                        T.wait_ge(dou.h, m - 1)
                    for s in range(8):
                        T.transpose(pt3[:, m % 2, s, :],
                                    y3bf[:, m % 2, s * 128:(s + 1) * 128], idb)
                    _tlast().then_inc(dyt.h, 1); dyt.v += 1
                    V.wait_ge(dyt.h, dyt.v)
                    V.wait_ge(dyep[m % 2].h, (m // 2 + 1) * 16)
                    if m >= 2:
                        V.wait_ge(dstp[m % 2].h, 16 * ((m - 2) // 2 + 1))
                    V.tensor_add(outt[:, m % 2, :, :], pt3[:, m % 2, :, :],
                                 y2rt[:, m % 2, :, :]).then_inc(dou.h, 1)
                    dou.v += 1
                    if m + 2 < 32:
                        Sy.wait_ge(dou.h, dou.v)
                        load_y2t(m + 2)
                    Sy.wait_ge(dou.h, dou.v)
                    Sy.dma_start(
                        out=out[:, m * 128:(m + 1) * 128].rearrange("(s p) f -> p s f", p=128),
                        in_=outt[:, m % 2, :, :]).then_inc(dstp[m % 2].h, 16)
                    dstp[m % 2].v += 16

                for m in range(32):
                    T.wait_ge(dwfp[m % 2].h, (m // 2 + 1) * 16)
                    if m == 0:
                        T.wait_ge(dt1.h, dt1.v)
                    if m >= 2:
                        T.wait_ge(dev2.h, 2 * (m - 1))
                    for span in range(2):
                        for c in range(32):
                            T.matmul(py3[:, 2 * (m % 2) + span, :], wff[:, m % 2, c, :],
                                     zfm[:, c, span * 512:(span + 1) * 512],
                                     start=(c == 0), stop=False)
                        T.wait_ge(dwx.h, dwx.v)
                        T.matmul(py3[:, 2 * (m % 2) + span, :], lBs[:, m, :],
                                 tp[:, span * 512:(span + 1) * 512],
                                 start=False, stop=True)
                    _tlast().then_inc(dmm.h, 1); dmm.v += 1
                    if m + 2 < 32:
                        G.wait_ge(dmm.h, m + 1)
                        load_wff(m + 2)
                    V.wait_ge(dmm.h, dmm.v)
                    if m >= 2:
                        V.wait_ge(dyt.h, m - 1)
                    V.tensor_copy(y3bf[:, m % 2, 0:512],
                                  py3[:, 2 * (m % 2), :]).then_inc(dev2.h, 1)
                    dev2.v += 1
                    V.tensor_copy(y3bf[:, m % 2, 512:1024],
                                  py3[:, 2 * (m % 2) + 1, :]).then_inc(dev2.h, 1)
                    dev2.v += 1
                    if m >= 1:
                        d_post(m - 1)
                d_post(31)

                Sy.wait_ge(dstp[0].h, dstp[0].v)
                Sy.wait_ge(dstp[1].h, dstp[1].v)
                G.wait_ge(dwfp[0].h, dwfp[0].v)
                G.wait_ge(dwfp[1].h, dwfp[1].v)

    return nc, dbg




def _host_prep(inputs):
    x = np.ascontiguousarray(np.asarray(inputs["L0_attn_hidden_states"], dtype=np.float32))
    Wq = np.asarray(inputs["L0_attn_Wq"], dtype=np.float32)
    Wkr = np.asarray(inputs["L0_attn_Wkr"], dtype=np.float32)
    Wdk = np.asarray(inputs["L0_attn_Wdk"], dtype=np.float32)
    Wupk = np.asarray(inputs["L0_attn_Wupk"], dtype=np.float32)
    Wupv = np.asarray(inputs["L0_attn_Wupv"], dtype=np.float32)
    Wo = np.asarray(inputs["L0_attn_Wo"], dtype=np.float32)
    Wff = np.asarray(inputs["L0_ffn_weight"], dtype=np.float32)
    lora_A = np.asarray(inputs["L0_ffn_lora_A"], dtype=np.float32)
    lora_B = np.asarray(inputs["L0_ffn_lora_B"], dtype=np.float32)
    gw1 = np.asarray(inputs["L0_ffn_gw1"], dtype=np.float32)
    gb1 = np.asarray(inputs["L0_ffn_gb1"], dtype=np.float32)
    lng = np.asarray(inputs["L0_ffn_ln_gamma"], dtype=np.float32)
    lnb = np.asarray(inputs["L0_ffn_ln_beta"], dtype=np.float32)
    gw2 = np.asarray(inputs["L0_ffn_gw2"], dtype=np.float32)
    gb2 = np.asarray(inputs["L0_ffn_gb2"], dtype=np.float32)

    blockperm = np.concatenate([np.arange(0, 32), np.arange(64, 96),
                                np.arange(32, 64), np.arange(96, 128)])
    qperm = np.concatenate([h * 128 + blockperm for h in range(32)])

    C = np.ascontiguousarray
    common = {
        "wqT": C(Wq.T[:, qperm]),
        "wkrT": C(Wkr.T),
        "wdkT": C(Wdk.T),
        "wupkT": C(Wupk.T),
        "wupvT": C(Wupv.T),
        "woT": C(Wo.T),
        "wffT": C(Wff.T),
        "lAT": C(lora_A.transpose(1, 0, 2).reshape(D, 128)),
        "lB": C(lora_B.reshape(128, D)),
        "gw1T": C(gw1.T),
        "gw2aug": C(np.concatenate([gw2.T, gb2[None, :]], axis=0)),
        "gcons": C(np.broadcast_to(
            np.concatenate([gb1, lng, lnb])[None, :], (128, 96)).copy()),
    }

    p = np.arange(128)
    common["tri"] = C(np.where(p[:, None] <= p[None, :], 0.0, NEG).astype(np.float32))
    import ml_dtypes
    common["ident"] = np.eye(128, dtype=ml_dtypes.bfloat16)
    common["identf"] = np.eye(128, dtype=np.float32)
    e16m = np.zeros((8, 128), dtype=np.float32)
    for e in range(8):
        e16m[e, e * 16:(e + 1) * 16] = 1.0 / 16.0
    common["e16"] = e16m
    common["onescol"] = np.ones((128, 1), dtype=np.float32)
    common["onesrow"] = np.ones((1, 128), dtype=np.float32)

    invf = (10000.0 ** (-np.arange(0, HD, 2, dtype=np.float32) / HD))
    pos = np.arange(S, dtype=np.float32)
    i32 = np.arange(32)
    qfreq = np.concatenate([invf[i32], invf[i32], invf[32 + i32], invf[32 + i32]])
    qsign = np.concatenate([-np.ones(32), np.ones(32), -np.ones(32), np.ones(32)])
    kfreq = np.concatenate([invf[i32]] * 4)
    ksign = np.concatenate([-np.ones(32), np.ones(32)] * 2)

    ang_k = kfreq[:, None] * pos[None, :]
    common["coskD"] = C(np.cos(ang_k).astype(np.float32))
    common["sinkD"] = C((np.sin(ang_k) * ksign[:, None]).astype(np.float32))

    per_core = []
    qpos_all = []
    for c in range(8):
        beta, j = c // 4, c % 4
        qpos = np.concatenate([np.arange(j * 512, (j + 1) * 512),
                               np.arange((7 - j) * 512, (8 - j) * 512)])
        qpos_all.append((beta, qpos))
        ang_q = qfreq[:, None] * qpos[None, :].astype(np.float32)
        d = dict(common)
        d["meta"] = np.array([[j]], dtype=np.int32)
        d["xb"] = C(x[beta])
        d["xq"] = C(x[beta][qpos])
        d["cosqD"] = C(np.cos(ang_q).astype(np.float32))
        d["sinqD"] = C((np.sin(ang_q) * qsign[:, None]).astype(np.float32))
        per_core.append(d)
    return per_core, qpos_all


def kernel(**inputs):
    per_core, qpos_all = _host_prep(inputs)
    nc, _ = _build()
    res = run_bass_kernel_spmd(nc, per_core, core_ids=list(range(8)))
    out = np.empty((B, S, D), dtype=np.float32)
    for c in range(8):
        beta, qpos = qpos_all[c]
        out[beta, qpos] = res.results[c]["out"]
    return out


# version 06
# speedup vs baseline: 1.2543x; 1.0052x over previous
import math
from contextlib import ExitStack
import numpy as np

import concourse.bass as bass
import concourse.mybir as mybir
from concourse.bass_utils import run_bass_kernel_spmd
from concourse.ordered_engine_set import OrderedEngineSet

F32 = mybir.dt.float32
BF16 = mybir.dt.bfloat16
I32 = mybir.dt.int32
AF = mybir.ActivationFunctionType
ALU = mybir.AluOpType
ET = mybir.EngineType
AX = mybir.AxisListType.X

D = 4096
S = 4096
B = 2
NH = 32
KVH = 8
HD = 128
CD = 512
RD = 512
NQ = 1024
NBLK = 8
SCALE = 1.0 / math.sqrt(HD)
SHIFT = -30.0
NEG = -1.0e9
EPS = 1e-5
CAPACITY = float(int(3.0 * (B * S) / 8))

DUMP = None
PH = "ABCD"
RGN = 8


_TLAST = None


def _set_tlast(inst):
    global _TLAST
    _TLAST = inst


def _tlast():
    return _TLAST


class Sem:
    def __init__(self, h):
        self.h = h
        self.v = 0

    def inc(self, inst, n=1):
        self.v += n
        inst.then_inc(self.h, n)


def _build():
    nc = bass.Bass("TRN2", target_bir_lowering=False)
    dt = nc.dram_tensor
    meta = dt("meta", [1, 1], I32, kind="ExternalInput").ap()
    xb = dt("xb", [S, D], F32, kind="ExternalInput").ap()
    xq = dt("xq", [NQ, D], F32, kind="ExternalInput").ap()
    wqT = dt("wqT", [D, D], F32, kind="ExternalInput").ap()
    wkrT = dt("wkrT", [D, RD], F32, kind="ExternalInput").ap()
    wdkT = dt("wdkT", [D, CD], F32, kind="ExternalInput").ap()
    wupkT = dt("wupkT", [CD, 512], F32, kind="ExternalInput").ap()
    wupvT = dt("wupvT", [CD, 1024], F32, kind="ExternalInput").ap()
    woT = dt("woT", [D, D], F32, kind="ExternalInput").ap()
    wffT = dt("wffT", [D, D], F32, kind="ExternalInput").ap()
    lAT = dt("lAT", [D, 128], F32, kind="ExternalInput").ap()
    lB = dt("lB", [128, D], F32, kind="ExternalInput").ap()
    gw1T = dt("gw1T", [D, 32], F32, kind="ExternalInput").ap()
    gw2aug = dt("gw2aug", [33, 8], F32, kind="ExternalInput").ap()
    gcons = dt("gcons", [128, 96], F32, kind="ExternalInput").ap()
    triD = dt("tri", [128, 128], F32, kind="ExternalInput").ap()
    identD = dt("ident", [128, 128], BF16, kind="ExternalInput").ap()
    identfD = dt("identf", [128, 128], F32, kind="ExternalInput").ap()
    e16D = dt("e16", [8, 128], F32, kind="ExternalInput").ap()
    onescolD = dt("onescol", [128, 1], F32, kind="ExternalInput").ap()
    onesrowD = dt("onesrow", [1, 128], F32, kind="ExternalInput").ap()
    cosqD = dt("cosqD", [128, NQ], F32, kind="ExternalInput").ap()
    sinqD = dt("sinqD", [128, NQ], F32, kind="ExternalInput").ap()
    coskD = dt("coskD", [128, S], F32, kind="ExternalInput").ap()
    sinkD = dt("sinkD", [128, S], F32, kind="ExternalInput").ap()
    out = dt("out", [NQ, D], F32, kind="ExternalOutput").ap()

    scrK = dt("scrK", [KVH, 128, S], BF16).ap()
    scrV = dt("scrV", [S, KVH * HD], BF16).ap()
    scrXfm = dt("scrXfm", [32, 128, S], BF16).ap()
    scrAv = dt("scrAv", [NQ, D], BF16).ap()
    scrY2 = dt("scrY2", [NQ, D], F32).ap()
    cntI = dt("cntI", [1, 16], F32).ap()
    cntO = dt("cntO", [1, 16], F32).ap()

    dbg = {}
    if DUMP == "A":
        dbg["dK"] = dt("dK", [KVH, 128, S], BF16, kind="ExternalOutput").ap()
        dbg["dV"] = dt("dV", [S, KVH * HD], BF16, kind="ExternalOutput").ap()
        dbg["dX"] = dt("dX", [32, 128, S], BF16, kind="ExternalOutput").ap()
    if DUMP == "B":
        dbg["dAv"] = dt("dAv", [NQ, D], BF16, kind="ExternalOutput").ap()
    if DUMP == "C":
        dbg["dY2"] = dt("dY2", [NQ, D], F32, kind="ExternalOutput").ap()
        dbg["dSt"] = dt("dSt", [128, 8, 4], F32, kind="ExternalOutput").ap()

    sems = {}

    def sem(name):
        if name not in sems:
            sems[name] = Sem(nc.semaphore(name).__enter__())
        return sems[name]

    al = nc.alloc_sbuf_tensor
    cst = al("cst", [128, 8], F32).ap()
    tri = al("tri_sb", [128, 128], F32).ap()
    idb = al("id_sb", [128, 128], BF16).ap()
    idf = al("idf_sb", [128, 128], F32).ap()
    e16 = al("e16_sb", [8, 128], BF16).ap()
    ones_c = al("ones_c", [128, 1], BF16).ap()
    ones_r = al("ones_r", [1, 128], BF16).ap()
    gcons_sb = al("gcons_sb", [128, 96], F32).ap()
    gw2_sb = al("gw2_sb", [33, 8], BF16).ap()
    cosq = al("cosq", [128, NQ], F32).ap()
    sinq = al("sinq", [128, NQ], F32).ap()
    stC = al("stC", [128, 8, 4], F32).ap()

    V = nc.vector
    Sc = nc.scalar
    G = nc.gpsimd
    Sy = nc.sync

    class _TProxy:
        def __init__(self, eng):
            self._e = eng

        def matmul(self, *a, **k):
            global _TLAST
            inst = self._e.matmul(*a, **k)
            _set_tlast(inst)
            return inst

        def transpose(self, *a, **k):
            inst = self._e.transpose(*a, **k)
            _set_tlast(inst)
            return inst

        def __getattr__(self, name):
            return getattr(self._e, name)

    T = _TProxy(nc.tensor)

    pro = sem("pro")
    prog = sem("prog")
    for dst_, src_ in ((tri, triD), (gcons_sb, gcons), (cosq, cosqD), (sinq, sinqD),
                       (idf, identfD)):
        Sy.dma_start(out=dst_, in_=src_).then_inc(pro.h, 16)
        pro.v += 16
    for dst_, src_ in ((idb, identD), (e16, e16D), (ones_c, onescolD),
                       (ones_r, onesrowD), (gw2_sb, gw2aug)):
        G.dma_start(out=dst_, in_=src_).then_inc(prog.h, 16)
        prog.v += 16
    cstg = sem("cstg")
    V.memset(cst[:, 0:1], SHIFT)
    V.memset(cst[:, 1:2], EPS)
    V.memset(cst[:, 2:3], -1.0)
    V.memset(cst[:, 3:4], CAPACITY)
    V.memset(cst[:, 4:5], 1.0)
    V.engine_nop().then_inc(cstg.h, 1)
    cstg.v += 1

    regs = nc.alloc_registers(
        "jreg", OrderedEngineSet([ET.SP, ET.Activation, ET.PE, ET.DVE, ET.Pool]))
    for r in regs:
        nc.reg_load(r, meta.to_broadcast((1, 1)))
    sv = nc.snap(regs, min_val=0, max_val=3)

    nc.all_engine_barrier()

    if "A" in PH:
        es = ExitStack()
        with es:
            def sb(name, shape, dtyp):
                return es.enter_context(nc.sbuf_tensor(name, shape, dtyp)).ap()

            def ps(name, shape, dtyp):
                return es.enter_context(nc.psum_tensor(name, shape, dtyp)).ap()

            wkr = sb("wkr_sb", [128, 32, RD], BF16)
            wdk = sb("wdk_sb", [128, 32, CD], BF16)
            wupk = sb("wupk_sb", [128, 4, 512], BF16)
            wupv = sb("wupv_sb", [128, 4, 1024], BF16)
            xt = sb("xt_sb", [128, 3, D], BF16)
            xln = sb("xln_sb", [128, 3, D], BF16)
            xfm = sb("xfm_sb", [128, 32, 512], BF16)
            ckv = sb("ckv_sb", [128, 4, 512], BF16)
            krp = sb("krp_sb", [128, 2, 512], BF16)
            kc = sb("kc_sb", [128, 2, 512], BF16)
            vst = sb("vst_sb", [128, 2, 1024], BF16)
            rk1 = sb("rk1_sb", [128, 512], F32)
            rk2 = sb("rk2_sb", [128, 512], F32)
            rk3 = sb("rk3_sb", [128, 512], F32)
            coskb = sb("cosk_sb", [128, 2, 512], F32)
            sinkb = sb("sink_sb", [128, 2, 512], F32)
            lns = sb("lns_sb", [128, 16], F32)
            trash = sb("trash_sb", [128, D], BF16)
            ptr = ps("ptr_ps", [128, 16, 128], BF16)
            prj = ps("prj_ps", [128, 4, 512], F32)
            vps = ps("v_ps", [128, 2, 512], F32)

            aw = sem("aw"); act = sem("act")
            axp = [sem("ax0"), sem("ax1"), sem("ax2")]
            aln = sem("aln"); alg = sem("alg"); aap = sem("aap")
            atp = sem("atp"); aev = sem("aev"); amm = sem("amm")
            akr = sem("akr"); ack = sem("ack"); akc = sem("akc")
            avv = sem("avv"); axw = sem("axw"); asq = sem("asq")
            akw = [sem("akw0"), sem("akw1")]
            kcwp = [sem("kcw0"), sem("kcw1")]
            avw = [sem("avw0"), sem("avw1")]
            actp = [sem("act0"), sem("act1")]

            prj_rota = []

            def issue_x(i):
                if i >= 3:
                    G.wait_ge(aap.h, i - 2)
                G.dma_start(out=xt[:, i % 3, :],
                            in_=xb[i * 128:(i + 1) * 128, :]).then_inc(axp[i % 3].h, 16)
                axp[i % 3].v += 16

            def issue_ct(b):
                if b >= 2:
                    Sy.wait_ge(akr.h, 4 * (b - 1))
                Sy.dma_start(out=coskb[:, b % 2, :],
                             in_=coskD[:, b * 512:(b + 1) * 512]).then_inc(actp[b % 2].h, 16)
                Sy.dma_start(out=sinkb[:, b % 2, :],
                             in_=sinkD[:, b * 512:(b + 1) * 512]).then_inc(actp[b % 2].h, 16)
                actp[b % 2].v += 32

            issue_x(0); issue_x(1); issue_x(2)
            G.dma_start(out=wkr, in_=wkrT.rearrange("(c p) f -> p c f", p=128)).then_inc(aw.h, 16)
            G.dma_start(out=wdk, in_=wdkT.rearrange("(c p) f -> p c f", p=128)).then_inc(aw.h, 16)
            G.dma_start(out=wupk, in_=wupkT.rearrange("(c p) f -> p c f", p=128)).then_inc(aw.h, 16)
            G.dma_start(out=wupv, in_=wupvT.rearrange("(c p) f -> p c f", p=128)).then_inc(aw.h, 16)
            aw.v = 64
            issue_ct(0); issue_ct(1)

            for blk in range(NBLK):
                for t in range(4):
                    i = blk * 4 + t
                    bi = i % 3
                    if i + 3 < 4 * NBLK:
                        issue_x(i + 3)
                    V.wait_ge(axp[i % 3].h, (i // 3 + 1) * 16)
                    Sc.wait_ge(axp[i % 3].h, (i // 3 + 1) * 16)
                    Sc.wait_ge(asq.h, asq.v)
                    if i >= 3:
                        Sc.wait_ge(aln.h, 2 * i - 5)
                    Sc.activation(trash, xt[:, bi, :], AF.Square,
                                  accum_out=lns[:, 8 + bi:9 + bi]).then_inc(asq.h, 1)
                    asq.v += 1
                    V.reduce_sum(lns[:, 0:1], xt[:, bi, :], axis=AX).then_inc(alg.h, 1)
                    alg.v += 1
                    V.wait_ge(alg.h, alg.v)
                    V.wait_ge(asq.h, asq.v)
                    V.tensor_scalar_mul(lns[:, 2:3], lns[:, 0:1], 1.0 / D).then_inc(alg.h, 1)
                    alg.v += 1
                    V.wait_ge(alg.h, alg.v)
                    V.scalar_tensor_tensor(lns[:, 3:4], lns[:, 2:3], 1.0,
                                           lns[:, 2:3], ALU.mult, ALU.mult).then_inc(alg.h, 1)
                    alg.v += 1
                    V.wait_ge(alg.h, alg.v)
                    V.tensor_scalar(lns[:, 4:5], lns[:, 8 + bi:9 + bi], 1.0 / D,
                                    lns[:, 3:4], ALU.mult, ALU.subtract).then_inc(aln.h, 1)
                    aln.v += 1
                    Sc.wait_ge(aln.h, aln.v)
                    Sc.wait_ge(cstg.h, 1)
                    Sc.activation(lns[:, 5:6], lns[:, 4:5], AF.Sqrt,
                                  bias=cst[:, 1:2], scale=1.0).then_inc(alg.h, 1)
                    alg.v += 1
                    V.wait_ge(alg.h, alg.v)
                    V.reciprocal(lns[:, 6:7], lns[:, 5:6]).then_inc(alg.h, 1)
                    alg.v += 1
                    V.wait_ge(alg.h, alg.v)
                    V.scalar_tensor_tensor(lns[:, 7:8], lns[:, 2:3], -1.0,
                                           lns[:, 6:7], ALU.mult, ALU.mult).then_inc(aln.h, 1)
                    aln.v += 1
                    Sc.wait_ge(aln.h, aln.v)
                    if i >= 3:
                        Sc.wait_ge(atp.h, 4 * (i - 2))
                    Sc.activation(xln[:, bi, :], xt[:, bi, :], AF.Identity,
                                  bias=lns[:, 7:8], scale=lns[:, 6:7]).then_inc(aap.h, 1)
                    aap.v += 1
                    T.wait_ge(aap.h, aap.v)
                    if i == 0:
                        T.wait_ge(pro.h, pro.v)
                        T.wait_ge(prog.h, prog.v)
                    for c8 in range(4):
                        bk = c8 % 2
                        if atp.v >= 2:
                            T.wait_ge(aev.h, atp.v - 1)
                        for cc in range(8):
                            c = c8 * 8 + cc
                            T.transpose(ptr[:, bk * 8 + cc, :],
                                        xln[:, bi, c * 128:(c + 1) * 128], idb)
                        _tlast().then_inc(atp.h, 1); atp.v += 1
                        Sc.wait_ge(atp.h, atp.v)
                        if blk > 0 and t == 0 and c8 == 0:
                            Sc.wait_ge(axw.h, blk * 16)
                        Sc.copy(
                            xfm[:, c8 * 8:(c8 + 1) * 8, t * 128:(t + 1) * 128],
                            ptr[:, bk * 8:bk * 8 + 8, :]).then_inc(aev.h, 1)
                        aev.v += 1

                Sy.wait_ge(aev.h, aev.v)
                Sy.dma_start(
                    out=scrXfm[:, :, blk * 512:(blk + 1) * 512].rearrange("c p t -> p c t"),
                    in_=xfm).then_inc(axw.h, 16)
                axw.v += 16

                T.wait_ge(aev.h, aev.v)
                if blk == 0:
                    T.wait_ge(aw.h, 64)
                for m in range(4):
                    bk = len(prj_rota) % 4
                    if len(prj_rota) >= 4:
                        T.wait_ge(*prj_rota[-4])
                    for c in range(32):
                        T.matmul(prj[:, bk, :], wkr[:, c, m * 128:(m + 1) * 128],
                                 xfm[:, c, :], start=(c == 0), stop=(c == 31))
                    _tlast().then_inc(amm.h, 1); amm.v += 1
                    V.wait_ge(amm.h, amm.v)
                    V.wait_ge(actp[blk % 2].h, (blk // 2 + 1) * 32)
                    V.wait_ge(akr.h, akr.v)
                    if akr.v >= 2:
                        n = akr.v
                        V.wait_ge(akw[n % 2].h, 32 * (n // 2))
                    for (d0, d1), (s0, s1) in (((0, 32), (32, 64)), ((32, 64), (0, 32)),
                                               ((64, 96), (96, 128)), ((96, 128), (64, 96))):
                        V.tensor_copy(rk1[d0:d1, :], prj[s0:s1, bk, :]).then_inc(alg.h, 1)
                        alg.v += 1
                    V.tensor_mul(rk2, prj[:, bk, :], coskb[:, blk % 2, :]).then_inc(alg.h, 1)
                    alg.v += 1
                    V.wait_ge(alg.h, alg.v)
                    V.tensor_mul(rk3, rk1, sinkb[:, blk % 2, :]).then_inc(alg.h, 1)
                    alg.v += 1
                    V.wait_ge(alg.h, alg.v)
                    V.tensor_add(krp[:, m % 2, :], rk2, rk3).then_inc(akr.h, 1)
                    akr.v += 1
                    prj_rota.append((akr.h, akr.v))
                    Sy.wait_ge(akr.h, akr.v)
                    n = akr.v - 1
                    Sy.dma_start(out=scrK[2 * m, 0:64, blk * 512:(blk + 1) * 512],
                                 in_=krp[0:64, m % 2, :]).then_inc(akw[n % 2].h, 16)
                    Sy.dma_start(out=scrK[2 * m + 1, 0:64, blk * 512:(blk + 1) * 512],
                                 in_=krp[64:128, m % 2, :]).then_inc(akw[n % 2].h, 16)
                    akw[n % 2].v += 32

                if blk + 2 < NBLK:
                    issue_ct(blk + 2)

                for m in range(4):
                    bk = len(prj_rota) % 4
                    T.wait_ge(*prj_rota[-4])
                    for c in range(32):
                        T.matmul(prj[:, bk, :], wdk[:, c, m * 128:(m + 1) * 128],
                                 xfm[:, c, :], start=(c == 0), stop=(c == 31))
                    _tlast().then_inc(amm.h, 1); amm.v += 1
                    V.wait_ge(amm.h, amm.v)
                    V.tensor_copy(ckv[:, m, :], prj[:, bk, :]).then_inc(ack.h, 1)
                    ack.v += 1
                    prj_rota.append((ack.h, ack.v))

                T.wait_ge(ack.h, ack.v)
                for m in range(4):
                    bk = len(prj_rota) % 4
                    T.wait_ge(*prj_rota[-4])
                    for c2 in range(4):
                        T.matmul(prj[:, bk, :], wupk[:, c2, m * 128:(m + 1) * 128],
                                 ckv[:, c2, :], start=(c2 == 0), stop=(c2 == 3))
                    _tlast().then_inc(amm.h, 1); amm.v += 1
                    V.wait_ge(amm.h, amm.v)
                    if akc.v >= 2:
                        n = akc.v
                        V.wait_ge(kcwp[n % 2].h, 32 * (n // 2))
                    V.tensor_copy(kc[:, m % 2, :], prj[:, bk, :]).then_inc(akc.h, 1)
                    akc.v += 1
                    prj_rota.append((akc.h, akc.v))
                    Sy.wait_ge(akc.h, akc.v)
                    n = akc.v - 1
                    Sy.dma_start(out=scrK[2 * m, 64:128, blk * 512:(blk + 1) * 512],
                                 in_=kc[0:64, m % 2, :]).then_inc(kcwp[n % 2].h, 16)
                    Sy.dma_start(out=scrK[2 * m + 1, 64:128, blk * 512:(blk + 1) * 512],
                                 in_=kc[64:128, m % 2, :]).then_inc(kcwp[n % 2].h, 16)
                    kcwp[n % 2].v += 32

                for t in range(4):
                    pr = 0
                    if True:
                        T.wait_ge(avv.h, avv.v)
                    for half in range(2):
                        for c2 in range(4):
                            T.matmul(vps[:, half, :],
                                     ckv[:, c2, t * 128:(t + 1) * 128],
                                     wupv[:, c2, half * 512:(half + 1) * 512],
                                     start=(c2 == 0), stop=(c2 == 3))
                    _tlast().then_inc(amm.h, 1); amm.v += 1
                    V.wait_ge(amm.h, amm.v)
                    nuse = avv.v // 2
                    if nuse >= 2:
                        V.wait_ge(avw[nuse % 2].h, 16 * (nuse // 2))
                    V.tensor_copy(vst[:, nuse % 2, 0:512], vps[:, 0, :]).then_inc(avv.h, 1)
                    avv.v += 1
                    V.tensor_copy(vst[:, nuse % 2, 512:1024], vps[:, 1, :]).then_inc(avv.h, 1)
                    avv.v += 1
                    Sy.wait_ge(avv.h, avv.v)
                    n = avv.v // 2 - 1
                    Sy.dma_start(
                        out=scrV[blk * 512 + t * 128: blk * 512 + (t + 1) * 128, :],
                        in_=vst[:, n % 2, :]).then_inc(avw[n % 2].h, 16)
                    avw[n % 2].v += 16

            for s_ in (akw[0], akw[1], kcwp[0], kcwp[1], avw[0], avw[1]):
                Sy.wait_ge(s_.h, s_.v)
            Sy.wait_ge(axw.h, axw.v)
            G.wait_ge(axp[0].h, axp[0].v)
            G.wait_ge(axp[1].h, axp[1].v)

            if DUMP == "A":
                dd = sem("dd")
                Sy.dma_start(out=dbg["dK"], in_=scrK).then_inc(dd.h, 16)
                Sy.dma_start(out=dbg["dV"], in_=scrV).then_inc(dd.h, 16)
                Sy.dma_start(out=dbg["dX"], in_=scrXfm).then_inc(dd.h, 16)
                dd.v += 48
                Sy.wait_ge(dd.h, dd.v)

        nc.all_engine_barrier()

    if "B" in PH:
        es = ExitStack()
        with es:
            def sb(name, shape, dtyp):
                return es.enter_context(nc.sbuf_tensor(name, shape, dtyp)).ap()

            def ps(name, shape, dtyp):
                return es.enter_context(nc.psum_tensor(name, shape, dtyp)).ap()

            xfq = sb("xfq_sb", [128, 32, NQ], BF16)
            kgb = sb("kg_sb", [128, 2, S], BF16)
            vgb = sb("vg_sb", [128, 2, 32, 128], BF16)
            wqb = sb("wq_sb", [128, 2, 32, 128], BF16)
            qq = sb("qq_sb", [128, 2, NQ], BF16)
            qr1 = sb("qr1_sb", [128, 512], F32)
            qr2 = sb("qr2_sb", [128, 512], F32)
            qr3 = sb("qr3_sb", [128, 512], F32)
            pp = sb("p_sb", [128, 6, 512], BF16)
            pss = sb("pss_sb", [128, NQ], F32)
            den = sb("den_sb", [128, 16], F32)
            avf = sb("avf_sb", [128, NQ], BF16)
            avt = sb("avt_sb", [128, 2, 8, 128], BF16)
            sc = ps("sc_ps", [128, 4, 512], F32)
            av = ps("av_ps", [128, 2, 512], F32)
            at = ps("at_ps", [128, 8, 128], BF16)
            pq = sc

            names = ["bxq", "bkv0", "bkv1", "bwq0", "bwq1", "bqm", "brp", "brg",
                     "bsc", "bmk", "bex", "bps", "bavm", "bpt", "bdn", "brc",
                     "bevc", "bavt", "bscl", "bao0", "bao1"]
            bs = {n: sem(n) for n in names}
            snap0 = {n: bs[n].v for n in names}
            deltas = []

            for jb in nc.Switch(engines=[Sy, Sc, nc.tensor, V, G], index=[sv] * 5, n=4):
                for n in names:
                    bs[n].v = snap0[n]
                (bxq, bkv0, bkv1, bwq0, bwq1, bqm, brp, brg, bsc, bmk, bex,
                 bps, bavm, bpt, bdn, brc, bevc, bavt, bscl, bao0, bao1) = (
                    bs[n] for n in names)
                bkvp = [bkv0, bkv1]
                bwqp = [bwq0, bwq1]
                baop = [bao0, bao1]
                base = snap0

                blocks = (jb, 7 - jb)
                spans = []
                for b in blocks:
                    sl = [(kt, 512, 0, False) for kt in range(4 * b)]
                    sl += [(4 * b + d, 512 - 128 * d, 128 * d, True) for d in range(4)]
                    spans.append(sl)

                for h2, b in enumerate(blocks):
                    Sy.dma_start(
                        out=xfq[:, :, h2 * 512:(h2 + 1) * 512],
                        in_=scrXfm[:, :, b * 512:(b + 1) * 512].rearrange("c p t -> p c t"),
                    ).then_inc(bxq.h, 16)
                    bxq.v += 16

                def load_kv(g):
                    if g >= 2:
                        Sy.wait_ge(bavm.h, base["bavm"] + 36 * 4 * (g - 1))
                    s_ = bkvp[g % 2]
                    Sy.dma_start(out=kgb[:, g % 2, :], in_=scrK[g]).then_inc(s_.h, 16)
                    Sy.dma_start(
                        out=vgb[:, g % 2, :, :],
                        in_=scrV[:, g * 128:(g + 1) * 128].rearrange("(c p) h -> p c h", p=128),
                    ).then_inc(s_.h, 16)
                    s_.v += 32

                def load_wq(h):
                    if h >= 2:
                        G.wait_ge(bqm.h, base["bqm"] + 2 * (h - 1))
                    s_ = bwqp[h % 2]
                    G.dma_start(
                        out=wqb[:, h % 2, :, :],
                        in_=wqT[:, h * 128:(h + 1) * 128].rearrange("(c p) f -> p c f", p=128),
                    ).then_inc(s_.h, 16)
                    s_.v += 16

                load_kv(0)
                load_wq(0); load_wq(1)

                for h in range(NH):
                    g = h // 4
                    if h % 4 == 0 and g + 1 < KVH:
                        load_kv(g + 1)
                    if h + 2 < NH:
                        load_wq(h + 2)

                    T.wait_ge(bwqp[h % 2].h, base["bwq%d" % (h % 2)] + (h // 2 + 1) * 16)
                    if h == 0:
                        T.wait_ge(bxq.h, base["bxq"] + 32)
                    if h >= 1:
                        T.wait_ge(bdn.h, base["bdn"] + 8 * h)
                    for span in range(2):
                        for c in range(32):
                            T.matmul(pq[:, span, :], wqb[:, h % 2, c, :],
                                     xfq[:, c, span * 512:(span + 1) * 512],
                                     start=(c == 0), stop=(c == 31))
                        _tlast().then_inc(bqm.h, 1); bqm.v += 1

                    for span in range(2):
                        V.wait_ge(bqm.h, bqm.v - 1 + span)
                        V.wait_ge(brg.h, brg.v)
                        for (d0, d1), (s0, s1) in (((0, 32), (32, 64)), ((32, 64), (0, 32)),
                                                   ((64, 96), (96, 128)), ((96, 128), (64, 96))):
                            V.tensor_copy(qr1[d0:d1, :], pq[s0:s1, span, :]).then_inc(brg.h, 1)
                            brg.v += 1
                        V.tensor_mul(qr2, pq[:, span, :],
                                     cosq[:, span * 512:(span + 1) * 512]).then_inc(brg.h, 1)
                        brg.v += 1
                        V.wait_ge(brg.h, brg.v)
                        V.tensor_mul(qr3, qr1,
                                     sinq[:, span * 512:(span + 1) * 512]).then_inc(brg.h, 1)
                        brg.v += 1
                        V.wait_ge(brg.h, brg.v)
                        if span == 0:
                            V.tensor_add(qq[:, h % 2, 0:512], qr2, qr3).then_inc(brg.h, 1)
                            brg.v += 1
                        else:
                            V.tensor_add(qq[:, h % 2, 512:1024], qr2, qr3).then_inc(brp.h, 1)
                            brp.v += 1

                    T.wait_ge(brp.h, base["brp"] + h + 1)
                    T.wait_ge(bkvp[g % 2].h, base["bkv%d" % (g % 2)] + (g // 2 + 1) * 32)
                    base_sc = bsc.v; base_ex = bex.v
                    base_av = bavm.v; base_ps = bps.v
                    slot = 0
                    nslots = sum(len(sl) for sl in spans)
                    flat = []
                    for span, sl in enumerate(spans):
                        for si, (kt, w, qoff, diag) in enumerate(sl):
                            flat.append((span, si, len(sl), kt, w, qoff, diag))

                    LAG = 3

                    def emit_av(j):
                        span, si, sln, kt, w, qoff, diag = flat[j]
                        if si == 0 and h >= 1:
                            T.wait_ge(bevc.h, base["bevc"] + 2 * (h - 1) + span + 1)
                        T.wait_ge(bex.h, base_ex + j + 1)
                        T.matmul(av[:, span, qoff:qoff + w],
                                 vgb[:, g % 2, kt, :], pp[:, j % 6, 0:w],
                                 start=(si == 0), stop=(si == sln - 1)
                                 ).then_inc(bavm.h, 1)
                        bavm.v += 1

                    for i, (span, si, sln, kt, w, qoff, diag) in enumerate(flat):
                        bk = i % 4
                        pb = i % 6
                        if i >= 4:
                            T.wait_ge(bex.h, base_ex + i - 3)
                        T.matmul(
                            sc[:, bk, 0:w],
                            kgb[:, g % 2, kt * 128:(kt + 1) * 128],
                            qq[:, h % 2, span * 512 + qoff: span * 512 + qoff + w],
                            start=True, stop=True).then_inc(bsc.h, 1)
                        bsc.v += 1
                        if diag:
                            V.wait_ge(bsc.h, base_sc + i + 1)
                            V.tensor_add(sc[:, bk, 0:128], sc[:, bk, 0:128],
                                         tri).then_inc(bmk.h, 1)
                            bmk.v += 1
                        Sc.wait_ge(bmk.h if diag else bsc.h,
                                   bmk.v if diag else base_sc + i + 1)
                        if i >= 6:
                            Sc.wait_ge(bavm.h, base_av + i - 5)
                            Sc.wait_ge(bps.h, base_ps + i - 5)
                        Sc.activation(pp[:, pb, 0:w], sc[:, bk, 0:w], AF.Exp,
                                      bias=cst[:, 0:1], scale=SCALE).then_inc(bex.h, 1)
                        bex.v += 1
                        V.wait_ge(bex.h, base_ex + i + 1)
                        V.wait_ge(bps.h, bps.v)
                        pcols = pss[:, span * 512 + qoff: span * 512 + qoff + w]
                        if si == 0:
                            if span == 0 and h >= 1:
                                V.wait_ge(bpt.h, base["bpt"] + h)
                            V.tensor_copy(pcols, pp[:, pb, 0:w]).then_inc(bps.h, 1)
                        else:
                            V.tensor_add(pcols, pcols, pp[:, pb, 0:w]).then_inc(bps.h, 1)
                        bps.v += 1
                        if i >= LAG:
                            emit_av(i - LAG)
                        if si == sln - 1:
                            pass
                        slot += 1
                    for j in range(nslots - LAG, nslots):
                        emit_av(j)
                    for span, sl in enumerate(spans):
                        V.wait_ge(bavm.h, base_av + sum(len(s) for s in spans[:span + 1]))
                        if h >= 1:
                            V.wait_ge(bavt.h, base["bavt"] + h)
                        V.tensor_copy(avf[:, span * 512:(span + 1) * 512],
                                      av[:, span, :]).then_inc(bevc.h, 1)
                        bevc.v += 1

                    T.wait_ge(bex.h, base_ex + nslots)
                    T.wait_ge(bps.h, base_ps + nslots)
                    for s8 in range(8):
                        T.transpose(sc[:, s8 // 2, (s8 % 2) * 128:(s8 % 2 + 1) * 128],
                                    pss[:, s8 * 128:(s8 + 1) * 128], idf)
                    _tlast().then_inc(bpt.h, 1); bpt.v += 1
                    V.wait_ge(bpt.h, bpt.v)
                    for s8 in range(8):
                        V.reduce_sum(den[:, s8:s8 + 1],
                                     sc[:, s8 // 2, (s8 % 2) * 128:(s8 % 2 + 1) * 128],
                                     axis=AX).then_inc(bdn.h, 1)
                        bdn.v += 1
                    V.wait_ge(bdn.h, bdn.v)
                    V.reciprocal(den[:, 8:16], den[:, 0:8]).then_inc(brc.h, 1)
                    brc.v += 1

                    T.wait_ge(bevc.h, base["bevc"] + 2 * (h + 1))
                    if h >= 1:
                        T.wait_ge(bscl.h, base["bscl"] + 8 * h)
                    for s8 in range(8):
                        T.transpose(at[:, s8, :], avf[:, s8 * 128:(s8 + 1) * 128], idb)
                    _tlast().then_inc(bavt.h, 1); bavt.v += 1
                    V.wait_ge(bavt.h, bavt.v)
                    V.wait_ge(brc.h, brc.v)
                    if h >= 2:
                        V.wait_ge(baop[h % 2].h,
                                  base["bao%d" % (h % 2)] + 16 * ((h - 2) // 2 + 1))
                    for s8 in range(8):
                        V.tensor_scalar(avt[:, h % 2, s8, :], at[:, s8, :],
                                        den[:, 8 + s8:9 + s8], None,
                                        ALU.mult).then_inc(bscl.h, 1)
                        bscl.v += 1
                    Sy.wait_ge(bscl.h, bscl.v)
                    Sy.dma_start(
                        out=scrAv[:, h * 128:(h + 1) * 128].rearrange("(s p) f -> p s f", p=128),
                        in_=avt[:, h % 2, :, :]).then_inc(baop[h % 2].h, 16)
                    baop[h % 2].v += 16

                Sy.wait_ge(bao0.h, bao0.v)
                Sy.wait_ge(bao1.h, bao1.v)
                G.wait_ge(bwq0.h, bwq0.v)
                G.wait_ge(bwq1.h, bwq1.v)
                deltas.append({n: bs[n].v - snap0[n] for n in names})

            assert all(d == deltas[0] for d in deltas[1:]), "switch bodies differ"
            for n in names:
                bs[n].v = snap0[n] + deltas[0][n]

            nc.all_engine_barrier()

            if DUMP == "B":
                dd = sem("dd2")
                Sy.dma_start(out=dbg["dAv"], in_=scrAv).then_inc(dd.h, 16)
                dd.v += 16
                Sy.wait_ge(dd.h, dd.v)
                nc.all_engine_barrier()

    if "C" in PH:
        es = ExitStack()
        with es:
            def sb(name, shape, dtyp):
                return es.enter_context(nc.sbuf_tensor(name, shape, dtyp)).ap()

            def ps(name, shape, dtyp):
                return es.enter_context(nc.psum_tensor(name, shape, dtyp)).ap()

            avT = sb("avT_sb", [128, 32, NQ], BF16)
            wo = sb("wo_sb", [128, 2, 32, 128], BF16)
            ybf = sb("ybf_sb", [128, 2, NQ], BF16)
            xqr = sb("xqr_sb", [128, 2, 8, 128], F32)
            y2t = sb("y2t_sb", [128, 2, 8, 128], F32)
            sumc = sb("sumc_sb", [128, 8, 32], F32)
            ssqc = sb("ssqc_sb", [128, 8, 32], F32)
            ctr = sb("ctrash_sb", [128, 8, 128], BF16)
            py = ps("py_ps", [128, 4, 512], F32)
            pt = ps("pt_ps", [128, 2, 8, 128], BF16)

            cav = sem("cav")
            cwop = [sem("cwo0"), sem("cwo1")]
            cxqp = [sem("cxq0"), sem("cxq1")]
            cswp = [sem("csw0"), sem("csw1")]
            cym = sem("cym"); cev = sem("cev"); ctp = sem("ctp")
            cy2 = sem("cy2"); cg = sem("cg")

            for c in range(32):
                Sy.dma_start_transpose(avT[:, c, :],
                                       scrAv[:, c * 128:(c + 1) * 128]).then_inc(cav.h, 16)
                cav.v += 16

            def load_wo(m):
                s_ = cwop[m % 2]
                G.dma_start(
                    out=wo[:, m % 2, :, :],
                    in_=woT[:, m * 128:(m + 1) * 128].rearrange("(c p) f -> p c f", p=128),
                ).then_inc(s_.h, 16)
                s_.v += 16

            def load_xq(m):
                s_ = cxqp[m % 2]
                Sy.dma_start(
                    out=xqr[:, m % 2, :, :],
                    in_=xq[:, m * 128:(m + 1) * 128].rearrange("(s p) f -> p s f", p=128),
                ).then_inc(s_.h, 16)
                s_.v += 16

            load_wo(0); load_wo(1)
            load_xq(0); load_xq(1)

            csq = sem("csq")

            def c_post(m):
                T.wait_ge(cev.h, 2 * (m + 1))
                if m >= 2:
                    T.wait_ge(cy2.h, 9 * (m - 1))
                for s in range(8):
                    T.transpose(pt[:, m % 2, s, :],
                                ybf[:, m % 2, s * 128:(s + 1) * 128], idb)
                _tlast().then_inc(ctp.h, 1); ctp.v += 1
                V.wait_ge(ctp.h, ctp.v)
                V.wait_ge(cxqp[m % 2].h, (m // 2 + 1) * 16)
                if m >= 2:
                    V.wait_ge(cswp[m % 2].h, 16 * ((m - 2) // 2 + 1))
                    V.wait_ge(csq.h, 8 * (m - 1))
                    V.wait_ge(cy2.h, 9 * (m - 1))
                V.tensor_add(y2t[:, m % 2, :, :], pt[:, m % 2, :, :],
                             xqr[:, m % 2, :, :]).then_inc(cy2.h, 1)
                cy2.v += 1
                V.wait_ge(cy2.h, cy2.v)
                Sc.wait_ge(cy2.h, cy2.v)
                Sc.wait_ge(csq.h, csq.v)
                for s in range(8):
                    V.reduce_sum(sumc[:, s, m:m + 1], y2t[:, m % 2, s, :],
                                 axis=AX).then_inc(cy2.h, 1)
                    cy2.v += 1
                    Sc.activation(ctr[:, s, :], y2t[:, m % 2, s, :], AF.Square,
                                  accum_out=ssqc[:, s, m:m + 1]).then_inc(csq.h, 1)
                    csq.v += 1
                Sy.wait_ge(cy2.h, cy2.v)
                Sy.wait_ge(csq.h, csq.v)
                Sy.dma_start(
                    out=scrY2[:, m * 128:(m + 1) * 128].rearrange("(s p) f -> p s f", p=128),
                    in_=y2t[:, m % 2, :, :]).then_inc(cswp[m % 2].h, 16)
                cswp[m % 2].v += 16
                if m + 2 < 32:
                    Sy.wait_ge(cy2.h, cy2.v)
                    load_xq(m + 2)

            for m in range(32):
                T.wait_ge(cwop[m % 2].h, (m // 2 + 1) * 16)
                if m == 0:
                    T.wait_ge(cav.h, cav.v)
                if m >= 2:
                    T.wait_ge(cev.h, 2 * (m - 1))
                for span in range(2):
                    for c in range(32):
                        T.matmul(py[:, 2 * (m % 2) + span, :], wo[:, m % 2, c, :],
                                 avT[:, c, span * 512:(span + 1) * 512],
                                 start=(c == 0), stop=(c == 31))
                _tlast().then_inc(cym.h, 1); cym.v += 1
                if m + 2 < 32:
                    G.wait_ge(cym.h, m + 1)
                    load_wo(m + 2)
                V.wait_ge(cym.h, cym.v)
                if m >= 2:
                    V.wait_ge(ctp.h, m - 1)
                V.tensor_copy(ybf[:, m % 2, 0:512], py[:, 2 * (m % 2), :]).then_inc(cev.h, 1)
                cev.v += 1
                V.tensor_copy(ybf[:, m % 2, 512:1024], py[:, 2 * (m % 2) + 1, :]).then_inc(cev.h, 1)
                cev.v += 1
                if m >= 1:
                    c_post(m - 1)
            c_post(31)

            V.wait_ge(cy2.h, cy2.v)
            V.wait_ge(csq.h, csq.v)
            for s in range(8):
                V.reduce_sum(stC[:, s, 0:1], sumc[:, s, :], axis=AX).then_inc(cg.h, 1)
                V.reduce_sum(stC[:, s, 1:2], ssqc[:, s, :], axis=AX).then_inc(cg.h, 1)
                cg.v += 2
            V.wait_ge(cg.h, cg.v)
            for s in range(8):
                V.tensor_scalar_mul(stC[:, s, 0:1], stC[:, s, 0:1], 1.0 / D).then_inc(cg.h, 1)
                V.tensor_scalar_mul(stC[:, s, 1:2], stC[:, s, 1:2], 1.0 / D).then_inc(cg.h, 1)
                cg.v += 2
            V.wait_ge(cg.h, cg.v)
            for s in range(8):
                V.scalar_tensor_tensor(stC[:, s, 2:3], stC[:, s, 0:1], 1.0,
                                       stC[:, s, 0:1], ALU.mult, ALU.mult).then_inc(cg.h, 1)
                cg.v += 1
            V.wait_ge(cg.h, cg.v)
            for s in range(8):
                V.tensor_tensor(stC[:, s, 1:2], stC[:, s, 1:2], stC[:, s, 2:3],
                                ALU.subtract).then_inc(cg.h, 1)
                cg.v += 1
            Sc.wait_ge(cg.h, cg.v)
            for s in range(8):
                Sc.activation(stC[:, s, 2:3], stC[:, s, 1:2], AF.Sqrt,
                              bias=cst[:, 1:2], scale=1.0).then_inc(cg.h, 1)
                cg.v += 1
            V.wait_ge(cg.h, cg.v)
            for s in range(8):
                V.reciprocal(stC[:, s, 3:4], stC[:, s, 2:3]).then_inc(cg.h, 1)
                cg.v += 1
            Sy.wait_ge(cswp[0].h, cswp[0].v)
            Sy.wait_ge(cswp[1].h, cswp[1].v)

            if DUMP == "C":
                dd = sem("dd3")
                Sy.dma_start(out=dbg["dY2"], in_=scrY2).then_inc(dd.h, 16)
                dd.v += 16
                Sy.wait_ge(cg.h, cg.v)
                Sy.dma_start(out=dbg["dSt"], in_=stC).then_inc(dd.h, 16)
                dd.v += 16
                Sy.wait_ge(dd.h, dd.v)

        nc.all_engine_barrier()

    if "D" in PH:
        es = ExitStack()
        with es:
            def sb(name, shape, dtyp):
                return es.enter_context(nc.sbuf_tensor(name, shape, dtyp)).ap()

            zfm = sb("zfm_sb", [128, 32, NQ], BF16)
            y2r = sb("y2r_sb", [128, 2, D], F32)
            ztm = sb("ztm_sb", [128, 2, D], BF16)
            gw1s = sb("gw1s_sb", [128, 32, 32], BF16)
            lAs = sb("lA_sb", [128, 32, 128], BF16)
            lBs = sb("lB_sb", [128, 32, 128], BF16)
            hraw = sb("hraw_sb", [32, NQ], F32)
            htm = sb("htm_sb", [128, 8, 32], F32)
            gst = sb("gst_sb", [128, 32], F32)
            gst2 = sb("gst2_sb", [128, 8], F32)
            g1a = sb("g1a_sb", [128, 8], F32)
            g2a = sb("g2a_sb", [128, 8], F32)
            hrel = sb("hrel_sb", [128, 32], F32)
            hrel2 = sb("hrel2_sb", [128, 32], F32)
            hrT = sb("hrT_sb", [33, 128], BF16)
            gts = sb("gts_sb", [128, 8, 8], F32)
            m1 = sb("m1_sb", [128, 8, 8], F32)
            m2 = sb("m2_sb", [128, 8, 8], F32)
            mc = sb("mc_sb", [128, 16], BF16)
            d21 = sb("d21_sb", [128, 8, 3], F32)
            wts = sb("wts_sb", [128, 8], F32)
            wts2 = sb("wts2_sb", [128, 8], F32)
            wT = sb("wT_sb", [8, NQ], BF16)
            cnt = sb("cnt_sb", [1, 16], F32)
            cnt2 = sb("cnt2_sb", [1, 16], F32)
            cntb = sb("cntb_sb", [1, 16], BF16)
            arep = sb("arep_sb", [128, 16], F32)
            tall = sb("tall_sb", [128, NQ], BF16)
            tp = sb("tp_sb", [128, NQ], BF16)
            wff = sb("wff_sb", [128, 2, 32, 128], BF16)
            y3bf = sb("y3bf_sb", [128, 2, NQ], BF16)
            y2rt = sb("y2rt_sb", [128, 2, 8, 128], F32)
            outt = sb("outt_sb", [128, 2, 8, 128], F32)
            dtr = sb("dtrash_sb", [128, 32], BF16)

            dld = sem("dld"); dzn = sem("dzn")
            dy2p = [sem("dy20"), sem("dy21")]
            dzt = sem("dzt"); dze = sem("dze"); dgm = sem("dgm")
            dgv = sem("dgv"); dgg = sem("dgg"); dcm = sem("dcm")
            dt1 = sem("dt1"); dwx = sem("dwx")
            dwfp = [sem("dwf0"), sem("dwf1")]
            dmm = sem("dmm"); dev2 = sem("dev2")
            dyep = [sem("dye0"), sem("dye1")]
            dstp = [sem("dst0"), sem("dst1")]
            dym = sem("dym"); dyt = sem("dyt"); dou = sem("dou")

            G.dma_start(out=gw1s, in_=gw1T.rearrange("(c p) f -> p c f", p=128)).then_inc(dld.h, 16)
            G.dma_start(out=lAs, in_=lAT.rearrange("(c p) f -> p c f", p=128)).then_inc(dld.h, 16)
            G.dma_start(out=lBs, in_=lB.rearrange("p (c f) -> p c f", f=128)).then_inc(dld.h, 16)
            dld.v += 48
            V.memset(hrT[32:33, :], 1.0)
            V.engine_nop().then_inc(dgg.h, 1); dgg.v += 1

            def load_y2(s):
                s_ = dy2p[s % 2]
                Sy.dma_start(out=y2r[:, s % 2, :],
                             in_=scrY2[s * 128:(s + 1) * 128, :]).then_inc(s_.h, 16)
                s_.v += 16

            load_y2(0); load_y2(1)

            es1 = ExitStack()
            with es1:
                def ps1(name, shape, dtyp):
                    return es1.enter_context(nc.psum_tensor(name, shape, dtyp)).ap()

                ztp = ps1("zt_ps", [128, 16, 128], BF16)
                pg = ps1("pg_ps", [32, 2, 512], F32)
                pt2 = ps1("pt2_ps", [128, 8, 32], F32)
                ptg1 = ps1("ptg1_ps", [32, 128], F32)
                pgt2 = ps1("pgt2_ps", [128, 32], F32)
                pcnt = ps1("pcnt_ps", [1, 16], F32)

                for s in range(8):
                    V.wait_ge(dy2p[s % 2].h, (s // 2 + 1) * 16)
                    if s >= 2:
                        V.wait_ge(dzt.h, 4 * (s - 1))
                    V.tensor_scalar(ztm[:, s % 2, :], y2r[:, s % 2, :],
                                    stC[:, s, 0:1], stC[:, s, 3:4],
                                    ALU.subtract, ALU.mult).then_inc(dzn.h, 1)
                    dzn.v += 1
                    if s + 2 < 8:
                        Sy.wait_ge(dzn.h, s + 1)
                        load_y2(s + 2)

                    T.wait_ge(dzn.h, dzn.v)
                    for c8 in range(4):
                        bk = c8 % 2
                        if dzt.v >= 2:
                            T.wait_ge(dze.h, dzt.v - 1)
                        for cc in range(8):
                            c = c8 * 8 + cc
                            T.transpose(ztp[:, bk * 8 + cc, :],
                                        ztm[:, s % 2, c * 128:(c + 1) * 128], idb)
                        _tlast().then_inc(dzt.h, 1); dzt.v += 1
                        V.wait_ge(dzt.h, dzt.v)
                        V.tensor_copy(zfm[:, c8 * 8:(c8 + 1) * 8, s * 128:(s + 1) * 128],
                                      ztp[:, bk * 8:bk * 8 + 8, :]).then_inc(dze.h, 1)
                        dze.v += 1

                T.wait_ge(dze.h, dze.v)
                T.wait_ge(dld.h, dld.v)
                for span in range(2):
                    for c in range(32):
                        T.matmul(pg[:, span, :], gw1s[:, c, :],
                                 zfm[:, c, span * 512:(span + 1) * 512],
                                 start=(c == 0), stop=(c == 31))
                _tlast().then_inc(dgm.h, 1); dgm.v += 1
                V.wait_ge(dgm.h, dgm.v)
                V.tensor_copy(hraw[:, 0:512], pg[:, 0, :]).then_inc(dgg.h, 1)
                dgg.v += 1
                V.tensor_copy(hraw[:, 512:1024], pg[:, 1, :]).then_inc(dgg.h, 1)
                dgg.v += 1

                T.wait_ge(dgg.h, dgg.v)
                for s in range(8):
                    T.transpose(pt2[:, s, :], hraw[:, s * 128:(s + 1) * 128], idf[0:32, 0:32])
                _tlast().then_inc(dgm.h, 1); dgm.v += 1
                V.wait_ge(dgm.h, dgm.v)
                for s in range(8):
                    V.tensor_add(htm[:, s, :], pt2[:, s, :],
                                 gcons_sb[:, 0:32]).then_inc(dgg.h, 1)
                    dgg.v += 1
                V.wait_ge(dgg.h, dgg.v)
                for s in range(8):
                    V.reduce_sum(gst[:, s:s + 1], htm[:, s, :],
                                 axis=AX).then_inc(dgg.h, 1)
                    dgg.v += 1
                    V.wait_ge(dgg.h, dgg.v)
                    V.scalar_tensor_tensor(dtr, htm[:, s, :], 1.0, htm[:, s, :],
                                           ALU.mult, ALU.mult,
                                           accum_out=gst[:, 8 + s:9 + s]).then_inc(dgg.h, 1)
                    dgg.v += 1
                    V.wait_ge(dgg.h, dgg.v)
                V.tensor_scalar_mul(gst[:, 0:8], gst[:, 0:8], 1.0 / 32).then_inc(dgg.h, 1)
                dgg.v += 1
                V.tensor_scalar_mul(gst[:, 8:16], gst[:, 8:16], 1.0 / 32).then_inc(dgg.h, 1)
                dgg.v += 1
                V.wait_ge(dgg.h, dgg.v)
                V.scalar_tensor_tensor(gst2[:, 0:8], gst[:, 0:8], 1.0,
                                       gst[:, 0:8], ALU.mult, ALU.mult).then_inc(dgg.h, 1)
                dgg.v += 1
                V.wait_ge(dgg.h, dgg.v)
                V.tensor_tensor(gst[:, 8:16], gst[:, 8:16], gst2[:, 0:8],
                                ALU.subtract).then_inc(dgg.h, 1)
                dgg.v += 1
                Sc.wait_ge(dgg.h, dgg.v)
                Sc.activation(gst[:, 16:24], gst[:, 8:16], AF.Sqrt,
                              bias=cst[:, 1:2], scale=1.0).then_inc(dgg.h, 1)
                dgg.v += 1
                V.wait_ge(dgg.h, dgg.v)
                V.reciprocal(gst[:, 24:32], gst[:, 16:24]).then_inc(dgg.h, 1)
                dgg.v += 1

                for s in range(8):
                    V.wait_ge(dgg.h, dgg.v)
                    V.wait_ge(dgv.h, dgv.v)
                    V.tensor_scalar(hrel, htm[:, s, :], gst[:, s:s + 1],
                                    gst[:, 24 + s:25 + s], ALU.subtract,
                                    ALU.mult).then_inc(dgv.h, 1)
                    dgv.v += 1
                    V.wait_ge(dgv.h, dgv.v)
                    V.scalar_tensor_tensor(hrel2, hrel, 1.0, gcons_sb[:, 32:64],
                                           ALU.mult, ALU.mult).then_inc(dgv.h, 1)
                    dgv.v += 1
                    V.wait_ge(dgv.h, dgv.v)
                    V.scalar_tensor_tensor(hrel, hrel2, 1.0, gcons_sb[:, 64:96],
                                           ALU.mult, ALU.add).then_inc(dgv.h, 1)
                    dgv.v += 1
                    V.wait_ge(dgv.h, dgv.v)
                    V.tensor_scalar_max(hrel2, hrel, 0.0).then_inc(dgv.h, 1)
                    dgv.v += 1
                    T.wait_ge(dgv.h, dgv.v)
                    T.transpose(ptg1, hrel2, idf).then_inc(dgm.h, 1); dgm.v += 1
                    V.wait_ge(dgm.h, dgm.v)
                    V.tensor_copy(hrT[0:32, :], ptg1[0:32, :]).then_inc(dgv.h, 1)
                    dgv.v += 1
                    T.wait_ge(dgv.h, dgv.v)
                    T.matmul(pgt2[:, 0:8], hrT, gw2_sb, start=True, stop=True
                             ).then_inc(dgm.h, 1)
                    dgm.v += 1
                    V.wait_ge(dgm.h, dgm.v)
                    V.tensor_copy(gts[:, s, :], pgt2[:, 0:8]).then_inc(dgv.h, 1)
                    dgv.v += 1
                    V.wait_ge(dgv.h, dgv.v)
                    V.reduce_max(d21[:, s, 0:1], gts[:, s, :], axis=AX).then_inc(dgv.h, 1)
                    dgv.v += 1
                    V.wait_ge(dgv.h, dgv.v)
                    V.tensor_scalar(m1[:, s, :], gts[:, s, :], d21[:, s, 0:1],
                                    None, ALU.is_equal).then_inc(dgv.h, 1)
                    dgv.v += 1
                    V.wait_ge(dgv.h, dgv.v)
                    V.scalar_tensor_tensor(m2[:, s, :], m1[:, s, :], -1e30,
                                           gts[:, s, :], ALU.mult, ALU.add).then_inc(dgv.h, 1)
                    dgv.v += 1
                    V.wait_ge(dgv.h, dgv.v)
                    V.reduce_max(d21[:, s, 1:2], m2[:, s, :], axis=AX).then_inc(dgv.h, 1)
                    dgv.v += 1
                    V.wait_ge(dgv.h, dgv.v)
                    V.tensor_scalar(m2[:, s, :], m2[:, s, :], d21[:, s, 1:2],
                                    None, ALU.is_equal).then_inc(dgv.h, 1)
                    dgv.v += 1
                    V.tensor_tensor(d21[:, s, 2:3], d21[:, s, 1:2], d21[:, s, 0:1],
                                    ALU.subtract).then_inc(dgv.h, 1)
                    dgv.v += 1
                    V.wait_ge(dgv.h, dgv.v)
                    V.wait_ge(dcm.h, dcm.v)
                    V.tensor_copy(mc[:, 0:8], m1[:, s, :]).then_inc(dgv.h, 1)
                    dgv.v += 1
                    V.tensor_copy(mc[:, 8:16], m2[:, s, :]).then_inc(dgv.h, 1)
                    dgv.v += 1
                    T.wait_ge(dgv.h, dgv.v)
                    T.matmul(pcnt, ones_c, mc, start=(s == 0), stop=(s == 7)
                             ).then_inc(dcm.h, 1)
                    dcm.v += 1

                Sc.wait_ge(dgv.h, dgv.v)
                Sc.activation(g1a, d21[:, :, 2:3].rearrange("p s c -> p (s c)"),
                              AF.Sigmoid, bias=0.0, scale=cst[:, 2:3]).then_inc(dgg.h, 1)
                dgg.v += 1
                Sc.activation(g2a, d21[:, :, 2:3].rearrange("p s c -> p (s c)"),
                              AF.Sigmoid, bias=0.0, scale=1.0).then_inc(dgg.h, 1)
                dgg.v += 1

                dccs = sem("dccs"); dccc = sem("dccc"); dccg = sem("dccg")
                V.wait_ge(dcm.h, dcm.v)
                V.tensor_copy(cnt, pcnt).then_inc(dgv.h, 1)
                dgv.v += 1
                Sy.wait_ge(dgv.h, dgv.v)
                Sy.dma_start(out=cntI, in_=cnt).then_inc(dccs.h, 16)
                dccs.v += 16
                G.wait_ge(dccs.h, dccs.v)
                if RGN > 1:
                    G.collective_compute(
                        "AllReduce", ALU.add, replica_groups=[list(range(RGN))],
                        ins=[cntI], outs=[cntO]).then_inc(dccc.h, 1)
                    dccc.v += 1
                    G.wait_ge(dccc.h, dccc.v)
                    G.dma_start(out=cnt2, in_=cntO).then_inc(dccg.h, 16)
                else:
                    G.dma_start(out=cnt2, in_=cntI).then_inc(dccg.h, 16)
                dccg.v += 16
                V.wait_ge(dccg.h, dccg.v)
                V.tensor_scalar(cnt, cnt2, cst[0:1, 3:4], None,
                                ALU.is_le).then_inc(dgv.h, 1)
                dgv.v += 1
                V.wait_ge(dgv.h, dgv.v)
                V.tensor_copy(cntb, cnt).then_inc(dgv.h, 1)
                dgv.v += 1
                T.wait_ge(dgv.h, dgv.v)
                T.matmul(pgt2[:, 0:16], ones_r, cntb, start=True, stop=True
                         ).then_inc(dgm.h, 1)
                dgm.v += 1
                V.wait_ge(dgm.h, dgm.v)
                V.tensor_copy(arep, pgt2[:, 0:16]).then_inc(dgv.h, 1)
                dgv.v += 1

                for s in range(8):
                    V.wait_ge(dgg.h, dgg.v)
                    V.wait_ge(dgv.h, dgv.v)
                    V.tensor_scalar(wts, m1[:, s, :], g1a[:, s:s + 1],
                                    None, ALU.mult).then_inc(dgv.h, 1)
                    dgv.v += 1
                    V.tensor_scalar(wts2, m2[:, s, :], g2a[:, s:s + 1],
                                    None, ALU.mult).then_inc(dgv.h, 1)
                    dgv.v += 1
                    V.wait_ge(dgv.h, dgv.v)
                    V.tensor_mul(wts, wts, arep[:, 0:8]).then_inc(dgv.h, 1)
                    dgv.v += 1
                    V.tensor_mul(wts2, wts2, arep[:, 8:16]).then_inc(dgv.h, 1)
                    dgv.v += 1
                    V.wait_ge(dgv.h, dgv.v)
                    V.tensor_add(wts, wts, wts2).then_inc(dgv.h, 1)
                    dgv.v += 1
                    T.wait_ge(dgv.h, dgv.v)
                    T.transpose(ptg1[0:8, :], wts, idf).then_inc(dgm.h, 1)
                    dgm.v += 1
                    V.wait_ge(dgm.h, dgm.v)
                    V.tensor_copy(wT[:, s * 128:(s + 1) * 128],
                                  ptg1[0:8, :]).then_inc(dgv.h, 1)
                    dgv.v += 1

            es2 = ExitStack()
            with es2:
                def ps2(name, shape, dtyp):
                    return es2.enter_context(nc.psum_tensor(name, shape, dtyp)).ap()

                py3 = ps2("py3_ps", [128, 4, 512], F32)
                pt3 = ps2("pt3_ps", [128, 2, 8, 128], BF16)
                pwx = ps2("pwx_ps", [128, 2, 512], F32)

                T.wait_ge(dld.h, dld.v)
                for span in range(2):
                    for c in range(32):
                        T.matmul(py3[:, span, :], lAs[:, c, :],
                                 zfm[:, c, span * 512:(span + 1) * 512],
                                 start=(c == 0), stop=(c == 31))
                _tlast().then_inc(dt1.h, 1); dt1.v += 1
                V.wait_ge(dt1.h, dt1.v)
                V.tensor_copy(tall[:, 0:512], py3[:, 0, :]).then_inc(dt1.h, 1)
                dt1.v += 1
                V.tensor_copy(tall[:, 512:1024], py3[:, 1, :]).then_inc(dt1.h, 1)
                dt1.v += 1

                T.wait_ge(dgv.h, dgv.v)
                T.wait_ge(dt1.h, dt1.v)
                for span in range(2):
                    T.matmul(pwx[:, span, :], e16,
                             wT[:, span * 512:(span + 1) * 512],
                             start=True, stop=True)
                _tlast().then_inc(dwx.h, 1); dwx.v += 1
                V.wait_ge(dwx.h, dwx.v)
                V.tensor_mul(tp[:, 0:512], tall[:, 0:512], pwx[:, 0, :]).then_inc(dwx.h, 1)
                dwx.v += 1
                V.tensor_mul(tp[:, 512:1024], tall[:, 512:1024],
                             pwx[:, 1, :]).then_inc(dwx.h, 1)
                dwx.v += 1

                def load_wff(m):
                    s_ = dwfp[m % 2]
                    G.dma_start(
                        out=wff[:, m % 2, :, :],
                        in_=wffT[:, m * 128:(m + 1) * 128].rearrange("(c p) f -> p c f", p=128),
                    ).then_inc(s_.h, 16)
                    s_.v += 16

                def load_y2t(m):
                    s_ = dyep[m % 2]
                    Sy.dma_start(
                        out=y2rt[:, m % 2, :, :],
                        in_=scrY2[:, m * 128:(m + 1) * 128].rearrange("(s p) f -> p s f", p=128),
                    ).then_inc(s_.h, 16)
                    s_.v += 16

                load_wff(0); load_wff(1)
                load_y2t(0); load_y2t(1)

                def d_post(m):
                    T.wait_ge(dev2.h, 2 * (m + 1))
                    if m == 0:
                        T.wait_ge(dgv.h, dgv.v)
                    if m >= 2:
                        T.wait_ge(dou.h, m - 1)
                    for s in range(8):
                        T.transpose(pt3[:, m % 2, s, :],
                                    y3bf[:, m % 2, s * 128:(s + 1) * 128], idb)
                    _tlast().then_inc(dyt.h, 1); dyt.v += 1
                    V.wait_ge(dyt.h, dyt.v)
                    V.wait_ge(dyep[m % 2].h, (m // 2 + 1) * 16)
                    if m >= 2:
                        V.wait_ge(dstp[m % 2].h, 16 * ((m - 2) // 2 + 1))
                    V.tensor_add(outt[:, m % 2, :, :], pt3[:, m % 2, :, :],
                                 y2rt[:, m % 2, :, :]).then_inc(dou.h, 1)
                    dou.v += 1
                    if m + 2 < 32:
                        Sy.wait_ge(dou.h, dou.v)
                        load_y2t(m + 2)
                    Sy.wait_ge(dou.h, dou.v)
                    Sy.dma_start(
                        out=out[:, m * 128:(m + 1) * 128].rearrange("(s p) f -> p s f", p=128),
                        in_=outt[:, m % 2, :, :]).then_inc(dstp[m % 2].h, 16)
                    dstp[m % 2].v += 16

                for m in range(32):
                    T.wait_ge(dwfp[m % 2].h, (m // 2 + 1) * 16)
                    if m == 0:
                        T.wait_ge(dt1.h, dt1.v)
                    if m >= 2:
                        T.wait_ge(dev2.h, 2 * (m - 1))
                    for span in range(2):
                        for c in range(32):
                            T.matmul(py3[:, 2 * (m % 2) + span, :], wff[:, m % 2, c, :],
                                     zfm[:, c, span * 512:(span + 1) * 512],
                                     start=(c == 0), stop=False)
                        T.wait_ge(dwx.h, dwx.v)
                        T.matmul(py3[:, 2 * (m % 2) + span, :], lBs[:, m, :],
                                 tp[:, span * 512:(span + 1) * 512],
                                 start=False, stop=True)
                    _tlast().then_inc(dmm.h, 1); dmm.v += 1
                    if m + 2 < 32:
                        G.wait_ge(dmm.h, m + 1)
                        load_wff(m + 2)
                    V.wait_ge(dmm.h, dmm.v)
                    if m >= 2:
                        V.wait_ge(dyt.h, m - 1)
                    V.tensor_copy(y3bf[:, m % 2, 0:512],
                                  py3[:, 2 * (m % 2), :]).then_inc(dev2.h, 1)
                    dev2.v += 1
                    V.tensor_copy(y3bf[:, m % 2, 512:1024],
                                  py3[:, 2 * (m % 2) + 1, :]).then_inc(dev2.h, 1)
                    dev2.v += 1
                    if m >= 1:
                        d_post(m - 1)
                d_post(31)

                Sy.wait_ge(dstp[0].h, dstp[0].v)
                Sy.wait_ge(dstp[1].h, dstp[1].v)
                G.wait_ge(dwfp[0].h, dwfp[0].v)
                G.wait_ge(dwfp[1].h, dwfp[1].v)

    return nc, dbg




def _host_prep(inputs):
    x = np.ascontiguousarray(np.asarray(inputs["L0_attn_hidden_states"], dtype=np.float32))
    Wq = np.asarray(inputs["L0_attn_Wq"], dtype=np.float32)
    Wkr = np.asarray(inputs["L0_attn_Wkr"], dtype=np.float32)
    Wdk = np.asarray(inputs["L0_attn_Wdk"], dtype=np.float32)
    Wupk = np.asarray(inputs["L0_attn_Wupk"], dtype=np.float32)
    Wupv = np.asarray(inputs["L0_attn_Wupv"], dtype=np.float32)
    Wo = np.asarray(inputs["L0_attn_Wo"], dtype=np.float32)
    Wff = np.asarray(inputs["L0_ffn_weight"], dtype=np.float32)
    lora_A = np.asarray(inputs["L0_ffn_lora_A"], dtype=np.float32)
    lora_B = np.asarray(inputs["L0_ffn_lora_B"], dtype=np.float32)
    gw1 = np.asarray(inputs["L0_ffn_gw1"], dtype=np.float32)
    gb1 = np.asarray(inputs["L0_ffn_gb1"], dtype=np.float32)
    lng = np.asarray(inputs["L0_ffn_ln_gamma"], dtype=np.float32)
    lnb = np.asarray(inputs["L0_ffn_ln_beta"], dtype=np.float32)
    gw2 = np.asarray(inputs["L0_ffn_gw2"], dtype=np.float32)
    gb2 = np.asarray(inputs["L0_ffn_gb2"], dtype=np.float32)

    blockperm = np.concatenate([np.arange(0, 32), np.arange(64, 96),
                                np.arange(32, 64), np.arange(96, 128)])
    qperm = np.concatenate([h * 128 + blockperm for h in range(32)])

    C = np.ascontiguousarray
    common = {
        "wqT": C(Wq.T[:, qperm]),
        "wkrT": C(Wkr.T),
        "wdkT": C(Wdk.T),
        "wupkT": C(Wupk.T),
        "wupvT": C(Wupv.T),
        "woT": C(Wo.T),
        "wffT": C(Wff.T),
        "lAT": C(lora_A.transpose(1, 0, 2).reshape(D, 128)),
        "lB": C(lora_B.reshape(128, D)),
        "gw1T": C(gw1.T),
        "gw2aug": C(np.concatenate([gw2.T, gb2[None, :]], axis=0)),
        "gcons": C(np.broadcast_to(
            np.concatenate([gb1, lng, lnb])[None, :], (128, 96)).copy()),
    }

    p = np.arange(128)
    common["tri"] = C(np.where(p[:, None] <= p[None, :], 0.0, NEG).astype(np.float32))
    import ml_dtypes
    common["ident"] = np.eye(128, dtype=ml_dtypes.bfloat16)
    common["identf"] = np.eye(128, dtype=np.float32)
    e16m = np.zeros((8, 128), dtype=np.float32)
    for e in range(8):
        e16m[e, e * 16:(e + 1) * 16] = 1.0 / 16.0
    common["e16"] = e16m
    common["onescol"] = np.ones((128, 1), dtype=np.float32)
    common["onesrow"] = np.ones((1, 128), dtype=np.float32)

    invf = (10000.0 ** (-np.arange(0, HD, 2, dtype=np.float32) / HD))
    pos = np.arange(S, dtype=np.float32)
    i32 = np.arange(32)
    qfreq = np.concatenate([invf[i32], invf[i32], invf[32 + i32], invf[32 + i32]])
    qsign = np.concatenate([-np.ones(32), np.ones(32), -np.ones(32), np.ones(32)])
    kfreq = np.concatenate([invf[i32]] * 4)
    ksign = np.concatenate([-np.ones(32), np.ones(32)] * 2)

    ang_k = kfreq[:, None] * pos[None, :]
    common["coskD"] = C(np.cos(ang_k).astype(np.float32))
    common["sinkD"] = C((np.sin(ang_k) * ksign[:, None]).astype(np.float32))

    per_core = []
    qpos_all = []
    for c in range(8):
        beta, j = c // 4, c % 4
        qpos = np.concatenate([np.arange(j * 512, (j + 1) * 512),
                               np.arange((7 - j) * 512, (8 - j) * 512)])
        qpos_all.append((beta, qpos))
        ang_q = qfreq[:, None] * qpos[None, :].astype(np.float32)
        d = dict(common)
        d["meta"] = np.array([[j]], dtype=np.int32)
        d["xb"] = C(x[beta])
        d["xq"] = C(x[beta][qpos])
        d["cosqD"] = C(np.cos(ang_q).astype(np.float32))
        d["sinqD"] = C((np.sin(ang_q) * qsign[:, None]).astype(np.float32))
        per_core.append(d)
    return per_core, qpos_all


def kernel(**inputs):
    per_core, qpos_all = _host_prep(inputs)
    nc, _ = _build()
    res = run_bass_kernel_spmd(nc, per_core, core_ids=list(range(8)))
    out = np.empty((B, S, D), dtype=np.float32)
    for c in range(8):
        beta, qpos = qpos_all[c]
        out[beta, qpos] = res.results[c]["out"]
    return out


# version 07 (final)
# speedup vs baseline: 1.2662x; 1.0095x over previous
import math
from contextlib import ExitStack
import numpy as np

import concourse.bass as bass
import concourse.mybir as mybir
from concourse.bass_utils import run_bass_kernel_spmd
from concourse.ordered_engine_set import OrderedEngineSet

F32 = mybir.dt.float32
BF16 = mybir.dt.bfloat16
I32 = mybir.dt.int32
AF = mybir.ActivationFunctionType
ALU = mybir.AluOpType
ET = mybir.EngineType
AX = mybir.AxisListType.X

D = 4096
S = 4096
B = 2
NH = 32
KVH = 8
HD = 128
CD = 512
RD = 512
NQ = 1024
NBLK = 8
SCALE = 1.0 / math.sqrt(HD)
SHIFT = -30.0
NEG = -1.0e9
EPS = 1e-5
CAPACITY = float(int(3.0 * (B * S) / 8))

DUMP = None
PH = "ABCD"
RGN = 8


_TLAST = None


def _set_tlast(inst):
    global _TLAST
    _TLAST = inst


def _tlast():
    return _TLAST


class Sem:
    def __init__(self, h):
        self.h = h
        self.v = 0

    def inc(self, inst, n=1):
        self.v += n
        inst.then_inc(self.h, n)


def _build():
    nc = bass.Bass("TRN2", target_bir_lowering=False)
    dt = nc.dram_tensor
    meta = dt("meta", [1, 1], I32, kind="ExternalInput").ap()
    xb = dt("xb", [S, D], F32, kind="ExternalInput").ap()
    xq = dt("xq", [NQ, D], F32, kind="ExternalInput").ap()
    wqT = dt("wqT", [D, D], F32, kind="ExternalInput").ap()
    wkrT = dt("wkrT", [D, RD], F32, kind="ExternalInput").ap()
    wdkT = dt("wdkT", [D, CD], F32, kind="ExternalInput").ap()
    wupkT = dt("wupkT", [CD, 512], F32, kind="ExternalInput").ap()
    wupvT = dt("wupvT", [CD, 1024], F32, kind="ExternalInput").ap()
    woT = dt("woT", [D, D], F32, kind="ExternalInput").ap()
    wffT = dt("wffT", [D, D], F32, kind="ExternalInput").ap()
    lAT = dt("lAT", [D, 128], F32, kind="ExternalInput").ap()
    lB = dt("lB", [128, D], F32, kind="ExternalInput").ap()
    gw1T = dt("gw1T", [D, 32], F32, kind="ExternalInput").ap()
    gw2aug = dt("gw2aug", [33, 8], F32, kind="ExternalInput").ap()
    gcons = dt("gcons", [128, 96], F32, kind="ExternalInput").ap()
    triD = dt("tri", [128, 128], F32, kind="ExternalInput").ap()
    identD = dt("ident", [128, 128], BF16, kind="ExternalInput").ap()
    identfD = dt("identf", [128, 128], F32, kind="ExternalInput").ap()
    e16D = dt("e16", [8, 128], F32, kind="ExternalInput").ap()
    onescolD = dt("onescol", [128, 1], F32, kind="ExternalInput").ap()
    onesrowD = dt("onesrow", [1, 128], F32, kind="ExternalInput").ap()
    cosqD = dt("cosqD", [128, NQ], F32, kind="ExternalInput").ap()
    sinqD = dt("sinqD", [128, NQ], F32, kind="ExternalInput").ap()
    coskD = dt("coskD", [128, S], F32, kind="ExternalInput").ap()
    sinkD = dt("sinkD", [128, S], F32, kind="ExternalInput").ap()
    out = dt("out", [NQ, D], F32, kind="ExternalOutput").ap()

    scrK = dt("scrK", [KVH, 128, S], BF16).ap()
    scrV = dt("scrV", [S, KVH * HD], BF16).ap()
    scrXfm = dt("scrXfm", [32, 128, S], BF16).ap()
    scrAv = dt("scrAv", [NQ, D], BF16).ap()
    scrY2 = dt("scrY2", [NQ, D], F32).ap()
    cntI = dt("cntI", [1, 16], F32).ap()
    cntO = dt("cntO", [1, 16], F32).ap()

    dbg = {}
    if DUMP == "A":
        dbg["dK"] = dt("dK", [KVH, 128, S], BF16, kind="ExternalOutput").ap()
        dbg["dV"] = dt("dV", [S, KVH * HD], BF16, kind="ExternalOutput").ap()
        dbg["dX"] = dt("dX", [32, 128, S], BF16, kind="ExternalOutput").ap()
    if DUMP == "B":
        dbg["dAv"] = dt("dAv", [NQ, D], BF16, kind="ExternalOutput").ap()
    if DUMP == "C":
        dbg["dY2"] = dt("dY2", [NQ, D], F32, kind="ExternalOutput").ap()
        dbg["dSt"] = dt("dSt", [128, 8, 4], F32, kind="ExternalOutput").ap()

    sems = {}

    def sem(name):
        if name not in sems:
            sems[name] = Sem(nc.semaphore(name).__enter__())
        return sems[name]

    al = nc.alloc_sbuf_tensor
    cst = al("cst", [128, 8], F32).ap()
    tri = al("tri_sb", [128, 128], F32).ap()
    idb = al("id_sb", [128, 128], BF16).ap()
    idf = al("idf_sb", [128, 128], F32).ap()
    e16 = al("e16_sb", [8, 128], BF16).ap()
    ones_c = al("ones_c", [128, 1], BF16).ap()
    ones_r = al("ones_r", [1, 128], BF16).ap()
    gcons_sb = al("gcons_sb", [128, 96], F32).ap()
    gw2_sb = al("gw2_sb", [33, 8], BF16).ap()
    cosq = al("cosq", [128, NQ], F32).ap()
    sinq = al("sinq", [128, NQ], F32).ap()
    stC = al("stC", [128, 8, 4], F32).ap()

    V = nc.vector
    Sc = nc.scalar
    G = nc.gpsimd
    Sy = nc.sync

    class _TProxy:
        def __init__(self, eng):
            self._e = eng

        def matmul(self, *a, **k):
            global _TLAST
            inst = self._e.matmul(*a, **k)
            _set_tlast(inst)
            return inst

        def transpose(self, *a, **k):
            inst = self._e.transpose(*a, **k)
            _set_tlast(inst)
            return inst

        def __getattr__(self, name):
            return getattr(self._e, name)

    T = _TProxy(nc.tensor)

    pro = sem("pro")
    prog = sem("prog")
    for dst_, src_ in ((tri, triD), (gcons_sb, gcons), (cosq, cosqD), (sinq, sinqD),
                       (idf, identfD)):
        Sy.dma_start(out=dst_, in_=src_).then_inc(pro.h, 16)
        pro.v += 16
    for dst_, src_ in ((idb, identD), (e16, e16D), (ones_c, onescolD),
                       (ones_r, onesrowD), (gw2_sb, gw2aug)):
        G.dma_start(out=dst_, in_=src_).then_inc(prog.h, 16)
        prog.v += 16
    cstg = sem("cstg")
    V.memset(cst[:, 0:1], SHIFT)
    V.memset(cst[:, 1:2], EPS)
    V.memset(cst[:, 2:3], -1.0)
    V.memset(cst[:, 3:4], CAPACITY)
    V.memset(cst[:, 4:5], 1.0)
    V.engine_nop().then_inc(cstg.h, 1)
    cstg.v += 1

    regs = nc.alloc_registers(
        "jreg", OrderedEngineSet([ET.SP, ET.Activation, ET.PE, ET.DVE, ET.Pool]))
    for r in regs:
        nc.reg_load(r, meta.to_broadcast((1, 1)))
    sv = nc.snap(regs, min_val=0, max_val=3)

    nc.all_engine_barrier()

    if "A" in PH:
        es = ExitStack()
        with es:
            def sb(name, shape, dtyp):
                return es.enter_context(nc.sbuf_tensor(name, shape, dtyp)).ap()

            def ps(name, shape, dtyp):
                return es.enter_context(nc.psum_tensor(name, shape, dtyp)).ap()

            wkr = sb("wkr_sb", [128, 32, RD], BF16)
            wdk = sb("wdk_sb", [128, 32, CD], BF16)
            wupk = sb("wupk_sb", [128, 4, 512], BF16)
            wupv = sb("wupv_sb", [128, 4, 1024], BF16)
            xt = sb("xt_sb", [128, 4, D], BF16)
            xln = sb("xln_sb", [128, 4, D], BF16)
            xfm = sb("xfm_sb", [128, 32, 512], BF16)
            ckv = sb("ckv_sb", [128, 4, 512], BF16)
            krp = sb("krp_sb", [128, 2, 512], BF16)
            kc = sb("kc_sb", [128, 2, 512], BF16)
            vst = sb("vst_sb", [128, 2, 1024], BF16)
            rk1 = sb("rk1_sb", [128, 512], F32)
            rk2 = sb("rk2_sb", [128, 512], F32)
            coskb = sb("cosk_sb", [128, 2, 512], F32)
            sinkb = sb("sink_sb", [128, 2, 512], F32)
            lns = sb("lns_sb", [128, 16], F32)
            ptr = ps("ptr_ps", [128, 16, 128], BF16)
            prj = ps("prj_ps", [128, 4, 512], F32)
            vps = ps("v_ps", [128, 2, 512], F32)

            aw = sem("aw"); act = sem("act")
            axp = [sem("ax0"), sem("ax1"), sem("ax2"), sem("ax3")]
            aln = sem("aln"); alg = sem("alg"); aap = sem("aap")
            atp = sem("atp"); aev = sem("aev"); amm = sem("amm")
            akr = sem("akr"); ack = sem("ack"); akc = sem("akc")
            avv = sem("avv"); axw = sem("axw"); asq = sem("asq")
            akw = [sem("akw0"), sem("akw1")]
            kcwp = [sem("kcw0"), sem("kcw1")]
            avw = [sem("avw0"), sem("avw1")]
            actp = [sem("act0"), sem("act1")]

            prj_rota = []

            def issue_x(i):
                if i >= 4:
                    G.wait_ge(aap.h, i - 3)
                G.dma_start(out=xt[:, i % 4, :],
                            in_=xb[i * 128:(i + 1) * 128, :]).then_inc(axp[i % 4].h, 16)
                axp[i % 4].v += 16

            def issue_ct(b):
                if b >= 2:
                    Sy.wait_ge(akr.h, 4 * (b - 1))
                Sy.dma_start(out=coskb[:, b % 2, :],
                             in_=coskD[:, b * 512:(b + 1) * 512]).then_inc(actp[b % 2].h, 16)
                Sy.dma_start(out=sinkb[:, b % 2, :],
                             in_=sinkD[:, b * 512:(b + 1) * 512]).then_inc(actp[b % 2].h, 16)
                actp[b % 2].v += 32

            issue_x(0); issue_x(1); issue_x(2); issue_x(3)
            G.dma_start(out=wkr, in_=wkrT.rearrange("(c p) f -> p c f", p=128)).then_inc(aw.h, 16)
            G.dma_start(out=wdk, in_=wdkT.rearrange("(c p) f -> p c f", p=128)).then_inc(aw.h, 16)
            G.dma_start(out=wupk, in_=wupkT.rearrange("(c p) f -> p c f", p=128)).then_inc(aw.h, 16)
            G.dma_start(out=wupv, in_=wupvT.rearrange("(c p) f -> p c f", p=128)).then_inc(aw.h, 16)
            aw.v = 64
            issue_ct(0); issue_ct(1)

            for blk in range(NBLK):
                for t in range(4):
                    i = blk * 4 + t
                    bi = i % 4
                    if i + 4 < 4 * NBLK:
                        issue_x(i + 4)
                    V.wait_ge(axp[i % 4].h, (i // 4 + 1) * 16)
                    Sc.wait_ge(axp[i % 4].h, (i // 4 + 1) * 16)
                    Sc.wait_ge(asq.h, asq.v)
                    if i >= 3:
                        Sc.wait_ge(aln.h, 2 * i - 5)
                    Sc.activation(xln[:, bi, :], xt[:, bi, :], AF.Square,
                                  accum_out=lns[:, 8 + (i % 3):9 + (i % 3)]).then_inc(asq.h, 1)
                    asq.v += 1
                    V.reduce_sum(lns[:, 0:1], xt[:, bi, :], axis=AX).then_inc(alg.h, 1)
                    alg.v += 1
                    V.wait_ge(alg.h, alg.v)
                    V.wait_ge(asq.h, asq.v)
                    V.tensor_scalar_mul(lns[:, 2:3], lns[:, 0:1], 1.0 / D).then_inc(alg.h, 1)
                    alg.v += 1
                    V.wait_ge(alg.h, alg.v)
                    V.scalar_tensor_tensor(lns[:, 3:4], lns[:, 2:3], 1.0,
                                           lns[:, 2:3], ALU.mult, ALU.mult).then_inc(alg.h, 1)
                    alg.v += 1
                    V.wait_ge(alg.h, alg.v)
                    V.tensor_scalar(lns[:, 4:5], lns[:, 8 + (i % 3):9 + (i % 3)], 1.0 / D,
                                    lns[:, 3:4], ALU.mult, ALU.subtract).then_inc(aln.h, 1)
                    aln.v += 1
                    Sc.wait_ge(aln.h, aln.v)
                    Sc.wait_ge(cstg.h, 1)
                    Sc.activation(lns[:, 5:6], lns[:, 4:5], AF.Sqrt,
                                  bias=cst[:, 1:2], scale=1.0).then_inc(alg.h, 1)
                    alg.v += 1
                    V.wait_ge(alg.h, alg.v)
                    V.reciprocal(lns[:, 6:7], lns[:, 5:6]).then_inc(alg.h, 1)
                    alg.v += 1
                    V.wait_ge(alg.h, alg.v)
                    V.scalar_tensor_tensor(lns[:, 7:8], lns[:, 2:3], -1.0,
                                           lns[:, 6:7], ALU.mult, ALU.mult).then_inc(aln.h, 1)
                    aln.v += 1
                    Sc.wait_ge(aln.h, aln.v)
                    if i >= 4:
                        Sc.wait_ge(atp.h, 4 * (i - 3))
                    Sc.activation(xln[:, bi, :], xt[:, bi, :], AF.Identity,
                                  bias=lns[:, 7:8], scale=lns[:, 6:7]).then_inc(aap.h, 1)
                    aap.v += 1

                for t in range(4):
                    i = blk * 4 + t
                    bi = i % 4
                    T.wait_ge(aap.h, aap.v - 3 + t)
                    if i == 0:
                        T.wait_ge(pro.h, pro.v)
                        T.wait_ge(prog.h, prog.v)
                    for c8 in range(4):
                        bk = c8 % 2
                        if atp.v >= 2:
                            T.wait_ge(aev.h, atp.v - 1)
                        for cc in range(8):
                            c = c8 * 8 + cc
                            T.transpose(ptr[:, bk * 8 + cc, :],
                                        xln[:, bi, c * 128:(c + 1) * 128], idb)
                        _tlast().then_inc(atp.h, 1); atp.v += 1
                        Sc.wait_ge(atp.h, atp.v)
                        if blk > 0 and t == 0 and c8 == 0:
                            Sc.wait_ge(axw.h, blk * 16)
                        Sc.copy(
                            xfm[:, c8 * 8:(c8 + 1) * 8, t * 128:(t + 1) * 128],
                            ptr[:, bk * 8:bk * 8 + 8, :]).then_inc(aev.h, 1)
                        aev.v += 1

                Sy.wait_ge(aev.h, aev.v)
                Sy.dma_start(
                    out=scrXfm[:, :, blk * 512:(blk + 1) * 512].rearrange("c p t -> p c t"),
                    in_=xfm).then_inc(axw.h, 16)
                axw.v += 16

                T.wait_ge(aev.h, aev.v)
                if blk == 0:
                    T.wait_ge(aw.h, 64)
                for m in range(4):
                    bk = len(prj_rota) % 4
                    if len(prj_rota) >= 4:
                        T.wait_ge(*prj_rota[-4])
                    for c in range(32):
                        T.matmul(prj[:, bk, :], wkr[:, c, m * 128:(m + 1) * 128],
                                 xfm[:, c, :], start=(c == 0), stop=(c == 31))
                    _tlast().then_inc(amm.h, 1); amm.v += 1
                    V.wait_ge(amm.h, amm.v)
                    V.wait_ge(actp[blk % 2].h, (blk // 2 + 1) * 32)
                    V.wait_ge(akr.h, akr.v)
                    if akr.v >= 2:
                        n = akr.v
                        V.wait_ge(akw[n % 2].h, 32 * (n // 2))
                    for (d0, d1), (s0, s1) in (((0, 32), (32, 64)), ((32, 64), (0, 32)),
                                               ((64, 96), (96, 128)), ((96, 128), (64, 96))):
                        V.tensor_copy(rk1[d0:d1, :], prj[s0:s1, bk, :]).then_inc(alg.h, 1)
                        alg.v += 1
                    V.tensor_mul(rk2, prj[:, bk, :], coskb[:, blk % 2, :]).then_inc(alg.h, 1)
                    alg.v += 1
                    V.wait_ge(alg.h, alg.v)
                    V.tensor_mul(rk1, rk1, sinkb[:, blk % 2, :]).then_inc(alg.h, 1)
                    alg.v += 1
                    V.wait_ge(alg.h, alg.v)
                    V.tensor_add(krp[:, m % 2, :], rk2, rk1).then_inc(akr.h, 1)
                    akr.v += 1
                    prj_rota.append((akr.h, akr.v))
                    Sy.wait_ge(akr.h, akr.v)
                    n = akr.v - 1
                    Sy.dma_start(out=scrK[2 * m, 0:64, blk * 512:(blk + 1) * 512],
                                 in_=krp[0:64, m % 2, :]).then_inc(akw[n % 2].h, 16)
                    Sy.dma_start(out=scrK[2 * m + 1, 0:64, blk * 512:(blk + 1) * 512],
                                 in_=krp[64:128, m % 2, :]).then_inc(akw[n % 2].h, 16)
                    akw[n % 2].v += 32

                if blk + 2 < NBLK:
                    issue_ct(blk + 2)

                for m in range(4):
                    bk = len(prj_rota) % 4
                    T.wait_ge(*prj_rota[-4])
                    for c in range(32):
                        T.matmul(prj[:, bk, :], wdk[:, c, m * 128:(m + 1) * 128],
                                 xfm[:, c, :], start=(c == 0), stop=(c == 31))
                    _tlast().then_inc(amm.h, 1); amm.v += 1
                    V.wait_ge(amm.h, amm.v)
                    V.tensor_copy(ckv[:, m, :], prj[:, bk, :]).then_inc(ack.h, 1)
                    ack.v += 1
                    prj_rota.append((ack.h, ack.v))

                T.wait_ge(ack.h, ack.v)
                for m in range(4):
                    bk = len(prj_rota) % 4
                    T.wait_ge(*prj_rota[-4])
                    for c2 in range(4):
                        T.matmul(prj[:, bk, :], wupk[:, c2, m * 128:(m + 1) * 128],
                                 ckv[:, c2, :], start=(c2 == 0), stop=(c2 == 3))
                    _tlast().then_inc(amm.h, 1); amm.v += 1
                    V.wait_ge(amm.h, amm.v)
                    if akc.v >= 2:
                        n = akc.v
                        V.wait_ge(kcwp[n % 2].h, 32 * (n // 2))
                    V.tensor_copy(kc[:, m % 2, :], prj[:, bk, :]).then_inc(akc.h, 1)
                    akc.v += 1
                    prj_rota.append((akc.h, akc.v))
                    Sy.wait_ge(akc.h, akc.v)
                    n = akc.v - 1
                    Sy.dma_start(out=scrK[2 * m, 64:128, blk * 512:(blk + 1) * 512],
                                 in_=kc[0:64, m % 2, :]).then_inc(kcwp[n % 2].h, 16)
                    Sy.dma_start(out=scrK[2 * m + 1, 64:128, blk * 512:(blk + 1) * 512],
                                 in_=kc[64:128, m % 2, :]).then_inc(kcwp[n % 2].h, 16)
                    kcwp[n % 2].v += 32

                for t in range(4):
                    pr = 0
                    if True:
                        T.wait_ge(avv.h, avv.v)
                    for half in range(2):
                        for c2 in range(4):
                            T.matmul(vps[:, half, :],
                                     ckv[:, c2, t * 128:(t + 1) * 128],
                                     wupv[:, c2, half * 512:(half + 1) * 512],
                                     start=(c2 == 0), stop=(c2 == 3))
                    _tlast().then_inc(amm.h, 1); amm.v += 1
                    V.wait_ge(amm.h, amm.v)
                    nuse = avv.v // 2
                    if nuse >= 2:
                        V.wait_ge(avw[nuse % 2].h, 16 * (nuse // 2))
                    V.tensor_copy(vst[:, nuse % 2, 0:512], vps[:, 0, :]).then_inc(avv.h, 1)
                    avv.v += 1
                    V.tensor_copy(vst[:, nuse % 2, 512:1024], vps[:, 1, :]).then_inc(avv.h, 1)
                    avv.v += 1
                    Sy.wait_ge(avv.h, avv.v)
                    n = avv.v // 2 - 1
                    Sy.dma_start(
                        out=scrV[blk * 512 + t * 128: blk * 512 + (t + 1) * 128, :],
                        in_=vst[:, n % 2, :]).then_inc(avw[n % 2].h, 16)
                    avw[n % 2].v += 16

            for s_ in (akw[0], akw[1], kcwp[0], kcwp[1], avw[0], avw[1]):
                Sy.wait_ge(s_.h, s_.v)
            Sy.wait_ge(axw.h, axw.v)
            G.wait_ge(axp[0].h, axp[0].v)
            G.wait_ge(axp[1].h, axp[1].v)

            if DUMP == "A":
                dd = sem("dd")
                Sy.dma_start(out=dbg["dK"], in_=scrK).then_inc(dd.h, 16)
                Sy.dma_start(out=dbg["dV"], in_=scrV).then_inc(dd.h, 16)
                Sy.dma_start(out=dbg["dX"], in_=scrXfm).then_inc(dd.h, 16)
                dd.v += 48
                Sy.wait_ge(dd.h, dd.v)

        nc.all_engine_barrier()

    if "B" in PH:
        es = ExitStack()
        with es:
            def sb(name, shape, dtyp):
                return es.enter_context(nc.sbuf_tensor(name, shape, dtyp)).ap()

            def ps(name, shape, dtyp):
                return es.enter_context(nc.psum_tensor(name, shape, dtyp)).ap()

            xfq = sb("xfq_sb", [128, 32, NQ], BF16)
            kgb = sb("kg_sb", [128, 2, S], BF16)
            vgb = sb("vg_sb", [128, 2, 32, 128], BF16)
            wqb = sb("wq_sb", [128, 2, 32, 128], BF16)
            qq = sb("qq_sb", [128, 2, NQ], BF16)
            qr1 = sb("qr1_sb", [128, 512], F32)
            qr2 = sb("qr2_sb", [128, 512], F32)
            qr3 = sb("qr3_sb", [128, 512], F32)
            pp = sb("p_sb", [128, 6, 512], BF16)
            pss = sb("pss_sb", [128, NQ], F32)
            den = sb("den_sb", [128, 16], F32)
            avf = sb("avf_sb", [128, NQ], BF16)
            avt = sb("avt_sb", [128, 2, 8, 128], BF16)
            sc = ps("sc_ps", [128, 4, 512], F32)
            av = ps("av_ps", [128, 2, 512], F32)
            at = ps("at_ps", [128, 8, 128], BF16)
            pq = sc

            names = ["bxq", "bkv0", "bkv1", "bwq0", "bwq1", "bqm", "brp", "brg",
                     "bsc", "bmk", "bex", "bps", "bavm", "bpt", "bdn", "brc",
                     "bevc", "bavt", "bscl", "bao0", "bao1"]
            bs = {n: sem(n) for n in names}
            snap0 = {n: bs[n].v for n in names}
            deltas = []

            for jb in nc.Switch(engines=[Sy, Sc, nc.tensor, V, G], index=[sv] * 5, n=4):
                for n in names:
                    bs[n].v = snap0[n]
                (bxq, bkv0, bkv1, bwq0, bwq1, bqm, brp, brg, bsc, bmk, bex,
                 bps, bavm, bpt, bdn, brc, bevc, bavt, bscl, bao0, bao1) = (
                    bs[n] for n in names)
                bkvp = [bkv0, bkv1]
                bwqp = [bwq0, bwq1]
                baop = [bao0, bao1]
                base = snap0

                blocks = (jb, 7 - jb)
                spans = []
                for b in blocks:
                    sl = [(kt, 512, 0, False) for kt in range(4 * b)]
                    sl += [(4 * b + d, 512 - 128 * d, 128 * d, True) for d in range(4)]
                    spans.append(sl)

                for h2, b in enumerate(blocks):
                    Sy.dma_start(
                        out=xfq[:, :, h2 * 512:(h2 + 1) * 512],
                        in_=scrXfm[:, :, b * 512:(b + 1) * 512].rearrange("c p t -> p c t"),
                    ).then_inc(bxq.h, 16)
                    bxq.v += 16

                def load_kv(g):
                    if g >= 2:
                        Sy.wait_ge(bavm.h, base["bavm"] + 36 * 4 * (g - 1))
                    s_ = bkvp[g % 2]
                    Sy.dma_start(out=kgb[:, g % 2, :], in_=scrK[g]).then_inc(s_.h, 16)
                    Sy.dma_start(
                        out=vgb[:, g % 2, :, :],
                        in_=scrV[:, g * 128:(g + 1) * 128].rearrange("(c p) h -> p c h", p=128),
                    ).then_inc(s_.h, 16)
                    s_.v += 32

                def load_wq(h):
                    if h >= 2:
                        G.wait_ge(bqm.h, base["bqm"] + 2 * (h - 1))
                    s_ = bwqp[h % 2]
                    G.dma_start(
                        out=wqb[:, h % 2, :, :],
                        in_=wqT[:, h * 128:(h + 1) * 128].rearrange("(c p) f -> p c f", p=128),
                    ).then_inc(s_.h, 16)
                    s_.v += 16

                load_kv(0)
                load_wq(0); load_wq(1)

                for h in range(NH):
                    g = h // 4
                    if h % 4 == 0 and g + 1 < KVH:
                        load_kv(g + 1)
                    if h + 2 < NH:
                        load_wq(h + 2)

                    T.wait_ge(bwqp[h % 2].h, base["bwq%d" % (h % 2)] + (h // 2 + 1) * 16)
                    if h == 0:
                        T.wait_ge(bxq.h, base["bxq"] + 32)
                    if h >= 1:
                        T.wait_ge(bdn.h, base["bdn"] + 8 * h)
                    for span in range(2):
                        for c in range(32):
                            T.matmul(pq[:, span, :], wqb[:, h % 2, c, :],
                                     xfq[:, c, span * 512:(span + 1) * 512],
                                     start=(c == 0), stop=(c == 31))
                        _tlast().then_inc(bqm.h, 1); bqm.v += 1

                    for span in range(2):
                        V.wait_ge(bqm.h, bqm.v - 1 + span)
                        V.wait_ge(brg.h, brg.v)
                        for (d0, d1), (s0, s1) in (((0, 32), (32, 64)), ((32, 64), (0, 32)),
                                                   ((64, 96), (96, 128)), ((96, 128), (64, 96))):
                            V.tensor_copy(qr1[d0:d1, :], pq[s0:s1, span, :]).then_inc(brg.h, 1)
                            brg.v += 1
                        V.tensor_mul(qr2, pq[:, span, :],
                                     cosq[:, span * 512:(span + 1) * 512]).then_inc(brg.h, 1)
                        brg.v += 1
                        V.wait_ge(brg.h, brg.v)
                        V.tensor_mul(qr3, qr1,
                                     sinq[:, span * 512:(span + 1) * 512]).then_inc(brg.h, 1)
                        brg.v += 1
                        V.wait_ge(brg.h, brg.v)
                        if span == 0:
                            V.tensor_add(qq[:, h % 2, 0:512], qr2, qr3).then_inc(brg.h, 1)
                            brg.v += 1
                        else:
                            V.tensor_add(qq[:, h % 2, 512:1024], qr2, qr3).then_inc(brp.h, 1)
                            brp.v += 1

                    T.wait_ge(brp.h, base["brp"] + h + 1)
                    T.wait_ge(bkvp[g % 2].h, base["bkv%d" % (g % 2)] + (g // 2 + 1) * 32)
                    base_sc = bsc.v; base_ex = bex.v
                    base_av = bavm.v; base_ps = bps.v
                    slot = 0
                    nslots = sum(len(sl) for sl in spans)
                    flat = []
                    for span, sl in enumerate(spans):
                        for si, (kt, w, qoff, diag) in enumerate(sl):
                            flat.append((span, si, len(sl), kt, w, qoff, diag))

                    LAG = 3

                    def emit_av(j):
                        span, si, sln, kt, w, qoff, diag = flat[j]
                        if si == 0 and h >= 1:
                            T.wait_ge(bevc.h, base["bevc"] + 2 * (h - 1) + span + 1)
                        T.wait_ge(bex.h, base_ex + j + 1)
                        T.matmul(av[:, span, qoff:qoff + w],
                                 vgb[:, g % 2, kt, :], pp[:, j % 6, 0:w],
                                 start=(si == 0), stop=(si == sln - 1)
                                 ).then_inc(bavm.h, 1)
                        bavm.v += 1

                    for i, (span, si, sln, kt, w, qoff, diag) in enumerate(flat):
                        bk = i % 4
                        pb = i % 6
                        if i >= 4:
                            T.wait_ge(bex.h, base_ex + i - 3)
                        T.matmul(
                            sc[:, bk, 0:w],
                            kgb[:, g % 2, kt * 128:(kt + 1) * 128],
                            qq[:, h % 2, span * 512 + qoff: span * 512 + qoff + w],
                            start=True, stop=True).then_inc(bsc.h, 1)
                        bsc.v += 1
                        if diag:
                            V.wait_ge(bsc.h, base_sc + i + 1)
                            V.tensor_add(sc[:, bk, 0:128], sc[:, bk, 0:128],
                                         tri).then_inc(bmk.h, 1)
                            bmk.v += 1
                        Sc.wait_ge(bmk.h if diag else bsc.h,
                                   bmk.v if diag else base_sc + i + 1)
                        if i >= 6:
                            Sc.wait_ge(bavm.h, base_av + i - 5)
                            Sc.wait_ge(bps.h, base_ps + i - 5)
                        Sc.activation(pp[:, pb, 0:w], sc[:, bk, 0:w], AF.Exp,
                                      bias=cst[:, 0:1], scale=SCALE).then_inc(bex.h, 1)
                        bex.v += 1
                        V.wait_ge(bex.h, base_ex + i + 1)
                        V.wait_ge(bps.h, bps.v)
                        pcols = pss[:, span * 512 + qoff: span * 512 + qoff + w]
                        if si == 0:
                            if span == 0 and h >= 1:
                                V.wait_ge(bpt.h, base["bpt"] + h)
                            V.tensor_copy(pcols, pp[:, pb, 0:w]).then_inc(bps.h, 1)
                        else:
                            V.tensor_add(pcols, pcols, pp[:, pb, 0:w]).then_inc(bps.h, 1)
                        bps.v += 1
                        if i >= LAG:
                            emit_av(i - LAG)
                        if si == sln - 1:
                            pass
                        slot += 1
                    for j in range(nslots - LAG, nslots):
                        emit_av(j)
                    for span, sl in enumerate(spans):
                        V.wait_ge(bavm.h, base_av + sum(len(s) for s in spans[:span + 1]))
                        if h >= 1:
                            V.wait_ge(bavt.h, base["bavt"] + h)
                        V.tensor_copy(avf[:, span * 512:(span + 1) * 512],
                                      av[:, span, :]).then_inc(bevc.h, 1)
                        bevc.v += 1

                    T.wait_ge(bex.h, base_ex + nslots)
                    T.wait_ge(bps.h, base_ps + nslots)
                    for s8 in range(8):
                        T.transpose(sc[:, s8 // 2, (s8 % 2) * 128:(s8 % 2 + 1) * 128],
                                    pss[:, s8 * 128:(s8 + 1) * 128], idf)
                    _tlast().then_inc(bpt.h, 1); bpt.v += 1
                    V.wait_ge(bpt.h, bpt.v)
                    for s8 in range(8):
                        V.reduce_sum(den[:, s8:s8 + 1],
                                     sc[:, s8 // 2, (s8 % 2) * 128:(s8 % 2 + 1) * 128],
                                     axis=AX).then_inc(bdn.h, 1)
                        bdn.v += 1
                    V.wait_ge(bdn.h, bdn.v)
                    V.reciprocal(den[:, 8:16], den[:, 0:8]).then_inc(brc.h, 1)
                    brc.v += 1

                    T.wait_ge(bevc.h, base["bevc"] + 2 * (h + 1))
                    if h >= 1:
                        T.wait_ge(bscl.h, base["bscl"] + 8 * h)
                    for s8 in range(8):
                        T.transpose(at[:, s8, :], avf[:, s8 * 128:(s8 + 1) * 128], idb)
                    _tlast().then_inc(bavt.h, 1); bavt.v += 1
                    V.wait_ge(bavt.h, bavt.v)
                    V.wait_ge(brc.h, brc.v)
                    if h >= 2:
                        V.wait_ge(baop[h % 2].h,
                                  base["bao%d" % (h % 2)] + 16 * ((h - 2) // 2 + 1))
                    for s8 in range(8):
                        V.tensor_scalar(avt[:, h % 2, s8, :], at[:, s8, :],
                                        den[:, 8 + s8:9 + s8], None,
                                        ALU.mult).then_inc(bscl.h, 1)
                        bscl.v += 1
                    Sy.wait_ge(bscl.h, bscl.v)
                    Sy.dma_start(
                        out=scrAv[:, h * 128:(h + 1) * 128].rearrange("(s p) f -> p s f", p=128),
                        in_=avt[:, h % 2, :, :]).then_inc(baop[h % 2].h, 16)
                    baop[h % 2].v += 16

                Sy.wait_ge(bao0.h, bao0.v)
                Sy.wait_ge(bao1.h, bao1.v)
                G.wait_ge(bwq0.h, bwq0.v)
                G.wait_ge(bwq1.h, bwq1.v)
                deltas.append({n: bs[n].v - snap0[n] for n in names})

            assert all(d == deltas[0] for d in deltas[1:]), "switch bodies differ"
            for n in names:
                bs[n].v = snap0[n] + deltas[0][n]

            nc.all_engine_barrier()

            if DUMP == "B":
                dd = sem("dd2")
                Sy.dma_start(out=dbg["dAv"], in_=scrAv).then_inc(dd.h, 16)
                dd.v += 16
                Sy.wait_ge(dd.h, dd.v)
                nc.all_engine_barrier()

    if "C" in PH:
        es = ExitStack()
        with es:
            def sb(name, shape, dtyp):
                return es.enter_context(nc.sbuf_tensor(name, shape, dtyp)).ap()

            def ps(name, shape, dtyp):
                return es.enter_context(nc.psum_tensor(name, shape, dtyp)).ap()

            avT = sb("avT_sb", [128, 32, NQ], BF16)
            wo = sb("wo_sb", [128, 2, 32, 128], BF16)
            ybf = sb("ybf_sb", [128, 2, NQ], BF16)
            xqr = sb("xqr_sb", [128, 2, 8, 128], F32)
            y2t = sb("y2t_sb", [128, 2, 8, 128], F32)
            sumc = sb("sumc_sb", [128, 8, 32], F32)
            ssqc = sb("ssqc_sb", [128, 8, 32], F32)
            ctr = sb("ctrash_sb", [128, 8, 128], BF16)
            py = ps("py_ps", [128, 4, 512], F32)
            pt = ps("pt_ps", [128, 2, 8, 128], BF16)

            cav = sem("cav")
            cwop = [sem("cwo0"), sem("cwo1")]
            cxqp = [sem("cxq0"), sem("cxq1")]
            cswp = [sem("csw0"), sem("csw1")]
            cym = sem("cym"); cev = sem("cev"); ctp = sem("ctp")
            cy2 = sem("cy2"); cg = sem("cg")

            for c in range(32):
                Sy.dma_start_transpose(avT[:, c, :],
                                       scrAv[:, c * 128:(c + 1) * 128]).then_inc(cav.h, 16)
                cav.v += 16

            def load_wo(m):
                s_ = cwop[m % 2]
                G.dma_start(
                    out=wo[:, m % 2, :, :],
                    in_=woT[:, m * 128:(m + 1) * 128].rearrange("(c p) f -> p c f", p=128),
                ).then_inc(s_.h, 16)
                s_.v += 16

            def load_xq(m):
                s_ = cxqp[m % 2]
                Sy.dma_start(
                    out=xqr[:, m % 2, :, :],
                    in_=xq[:, m * 128:(m + 1) * 128].rearrange("(s p) f -> p s f", p=128),
                ).then_inc(s_.h, 16)
                s_.v += 16

            load_wo(0); load_wo(1)
            load_xq(0); load_xq(1)

            csq = sem("csq")

            def c_post(m):
                T.wait_ge(cev.h, 2 * (m + 1))
                if m >= 2:
                    T.wait_ge(cy2.h, 9 * (m - 1))
                for s in range(8):
                    T.transpose(pt[:, m % 2, s, :],
                                ybf[:, m % 2, s * 128:(s + 1) * 128], idb)
                _tlast().then_inc(ctp.h, 1); ctp.v += 1
                V.wait_ge(ctp.h, ctp.v)
                V.wait_ge(cxqp[m % 2].h, (m // 2 + 1) * 16)
                if m >= 2:
                    V.wait_ge(cswp[m % 2].h, 16 * ((m - 2) // 2 + 1))
                    V.wait_ge(csq.h, 8 * (m - 1))
                    V.wait_ge(cy2.h, 9 * (m - 1))
                V.tensor_add(y2t[:, m % 2, :, :], pt[:, m % 2, :, :],
                             xqr[:, m % 2, :, :]).then_inc(cy2.h, 1)
                cy2.v += 1
                V.wait_ge(cy2.h, cy2.v)
                Sc.wait_ge(cy2.h, cy2.v)
                Sc.wait_ge(csq.h, csq.v)
                for s in range(8):
                    V.reduce_sum(sumc[:, s, m:m + 1], y2t[:, m % 2, s, :],
                                 axis=AX).then_inc(cy2.h, 1)
                    cy2.v += 1
                    Sc.activation(ctr[:, s, :], y2t[:, m % 2, s, :], AF.Square,
                                  accum_out=ssqc[:, s, m:m + 1]).then_inc(csq.h, 1)
                    csq.v += 1
                Sy.wait_ge(cy2.h, cy2.v)
                Sy.wait_ge(csq.h, csq.v)
                Sy.dma_start(
                    out=scrY2[:, m * 128:(m + 1) * 128].rearrange("(s p) f -> p s f", p=128),
                    in_=y2t[:, m % 2, :, :]).then_inc(cswp[m % 2].h, 16)
                cswp[m % 2].v += 16
                if m + 2 < 32:
                    Sy.wait_ge(cy2.h, cy2.v)
                    load_xq(m + 2)

            for m in range(32):
                T.wait_ge(cwop[m % 2].h, (m // 2 + 1) * 16)
                if m == 0:
                    T.wait_ge(cav.h, cav.v)
                if m >= 2:
                    T.wait_ge(cev.h, 2 * (m - 1))
                for span in range(2):
                    for c in range(32):
                        T.matmul(py[:, 2 * (m % 2) + span, :], wo[:, m % 2, c, :],
                                 avT[:, c, span * 512:(span + 1) * 512],
                                 start=(c == 0), stop=(c == 31))
                _tlast().then_inc(cym.h, 1); cym.v += 1
                if m + 2 < 32:
                    G.wait_ge(cym.h, m + 1)
                    load_wo(m + 2)
                V.wait_ge(cym.h, cym.v)
                if m >= 2:
                    V.wait_ge(ctp.h, m - 1)
                V.tensor_copy(ybf[:, m % 2, 0:512], py[:, 2 * (m % 2), :]).then_inc(cev.h, 1)
                cev.v += 1
                V.tensor_copy(ybf[:, m % 2, 512:1024], py[:, 2 * (m % 2) + 1, :]).then_inc(cev.h, 1)
                cev.v += 1
                if m >= 1:
                    c_post(m - 1)
            c_post(31)

            V.wait_ge(cy2.h, cy2.v)
            V.wait_ge(csq.h, csq.v)
            for s in range(8):
                V.reduce_sum(stC[:, s, 0:1], sumc[:, s, :], axis=AX).then_inc(cg.h, 1)
                V.reduce_sum(stC[:, s, 1:2], ssqc[:, s, :], axis=AX).then_inc(cg.h, 1)
                cg.v += 2
            V.wait_ge(cg.h, cg.v)
            for s in range(8):
                V.tensor_scalar_mul(stC[:, s, 0:1], stC[:, s, 0:1], 1.0 / D).then_inc(cg.h, 1)
                V.tensor_scalar_mul(stC[:, s, 1:2], stC[:, s, 1:2], 1.0 / D).then_inc(cg.h, 1)
                cg.v += 2
            V.wait_ge(cg.h, cg.v)
            for s in range(8):
                V.scalar_tensor_tensor(stC[:, s, 2:3], stC[:, s, 0:1], 1.0,
                                       stC[:, s, 0:1], ALU.mult, ALU.mult).then_inc(cg.h, 1)
                cg.v += 1
            V.wait_ge(cg.h, cg.v)
            for s in range(8):
                V.tensor_tensor(stC[:, s, 1:2], stC[:, s, 1:2], stC[:, s, 2:3],
                                ALU.subtract).then_inc(cg.h, 1)
                cg.v += 1
            Sc.wait_ge(cg.h, cg.v)
            for s in range(8):
                Sc.activation(stC[:, s, 2:3], stC[:, s, 1:2], AF.Sqrt,
                              bias=cst[:, 1:2], scale=1.0).then_inc(cg.h, 1)
                cg.v += 1
            V.wait_ge(cg.h, cg.v)
            for s in range(8):
                V.reciprocal(stC[:, s, 3:4], stC[:, s, 2:3]).then_inc(cg.h, 1)
                cg.v += 1
            Sy.wait_ge(cswp[0].h, cswp[0].v)
            Sy.wait_ge(cswp[1].h, cswp[1].v)

            if DUMP == "C":
                dd = sem("dd3")
                Sy.dma_start(out=dbg["dY2"], in_=scrY2).then_inc(dd.h, 16)
                dd.v += 16
                Sy.wait_ge(cg.h, cg.v)
                Sy.dma_start(out=dbg["dSt"], in_=stC).then_inc(dd.h, 16)
                dd.v += 16
                Sy.wait_ge(dd.h, dd.v)

        nc.all_engine_barrier()

    if "D" in PH:
        es = ExitStack()
        with es:
            def sb(name, shape, dtyp):
                return es.enter_context(nc.sbuf_tensor(name, shape, dtyp)).ap()

            zfm = sb("zfm_sb", [128, 32, NQ], BF16)
            y2r = sb("y2r_sb", [128, 2, D], F32)
            ztm = sb("ztm_sb", [128, 2, D], BF16)
            gw1s = sb("gw1s_sb", [128, 32, 32], BF16)
            lAs = sb("lA_sb", [128, 32, 128], BF16)
            lBs = sb("lB_sb", [128, 32, 128], BF16)
            hraw = sb("hraw_sb", [32, NQ], F32)
            htm = sb("htm_sb", [128, 8, 32], F32)
            gst = sb("gst_sb", [128, 32], F32)
            gst2 = sb("gst2_sb", [128, 8], F32)
            g1a = sb("g1a_sb", [128, 8], F32)
            g2a = sb("g2a_sb", [128, 8], F32)
            hrel = sb("hrel_sb", [128, 32], F32)
            hrel2 = sb("hrel2_sb", [128, 32], F32)
            hrT = sb("hrT_sb", [33, 128], BF16)
            gts = sb("gts_sb", [128, 8, 8], F32)
            m1 = sb("m1_sb", [128, 8, 8], F32)
            m2 = sb("m2_sb", [128, 8, 8], F32)
            mc = sb("mc_sb", [128, 16], BF16)
            d21 = sb("d21_sb", [128, 8, 3], F32)
            wts = sb("wts_sb", [128, 8], F32)
            wts2 = sb("wts2_sb", [128, 8], F32)
            wT = sb("wT_sb", [8, NQ], BF16)
            cnt = sb("cnt_sb", [1, 16], F32)
            cnt2 = sb("cnt2_sb", [1, 16], F32)
            cntb = sb("cntb_sb", [1, 16], BF16)
            arep = sb("arep_sb", [128, 16], F32)
            tall = sb("tall_sb", [128, NQ], BF16)
            tp = sb("tp_sb", [128, NQ], BF16)
            wff = sb("wff_sb", [128, 2, 32, 128], BF16)
            y3bf = sb("y3bf_sb", [128, 2, NQ], BF16)
            y2rt = sb("y2rt_sb", [128, 2, 8, 128], F32)
            outt = sb("outt_sb", [128, 2, 8, 128], F32)
            dtr = sb("dtrash_sb", [128, 32], BF16)

            dld = sem("dld"); dzn = sem("dzn")
            dy2p = [sem("dy20"), sem("dy21")]
            dzt = sem("dzt"); dze = sem("dze"); dgm = sem("dgm")
            dgv = sem("dgv"); dgg = sem("dgg"); dcm = sem("dcm")
            dt1 = sem("dt1"); dwx = sem("dwx")
            dwfp = [sem("dwf0"), sem("dwf1")]
            dmm = sem("dmm"); dev2 = sem("dev2")
            dyep = [sem("dye0"), sem("dye1")]
            dstp = [sem("dst0"), sem("dst1")]
            dym = sem("dym"); dyt = sem("dyt"); dou = sem("dou")

            G.dma_start(out=gw1s, in_=gw1T.rearrange("(c p) f -> p c f", p=128)).then_inc(dld.h, 16)
            G.dma_start(out=lAs, in_=lAT.rearrange("(c p) f -> p c f", p=128)).then_inc(dld.h, 16)
            G.dma_start(out=lBs, in_=lB.rearrange("p (c f) -> p c f", f=128)).then_inc(dld.h, 16)
            dld.v += 48
            V.memset(hrT[32:33, :], 1.0)
            V.engine_nop().then_inc(dgg.h, 1); dgg.v += 1

            def load_y2(s):
                s_ = dy2p[s % 2]
                Sy.dma_start(out=y2r[:, s % 2, :],
                             in_=scrY2[s * 128:(s + 1) * 128, :]).then_inc(s_.h, 16)
                s_.v += 16

            load_y2(0); load_y2(1)

            es1 = ExitStack()
            with es1:
                def ps1(name, shape, dtyp):
                    return es1.enter_context(nc.psum_tensor(name, shape, dtyp)).ap()

                ztp = ps1("zt_ps", [128, 16, 128], BF16)
                pg = ps1("pg_ps", [32, 2, 512], F32)
                pt2 = ps1("pt2_ps", [128, 8, 32], F32)
                ptg1 = ps1("ptg1_ps", [32, 128], F32)
                pgt2 = ps1("pgt2_ps", [128, 32], F32)
                pcnt = ps1("pcnt_ps", [1, 16], F32)

                for s in range(8):
                    V.wait_ge(dy2p[s % 2].h, (s // 2 + 1) * 16)
                    if s >= 2:
                        V.wait_ge(dzt.h, 4 * (s - 1))
                    V.tensor_scalar(ztm[:, s % 2, :], y2r[:, s % 2, :],
                                    stC[:, s, 0:1], stC[:, s, 3:4],
                                    ALU.subtract, ALU.mult).then_inc(dzn.h, 1)
                    dzn.v += 1
                    if s + 2 < 8:
                        Sy.wait_ge(dzn.h, s + 1)
                        load_y2(s + 2)

                    T.wait_ge(dzn.h, dzn.v)
                    for c8 in range(4):
                        bk = c8 % 2
                        if dzt.v >= 2:
                            T.wait_ge(dze.h, dzt.v - 1)
                        for cc in range(8):
                            c = c8 * 8 + cc
                            T.transpose(ztp[:, bk * 8 + cc, :],
                                        ztm[:, s % 2, c * 128:(c + 1) * 128], idb)
                        _tlast().then_inc(dzt.h, 1); dzt.v += 1
                        V.wait_ge(dzt.h, dzt.v)
                        V.tensor_copy(zfm[:, c8 * 8:(c8 + 1) * 8, s * 128:(s + 1) * 128],
                                      ztp[:, bk * 8:bk * 8 + 8, :]).then_inc(dze.h, 1)
                        dze.v += 1

                T.wait_ge(dze.h, dze.v)
                T.wait_ge(dld.h, dld.v)
                for span in range(2):
                    for c in range(32):
                        T.matmul(pg[:, span, :], gw1s[:, c, :],
                                 zfm[:, c, span * 512:(span + 1) * 512],
                                 start=(c == 0), stop=(c == 31))
                _tlast().then_inc(dgm.h, 1); dgm.v += 1
                V.wait_ge(dgm.h, dgm.v)
                V.tensor_copy(hraw[:, 0:512], pg[:, 0, :]).then_inc(dgg.h, 1)
                dgg.v += 1
                V.tensor_copy(hraw[:, 512:1024], pg[:, 1, :]).then_inc(dgg.h, 1)
                dgg.v += 1

                T.wait_ge(dgg.h, dgg.v)
                for s in range(8):
                    T.transpose(pt2[:, s, :], hraw[:, s * 128:(s + 1) * 128], idf[0:32, 0:32])
                _tlast().then_inc(dgm.h, 1); dgm.v += 1
                V.wait_ge(dgm.h, dgm.v)
                for s in range(8):
                    V.tensor_add(htm[:, s, :], pt2[:, s, :],
                                 gcons_sb[:, 0:32]).then_inc(dgg.h, 1)
                    dgg.v += 1
                V.wait_ge(dgg.h, dgg.v)
                for s in range(8):
                    V.reduce_sum(gst[:, s:s + 1], htm[:, s, :],
                                 axis=AX).then_inc(dgg.h, 1)
                    dgg.v += 1
                    V.wait_ge(dgg.h, dgg.v)
                    V.scalar_tensor_tensor(dtr, htm[:, s, :], 1.0, htm[:, s, :],
                                           ALU.mult, ALU.mult,
                                           accum_out=gst[:, 8 + s:9 + s]).then_inc(dgg.h, 1)
                    dgg.v += 1
                    V.wait_ge(dgg.h, dgg.v)
                V.tensor_scalar_mul(gst[:, 0:8], gst[:, 0:8], 1.0 / 32).then_inc(dgg.h, 1)
                dgg.v += 1
                V.tensor_scalar_mul(gst[:, 8:16], gst[:, 8:16], 1.0 / 32).then_inc(dgg.h, 1)
                dgg.v += 1
                V.wait_ge(dgg.h, dgg.v)
                V.scalar_tensor_tensor(gst2[:, 0:8], gst[:, 0:8], 1.0,
                                       gst[:, 0:8], ALU.mult, ALU.mult).then_inc(dgg.h, 1)
                dgg.v += 1
                V.wait_ge(dgg.h, dgg.v)
                V.tensor_tensor(gst[:, 8:16], gst[:, 8:16], gst2[:, 0:8],
                                ALU.subtract).then_inc(dgg.h, 1)
                dgg.v += 1
                Sc.wait_ge(dgg.h, dgg.v)
                Sc.activation(gst[:, 16:24], gst[:, 8:16], AF.Sqrt,
                              bias=cst[:, 1:2], scale=1.0).then_inc(dgg.h, 1)
                dgg.v += 1
                V.wait_ge(dgg.h, dgg.v)
                V.reciprocal(gst[:, 24:32], gst[:, 16:24]).then_inc(dgg.h, 1)
                dgg.v += 1

                for s in range(8):
                    V.wait_ge(dgg.h, dgg.v)
                    V.wait_ge(dgv.h, dgv.v)
                    V.tensor_scalar(hrel, htm[:, s, :], gst[:, s:s + 1],
                                    gst[:, 24 + s:25 + s], ALU.subtract,
                                    ALU.mult).then_inc(dgv.h, 1)
                    dgv.v += 1
                    V.wait_ge(dgv.h, dgv.v)
                    V.scalar_tensor_tensor(hrel2, hrel, 1.0, gcons_sb[:, 32:64],
                                           ALU.mult, ALU.mult).then_inc(dgv.h, 1)
                    dgv.v += 1
                    V.wait_ge(dgv.h, dgv.v)
                    V.scalar_tensor_tensor(hrel, hrel2, 1.0, gcons_sb[:, 64:96],
                                           ALU.mult, ALU.add).then_inc(dgv.h, 1)
                    dgv.v += 1
                    V.wait_ge(dgv.h, dgv.v)
                    V.tensor_scalar_max(hrel2, hrel, 0.0).then_inc(dgv.h, 1)
                    dgv.v += 1
                    T.wait_ge(dgv.h, dgv.v)
                    T.transpose(ptg1, hrel2, idf).then_inc(dgm.h, 1); dgm.v += 1
                    V.wait_ge(dgm.h, dgm.v)
                    V.tensor_copy(hrT[0:32, :], ptg1[0:32, :]).then_inc(dgv.h, 1)
                    dgv.v += 1
                    T.wait_ge(dgv.h, dgv.v)
                    T.matmul(pgt2[:, 0:8], hrT, gw2_sb, start=True, stop=True
                             ).then_inc(dgm.h, 1)
                    dgm.v += 1
                    V.wait_ge(dgm.h, dgm.v)
                    V.tensor_copy(gts[:, s, :], pgt2[:, 0:8]).then_inc(dgv.h, 1)
                    dgv.v += 1
                    V.wait_ge(dgv.h, dgv.v)
                    V.reduce_max(d21[:, s, 0:1], gts[:, s, :], axis=AX).then_inc(dgv.h, 1)
                    dgv.v += 1
                    V.wait_ge(dgv.h, dgv.v)
                    V.tensor_scalar(m1[:, s, :], gts[:, s, :], d21[:, s, 0:1],
                                    None, ALU.is_equal).then_inc(dgv.h, 1)
                    dgv.v += 1
                    V.wait_ge(dgv.h, dgv.v)
                    V.scalar_tensor_tensor(m2[:, s, :], m1[:, s, :], -1e30,
                                           gts[:, s, :], ALU.mult, ALU.add).then_inc(dgv.h, 1)
                    dgv.v += 1
                    V.wait_ge(dgv.h, dgv.v)
                    V.reduce_max(d21[:, s, 1:2], m2[:, s, :], axis=AX).then_inc(dgv.h, 1)
                    dgv.v += 1
                    V.wait_ge(dgv.h, dgv.v)
                    V.tensor_scalar(m2[:, s, :], m2[:, s, :], d21[:, s, 1:2],
                                    None, ALU.is_equal).then_inc(dgv.h, 1)
                    dgv.v += 1
                    V.tensor_tensor(d21[:, s, 2:3], d21[:, s, 1:2], d21[:, s, 0:1],
                                    ALU.subtract).then_inc(dgv.h, 1)
                    dgv.v += 1
                    V.wait_ge(dgv.h, dgv.v)
                    V.wait_ge(dcm.h, dcm.v)
                    V.tensor_copy(mc[:, 0:8], m1[:, s, :]).then_inc(dgv.h, 1)
                    dgv.v += 1
                    V.tensor_copy(mc[:, 8:16], m2[:, s, :]).then_inc(dgv.h, 1)
                    dgv.v += 1
                    T.wait_ge(dgv.h, dgv.v)
                    T.matmul(pcnt, ones_c, mc, start=(s == 0), stop=(s == 7)
                             ).then_inc(dcm.h, 1)
                    dcm.v += 1

                Sc.wait_ge(dgv.h, dgv.v)
                Sc.activation(g1a, d21[:, :, 2:3].rearrange("p s c -> p (s c)"),
                              AF.Sigmoid, bias=0.0, scale=cst[:, 2:3]).then_inc(dgg.h, 1)
                dgg.v += 1
                Sc.activation(g2a, d21[:, :, 2:3].rearrange("p s c -> p (s c)"),
                              AF.Sigmoid, bias=0.0, scale=1.0).then_inc(dgg.h, 1)
                dgg.v += 1

                dccs = sem("dccs"); dccc = sem("dccc"); dccg = sem("dccg")
                V.wait_ge(dcm.h, dcm.v)
                V.tensor_copy(cnt, pcnt).then_inc(dgv.h, 1)
                dgv.v += 1
                Sy.wait_ge(dgv.h, dgv.v)
                Sy.dma_start(out=cntI, in_=cnt).then_inc(dccs.h, 16)
                dccs.v += 16
                G.wait_ge(dccs.h, dccs.v)
                if RGN > 1:
                    G.collective_compute(
                        "AllReduce", ALU.add, replica_groups=[list(range(RGN))],
                        ins=[cntI], outs=[cntO]).then_inc(dccc.h, 1)
                    dccc.v += 1
                    G.wait_ge(dccc.h, dccc.v)
                    G.dma_start(out=cnt2, in_=cntO).then_inc(dccg.h, 16)
                else:
                    G.dma_start(out=cnt2, in_=cntI).then_inc(dccg.h, 16)
                dccg.v += 16
                V.wait_ge(dccg.h, dccg.v)
                V.tensor_scalar(cnt, cnt2, cst[0:1, 3:4], None,
                                ALU.is_le).then_inc(dgv.h, 1)
                dgv.v += 1
                V.wait_ge(dgv.h, dgv.v)
                V.tensor_copy(cntb, cnt).then_inc(dgv.h, 1)
                dgv.v += 1
                T.wait_ge(dgv.h, dgv.v)
                T.matmul(pgt2[:, 0:16], ones_r, cntb, start=True, stop=True
                         ).then_inc(dgm.h, 1)
                dgm.v += 1
                V.wait_ge(dgm.h, dgm.v)
                V.tensor_copy(arep, pgt2[:, 0:16]).then_inc(dgv.h, 1)
                dgv.v += 1

                for s in range(8):
                    V.wait_ge(dgg.h, dgg.v)
                    V.wait_ge(dgv.h, dgv.v)
                    V.tensor_scalar(wts, m1[:, s, :], g1a[:, s:s + 1],
                                    None, ALU.mult).then_inc(dgv.h, 1)
                    dgv.v += 1
                    V.tensor_scalar(wts2, m2[:, s, :], g2a[:, s:s + 1],
                                    None, ALU.mult).then_inc(dgv.h, 1)
                    dgv.v += 1
                    V.wait_ge(dgv.h, dgv.v)
                    V.tensor_mul(wts, wts, arep[:, 0:8]).then_inc(dgv.h, 1)
                    dgv.v += 1
                    V.tensor_mul(wts2, wts2, arep[:, 8:16]).then_inc(dgv.h, 1)
                    dgv.v += 1
                    V.wait_ge(dgv.h, dgv.v)
                    V.tensor_add(wts, wts, wts2).then_inc(dgv.h, 1)
                    dgv.v += 1
                    T.wait_ge(dgv.h, dgv.v)
                    T.transpose(ptg1[0:8, :], wts, idf).then_inc(dgm.h, 1)
                    dgm.v += 1
                    V.wait_ge(dgm.h, dgm.v)
                    V.tensor_copy(wT[:, s * 128:(s + 1) * 128],
                                  ptg1[0:8, :]).then_inc(dgv.h, 1)
                    dgv.v += 1

            es2 = ExitStack()
            with es2:
                def ps2(name, shape, dtyp):
                    return es2.enter_context(nc.psum_tensor(name, shape, dtyp)).ap()

                py3 = ps2("py3_ps", [128, 4, 512], F32)
                pt3 = ps2("pt3_ps", [128, 2, 8, 128], BF16)
                pwx = ps2("pwx_ps", [128, 2, 512], F32)

                T.wait_ge(dld.h, dld.v)
                for span in range(2):
                    for c in range(32):
                        T.matmul(py3[:, span, :], lAs[:, c, :],
                                 zfm[:, c, span * 512:(span + 1) * 512],
                                 start=(c == 0), stop=(c == 31))
                _tlast().then_inc(dt1.h, 1); dt1.v += 1
                V.wait_ge(dt1.h, dt1.v)
                V.tensor_copy(tall[:, 0:512], py3[:, 0, :]).then_inc(dt1.h, 1)
                dt1.v += 1
                V.tensor_copy(tall[:, 512:1024], py3[:, 1, :]).then_inc(dt1.h, 1)
                dt1.v += 1

                T.wait_ge(dgv.h, dgv.v)
                T.wait_ge(dt1.h, dt1.v)
                for span in range(2):
                    T.matmul(pwx[:, span, :], e16,
                             wT[:, span * 512:(span + 1) * 512],
                             start=True, stop=True)
                _tlast().then_inc(dwx.h, 1); dwx.v += 1
                V.wait_ge(dwx.h, dwx.v)
                V.tensor_mul(tp[:, 0:512], tall[:, 0:512], pwx[:, 0, :]).then_inc(dwx.h, 1)
                dwx.v += 1
                V.tensor_mul(tp[:, 512:1024], tall[:, 512:1024],
                             pwx[:, 1, :]).then_inc(dwx.h, 1)
                dwx.v += 1

                def load_wff(m):
                    s_ = dwfp[m % 2]
                    G.dma_start(
                        out=wff[:, m % 2, :, :],
                        in_=wffT[:, m * 128:(m + 1) * 128].rearrange("(c p) f -> p c f", p=128),
                    ).then_inc(s_.h, 16)
                    s_.v += 16

                def load_y2t(m):
                    s_ = dyep[m % 2]
                    Sy.dma_start(
                        out=y2rt[:, m % 2, :, :],
                        in_=scrY2[:, m * 128:(m + 1) * 128].rearrange("(s p) f -> p s f", p=128),
                    ).then_inc(s_.h, 16)
                    s_.v += 16

                load_wff(0); load_wff(1)
                load_y2t(0); load_y2t(1)

                def d_post(m):
                    T.wait_ge(dev2.h, 2 * (m + 1))
                    if m == 0:
                        T.wait_ge(dgv.h, dgv.v)
                    if m >= 2:
                        T.wait_ge(dou.h, m - 1)
                    for s in range(8):
                        T.transpose(pt3[:, m % 2, s, :],
                                    y3bf[:, m % 2, s * 128:(s + 1) * 128], idb)
                    _tlast().then_inc(dyt.h, 1); dyt.v += 1
                    V.wait_ge(dyt.h, dyt.v)
                    V.wait_ge(dyep[m % 2].h, (m // 2 + 1) * 16)
                    if m >= 2:
                        V.wait_ge(dstp[m % 2].h, 16 * ((m - 2) // 2 + 1))
                    V.tensor_add(outt[:, m % 2, :, :], pt3[:, m % 2, :, :],
                                 y2rt[:, m % 2, :, :]).then_inc(dou.h, 1)
                    dou.v += 1
                    if m + 2 < 32:
                        Sy.wait_ge(dou.h, dou.v)
                        load_y2t(m + 2)
                    Sy.wait_ge(dou.h, dou.v)
                    Sy.dma_start(
                        out=out[:, m * 128:(m + 1) * 128].rearrange("(s p) f -> p s f", p=128),
                        in_=outt[:, m % 2, :, :]).then_inc(dstp[m % 2].h, 16)
                    dstp[m % 2].v += 16

                for m in range(32):
                    T.wait_ge(dwfp[m % 2].h, (m // 2 + 1) * 16)
                    if m == 0:
                        T.wait_ge(dt1.h, dt1.v)
                    if m >= 2:
                        T.wait_ge(dev2.h, 2 * (m - 1))
                    for span in range(2):
                        for c in range(32):
                            T.matmul(py3[:, 2 * (m % 2) + span, :], wff[:, m % 2, c, :],
                                     zfm[:, c, span * 512:(span + 1) * 512],
                                     start=(c == 0), stop=False)
                        T.wait_ge(dwx.h, dwx.v)
                        T.matmul(py3[:, 2 * (m % 2) + span, :], lBs[:, m, :],
                                 tp[:, span * 512:(span + 1) * 512],
                                 start=False, stop=True)
                    _tlast().then_inc(dmm.h, 1); dmm.v += 1
                    if m + 2 < 32:
                        G.wait_ge(dmm.h, m + 1)
                        load_wff(m + 2)
                    V.wait_ge(dmm.h, dmm.v)
                    if m >= 2:
                        V.wait_ge(dyt.h, m - 1)
                    V.tensor_copy(y3bf[:, m % 2, 0:512],
                                  py3[:, 2 * (m % 2), :]).then_inc(dev2.h, 1)
                    dev2.v += 1
                    V.tensor_copy(y3bf[:, m % 2, 512:1024],
                                  py3[:, 2 * (m % 2) + 1, :]).then_inc(dev2.h, 1)
                    dev2.v += 1
                    if m >= 1:
                        d_post(m - 1)
                d_post(31)

                Sy.wait_ge(dstp[0].h, dstp[0].v)
                Sy.wait_ge(dstp[1].h, dstp[1].v)
                G.wait_ge(dwfp[0].h, dwfp[0].v)
                G.wait_ge(dwfp[1].h, dwfp[1].v)

    return nc, dbg




def _host_prep(inputs):
    x = np.ascontiguousarray(np.asarray(inputs["L0_attn_hidden_states"], dtype=np.float32))
    Wq = np.asarray(inputs["L0_attn_Wq"], dtype=np.float32)
    Wkr = np.asarray(inputs["L0_attn_Wkr"], dtype=np.float32)
    Wdk = np.asarray(inputs["L0_attn_Wdk"], dtype=np.float32)
    Wupk = np.asarray(inputs["L0_attn_Wupk"], dtype=np.float32)
    Wupv = np.asarray(inputs["L0_attn_Wupv"], dtype=np.float32)
    Wo = np.asarray(inputs["L0_attn_Wo"], dtype=np.float32)
    Wff = np.asarray(inputs["L0_ffn_weight"], dtype=np.float32)
    lora_A = np.asarray(inputs["L0_ffn_lora_A"], dtype=np.float32)
    lora_B = np.asarray(inputs["L0_ffn_lora_B"], dtype=np.float32)
    gw1 = np.asarray(inputs["L0_ffn_gw1"], dtype=np.float32)
    gb1 = np.asarray(inputs["L0_ffn_gb1"], dtype=np.float32)
    lng = np.asarray(inputs["L0_ffn_ln_gamma"], dtype=np.float32)
    lnb = np.asarray(inputs["L0_ffn_ln_beta"], dtype=np.float32)
    gw2 = np.asarray(inputs["L0_ffn_gw2"], dtype=np.float32)
    gb2 = np.asarray(inputs["L0_ffn_gb2"], dtype=np.float32)

    blockperm = np.concatenate([np.arange(0, 32), np.arange(64, 96),
                                np.arange(32, 64), np.arange(96, 128)])
    qperm = np.concatenate([h * 128 + blockperm for h in range(32)])

    C = np.ascontiguousarray
    common = {
        "wqT": C(Wq.T[:, qperm]),
        "wkrT": C(Wkr.T),
        "wdkT": C(Wdk.T),
        "wupkT": C(Wupk.T),
        "wupvT": C(Wupv.T),
        "woT": C(Wo.T),
        "wffT": C(Wff.T),
        "lAT": C(lora_A.transpose(1, 0, 2).reshape(D, 128)),
        "lB": C(lora_B.reshape(128, D)),
        "gw1T": C(gw1.T),
        "gw2aug": C(np.concatenate([gw2.T, gb2[None, :]], axis=0)),
        "gcons": C(np.broadcast_to(
            np.concatenate([gb1, lng, lnb])[None, :], (128, 96)).copy()),
    }

    p = np.arange(128)
    common["tri"] = C(np.where(p[:, None] <= p[None, :], 0.0, NEG).astype(np.float32))
    import ml_dtypes
    common["ident"] = np.eye(128, dtype=ml_dtypes.bfloat16)
    common["identf"] = np.eye(128, dtype=np.float32)
    e16m = np.zeros((8, 128), dtype=np.float32)
    for e in range(8):
        e16m[e, e * 16:(e + 1) * 16] = 1.0 / 16.0
    common["e16"] = e16m
    common["onescol"] = np.ones((128, 1), dtype=np.float32)
    common["onesrow"] = np.ones((1, 128), dtype=np.float32)

    invf = (10000.0 ** (-np.arange(0, HD, 2, dtype=np.float32) / HD))
    pos = np.arange(S, dtype=np.float32)
    i32 = np.arange(32)
    qfreq = np.concatenate([invf[i32], invf[i32], invf[32 + i32], invf[32 + i32]])
    qsign = np.concatenate([-np.ones(32), np.ones(32), -np.ones(32), np.ones(32)])
    kfreq = np.concatenate([invf[i32]] * 4)
    ksign = np.concatenate([-np.ones(32), np.ones(32)] * 2)

    ang_k = kfreq[:, None] * pos[None, :]
    common["coskD"] = C(np.cos(ang_k).astype(np.float32))
    common["sinkD"] = C((np.sin(ang_k) * ksign[:, None]).astype(np.float32))

    per_core = []
    qpos_all = []
    for c in range(8):
        beta, j = c // 4, c % 4
        qpos = np.concatenate([np.arange(j * 512, (j + 1) * 512),
                               np.arange((7 - j) * 512, (8 - j) * 512)])
        qpos_all.append((beta, qpos))
        ang_q = qfreq[:, None] * qpos[None, :].astype(np.float32)
        d = dict(common)
        d["meta"] = np.array([[j]], dtype=np.int32)
        d["xb"] = C(x[beta])
        d["xq"] = C(x[beta][qpos])
        d["cosqD"] = C(np.cos(ang_q).astype(np.float32))
        d["sinqD"] = C((np.sin(ang_q) * qsign[:, None]).astype(np.float32))
        per_core.append(d)
    return per_core, qpos_all


def kernel(**inputs):
    per_core, qpos_all = _host_prep(inputs)
    nc, _ = _build()
    res = run_bass_kernel_spmd(nc, per_core, core_ids=list(range(8)))
    out = np.empty((B, S, D), dtype=np.float32)
    for c in range(8):
        beta, qpos = qpos_all[c]
        out[beta, qpos] = res.results[c]["out"]
    return out
